# version 18
# baseline (speedup 1.0000x reference)
import numpy as np
import ml_dtypes
import concourse.bass as bass
import concourse.mybir as mybir
from concourse.bass_utils import run_bass_kernel_spmd

F32 = mybir.dt.float32
BF16 = mybir.dt.bfloat16
AF = mybir.ActivationFunctionType
ALU = mybir.AluOpType
AX = mybir.AxisListType
BF16NP = ml_dtypes.bfloat16

L = 2048
D = 1024
DM = 512
DI = 1024
DS = 16
DTR = 32
TCH = 512
NCH = L // TCH
HEADS = 8
DH = 64
WIN = 512
NW = L // WIN
HID = 2730
NPAIR = 22
HIDP = NPAIR * 128
XFALL = ["xfscr"] + [f"xfr{m}" for m in range(8)]
EPS = 1e-5

DEBUG_REQUEST = []
STRICT_SAME_ENGINE = [True]
ENGINES = ("tensor", "vector", "scalar", "gpsimd", "sync")


class Sched:
    def __init__(self):
        self.units = []
        self.cnt = {e: 0 for e in ENGINES}
        self.last_w = {}
        self.readers = {}

    def op(self, engine, emit, reads=(), writes=(), dma=False, ses=False,
           sem=None):
        assert engine in ENGINES
        key = sem if sem is not None else engine
        inc = 16 if dma else 1
        waits = {}
        strict = ses or STRICT_SAME_ENGINE[0] or engine != "vector"

        def need(eng2, cnt2):
            if eng2 == engine and engine != "sync" and not strict:
                return
            if waits.get(eng2, 0) < cnt2:
                waits[eng2] = cnt2

        for t in reads:
            w = self.last_w.get(t)
            if w is not None:
                need(*w)
        for t in writes:
            w = self.last_w.get(t)
            if w is not None:
                need(*w)
            for eng2, cnt2 in self.readers.get(t, {}).items():
                need(eng2, cnt2)

        self.cnt[key] = self.cnt.get(key, 0) + inc
        me = (key, self.cnt[key])
        for t in reads:
            self.readers.setdefault(t, {})[key] = self.cnt[key]
        for t in writes:
            self.last_w[t] = me
            self.readers[t] = {}
        self.units.append((engine, emit, inc, waits, key))

    def barrier(self):
        snapshot = dict(self.cnt)
        for e in ENGINES:
            waits = {e2: c for e2, c in snapshot.items()
                     if (e2 != e or e == "sync") and c > 0}
            self.cnt[e] += 1
            self.units.append((e, None, 1, waits, e))
        self.last_w = {}
        self.readers = {}

    def finalize(self, block, sems):
        per_engine = {e: [] for e in ENGINES}
        for engine, emit, inc, waits, key in self.units:
            per_engine[engine].append((emit, inc, waits, key))
        final_cnt = dict(self.cnt)

        def body_for(engname):
            def body(eng):
                watermark = {e: 0 for e in final_cnt}
                for emit, inc, waits, key in per_engine[engname]:
                    for e2, c2 in sorted(waits.items()):
                        if watermark[e2] < c2:
                            eng.wait_ge(sems[e2], c2)
                            watermark[e2] = c2
                    inst = eng.nop() if emit is None else emit(eng)
                    inst.then_inc(sems[key], inc)
                if engname == "sync":
                    for e2, c2 in final_cnt.items():
                        eng.wait_ge(sems[e2], c2)
            return body

        block.sync(body_for("sync"))
        block.tensor(body_for("tensor"))
        block.vector(body_for("vector"))
        block.scalar(body_for("scalar"))
        block.gpsimd(body_for("gpsimd"))



def _rotary_tables():
    t = np.arange(L, dtype=np.float64)
    half = np.arange(0, DH, 2, dtype=np.float64)
    inv_freq = 1.0 / (10000.0 ** (half / DH))
    freqs = t[:, None] * inv_freq[None, :]
    freqs = np.concatenate([freqs, freqs], axis=1)
    base = (half + 0.4 * DH) / (1.4 * DH)
    power = (t - L // 2) / float(WIN // 2)
    scale = base[None, :] ** power[:, None]
    scale = np.concatenate([scale, scale], axis=1)
    cos = np.cos(freqs)
    sin = np.sin(freqs)
    sign = np.ones((DH,))
    sign[:DH // 2] = -1.0
    qa = cos * scale * 8.0
    qb = sign[None, :] * sin * scale * 8.0
    ka = cos / scale
    kb = sign[None, :] * sin / scale

    def rep(x):
        return np.ascontiguousarray(np.tile(x.astype(np.float32), (1, HEADS)))
    return rep(qa), rep(qb), rep(ka), rep(kb)


def _band_masks():
    i = np.arange(128)[:, None]
    j = np.arange(128)[None, :]
    maskL = (j <= i).astype(BF16NP)
    maskU = (j >= i).astype(BF16NP)
    return maskL, maskU


def _pg(x):
    x = np.asarray(x)
    if x.ndim == 1:
        x = x[:, None]
    k = x.shape[1]
    return np.ascontiguousarray(
        x.reshape(8, 128, k).transpose(1, 0, 2).reshape(128, 8 * k))


def _w1_perm_cols(w1):
    a = w1[:, :HID]
    g = w1[:, HID:]
    cols = []
    zpad = np.zeros((w1.shape[0], 128 - 42), w1.dtype)
    for j in range(21):
        cols.append(a[:, 128 * j:128 * (j + 1)])
        cols.append(g[:, 128 * j:128 * (j + 1)])
    cols.append(np.concatenate([a[:, 2688:], zpad], axis=1))
    cols.append(np.concatenate([g[:, 2688:], zpad], axis=1))
    return np.ascontiguousarray(np.concatenate(cols, axis=1))


def _b1_perm(b1):
    a = b1[:HID]
    g = b1[HID:]
    cols = []
    for j in range(21):
        cols.append(a[128 * j:128 * (j + 1)])
        cols.append(g[128 * j:128 * (j + 1)])
    pad = np.zeros(128 - 42, b1.dtype)
    cols.append(np.concatenate([a[2688:], pad]))
    cols.append(np.concatenate([g[2688:], pad]))
    return np.ascontiguousarray(np.stack(cols, axis=1))


def _w2_perm_rows(w2):
    rows = [w2[128 * j:128 * (j + 1)] for j in range(21)]
    rows.append(np.concatenate(
        [w2[2688:], np.zeros((128 - 42, w2.shape[1]), w2.dtype)], axis=0))
    return np.ascontiguousarray(np.concatenate(rows, axis=0))



class Build:
    def __init__(self, nc, S, stk, dbg):
        self.nc = nc
        self.S = S
        self.stk = stk
        self.dbg = list(dbg)
        self.dram = {}

    def sb(self, stk, name, shape, dt=F32):
        return stk.enter_context(self.nc.sbuf_tensor(name, shape, dt))

    def ps(self, stk, name, shape, dt=F32):
        return stk.enter_context(self.nc.psum_tensor(name, shape, dt))

    def din(self, name, shape, dt=F32):
        h = self.nc.dram_tensor(name, list(shape), dt, kind="ExternalInput")
        self.dram[name] = h
        return h

    def dscratch(self, name, shape, dt=F32):
        h = self.nc.dram_tensor(name, list(shape), dt, kind="Internal")
        self.dram[name] = h
        return h

    def dout(self, name, shape, dt=F32):
        h = self.nc.dram_tensor(name, list(shape), dt, kind="ExternalOutput")
        self.dram[name] = h
        return h

    def dump(self, name, ap, shape, dt=F32, tags=()):
        if name not in self.dbg:
            return
        h = self.dout("dbg_" + name, shape, dt)
        self.S.op("sync", lambda e, h=h, ap=ap: e.dma_start(out=h.ap(), in_=ap),
                  reads=list(tags), dma=True)


def build_program(dbg=()):
    from contextlib import ExitStack
    nc = bass.Bass("TRN2", target_bir_lowering=False)
    S = Sched()
    stk = ExitStack()
    B = Build(nc, S, stk, dbg)

    B.din("x", (L, D))
    for d in (1, 2):
        B.din(f"inwT{d}", (DM, 2 * DI))
        B.din(f"xprojT{d}", (DI, 2 * DTR))
        B.din(f"dtwT{d}", (DTR, DI))
        B.din(f"outwT{d}", (DI, DM))
        B.din(f"convw{d}", (128, 32))
        B.din(f"convb{d}", (128, 8))
        B.din(f"dtb{d}", (128, 8))
        B.din(f"dd{d}", (128, 8))
        B.din(f"alog{d}", (128, 128))
    for l in (0, 1):
        B.din(f"wqkv{l}", (D, 3 * 512))
        B.din(f"wo{l}", (512, D))
        B.din(f"w1p{l}", (D, NPAIR * 256))
        B.din(f"b1p{l}", (128, 2 * NPAIR))
        B.din(f"w2p{l}", (HIDP, D))
        B.din(f"lng{l}", (128, 8))
        B.din(f"lnb{l}", (128, 8))
        B.din(f"b2{l}", (128, 8))
        B.din(f"qsrow{l}", (1, 512))
        B.din(f"ksrow{l}", (1, 512))
    B.din("rqa", (L, 512))
    B.din("rqb", (L, 512))
    B.din("rka", (L, 512))
    B.din("rkb", (L, 512))
    B.din("bsel", (32, 32 * 128), BF16)
    B.din("lnsel", (64, 128), BF16)
    B.din("identb", (128, 128), BF16)
    B.din("identf", (128, 128), F32)
    B.din("jflipb", (128, 128), BF16)
    B.din("maskl", (128, 128), BF16)
    B.din("masku", (128, 128), BF16)
    B.din("masklaf", (128, 128), F32)
    B.din("maskuaf", (128, 128), F32)

    B.dscratch("ln1scr", (L, D), BF16)
    B.dscratch("x1scr", (D, L))
    B.dscratch("xfscr", (D, L))
    out = B.dout("out", (L, D))

    with stk:
        B.gstk = stk
        sems = {e: stk.enter_context(nc.semaphore("sem_" + e)) for e in ENGINES}
        for k in range(4):
            sems[f"dsl{k}"] = stk.enter_context(nc.semaphore(f"sem_dsl{k}"))
        for k in range(2):
            sems[f"dtb{k}"] = stk.enter_context(nc.semaphore(f"sem_dtb{k}"))
        for k in range(2):
            sems[f"dj{k}"] = stk.enter_context(nc.semaphore(f"sem_dj{k}"))
        block = stk.enter_context(nc.Block())
        C = {}
        C["identb"] = B.sb(stk, "c_identb", [128, 128], BF16)
        C["identf"] = B.sb(stk, "c_identf", [128, 128], F32)
        C["jflipb"] = B.sb(stk, "c_jflipb", [128, 128], BF16)
        C["maskl"] = B.sb(stk, "c_maskl", [128, 128], BF16)
        C["masku"] = B.sb(stk, "c_masku", [128, 128], BF16)
        C["masklaf"] = B.sb(stk, "c_masklaf", [128, 128], F32)
        C["maskuaf"] = B.sb(stk, "c_maskuaf", [128, 128], F32)
        C["cst"] = B.sb(stk, "c_cst", [128, 2], F32)
        C["lnsel"] = B.sb(stk, "c_lnsel", [64, 128], BF16)
        C["onesb"] = B.sb(stk, "c_onesb", [128, 1], BF16)
        C["onesr"] = B.sb(stk, "c_onesr", [1, 128], BF16)
        B.common = C
        B.mjw = (B.sb(stk, "mj_wst0", [128, 256]),
                 B.sb(stk, "mj_wsb0", [128, 256], BF16))
        for nm in ("identb", "identf", "jflipb", "maskl", "masku",
                   "masklaf", "maskuaf", "lnsel"):
            S.op("sync", lambda e, nm=nm: e.dma_start(
                out=C[nm][:], in_=B.dram[nm].ap()), writes=[nm], dma=True)
        S.op("vector", lambda e: e.memset(C["onesb"][:], 1.0), writes=["onesb"])
        S.op("vector", lambda e: e.memset(C["onesr"][:], 1.0), writes=["onesb"])
        S.op("vector", lambda e: e.memset(C["cst"][:, 0:1], EPS), writes=["cst"])
        S.op("vector", lambda e: e.memset(C["cst"][:, 1:2], 1.0), writes=["cst"])
        build_phase0(B)
        build_mamba(B)
        build_ffn(B)
        build_final(B, out)
        S.finalize(block, sems)
    return nc, B



def build_phase0(B):
    from contextlib import ExitStack
    nc, S = B.nc, B.S
    x = B.dram["x"]
    ln1scr = B.dram["ln1scr"]
    with ExitStack() as stk:
        xt = B.sb(stk, "p0_x", [128, D])
        x2t = B.sb(stk, "p0_x2", [128, D])
        lnt = B.sb(stk, "p0_ln", [128, D], BF16)
        st = B.sb(stk, "p0_st", [128, 8])
        cst = B.common["cst"][:, 0:1]

        jobs = []
        for l in (0, 1):
            B.dscratch(f"w1pb{l}", (D, NPAIR * 256), BF16)
            B.dscratch(f"w2pb{l}", (HIDP, D), BF16)
            B.dscratch(f"wqkvbb{l}", (D, 3 * 512), BF16)
            B.dscratch(f"wobb{l}", (512, D), BF16)
            def tile(srcnm, dstnm, rows, width):
                for r in range(rows):
                    for c0 in range(0, width, 256):
                        wdt = min(256, width - c0)
                        jobs.append((
                            B.dram[srcnm].ap()[128 * r:128 * (r + 1),
                                               c0:c0 + wdt],
                            B.dram[dstnm].ap()[128 * r:128 * (r + 1),
                                               c0:c0 + wdt], wdt))
            tile(f"w1p{l}", f"w1pb{l}", 8, NPAIR * 256)
            tile(f"w2p{l}", f"w2pb{l}", 22, 1024)
            tile(f"wqkv{l}", f"wqkvbb{l}", 8, 1536)
            tile(f"wo{l}", f"wobb{l}", 4, 1024)
        B.wjobs = jobs

        for i in range(L // 128):
            S.op("sync", lambda e, i=i: e.dma_start(
                out=xt[:], in_=x.ap()[128 * i:128 * (i + 1), :]),
                writes=["p0x"], dma=True)
            S.op("vector", lambda e: e.reduce_sum(st[:, 0:1], xt[:], axis=AX.X),
                 reads=["p0x"], writes=["p0st"], ses=True)
            S.op("scalar", lambda e: e.activation(x2t[:], xt[:], AF.Square),
                 reads=["p0x"], writes=["p0x2"])
            S.op("vector", lambda e: e.reduce_sum(st[:, 1:2], x2t[:], axis=AX.X),
                 reads=["p0x2"], writes=["p0st"], ses=True)
            S.op("vector", lambda e: e.tensor_scalar_mul(st[:, 2:3], st[:, 0:1], 1.0 / D),
                 reads=["p0st"], writes=["p0st"], ses=True)
            S.op("vector", lambda e: e.tensor_scalar_mul(st[:, 3:4], st[:, 1:2], 1.0 / D),
                 reads=["p0st"], writes=["p0st"], ses=True)
            S.op("vector", lambda e: e.tensor_tensor(
                st[:, 4:5], st[:, 2:3], st[:, 2:3], ALU.mult),
                 reads=["p0st"], writes=["p0st"], ses=True)
            S.op("vector", lambda e: e.tensor_sub(st[:, 3:4], st[:, 3:4], st[:, 4:5]),
                 reads=["p0st"], writes=["p0st"], ses=True)
            S.op("scalar", lambda e: e.activation(st[:, 5:6], st[:, 3:4], AF.Sqrt, bias=cst),
                 reads=["p0st", "cst"], writes=["p0st"])
            S.op("vector", lambda e: e.reciprocal(st[:, 6:7], st[:, 5:6]),
                 reads=["p0st"], writes=["p0st"], ses=True)
            S.op("vector", lambda e: e.scalar_tensor_tensor(
                st[:, 7:8], st[:, 2:3], -1.0, st[:, 6:7], ALU.mult, ALU.mult),
                 reads=["p0st"], writes=["p0st"], ses=True)
            S.op("vector", lambda e: e.tensor_scalar(
                lnt[:], xt[:], st[:, 6:7], st[:, 7:8], ALU.mult, ALU.add),
                 reads=["p0x", "p0st"], writes=["p0ln"], ses=True)
            S.op("sync", lambda e, i=i: e.dma_start(
                out=ln1scr.ap()[128 * i:128 * (i + 1), :], in_=lnt[:]),
                reads=["p0ln"], writes=["ln1scr"], dma=True)
    S.barrier()



def build_mamba(B):
    from contextlib import ExitStack
    nc, S = B.nc, B.S
    dr = B.dram
    with ExitStack() as stk:
        E = {}
        sb, ps = B.sb, B.ps
        E["identb"] = B.common["identb"]
        E["identf"] = B.common["identf"]
        E["jflipb"] = B.common["jflipb"]
        E["onesb"] = B.common["onesb"]
        E["onesr"] = B.common["onesr"]
        E["cst"] = B.common["cst"]
        E["sm1"] = sb(stk, "m_small1", [128, 64])
        E["sm2"] = sb(stk, "m_small2", [128, 64])
        E["A1"] = sb(stk, "m_A1", [128, 128])
        E["A2"] = sb(stk, "m_A2", [128, 128])
        E["xprojT"] = sb(stk, "m_xprojT", [128, 2 * 512], BF16)
        E["dtwT"] = sb(stk, "m_dtwT", [32, 2 * DI], BF16)
        E["inwTb"] = sb(stk, "m_inwT", [128, 4 * 1024], BF16)
        E["outwTb"] = sb(stk, "m_outwT", [128, 8 * 512], BF16)
        E["ln1f"] = sb(stk, "m_ln1f", [128, 4 * TCH], BF16)
        E["lntm"] = sb(stk, "m_lntm", [128, 512], BF16)
        E["xh"] = sb(stk, "m_xh", [128, 8 * (TCH + 3)])
        E["halo"] = sb(stk, "m_halo", [128, 2 * 24])
        E["xc"] = sb(stk, "m_xc", [128, 8 * TCH])
        E["sz"] = sb(stk, "m_sz", [128, 8 * TCH])
        E["du"] = sb(stk, "m_du", [128, 8 * TCH])
        E["ubuf"] = sb(stk, "m_ubuf", [128, 8 * TCH])
        E["ybuf"] = sb(stk, "m_y", [128, 8 * TCH])
        E["bsel"] = sb(stk, "m_bsel", [32, 32 * 128], BF16)
        E["bcf32"] = sb(stk, "m_bcf32", [32, TCH])
        E["bchi"] = sb(stk, "m_bchi", [32, TCH], BF16)
        E["bclo"] = sb(stk, "m_bclo", [32, TCH], BF16)
        E["dbcb"] = sb(stk, "m_dbcb", [32, TCH], BF16)
        E["sp"] = sb(stk, "m_sp", [128, 2 * TCH])
        E["H"] = sb(stk, "m_H", [128, 3 * 2 * TCH])
        E["dA"] = sb(stk, "m_dA", [128, 2 * 2 * TCH])
        E["dBu"] = sb(stk, "m_dBu", [128, 2 * 2 * TCH])
        E["bcB"] = sb(stk, "m_bcB", [128, 2 * TCH])
        E["bcC"] = sb(stk, "m_bcC", [128, 2 * TCH])
        E["carry"] = sb(stk, "m_carry", [128, 2 * 128])
        E["ygb"] = sb(stk, "m_ygb", [128, 8 * TCH], BF16)
        E["hid"] = sb(stk, "m_hid", [128, 8 * TCH])
        E["srow"] = E["bcC"][0:1, 0:512]
        E["srow2"] = E["bcC"][0:1, 512:1024]
        E["rw0"] = E["bcB"][0:1, 0:256].bitcast(BF16)
        E["rw1"] = E["bcB"][0:1, 256:512].bitcast(BF16)
        E["rw2"] = E["bcB"][0:1, 512:768].bitcast(BF16)
        E["rw3"] = E["bcB"][0:1, 768:1024].bitcast(BF16)

        E["hbb"] = E["ygb"]
        E["xcb"] = E["ygb"]
        E["xfmc"] = E["sz"]
        E["x1c"] = E["ybuf"]
        E["xfc"] = E["du"]
        for i in range(4):
            E[f"ps{i}"] = ps(stk, f"m_ps{i}", [128, 512])
        E["psbc0"] = ps(stk, "m_psbc0", [128, 512])
        E["psbc1"] = ps(stk, "m_psbc1", [128, 512])
        E["pst"] = ps(stk, "m_pst", [128, 512])
        E["pstb"] = ps(stk, "m_pstb", [128, 1024], BF16)

        def load(dst_ap, src_ap, tag):
            S.op("sync", lambda e, dst_ap=dst_ap, src_ap=src_ap: e.dma_start(
                out=dst_ap, in_=src_ap), writes=[tag], dma=True)

        load(E["identb"][:], dr["identb"].ap(), "identb")
        load(E["bsel"][:], dr["bsel"].ap(), "bsel")
        load(E["identf"][:], dr["identf"].ap(), "identf")
        load(E["jflipb"][:], dr["jflipb"].ap(), "jflipb")
        S.op("vector", lambda e: e.memset(E["onesb"][:], 1.0), writes=["onesb"])
        S.op("vector", lambda e: e.memset(E["onesr"][:], 1.0), writes=["onesb"])
        S.op("vector", lambda e: e.memset(E["cst"][:, 0:1], EPS), writes=["cst"])
        S.op("vector", lambda e: e.memset(E["cst"][:, 1:2], 1.0), writes=["cst"])
        E["inwTscr"] = {}
        E["outwTscr"] = {}
        with ExitStack() as pstk:
            wstage = E["xc"][:, 0:2048]
            wstb = E["ln1f"]
            for d in (1, 2):
                sm = E[f"sm{d}"]
                load(sm[:, 0:32], dr[f"convw{d}"].ap(), f"sm{d}")
                load(sm[:, 32:40], dr[f"convb{d}"].ap(), f"sm{d}")
                load(sm[:, 40:48], dr[f"dtb{d}"].ap(), f"sm{d}")
                load(sm[:, 48:56], dr[f"dd{d}"].ap(), f"sm{d}")
                A = E[f"A{d}"]
                load(A[:], dr[f"alog{d}"].ap(), f"A{d}")
                S.op("scalar", lambda e, A=A: e.activation(A[:], A[:], AF.Exp),
                     reads=[f"A{d}"], writes=[f"A{d}"])
                S.op("vector", lambda e, A=A: e.tensor_scalar_mul(A[:], A[:], -1.0),
                     reads=[f"A{d}"], writes=[f"A{d}"])
                src = dr[f"xprojT{d}"].ap().rearrange("(g p) e -> p g e", p=128)
                load(wstage[:, 0:512].rearrange("p (g e) -> p g e", g=8), src, "wstage")
                S.op("vector", lambda e, d=d: e.tensor_copy(
                    E["xprojT"][:, 512 * (d - 1):512 * d], wstage[:, 0:512]),
                     reads=["wstage"], writes=["xprojT"])
                load(wstage[0:32, 0:DI], dr[f"dtwT{d}"].ap(), "wstage")
                S.op("vector", lambda e, d=d: e.tensor_copy(
                    E["dtwT"][:, DI * (d - 1):DI * d], wstage[0:32, 0:DI]),
                     reads=["wstage"], writes=["dtwT"])

            for d in (1, 2):
                E["inwTscr"][d] = B.dscratch(f"inwTbscr{d}", (DM, 2 * DI), BF16)
                E["outwTscr"][d] = B.dscratch(f"outwTbscr{d}", (DI, DM), BF16)
                for piece in range(4):
                    sl = slice(128 * piece, 128 * (piece + 1))
                    load(wstage[:, 0:2048], dr[f"inwT{d}"].ap()[sl, :], "wstage")
                    S.op("vector", lambda e: e.tensor_copy(
                        wstb[:], wstage[:, 0:2048]),
                         reads=["wstage"], writes=["wstb"])
                    S.op("sync", lambda e, d=d, sl=sl: e.dma_start(
                        out=E["inwTscr"][d].ap()[sl, :], in_=wstb[:]),
                        reads=["wstb"], writes=["wstage", f"inwTscr{d}"], dma=True)
                for piece in range(2):
                    src = dr[f"outwT{d}"].ap()[512 * piece:512 * (piece + 1), :]
                    src = src.rearrange("(g p) m -> p g m", p=128)
                    load(wstage[:, 0:2048].rearrange("p (g m) -> p g m", g=4), src, "wstage")
                    S.op("vector", lambda e: e.tensor_copy(
                        wstb[:], wstage[:, 0:2048]),
                         reads=["wstage"], writes=["wstb"])
                    dst = E["outwTscr"][d].ap()[512 * piece:512 * (piece + 1), :]
                    dst = dst.rearrange("(g p) m -> p g m", p=128)
                    S.op("sync", lambda e, dst=dst: e.dma_start(
                        out=dst, in_=wstb[:].rearrange("p (g m) -> p g m", g=4)),
                        reads=["wstb"], writes=["wstage", f"outwTscr{d}"], dma=True)
            S.barrier()

        S.op("vector", lambda e: e.memset(E["carry"][:], 0.0), writes=["carry"])
        S.op("vector", lambda e: e.memset(E["halo"][:], 0.0), writes=["halo"])

        wsts = [B.mjw[0]]
        wsbs = [B.mjw[1]]
        jobs = B.wjobs
        tick_bound = [len(jobs) // 2]
        B.tick_bound = tick_bound

        def job_in(ji):
            sap, dap, wdt = jobs[ji]
            wst = wsts[0]
            S.op("sync", lambda e, sap=sap, wst=wst, wdt=wdt: e.dma_start(
                out=wst[:, 0:wdt], in_=sap),
                writes=["p0wst0"], dma=True, sem="dj0")

        def job_castout(ji):
            sap, dap, wdt = jobs[ji]
            wst = wsts[0]
            wsb = wsbs[0]
            S.op("scalar", lambda e, wst=wst, wsb=wsb, wdt=wdt: e.copy(
                wsb[:, 0:wdt], wst[:, 0:wdt]),
                 reads=["p0wst0"], writes=["p0wsb0"])
            S.op("scalar", lambda e, dap=dap, wsb=wsb, wdt=wdt: e.dma_start(
                out=dap, in_=wsb[:, 0:wdt]),
                reads=["p0wsb0"], writes=["p0out0"], dma=True, sem="dj0")

        tick_state = [0, 0]

        def job_tick():
            tick_state[0] += 1
            ji = tick_state[1]
            if ji >= tick_bound[0]:
                return
            job_castout(ji)
            if ji + 1 < len(jobs):
                job_in(ji + 1)
            tick_state[1] = ji + 1

        def job_flush():
            while tick_state[1] < tick_bound[0]:
                ji = tick_state[1]
                job_castout(ji)
                if ji + 1 < len(jobs):
                    job_in(ji + 1)
                tick_state[1] = ji + 1
        B.job_tick = job_tick
        B.job_flush = job_flush
        job_in(0)

        for c in range(NCH):
            for d in (1, 2):
                _mamba_dir_chunk(B, E, c, d)
            _mix_tail_chunk(B, E, c)
        job_flush()
    S.barrier()


def _mamba_dir_chunk(B, E, c, d):
    nc, S = B.nc, B.S
    dr = B.dram
    T = TCH
    ln1scr = dr["ln1scr"]
    A = E[f"A{d}"]
    sm = E[f"sm{d}"]
    smtag = f"sm{d}"
    inwTscr = E["inwTscr"][d]
    outwTscr = E["outwTscr"][d]
    inwTb, outwTb = E["inwTb"], E["outwTb"]
    ln1f, lntm = E["ln1f"], E["lntm"]
    xh, halo, xc, xcb, sz = E["xh"], E["halo"], E["xc"], E["xcb"], E["sz"]
    du, ybuf, dbcb, sp = E["du"], E["ybuf"], E["dbcb"], E["sp"]
    prod = sp[:, TCH:2 * TCH]
    bcf32, bchi, bclo, bsel = E["bcf32"], E["bchi"], E["bclo"], E["bsel"]
    Hbuf, dAbuf, dBubuf = E["H"], E["dA"], E["dBu"]
    bcB, bcC, carry, ygb, hid = E["bcB"], E["bcC"], E["carry"], E["ygb"], E["hid"]
    onesb = E["onesb"]
    psrot = [E["ps0"], E["ps1"], E["ps2"], E["ps3"]]
    psbc0, psbc1, pstb = E["psbc0"], E["psbc1"], E["pstb"]

    xh_g = xh[:].rearrange("p (g t) -> p g t", g=8)
    xc_g = xc[:].rearrange("p (g t) -> p g t", g=8)
    sz_g = sz[:].rearrange("p (g t) -> p g t", g=8)
    du_g = du[:].rearrange("p (g t) -> p g t", g=8)
    y_g = ybuf[:].rearrange("p (g t) -> p g t", g=8)
    yg_g = ygb[:].rearrange("p (g t) -> p g t", g=8)
    hid_g = hid[:].rearrange("p (g t) -> p g t", g=8)
    halo_d = halo[:].rearrange("p (d g k) -> p d g k", d=2, g=8)
    ln1f_g = ln1f[:].rearrange("p (f t) -> p f t", f=4)


    for q in range(T // 128):
        if d == 1:
            trow = c * T + q * 128
            src = ln1scr.ap()[trow:trow + 128, 0:DM]
        else:
            trow = L - (c * T + q * 128) - 128
            src = ln1scr.ap()[trow:trow + 128, DM:D]
        S.op("sync", lambda e, src=src: e.dma_start(out=lntm[:], in_=src),
             reads=["ln1scr"], writes=["lntm"], dma=True)
        for p in range(4):
            def tr(e, p=p, d=d):
                ident = E["identb"] if d == 1 else E["jflipb"]
                return e.transpose(pstb[:, 128 * p:128 * (p + 1)],
                                   lntm[:, 128 * p:128 * (p + 1)], ident[:])
            S.op("tensor", tr, reads=["lntm", "identb", "jflipb"], writes=["pstb"])

        def evac(e, q=q):
            return e.tensor_copy(ln1f_g[:, :, 128 * q:128 * (q + 1)],
                                 pstb[:, 0:512].rearrange("p (f t) -> p f t", f=4))
        S.op("vector", evac, reads=["pstb"], writes=["ln1f"])

    for half in range(2):
        for cc in range(4):
            src_ap = inwTscr.ap()[128 * cc:128 * (cc + 1),
                                  1024 * half:1024 * (half + 1)]
            S.op("sync", lambda e, cc=cc, src_ap=src_ap: e.dma_start(
                out=inwTb[:, 1024 * cc:1024 * (cc + 1)], in_=src_ap),
                reads=[f"inwTscr{d}"], writes=["inwTb"], dma=True)
        for sub in range(8):
            eg = half * 8 + sub
            psu = psrot[sub % 4]
            pstag = f"mps{sub % 4}"

            def mm(e, sub=sub, psu=psu):
                for cc in range(4):
                    lhsT = inwTb[:, 1024 * cc + 128 * sub: 1024 * cc + 128 * (sub + 1)]
                    inst = e.matmul(psu[:, 0:T], lhsT, ln1f_g[:, cc, :],
                                    start=(cc == 0), stop=(cc == 3))
                return inst
            S.op("tensor", mm, reads=["inwTb", "ln1f"], writes=[pstag])
            if eg < 8:
                def evacxh(e, eg=eg, psu=psu):
                    return e.copy(xh_g[:, eg, 3:3 + T], psu[:, 0:T])
                S.op("scalar", evacxh, reads=[pstag], writes=["xh"])
            else:
                def evacz(e, eg=eg, psu=psu):
                    return e.activation(sz_g[:, eg - 8, :], psu[:, 0:T], AF.Silu)
                S.op("scalar", evacz, reads=[pstag], writes=["sz"])

    S.op("vector", lambda e: e.tensor_copy(xh_g[:, :, 0:3], halo_d[:, d - 1]),
         reads=["halo"], writes=["xh"])
    for dg in range(8):
        acc = sp[:, 0:T]

        def conv0(e, dg=dg, acc=acc):
            return e.tensor_scalar_mul(acc, xh_g[:, dg, 0:T], sm[:, 4 * dg:4 * dg + 1])
        S.op("vector", conv0, reads=["xh", smtag], writes=["spA"])
        for k in (1, 2, 3):
            def convk(e, dg=dg, k=k, acc=acc):
                return e.scalar_tensor_tensor(
                    acc, xh_g[:, dg, k:k + T], sm[:, 4 * dg + k:4 * dg + k + 1],
                    acc, ALU.mult, ALU.add)
            S.op("vector", convk, reads=["xh", smtag, "spA"], writes=["spA"])

        def silu(e, dg=dg, acc=acc):
            return e.activation(xc_g[:, dg, :], acc, AF.Silu,
                                bias=sm[:, 32 + dg:33 + dg])
        S.op("scalar", silu, reads=["spA", smtag], writes=["xc"])
    S.op("vector", lambda e: e.tensor_copy(xcb[:], xc[:]), reads=["xc"], writes=["ygb"])
    S.op("vector", lambda e, d=d: e.tensor_copy(
        halo_d[:, d - 1], xh_g[:, :, T:T + 3]), reads=["xh"], writes=["halo"])

    xcb_g = xcb[:].rearrange("p (g t) -> p g t", g=8)

    def xproj(e, d=d):
        xp = E["xprojT"][:, 512 * (d - 1):512 * d].rearrange("p (g e) -> p g e", g=8)
        for cc in range(8):
            inst = e.matmul(E["ps0"][0:64, 0:T], xp[:, cc, :], xcb_g[:, cc, :],
                            start=(cc == 0), stop=(cc == 7))
        return inst
    S.op("tensor", xproj, reads=["xprojT", "ygb"], writes=["mps0"])
    S.op("scalar", lambda e: e.copy(dbcb[:, 0:T], E["ps0"][0:32, 0:T]),
         reads=["mps0"], writes=["dbcb"])
    S.op("scalar", lambda e: e.copy(bcf32[:, 0:T], E["ps0"][32:64, 0:T]),
         reads=["mps0"], writes=["dbc"])
    S.op("vector", lambda e: e.tensor_copy(bchi[:, 0:T], bcf32[:, 0:T]),
         reads=["dbc"], writes=["bchi"], ses=True)
    S.op("vector", lambda e: e.tensor_tensor(bclo[:, 0:T], bcf32[:, 0:T],
                                             bchi[:, 0:T], ALU.subtract),
         reads=["dbc", "bchi"], writes=["bchi"], ses=True)

    for mg in range(8):
        def dtmm(e, mg=mg, d=d):
            return e.matmul(
                E["ps1"][:, 0:T],
                E["dtwT"][:, DI * (d - 1) + 128 * mg: DI * (d - 1) + 128 * (mg + 1)],
                dbcb[:, 0:T], start=True, stop=True)
        S.op("tensor", dtmm, reads=["dtwT", "dbcb"], writes=["mps1"])
        raw = sp[:, 0:T]
        t1 = sp[:, T:2 * T]

        def rawev(e, mg=mg):
            return e.add(raw, E["ps1"][:, 0:T], sm[:, 40 + mg:41 + mg])
        S.op("scalar", rawev, reads=["mps1", smtag], writes=["spA"])
        S.op("vector", lambda e: e.scalar_tensor_tensor(
            t1, raw, -1.0, raw, ALU.mult, ALU.min),
             reads=["spA"], writes=["spB"])
        S.op("scalar", lambda e: e.activation(t1, t1, AF.Exp),
             reads=["spB"], writes=["spB"])
        S.op("scalar", lambda e: e.activation(t1, t1, AF.Ln, bias=E["cst"][:, 1:2]),
             reads=["spB", "cst"], writes=["spB"])

        def sfin(e, mg=mg):
            return e.scalar_tensor_tensor(du_g[:, mg, :], raw, 0.0, t1,
                                          ALU.max, ALU.add)
        S.op("vector", sfin, reads=["spA", "spB"], writes=["du"])

    ubuf = E["ubuf"]
    u_g = ubuf[:].rearrange("p (g t) -> p g t", g=8)
    S.op("vector", lambda e: e.tensor_mul(ubuf[:], du[:], xc[:]),
         reads=["du", "xc"], writes=["ubuf"])

    prodA = sp[:, 0:2 * T]
    treA = ln1f[:].bitcast(F32)[:, 0:T]
    prodB = xh[:, 0:2 * T]
    treB = xh[:, 2 * T:3 * T]
    NG = 8
    for grp in range(NG):
        SLg = slice((grp % 2) * 2 * T, ((grp % 2) + 1) * 2 * T)
        bcB2 = bcB[:, 0:2 * T]
        bcC2 = bcC[:, 0:2 * T]
        tBc, tCc = "bcB", "bcC"
        for i in range(2):
            s = 2 * grp + i

            def bcmmB(e, s=s):
                sel = bsel[:, 128 * s:128 * (s + 1)]
                e.matmul(psbc0[:, 0:T], sel, bchi[:, 0:T], start=True, stop=False)
                return e.matmul(psbc0[:, 0:T], sel, bclo[:, 0:T], start=False, stop=True)
            S.op("tensor", bcmmB, reads=["bchi", "bsel"], writes=["psbc0"])
            S.op("scalar", lambda e, i=i, bcB2=bcB2: e.copy(
                bcB2[:, i * T:(i + 1) * T], psbc0[:, 0:T]),
                 reads=["psbc0"], writes=[tBc])

            def bcmmC(e, s=s):
                sel = bsel[:, 128 * (16 + s):128 * (17 + s)]
                e.matmul(psbc1[:, 0:T], sel, bchi[:, 0:T], start=True, stop=False)
                return e.matmul(psbc1[:, 0:T], sel, bclo[:, 0:T], start=False, stop=True)
            S.op("tensor", bcmmC, reads=["bchi", "bsel"], writes=["psbc1"])
            S.op("scalar", lambda e, i=i, bcC2=bcC2: e.copy(
                bcC2[:, i * T:(i + 1) * T], psbc1[:, 0:T]),
                 reads=["psbc1"], writes=[tCc])

        def stage_p1(dg, grp=grp):
            slot = dg % 2
            dA2 = dAbuf[:, slot * 2 * T:(slot + 1) * 2 * T]

            def p1(e, dg=dg, grp=grp, dA2=dA2):
                for i in range(2):
                    s = 2 * grp + i
                    inst = e.activation(dA2[:, i * T:(i + 1) * T], du_g[:, dg, :],
                                        AF.Exp, scale=A[:, 16 * dg + s:16 * dg + s + 1])
                return inst
            S.op("scalar", p1, reads=["du", f"A{d}"], writes=[f"dA{slot}"])

        def stage_p2(dg, grp=grp):
            slot = dg % 2
            dBu2 = dBubuf[:, slot * 2 * T:(slot + 1) * 2 * T]

            def p2(e, dg=dg, dBu2=dBu2):
                ub = u_g[:, dg, :].unsqueeze(1).broadcast_to((128, 2, T))
                return e.tensor_tensor(
                    dBu2.rearrange("p (i t) -> p i t", i=2), ub,
                    bcB2.rearrange("p (i t) -> p i t", i=2), ALU.mult)
            S.op("gpsimd", p2, reads=["ubuf", tBc], writes=[f"dB{slot}"])

        def stage_p3(dg, grp=grp):
            slot = dg % 2
            hslot = dg % 3
            dA2 = dAbuf[:, slot * 2 * T:(slot + 1) * 2 * T]
            dBu2 = dBubuf[:, slot * 2 * T:(slot + 1) * 2 * T]
            H2 = Hbuf[:, hslot * 2 * T:(hslot + 1) * 2 * T]

            def p3(e, dg=dg, grp=grp, dA2=dA2, dBu2=dBu2, H2=H2, d=d):
                for i in range(2):
                    s = 2 * grp + i
                    init = carry[:, 128 * (d - 1) + 16 * dg + s:
                                 128 * (d - 1) + 16 * dg + s + 1]
                    inst = e.tensor_tensor_scan(
                        H2[:, i * T:(i + 1) * T], dA2[:, i * T:(i + 1) * T],
                        dBu2[:, i * T:(i + 1) * T], init, ALU.mult, ALU.add)
                return inst
            S.op("vector", p3, reads=[f"dA{slot}", f"dB{slot}", "carry"],
                 writes=[f"H{hslot}", "carry"])

        def stage_p4a(dg, grp=grp):
            hslot = dg % 3
            pslot = dg % 2
            H2 = Hbuf[:, hslot * 2 * T:(hslot + 1) * 2 * T]
            prod2 = prodA if pslot == 0 else prodB
            ptags = ["spA", "spB"] if pslot == 0 else ["xh"]

            def p4a(e, H2=H2, prod2=prod2):
                return e.tensor_mul(prod2, H2, bcC2)
            S.op("gpsimd", p4a, reads=[f"H{hslot}", tCc], writes=ptags)

        def stage_p4t(dg, grp=grp):
            hslot = dg % 3
            pslot = dg % 2
            H2 = Hbuf[:, hslot * 2 * T:(hslot + 1) * 2 * T]
            prod2 = prodA if pslot == 0 else prodB
            tre = treA if pslot == 0 else treB
            ptags = ["spA", "spB"] if pslot == 0 else ["xh"]
            ttags = ["ln1f"] if pslot == 0 else ["xh"]

            def cup(e, dg=dg, grp=grp, d=d, H2=H2):
                dst = carry[:, 128 * (d - 1) + 16 * dg + 2 * grp:
                            128 * (d - 1) + 16 * dg + 2 * grp + 2]
                return e.copy(
                    dst, H2.rearrange("p (i t) -> p i t", i=2)[:, :, T - 1:T])
            S.op("scalar", cup, reads=[f"H{hslot}"], writes=["carry"])

            def p4t(e, dg=dg, grp=grp, prod2=prod2, tre=tre):
                e.tensor_add(tre, prod2[:, 0:T], prod2[:, T:2 * T])
                if grp == 0:
                    return e.tensor_copy(y_g[:, dg, :], tre)
                return e.tensor_add(y_g[:, dg, :], y_g[:, dg, :], tre)
            S.op("vector", p4t, reads=ptags + ["ybuf"] + ttags,
                 writes=["ybuf"] + ttags)

        stage_p1(0); stage_p2(0)
        stage_p1(1); stage_p2(1)
        stage_p3(0)
        for dg in range(8):
            if getattr(B, "job_tick", None):
                B.job_tick()
            if dg + 1 < 8:
                stage_p3(dg + 1)
            stage_p4a(dg)
            if dg + 2 < 8:
                stage_p1(dg + 2)
                stage_p2(dg + 2)
            stage_p4t(dg)

    for dg in range(8):
        def gate1(e, dg=dg):
            return e.scalar_tensor_tensor(
                y_g[:, dg, :], xc_g[:, dg, :], sm[:, 48 + dg:49 + dg],
                y_g[:, dg, :], ALU.mult, ALU.add)
        S.op("vector", gate1, reads=["xc", smtag, "ybuf"], writes=["ybuf"])

        def gate2(e, dg=dg):
            return e.tensor_mul(yg_g[:, dg, :], y_g[:, dg, :], sz_g[:, dg, :])
        S.op("vector", gate2, reads=["ybuf", "sz"], writes=["ygb"])

    for gg in range(8):
        S.op("sync", lambda e, gg=gg: e.dma_start(
            out=outwTb[:, 512 * gg:512 * (gg + 1)],
            in_=outwTscr.ap()[128 * gg:128 * (gg + 1), :]),
            reads=[f"outwTscr{d}"], writes=["outwTb"], dma=True)
    for mg in range(4):
        psu = psrot[mg % 2]
        pstag = f"mps{mg % 2}"

        def opmm(e, mg=mg, psu=psu):
            for cc in range(8):
                lhsT = outwTb[:, 512 * cc + 128 * mg: 512 * cc + 128 * (mg + 1)]
                inst = e.matmul(psu[:, 0:T], lhsT, yg_g[:, cc, :],
                                start=(cc == 0), stop=(cc == 7))
            return inst
        S.op("tensor", opmm, reads=["outwTb", "ygb"], writes=[pstag])
        row = (d - 1) * 4 + mg

        def opev(e, row=row, psu=psu):
            return e.copy(hid_g[:, row, :], psu[:, 0:T])
        S.op("scalar", opev, reads=[pstag], writes=["hid"])

    if c == 0:
        B.dump(f"dA_s15_d{d}", dAbuf[:], (128, 2 * TCH), F32, tags=["dA0", "dA1"])
        B.dump(f"bcB_s15_d{d}", bcB[:], (128, TCH), F32, tags=["bcB"])
        B.dump(f"bcC_s15_d{d}", bcC[:], (128, TCH), F32, tags=["bcC"])
        B.dump(f"H_s15_d{d}", Hbuf[:], (128, 2 * TCH), F32, tags=["H0", "H1"])
        B.dump(f"sz_c0_d{d}", sz[:], (128, 8 * TCH), F32, tags=["sz"])
        B.dump(f"u_c0_d{d}", E["ubuf"][:], (128, 8 * TCH), F32, tags=["ubuf"])
    B.dump(f"xc_c{c}_d{d}", xc[:], (128, 8 * TCH), F32, tags=["xc"])
    B.dump(f"du_c{c}_d{d}", du[:], (128, 8 * TCH), F32, tags=["du"])
    B.dump(f"y_c{c}_d{d}", ybuf[:], (128, 8 * TCH), F32, tags=["ybuf"])
    B.dump(f"hid_c{c}_d{d}", hid[:], (128, 8 * TCH), F32, tags=["hid"])


def _fm_ln(B, E, src_ap, src_tag, T):
    nc, S = B.nc, B.S
    hbb, onesb = E["hbb"], E["onesb"]
    srow, srow2, sp = E["srow"], E["srow2"], E["sp"]
    rw0, rw1, rw2, rw3 = E["rw0"], E["rw1"], E["rw2"], E["rw3"]
    T_ = TCH
    psbc0, psbc1 = E["psbc0"], E["psbc1"]
    hb_g = hbb[:].rearrange("p (g t) -> p g t", g=8)

    S.op("scalar", lambda e: e.activation(hbb[:], src_ap, AF.Copy),
         reads=[src_tag], writes=["ygb"])

    def mm1(e):
        for cc in range(8):
            inst = e.matmul(psbc0[0:1, 0:T], onesb[:, 0:1], hb_g[:, cc, :],
                            start=(cc == 0), stop=(cc == 7))
        return inst
    S.op("tensor", mm1, reads=["ygb", "onesb"], writes=["psbc0"])
    S.op("scalar", lambda e: e.copy(srow[0:1, 0:T], psbc0[0:1, 0:T]),
         reads=["psbc0"], writes=["bcC"])
    S.op("scalar", lambda e: e.activation(hbb[:], src_ap, AF.Square),
         reads=[src_tag], writes=["ygb"])

    def mm2(e):
        for cc in range(8):
            inst = e.matmul(psbc0[0:1, 0:T], onesb[:, 0:1], hb_g[:, cc, :],
                            start=(cc == 0), stop=(cc == 7))
        return inst
    S.op("tensor", mm2, reads=["ygb", "onesb"], writes=["psbc0"])
    S.op("scalar", lambda e: e.copy(srow2[0:1, 0:T], psbc0[0:1, 0:T]),
         reads=["psbc0"], writes=["bcC"])

    S.op("vector", lambda e: e.tensor_scalar_mul(srow[0:1, 0:T], srow[0:1, 0:T], 1.0 / D),
         reads=["bcC"], writes=["bcC"], ses=True)
    S.op("vector", lambda e: e.tensor_scalar_mul(srow2[0:1, 0:T], srow2[0:1, 0:T], 1.0 / D),
         reads=["bcC"], writes=["bcC"], ses=True)
    S.op("vector", lambda e: e.tensor_tensor(sp[0:1, 0:T], srow[0:1, 0:T],
                                             srow[0:1, 0:T], ALU.mult),
         reads=["bcC"], writes=["spA"], ses=True)
    S.op("vector", lambda e: e.tensor_sub(srow2[0:1, 0:T], srow2[0:1, 0:T], sp[0:1, 0:T]),
         reads=["bcC", "spA"], writes=["bcC"], ses=True)
    S.op("scalar", lambda e: e.activation(srow2[0:1, 0:T], srow2[0:1, 0:T],
                                          AF.Sqrt, bias=E["cst"][0:1, 0:1]),
         reads=["bcC", "cst"], writes=["bcC"])
    S.op("vector", lambda e: e.reciprocal(srow2[0:1, 0:T], srow2[0:1, 0:T]),
         reads=["bcC"], writes=["bcC"], ses=True)
    S.op("vector", lambda e: e.scalar_tensor_tensor(
        srow[0:1, 0:T], srow[0:1, 0:T], -1.0, srow2[0:1, 0:T], ALU.mult, ALU.mult),
         reads=["bcC"], writes=["bcC"], ses=True)

    def hilo(e):
        e.tensor_copy(rw0, srow2[0:1, 0:T])
        e.tensor_tensor(rw1, srow2[0:1, 0:T], rw0, ALU.subtract)
        e.tensor_copy(rw2, srow[0:1, 0:T])
        return e.tensor_tensor(rw3, srow[0:1, 0:T], rw2, ALU.subtract)
    S.op("vector", hilo, reads=["bcC"], writes=["rowhl", "bcB"], ses=True)

    def bmr(e):
        onesr = E["onesr"]
        e.matmul(psbc0[:, 0:T], onesr[:], rw0, start=True, stop=False)
        return e.matmul(psbc0[:, 0:T], onesr[:], rw1, start=False, stop=True)
    S.op("tensor", bmr, reads=["rowhl", "bcB", "onesb"], writes=["psbc0"])

    def bmn(e):
        onesr = E["onesr"]
        e.matmul(psbc1[:, 0:T], onesr[:], rw2, start=True, stop=False)
        return e.matmul(psbc1[:, 0:T], onesr[:], rw3, start=False, stop=True)
    S.op("tensor", bmn, reads=["rowhl", "bcB", "onesb"], writes=["psbc1"])


def _mix_tail_chunk(B, E, c):
    nc, S = B.nc, B.S
    T = TCH
    x = B.dram["x"]
    x1scr = B.dram["x1scr"]
    xfscr = B.dram["xfscr"]
    hid, sp = E["hid"], E["sp"]
    xtm = E["ln1f"][:].bitcast(F32)
    xfmc, x1c, xfc = E["xfmc"], E["x1c"], E["xfc"]
    psbc0, psbc1, pst = E["psbc0"], E["psbc1"], E["pst"]
    identf = E["identf"]
    hid_g = hid[:].rearrange("p (g t) -> p g t", g=8)
    x1_g = x1c[:].rearrange("p (g t) -> p g t", g=8)
    xf_g = xfc[:].rearrange("p (g t) -> p g t", g=8)
    xfm_g = xfmc[:].rearrange("p (g t) -> p g t", g=8)

    for q in range(T // 128):
        S.op("sync", lambda e, q=q: e.dma_start(
            out=xtm, in_=x.ap()[c * T + 128 * q: c * T + 128 * (q + 1), :]),
            writes=["ln1f"], dma=True)
        for dgp in range(4):
            def trx(e, dgp=dgp):
                e.transpose(pst[:, 0:128], xtm[:, 256 * dgp:256 * dgp + 128], identf[:])
                return e.transpose(pst[:, 128:256],
                                   xtm[:, 256 * dgp + 128:256 * dgp + 256], identf[:])
            S.op("tensor", trx, reads=["ln1f", "identf"], writes=["pstT"])

            def evx(e, q=q, dgp=dgp):
                return e.tensor_copy(
                    xfm_g[:, 2 * dgp:2 * dgp + 2, 128 * q:128 * (q + 1)],
                    pst[:, 0:256].rearrange("p (g t) -> p g t", g=2))
            S.op("vector", evx, reads=["pstT"], writes=["sz"])

    _fm_ln(B, E, hid[:], "hid", T)
    for dg in range(8):
        S.op("vector", lambda e, dg=dg: e.tensor_mul(
            sp[:, 0:T], hid_g[:, dg, :], psbc0[:, 0:T]),
             reads=["hid", "psbc0"], writes=["spA"])
        S.op("vector", lambda e: e.tensor_add(sp[:, 0:T], sp[:, 0:T], psbc1[:, 0:T]),
             reads=["spA", "psbc1"], writes=["spA"])
        S.op("vector", lambda e, dg=dg: e.tensor_add(
            x1_g[:, dg, :], sp[:, 0:T], xfm_g[:, dg, :]),
             reads=["spA", "sz"], writes=["ybuf"])
    S.op("sync", lambda e, c=c: e.dma_start(
        out=x1scr.ap().rearrange("(g p) t -> p g t", p=128)[:, :, c * T:(c + 1) * T],
        in_=x1_g), reads=["ybuf"], writes=["x1scr"], dma=True)

    _fm_ln(B, E, x1c[:], "ybuf", T)
    for dg in range(8):
        S.op("vector", lambda e, dg=dg: e.tensor_mul(
            sp[:, 0:T], x1_g[:, dg, :], psbc0[:, 0:T]),
             reads=["ybuf", "psbc0"], writes=["spA"])
        S.op("vector", lambda e, dg=dg: e.tensor_add(
            xf_g[:, dg, :], sp[:, 0:T], psbc1[:, 0:T]),
             reads=["spA", "psbc1"], writes=["du"])
    S.op("sync", lambda e, c=c: e.dma_start(
        out=xfscr.ap().rearrange("(g p) t -> p g t", p=128)[:, :, c * T:(c + 1) * T],
        in_=xf_g), reads=["du"], writes=["xfscr"], dma=True)

    B.dump(f"x1_c{c}", x1c[:], (128, 8 * TCH), F32, tags=["ybuf"])
    B.dump(f"xf_c{c}", xfc[:], (128, 8 * TCH), F32, tags=["du"])



def build_ffn(B):
    from contextlib import ExitStack
    S = B.S
    for l in (0, 1):
        if l == 0:
            B.tick_bound[0] = len(B.wjobs)
        with ExitStack() as stk:
            _ffn_attn(B, stk, l)
        S.barrier()
        with ExitStack() as stk:
            _ffn_mlp(B, stk, l)
            if l == 0:
                B.job_flush()
                B.job_tick = None
        S.barrier()


def _ffn_attn(B, stk, l):
    nc, S = B.nc, B.S
    dr = B.dram
    C = B.common
    def sb(stk_, name, shape, dt=F32):
        return B.sb(stk_, f"{name}_{l}", shape, dt)
    def ps(stk_, name, shape, dt=F32):
        return B.ps(stk_, f"{name}_{l}", shape, dt)
    E = {}
    E["hb"] = sb(stk, "f_hb", [128, 8 * L], BF16)
    E["wqkvb"] = sb(stk, "f_wqkvb", [128, 8 * 1536], BF16)
    E["wob"] = sb(stk, "f_wob", [128, 4 * 1024], BF16)

    E["qfm"] = sb(stk, "f_qfm", [128, 4 * L], BF16)
    E["kfm"] = sb(stk, "f_kfm", [128, 4 * L], BF16)
    E["vsb"] = sb(stk, "f_vsb", [128, 16 * 512], BF16)
    E["osb"] = sb(stk, "f_osb", [128, 4 * L], BF16)
    E["qsbc"] = sb(stk, "f_qsbc", [128, 512], F32)
    E["ksbc"] = sb(stk, "f_ksbc", [128, 512], F32)
    E["tbl"] = sb(stk, "f_tbl", [128, 2 * 4 * 512], F32)
    E["psq"] = sb(stk, "f_psq", [128, 2 * 512], F32)
    E["rst"] = sb(stk, "f_rst", [128, 64], F32)
    E["qtm"] = sb(stk, "f_qtm", [128, 2 * 512], F32)
    E["qtb"] = sb(stk, "f_qtb", [128, 2 * 512], BF16)
    E["attn"] = sb(stk, "f_attn", [128, 2 * 640], BF16)
    E["attnT"] = sb(stk, "f_attnT", [128, 2 * 8 * 512], BF16)
    E["nmx"] = sb(stk, "f_nmx", [128, 8], F32)
    E["slab2"] = sb(stk, "f_slab2", [128, 4 * 512], F32)
    E["slab"] = sb(stk, "f_slab", [128, L], F32)
    E["sp"] = sb(stk, "f_sp", [128, 512], F32)
    psQ = ps(stk, "f_psQ", [128, 512], F32)
    psK = ps(stk, "f_psK", [128, 512], F32)
    psV = ps(stk, "f_psV", [128, 512], F32)
    psS0 = ps(stk, "f_psS0", [128, 1024], F32)
    psS1 = ps(stk, "f_psS1", [128, 1024], F32)
    psTa = ps(stk, "f_psTa", [128, 1024], BF16)

    xfscr = dr["xfscr"]
    xf_r = xfscr.ap().rearrange("(g p) t -> p g t", p=128)
    hb = E["hb"]
    hb_g = hb[:].rearrange("p (g t) -> p g t", g=8)
    identb = C["identb"]

    def load(dst, srcap, tag, rtags=()):
        S.op("sync", lambda e, dst=dst, srcap=srcap: e.dma_start(
            out=dst, in_=srcap), reads=list(rtags), writes=[tag], dma=True)

    for cc in range(8):
        load(E["wqkvb"][:, 1536 * cc:1536 * (cc + 1)],
             dr[f"wqkvbb{l}"].ap()[128 * cc:128 * (cc + 1), :], "wqkvb", ["wqkvb"])
    for cc in range(4):
        load(E["wob"][:, 1024 * cc:1024 * (cc + 1)],
             dr[f"wobb{l}"].ap()[128 * cc:128 * (cc + 1), :], "wob", ["wob"])
    load(E["qsbc"][:], dr[f"qsrow{l}"].ap().to_broadcast((128, 512)), "qsbc")
    load(E["ksbc"][:], dr[f"ksrow{l}"].ap().to_broadcast((128, 512)), "ksbc")

    for dg in range(8):
        S.op("sync", lambda e, dg=dg: e.dma_start(
            out=E["slab"][:], in_=xf_r[:, dg, :]),
            reads=XFALL, writes=["fslab"], dma=True)
        if dg < 4:
            S.op("scalar", lambda e, dg=dg: e.copy(hb_g[:, dg, :], E["slab"][:]),
                 reads=["fslab"], writes=["hbf"])
        else:
            S.op("vector", lambda e, dg=dg: e.memset(hb_g[:, dg, 0:1], 0.0),
                 writes=["hbf"])
            S.op("scalar", lambda e, dg=dg: e.copy(
                hb_g[:, dg, 1:L], E["slab"][:, 0:L - 1]),
                 reads=["fslab"], writes=["hbf"])

    qfm, kfm, vsb = E["qfm"], E["kfm"], E["vsb"]

    def qkv_slots(sl):
        if sl == 0:
            return psQ[:], psK[:], psV[:], "psQ", "psK", "psV"
        return (psS0[:, 0:512], psS0[:, 512:1024], psS1[:, 0:512],
                "fpsS0", "fpsS0", "fpsS1")

    def qkv_mm(tt):
        sl = tt % 2
        t_tbl = f"tbl{sl}"
        tbl = E["tbl"][:, 2048 * sl:2048 * (sl + 1)]
        psQh, psKh, psVh, tQ, tK, tV = qkv_slots(sl)
        for ti, nm in enumerate(("rqa", "rqb", "rka", "rkb")):
            S.op("sync", lambda e, ti=ti, nm=nm, tt=tt, tbl=tbl: e.dma_start(
                out=tbl[:, 512 * ti:512 * (ti + 1)],
                in_=dr[nm].ap()[128 * tt:128 * (tt + 1), :]),
                reads=[t_tbl], writes=[t_tbl], dma=True, sem=f"dtb{sl}")

        def mmq(e, tt=tt, psQh=psQh, psKh=psKh, psVh=psVh):
            for cc in range(8):
                lhsT = hb_g[:, cc, 128 * tt:128 * (tt + 1)]
                e.matmul(psQh, lhsT,
                         E["wqkvb"][:, 1536 * cc:1536 * cc + 512],
                         start=(cc == 0), stop=(cc == 7))
                e.matmul(psKh, lhsT,
                         E["wqkvb"][:, 1536 * cc + 512:1536 * cc + 1024],
                         start=(cc == 0), stop=(cc == 7))
                inst = e.matmul(psVh, lhsT,
                                E["wqkvb"][:, 1536 * cc + 1024:1536 * cc + 1536],
                                start=(cc == 0), stop=(cc == 7))
            return inst
        S.op("tensor", mmq, reads=["hbf", "wqkvb"], writes=[tQ, tK, tV])

    def qkv_prep(tt):
        sl = tt % 2
        t_tbl, t_psq, t_qtb = f"tbl{sl}", f"psq{sl}", f"qtb{sl}"
        tbl = E["tbl"][:, 2048 * sl:2048 * (sl + 1)]
        psq = E["psq"][:, 512 * sl:512 * (sl + 1)]
        qtb = E["qtb"][:, 512 * sl:512 * (sl + 1)]
        psQh, psKh, psVh, tQ, tK, tV = qkv_slots(sl)
        S.op("scalar", lambda e, tt=tt, psVh=psVh: e.copy(
            vsb[:, 512 * tt:512 * (tt + 1)], psVh),
             reads=[tV], writes=["vsb"])
        for which, psu, pstag, qsb, tb0, tb1, dstfm, dtag in (
                ("q", psQh, tQ, E["qsbc"], 0, 1, qfm, "qfm"),
                ("k", psKh, tK, E["ksbc"], 2, 3, kfm, "kfm")):
            rst = E["rst"][:, 32 * sl:32 * (sl + 1)]
            t_rst = f"rst{sl}"
            qtm = E["qtm"]
            off = 0 if which == "q" else 8
            S.op("scalar", lambda e, psu=psu, psq=psq: e.activation(
                psq, psu, AF.Square),
                 reads=[pstag], writes=[t_psq])
            S.op("vector", lambda e, off=off, psq=psq, rst=rst: e.tensor_reduce(
                rst[:, off:off + 8],
                psq.rearrange("p (h d) -> p h d", h=8), AX.X, ALU.add),
                 reads=[t_psq], writes=[t_rst])
            S.op("scalar", lambda e, off=off, rst=rst: e.activation(
                rst[:, 16 + off:24 + off], rst[:, off:off + 8], AF.Sqrt),
                 reads=[t_rst], writes=[t_rst])
            S.op("vector", lambda e, off=off, rst=rst: e.tensor_scalar_max(
                rst[:, 16 + off:24 + off], rst[:, 16 + off:24 + off], 1e-12),
                 reads=[t_rst], writes=[t_rst], ses=True)
            S.op("vector", lambda e, off=off, rst=rst: e.reciprocal(
                rst[:, 16 + off:24 + off], rst[:, 16 + off:24 + off]),
                 reads=[t_rst], writes=[t_rst], ses=True)
            rbc = rst[:, 16 + off:24 + off].unsqueeze(2).broadcast_to((128, 8, 64))
            S.op("vector", lambda e, psu=psu, rbc=rbc: e.tensor_tensor(
                qtm[:, 0:512].rearrange("p (h d) -> p h d", h=8),
                psu.rearrange("p (h d) -> p h d", h=8), rbc, ALU.mult),
                 reads=[pstag, t_rst], writes=["qtm"])
            S.op("gpsimd", lambda e, qsb=qsb: e.tensor_mul(
                qtm[:, 512:1024], qtm[:, 0:512], qsb[:]),
                 reads=["qtm", "qsbc", "ksbc"], writes=["qtm"])
            S.op("gpsimd", lambda e, tb0=tb0, tbl=tbl: e.tensor_mul(
                qtm[:, 0:512], qtm[:, 512:1024],
                tbl[:, 512 * tb0:512 * (tb0 + 1)]),
                 reads=["qtm", t_tbl], writes=["qtm"])

            def swmul(e, tb1=tb1, tbl=tbl):
                qa = qtm[:, 512:1024].rearrange("p (h s d) -> p h s d", h=8, s=2)
                tblB = tbl[:, 512 * tb1:512 * (tb1 + 1)].rearrange(
                    "p (h s d) -> p h s d", h=8, s=2)
                spv = E["sp"][:, 0:512].rearrange("p (h s d) -> p h s d", h=8, s=2)
                e.tensor_tensor(spv[:, :, 0, :], qa[:, :, 1, :],
                                tblB[:, :, 0, :], ALU.mult)
                return e.tensor_tensor(spv[:, :, 1, :], qa[:, :, 0, :],
                                       tblB[:, :, 1, :], ALU.mult)
            S.op("gpsimd", swmul, reads=["qtm", t_tbl], writes=["fspA"])
            S.op("vector", lambda e, qtb=qtb: e.tensor_add(
                qtb, qtm[:, 0:512], E["sp"][:, 0:512]),
                 reads=["qtm", "fspA"], writes=[t_qtb])

            def trq(e, qtb=qtb):
                for hp in range(4):
                    inst = e.transpose(psTa[:, 128 * hp:128 * (hp + 1)],
                                       qtb[:, 128 * hp:128 * (hp + 1)],
                                       identb[:])
                return inst
            S.op("tensor", trq, reads=[t_qtb, "identb"], writes=["psTa"])

            def evq(e, tt=tt, dstfm=dstfm):
                return e.copy(
                    dstfm[:].rearrange("p (f t) -> p f t", f=4)[
                        :, :, 128 * tt:128 * (tt + 1)],
                    psTa[:, 0:512].rearrange("p (f t) -> p f t", f=4))
            S.op("scalar", evq, reads=["psTa"], writes=[dtag])

    qkv_mm(0)
    for tt in range(16):
        if getattr(B, "job_tick", None):
            B.job_tick()
        if tt + 1 < 16:
            qkv_mm(tt + 1)
        qkv_prep(tt)

    wob, osb = E["wob"], E["osb"]
    for w in range(NW):
        for h in range(HEADS):
            hp, hs = h // 2, (h % 2) * 64
            asl = h % 2
            attnT = E["attnT"][:, 4096 * asl:4096 * (asl + 1)]
            t_aT = f"fattnT{asl}"
            qh = qfm[:].rearrange("p (f t) -> p f t", f=4)[hs:hs + 64, hp, :]
            kh = kfm[:].rearrange("p (f t) -> p f t", f=4)[hs:hs + 64, hp, :]
            kb_lo = 4 if w == 0 else 0

            def qctx(qb):
                ssl = qb % 2
                return dict(
                    qb=qb, ssl=ssl,
                    psS=psS0 if ssl == 0 else psS1,
                    t_psS=f"fpsS{ssl}", t_psS2=f"fpsS{ssl}",
                    attn=E["attn"][:, 640 * ssl:640 * (ssl + 1)],
                    t_attn=f"fattn{ssl}",
                    nmx=E["nmx"][:, 4 * ssl:4 * (ssl + 1)],
                    t_nmx=f"nmx{ssl}",
                    kbs=[kb for kb in range(qb, qb + 5) if kb >= kb_lo],
                )

            def st_mms(cx, w=w, qh=qh, kh=kh):
                qb, kbs, psS = cx["qb"], cx["kbs"], cx["psS"]
                kb0 = kbs[0]

                def mms(e, w=w, qb=qb, kbs=kbs, kb0=kb0, qh=qh, kh=kh, psS=psS):
                    used_banks = set()
                    for kb in kbs:
                        koff = (w - 1) * 512 + 128 * kb
                        so = 128 * (kb - kb0)
                        bank = so // 512
                        st = bank not in used_banks
                        used_banks.add(bank)
                        inst = e.matmul(
                            psS[:, so:so + 128],
                            qh[:, 512 * w + 128 * qb: 512 * w + 128 * (qb + 1)],
                            kh[:, koff:koff + 128], start=st, stop=True)
                    return inst
                S.op("tensor", mms, reads=["qfm", "kfm"],
                     writes=[cx["t_psS"], cx["t_psS2"]])

            def st_maskmax(cx):
                qb, kbs, psS, nmx = cx["qb"], cx["kbs"], cx["psS"], cx["nmx"]
                nv = len(kbs)
                if kbs[0] == qb:
                    S.op("vector", lambda e, psS=psS: e.tensor_add(
                        psS[:, 0:128], psS[:, 0:128], C["maskuaf"][:]),
                         reads=[cx["t_psS"], "maskuaf"],
                         writes=[cx["t_psS"], cx["t_psS2"]])
                S.op("vector", lambda e, nv=nv, psS=psS: e.tensor_add(
                    psS[:, 128 * (nv - 1):128 * nv],
                    psS[:, 128 * (nv - 1):128 * nv], C["masklaf"][:]),
                     reads=[cx["t_psS"], "masklaf"],
                     writes=[cx["t_psS"], cx["t_psS2"]])
                S.op("vector", lambda e, nv=nv, psS=psS, nmx=nmx: e.reduce_max(
                    nmx[:, 0:1], psS[:, 0:128 * nv], axis=AX.X, negate=True),
                     reads=[cx["t_psS"], cx["t_psS2"]], writes=[cx["t_nmx"]])

            def st_exp(cx):
                nv = len(cx["kbs"])
                psS, nmx, attn = cx["psS"], cx["nmx"], cx["attn"]
                S.op("scalar", lambda e, nv=nv, psS=psS, nmx=nmx, attn=attn: e.activation(
                    attn[:, 0:128 * nv], psS[:, 0:128 * nv], AF.Exp,
                    bias=nmx[:, 0:1], accum_out=nmx[:, 1:2]),
                     reads=[cx["t_psS"], cx["t_psS2"], cx["t_nmx"]],
                     writes=[cx["t_attn"], cx["t_nmx"]])

            def st_recip(cx):
                nmx = cx["nmx"]
                S.op("vector", lambda e, nmx=nmx: e.reciprocal(
                    nmx[:, 2:3], nmx[:, 1:2]),
                     reads=[cx["t_nmx"]], writes=[cx["t_nmx"]], ses=True)

            def st_scale(cx):
                nv = len(cx["kbs"])
                attn, nmx = cx["attn"], cx["nmx"]
                S.op("scalar", lambda e, nv=nv, attn=attn, nmx=nmx: e.activation(
                    attn[:, 0:128 * nv], attn[:, 0:128 * nv],
                    AF.Copy, scale=nmx[:, 2:3]),
                     reads=[cx["t_attn"], cx["t_nmx"]], writes=[cx["t_attn"]])

            def st_mta(cx):
                kbs, attn = cx["kbs"], cx["attn"]
                kb0 = kbs[0]

                def mta(e, kbs=kbs, kb0=kb0, attn=attn):
                    for i, kb in enumerate(kbs):
                        so = 128 * (kb - kb0)
                        inst = e.transpose(psTa[:, 128 * i:128 * (i + 1)],
                                           attn[:, so:so + 128], identb[:])
                    return inst
                S.op("tensor", mta, reads=[cx["t_attn"], "identb"], writes=["psTa"])

            def st_eta(cx, attnT=attnT, t_aT=t_aT):
                kbs, qb = cx["kbs"], cx["qb"]
                engn = "vector" if qb % 2 == 0 else "scalar"

                def eta(e, kbs=kbs, qb=qb, attnT=attnT, engn=engn):
                    cp = e.tensor_copy if engn == "vector" else e.copy
                    for i, kb in enumerate(kbs):
                        inst = cp(
                            attnT[:, 512 * kb + 128 * qb:
                                  512 * kb + 128 * (qb + 1)],
                            psTa[:, 128 * i:128 * (i + 1)])
                    return inst
                S.op(engn, eta, reads=["psTa"], writes=[t_aT])

            cxs = [qctx(qb) for qb in range(4)]
            st_mms(cxs[0]); st_maskmax(cxs[0])
            st_mms(cxs[1])
            for k in range(4):
                if getattr(B, "job_tick", None):
                    B.job_tick()
                st_exp(cxs[k])
                if k + 1 < 4:
                    st_maskmax(cxs[k + 1])
                st_recip(cxs[k])
                st_scale(cxs[k])
                if k + 2 < 4:
                    st_mms(cxs[k + 2])
                st_mta(cxs[k])
                st_eta(cxs[k])

            def mav(e, w=w, h=h, kb_lo=kb_lo, attnT=attnT):
                started = False
                for kb in range(kb_lo, 8):
                    qmin = max(0, kb - 4)
                    qmax = min(3, kb)
                    if qmin > qmax:
                        continue
                    kt = 4 * (w - 1) + kb
                    lhsT = vsb[:, 512 * kt + 64 * h: 512 * kt + 64 * (h + 1)]
                    rhs = attnT[:, 512 * kb + 128 * qmin:
                                512 * kb + 128 * (qmax + 1)]
                    inst = e.matmul(psV[0:64, 128 * qmin:128 * (qmax + 1)],
                                    lhsT, rhs, start=(not started), stop=True)
                    started = True
                return inst
            S.op("tensor", mav, reads=["vsb", t_aT], writes=["psV"])
            S.op("scalar", lambda e, w=w, hp=hp, hs=hs: e.copy(
                osb[hs:hs + 64, 2048 * hp + 512 * w: 2048 * hp + 512 * (w + 1)],
                psV[0:64, 0:512]),
                 reads=["psV"], writes=["osb"])

        def wo_in(mg, w=w):
            sl2 = E["slab2"][:, 512 * (mg % 4):512 * (mg % 4 + 1)]
            S.op("sync", lambda e, w=w, mg=mg, sl2=sl2: e.dma_start(
                out=sl2, in_=xf_r[:, mg, 512 * w:512 * (w + 1)]),
                reads=["xfscr", f"xfr{mg}"], writes=[f"fslab2{mg % 4}"],
                dma=True, sem=f"dsl{mg % 4}")
        for mg in range(3):
            wo_in(mg)
        for mg in range(8):
            if getattr(B, "job_tick", None):
                B.job_tick()
            msl = mg % 2
            psu = psQ[:] if msl == 0 else psK[:]
            pstag = "psQ" if msl == 0 else "psK"
            sl2 = E["slab2"][:, 512 * (mg % 4):512 * (mg % 4 + 1)]
            t_sl2 = f"fslab2{mg % 4}"

            def mmo(e, w=w, mg=mg, psu=psu):
                for cc in range(4):
                    lhsT = wob[:, 1024 * cc + 128 * mg: 1024 * cc + 128 * (mg + 1)]
                    rhs = osb[:, 2048 * cc + 512 * w: 2048 * cc + 512 * (w + 1)]
                    inst = e.matmul(psu, lhsT, rhs,
                                    start=(cc == 0), stop=(cc == 3))
                return inst
            S.op("tensor", mmo, reads=["wob", "osb"], writes=[pstag])
            S.op("vector", lambda e, psu=psu, sl2=sl2: e.tensor_add(
                sl2, sl2, psu),
                 reads=[t_sl2, pstag], writes=[t_sl2])
            S.op("scalar", lambda e, w=w, mg=mg, sl2=sl2: e.dma_start(
                out=xf_r[:, mg, 512 * w:512 * (w + 1)], in_=sl2),
                reads=[t_sl2], writes=[f"xfr{mg}"], dma=True,
                sem=f"dsl{mg % 4}")
            if mg + 3 < 8:
                wo_in(mg + 3)

    B.dump(f"attnout_l{l}", E["osb"][:], (128, 4 * L), BF16, tags=["osb"])
    B.dump(f"wqkvb_l{l}", E["wqkvb"][:], (128, 8 * 1536), BF16, tags=["wqkvb"])
    if l == 0 and "w1pb_l0" in B.dbg:
        B.dump("w1pb_l0", B.dram["w1pb0"].ap(), (D, NPAIR * 256), BF16, tags=[])
    B.dump(f"hb_l{l}", E["hb"][:], (128, 8 * L), BF16, tags=["hbf"])
    B.dump(f"qfm_l{l}", E["qfm"][:], (128, 4 * L), BF16, tags=["qfm"])
    B.dump(f"kfm_l{l}", E["kfm"][:], (128, 4 * L), BF16, tags=["kfm"])
    B.dump(f"vsb_l{l}", E["vsb"][:], (128, 16 * 512), BF16, tags=["vsb"])
    B.dump(f"xfattn_l{l}", B.dram["xfscr"].ap(), (D, L), F32, tags=XFALL)


def _ffn_mlp(B, stk, l):
    nc, S = B.nc, B.S
    dr = B.dram
    C = B.common
    def sb(stk_, name, shape, dt=F32):
        return B.sb(stk_, f"{name}_{l}", shape, dt)
    def ps(stk_, name, shape, dt=F32):
        return B.ps(stk_, f"{name}_{l}", shape, dt)
    E = {}
    E["slab"] = sb(stk, "g_slab", [128, 2 * L], F32)
    E["slab2"] = sb(stk, "g_slab2", [128, 4 * 512], F32)
    E["shb"] = sb(stk, "g_shb", [128, L], BF16)
    E["srow"] = sb(stk, "g_srow", [1, L], F32)
    E["srow2"] = sb(stk, "g_srow2", [1, L], F32)
    E["fmu2"] = sb(stk, "g_fmu2", [1, L], F32)
    E["rwst"] = sb(stk, "g_rwst", [64, 2 * L], BF16)
    E["fmu2"] = E["rwst"][0:1, 0:2 * L].bitcast(F32)
    E["smc"] = sb(stk, "g_smc", [128, 68], F32)
    E["ln2h"] = sb(stk, "g_ln2h", [128, 8 * L], BF16)
    E["sp"] = sb(stk, "g_sp", [128, 2 * 512], F32)
    E["w1a"] = sb(stk, "g_w1a", [128, 2048], BF16)
    E["w1b"] = sb(stk, "g_w1b", [128, 2048], BF16)
    E["gga"] = sb(stk, "g_gga", [128, 2 * 512], F32)
    E["ggb"] = sb(stk, "g_ggb", [128, 2 * 512], F32)
    E["gg"] = sb(stk, "g_gg", [128, 22 * 1024], BF16)
    E["w2b"] = sb(stk, "g_w2b", [128, 22 * 1024], BF16)
    psA = ps(stk, "g_psA", [128, 512], F32)
    psB = ps(stk, "g_psB", [128, 512], F32)
    psC = ps(stk, "g_psC", [128, 512], F32)
    psD = ps(stk, "g_psD", [128, 512], F32)
    psq4 = [psA, psB, psC, psD]
    pstags = ["fpsA", "fpsB", "fpsC", "fpsD"]

    xfscr = dr["xfscr"]
    xf_r = xfscr.ap().rearrange("(g p) t -> p g t", p=128)
    smc = E["smc"]
    onesb = C["onesb"]
    ln2h = E["ln2h"]
    ln2h_g = ln2h[:].rearrange("p (g t) -> p g t", g=8)
    gg = E["gg"]

    def load(dst, srcap, tag, rtags=()):
        S.op("sync", lambda e, dst=dst, srcap=srcap: e.dma_start(
            out=dst, in_=srcap), reads=list(rtags), writes=[tag], dma=True)

    load(smc[:, 0:44], dr[f"b1p{l}"].ap(), "smc")
    load(smc[:, 44:52], dr[f"b2{l}"].ap(), "smc")
    load(smc[:, 52:60], dr[f"lng{l}"].ap(), "smc")
    load(smc[:, 60:68], dr[f"lnb{l}"].ap(), "smc")

    for pas in range(2):
        dstrow = E["srow"] if pas == 0 else E["srow2"]
        rtag = "fsrow"
        for dg in range(8):
            if getattr(B, "job_tick", None):
                B.job_tick()
            sl = dg % 2
            slab = E["slab"][:, L * sl:L * (sl + 1)]
            shb = E["shb"][:]
            t_sl, t_sh = f"fslab{sl}", "fshb"
            S.op("sync", lambda e, dg=dg, slab=slab: e.dma_start(
                out=slab, in_=xf_r[:, dg, :]),
                reads=XFALL, writes=[t_sl], dma=True)
            func = AF.Copy if pas == 0 else AF.Square
            if dg < 4:
                S.op("scalar", lambda e, func=func, slab=slab, shb=shb: e.activation(
                    shb, slab, func),
                     reads=[t_sl], writes=[t_sh])
            else:
                S.op("vector", lambda e, shb=shb: e.memset(shb[:, 0:1], 0.0),
                     writes=[t_sh])
                S.op("scalar", lambda e, func=func, slab=slab, shb=shb: e.activation(
                    shb[:, 1:L], slab[:, 0:L - 1], func),
                     reads=[t_sl], writes=[t_sh])

            def mmsum(e, dg=dg, shb=shb):
                for ch in range(4):
                    inst = e.matmul(psq4[ch][0:1, 0:512], onesb[:, 0:1],
                                    shb[:, 512 * ch:512 * (ch + 1)],
                                    start=(dg == 0), stop=(dg == 7))
                return inst
            S.op("tensor", mmsum, reads=[t_sh, "onesb"], writes=pstags)
        for ch in range(4):
            S.op("scalar", lambda e, ch=ch, dstrow=dstrow: e.copy(
                dstrow[0:1, 512 * ch:512 * (ch + 1)], psq4[ch][0:1, 0:512]),
                 reads=pstags, writes=[rtag])

    S.op("vector", lambda e: e.tensor_scalar_mul(
        E["srow"][0:1, :], E["srow"][0:1, :], 1.0 / D),
         reads=["fsrow"], writes=["fsrow"], ses=True)
    S.op("vector", lambda e: e.tensor_scalar_mul(
        E["srow2"][0:1, :], E["srow2"][0:1, :], 1.0 / D),
         reads=["fsrow"], writes=["fsrow"], ses=True)
    S.op("vector", lambda e: e.tensor_tensor(
        E["fmu2"], E["srow"][0:1, :], E["srow"][0:1, :], ALU.mult),
         reads=["fsrow"], writes=["frwst"], ses=True)
    S.op("vector", lambda e: e.tensor_sub(
        E["srow2"][0:1, :], E["srow2"][0:1, :], E["fmu2"]),
         reads=["fsrow", "frwst"], writes=["fsrow"], ses=True)
    S.op("scalar", lambda e: e.activation(
        E["srow2"][0:1, :], E["srow2"][0:1, :], AF.Sqrt, bias=C["cst"][0:1, 0:1]),
         reads=["fsrow", "cst"], writes=["fsrow"])
    S.op("vector", lambda e: e.reciprocal(E["srow2"][0:1, :], E["srow2"][0:1, :]),
         reads=["fsrow"], writes=["fsrow"], ses=True)
    S.op("vector", lambda e: e.scalar_tensor_tensor(
        E["srow"][0:1, :], E["srow"][0:1, :], -1.0, E["srow2"][0:1, :],
        ALU.mult, ALU.mult),
         reads=["fsrow"], writes=["fsrow"], ses=True)
    S.op("vector", lambda e: e.memset(E["rwst"][:], 0.0), writes=["frwst"])

    def hilo(e):
        e.tensor_copy(E["rwst"][0:1, 0:L], E["srow2"][0:1, :])
        e.tensor_tensor(E["rwst"][32:33, 0:L], E["srow2"][0:1, :],
                        E["rwst"][0:1, 0:L], ALU.subtract)
        e.tensor_copy(E["rwst"][0:1, L:2 * L], E["srow"][0:1, :])
        return e.tensor_tensor(E["rwst"][32:33, L:2 * L], E["srow"][0:1, :],
                               E["rwst"][0:1, L:2 * L], ALU.subtract)
    S.op("vector", hilo, reads=["fsrow"], writes=["frwst"], ses=True)

    for dg in range(8):
        if getattr(B, "job_tick", None):
            B.job_tick()
        sl = dg % 2
        slab = E["slab"][:, L * sl:L * (sl + 1)]
        t_sl = f"fslab{sl}"
        S.op("sync", lambda e, dg=dg, slab=slab: e.dma_start(
            out=slab, in_=xf_r[:, dg, :]),
            reads=XFALL, writes=[t_sl], dma=True)
        for ch in range(4):
            csl = ch % 2
            psR = psA if csl == 0 else psC
            psN = psB if csl == 0 else psD
            tR = "fpsA" if csl == 0 else "fpsC"
            tN = "fpsB" if csl == 0 else "fpsD"
            spx = E["sp"][:, 0:512] if csl == 0 else E["gga"][:, 512:1024]
            tspx = "fspA" if csl == 0 else "fggax"
            spy = E["sp"][:, 512:1024] if csl == 0 else E["ggb"][:, 512:1024]
            tspy = "fspB" if csl == 0 else "fggbx"

            def bmr(e, ch=ch, psR=psR):
                return e.matmul(psR[:, 0:512], C["lnsel"][:],
                                E["rwst"][:, 512 * ch:512 * (ch + 1)],
                                start=True, stop=True)
            S.op("tensor", bmr, reads=["frwst", "lnsel"], writes=[tR])

            def bmn(e, ch=ch, psN=psN):
                return e.matmul(psN[:, 0:512], C["lnsel"][:],
                                E["rwst"][:, L + 512 * ch:L + 512 * (ch + 1)],
                                start=True, stop=True)
            S.op("tensor", bmn, reads=["frwst", "lnsel"], writes=[tN])
            if dg < 4:
                S.op("vector", lambda e, ch=ch, slab=slab, psR=psR, spx=spx: e.tensor_mul(
                    spx, slab[:, 512 * ch:512 * (ch + 1)], psR[:, 0:512]),
                     reads=[t_sl, tR], writes=[tspx])
            elif ch == 0:
                S.op("vector", lambda e, spx=spx: e.memset(spx[:, 0:1], 0.0),
                     writes=[tspx])
                S.op("vector", lambda e, slab=slab, psR=psR, spx=spx: e.tensor_mul(
                    spx[:, 1:512], slab[:, 0:511], psR[:, 1:512]),
                     reads=[t_sl, tR], writes=[tspx])
            else:
                S.op("vector", lambda e, ch=ch, slab=slab, psR=psR, spx=spx: e.tensor_mul(
                    spx, slab[:, 512 * ch - 1:512 * (ch + 1) - 1],
                    psR[:, 0:512]),
                     reads=[t_sl, tR], writes=[tspx])
            S.op("vector", lambda e, spx=spx, spy=spy, psN=psN: e.tensor_add(
                spy, spx, psN[:, 0:512]),
                 reads=[tspx, tN], writes=[tspy])
            S.op("vector", lambda e, dg=dg, ch=ch, spy=spy: e.tensor_scalar(
                ln2h_g[:, dg, 512 * ch:512 * (ch + 1)], spy,
                smc[:, 52 + dg:53 + dg], smc[:, 60 + dg:61 + dg],
                ALU.mult, ALU.add),
                 reads=[tspy, "smc"], writes=["fln2h"])

    for half in range(2):
        rows = slice(128 * 11 * half, 128 * 11 * (half + 1))
        cols = slice(11264 * half, 11264 * (half + 1))
        src_ap = dr[f"w2pb{l}"].ap()[rows, :].rearrange("(j p) m -> p j m", p=128)
        dst_ap = E["w2b"][:, cols].rearrange("p (j m) -> p j m", j=11)
        S.op("sync", lambda e, src_ap=src_ap, dst_ap=dst_ap: e.dma_start(
            out=dst_ap, in_=src_ap),
            reads=["fw2b"], writes=["fw2b"], dma=True)

    for th in range(2):
        for j in range(NPAIR):
            if getattr(B, "job_tick", None):
                B.job_tick()
            wslot = E["w1a"] if j % 2 == 0 else E["w1b"]
            wstag = f"fw1a{j % 2}"
            src_ap = dr[f"w1pb{l}"].ap().rearrange("(c p) m -> p c m", p=128)[
                :, :, 256 * j:256 * (j + 1)]
            S.op("sync", lambda e, src_ap=src_ap, wslot=wslot: e.dma_start(
                out=wslot[:].rearrange("p (c m) -> p c m", c=8),
                in_=src_ap),
                reads=[wstag], writes=[wstag], dma=True)
            for tc2 in range(2):
                tc = 2 * th + tc2
                psA_ = psA if tc2 % 2 == 0 else psC
                psB_ = psB if tc2 % 2 == 0 else psD
                tA_ = "fpsA" if tc2 % 2 == 0 else "fpsC"
                tB_ = "fpsB" if tc2 % 2 == 0 else "fpsD"

                def mm1(e, tc=tc, wslot=wslot, psA_=psA_, psB_=psB_):
                    for cc in range(8):
                        a_t = wslot[:, 256 * cc + 0:256 * cc + 128]
                        g_t = wslot[:, 256 * cc + 128:256 * cc + 256]
                        e.matmul(psA_[:, 0:512], a_t,
                                 ln2h_g[:, cc, 512 * tc:512 * (tc + 1)],
                                 start=(cc == 0), stop=(cc == 7))
                        inst = e.matmul(psB_[:, 0:512], g_t,
                                        ln2h_g[:, cc, 512 * tc:512 * (tc + 1)],
                                        start=(cc == 0), stop=(cc == 7))
                    return inst
                S.op("tensor", mm1, reads=[wstag, "fln2h"], writes=[tA_, tB_])
                gsl = tc2 % 2
                gga = E["gga"][:, 512 * gsl:512 * (gsl + 1)]
                ggb = E["ggb"][:, 512 * gsl:512 * (gsl + 1)]
                t_ga, t_gb = f"fgga{gsl}", f"fggb{gsl}"
                S.op("scalar", lambda e, j=j, gga=gga, psA_=psA_: e.add(
                    gga, psA_[:, 0:512], smc[:, 2 * j:2 * j + 1]),
                     reads=[tA_, "smc"], writes=[t_ga])
                S.op("scalar", lambda e, j=j, ggb=ggb, psB_=psB_: e.activation(
                    ggb, psB_[:, 0:512], AF.Gelu,
                    bias=smc[:, 2 * j + 1:2 * j + 2]),
                     reads=[tB_, "smc"], writes=[t_gb])
                S.op("vector", lambda e, j=j, tc2=tc2, gga=gga, ggb=ggb: e.tensor_mul(
                    gg[:, 1024 * j + 512 * tc2: 1024 * j + 512 * (tc2 + 1)],
                    gga, ggb),
                     reads=[t_ga, t_gb], writes=["fgg"])

        for tc2 in range(2):
            tc = 2 * th + tc2

            def r_in(mg, tc=tc):
                sl2 = E["slab2"][:, 512 * (mg % 4):512 * (mg % 4 + 1)]
                S.op("sync", lambda e, tc=tc, mg=mg, sl2=sl2: e.dma_start(
                    out=sl2, in_=xf_r[:, mg, 512 * tc:512 * (tc + 1)]),
                    reads=["xfscr", f"xfr{mg}"], writes=[f"fslab2{mg % 4}"],
                    dma=True, sem=f"dsl{mg % 4}")
            for mg in range(3):
                r_in(mg)
            for mg in range(8):
                if getattr(B, "job_tick", None):
                    B.job_tick()
                psu = [psA, psB, psC, psD][mg % 4]
                pstag = ["fpsA", "fpsB", "fpsC", "fpsD"][mg % 4]
                sl2 = E["slab2"][:, 512 * (mg % 4):512 * (mg % 4 + 1)]
                t_sl2 = f"fslab2{mg % 4}"

                def mm2(e, tc2=tc2, mg=mg, psu=psu):
                    for j in range(NPAIR):
                        lhsT = E["w2b"][:, 1024 * j + 128 * mg:
                                        1024 * j + 128 * (mg + 1)]
                        rhs = gg[:, 1024 * j + 512 * tc2: 1024 * j + 512 * (tc2 + 1)]
                        inst = e.matmul(psu[:, 0:512], lhsT, rhs,
                                        start=(j == 0), stop=(j == NPAIR - 1))
                    return inst
                S.op("tensor", mm2, reads=["fw2b", "fgg"], writes=[pstag])
                S.op("vector", lambda e, mg=mg, psu=psu, sl2=sl2: e.scalar_tensor_tensor(
                    sl2, psu[:, 0:512], smc[:, 44 + mg:45 + mg],
                    sl2, ALU.add, ALU.add),
                     reads=[t_sl2, pstag, "smc"], writes=[t_sl2])
                S.op("scalar", lambda e, tc=tc, mg=mg, sl2=sl2: e.dma_start(
                    out=xf_r[:, mg, 512 * tc:512 * (tc + 1)], in_=sl2),
                    reads=[t_sl2], writes=[f"xfr{mg}"], dma=True,
                    sem=f"dsl{mg % 4}")
                if mg + 3 < 8:
                    r_in(mg + 3)
    B.dump(f"ln2h_l{l}", E["ln2h"][:], (128, 8 * L), BF16, tags=["fln2h"])
    B.dump(f"xfmlp_l{l}", B.dram["xfscr"].ap(), (D, L), F32, tags=XFALL)


def build_final(B, out):
    from contextlib import ExitStack
    nc, S = B.nc, B.S
    dr = B.dram
    C = B.common
    onesb = C["onesb"]
    with ExitStack() as stk:
        sb, ps = B.sb, B.ps
        xfc = sb(stk, "z_xfc", [128, 8 * 512], F32)
        x1c = sb(stk, "z_x1c", [128, 8 * 512], F32)
        hbb = sb(stk, "z_hbb", [128, 8 * 512], BF16)
        srow = sb(stk, "z_srow", [1, 512], F32)
        srow2 = sb(stk, "z_srow2", [1, 512], F32)
        mu2 = sb(stk, "z_mu2", [1, 512], F32)
        rwst = sb(stk, "z_rwst", [64, 2 * 512], BF16)
        sp = sb(stk, "z_sp", [128, 512], F32)
        ofm = sb(stk, "z_ofm", [128, 8 * 512], F32)
        otm = sb(stk, "z_otm", [128, 1024], F32)
        psb0 = ps(stk, "z_psb0", [128, 512], F32)
        psb1 = ps(stk, "z_psb1", [128, 512], F32)
        pst = ps(stk, "z_pst", [128, 512], F32)
        xf_r = dr["xfscr"].ap().rearrange("(g p) t -> p g t", p=128)
        x1_r = dr["x1scr"].ap().rearrange("(g p) t -> p g t", p=128)
        xf_g = xfc[:].rearrange("p (g t) -> p g t", g=8)
        x1_g = x1c[:].rearrange("p (g t) -> p g t", g=8)
        o_g = ofm[:].rearrange("p (g t) -> p g t", g=8)
        hb_g = hbb[:].rearrange("p (g t) -> p g t", g=8)
        T = 512
        for c in range(4):
            S.op("sync", lambda e, c=c: e.dma_start(
                out=xf_g, in_=xf_r[:, :, c * T:(c + 1) * T]),
                reads=XFALL, writes=["zxf"], dma=True)
            S.op("sync", lambda e, c=c: e.dma_start(
                out=x1_g, in_=x1_r[:, :, c * T:(c + 1) * T]),
                reads=["x1scr"], writes=["zx1"], dma=True)
            for pas in range(2):
                func = AF.Copy if pas == 0 else AF.Square
                dstrow = srow if pas == 0 else srow2
                S.op("scalar", lambda e, func=func: e.activation(
                    hbb[:], xfc[:], func), reads=["zxf"], writes=["zhb"])

                def mmsum(e):
                    for cc in range(8):
                        inst = e.matmul(psb0[0:1, 0:T], onesb[:, 0:1],
                                        hb_g[:, cc, :],
                                        start=(cc == 0), stop=(cc == 7))
                    return inst
                S.op("tensor", mmsum, reads=["zhb", "onesb"], writes=["zps0"])
                S.op("scalar", lambda e, dstrow=dstrow: e.copy(
                    dstrow[0:1, :], psb0[0:1, 0:T]),
                     reads=["zps0"], writes=["zrow"])
            S.op("vector", lambda e: e.tensor_scalar_mul(
                srow[0:1, :], srow[0:1, :], 1.0 / D),
                 reads=["zrow"], writes=["zrow"], ses=True)
            S.op("vector", lambda e: e.tensor_scalar_mul(
                srow2[0:1, :], srow2[0:1, :], 1.0 / D),
                 reads=["zrow"], writes=["zrow"], ses=True)
            S.op("vector", lambda e: e.tensor_tensor(
                mu2[0:1, :], srow[0:1, :], srow[0:1, :], ALU.mult),
                 reads=["zrow"], writes=["zmu2"], ses=True)
            S.op("vector", lambda e: e.tensor_sub(
                srow2[0:1, :], srow2[0:1, :], mu2[0:1, :]),
                 reads=["zrow", "zmu2"], writes=["zrow"], ses=True)
            S.op("scalar", lambda e: e.activation(
                srow2[0:1, :], srow2[0:1, :], AF.Sqrt, bias=C["cst"][0:1, 0:1]),
                 reads=["zrow", "cst"], writes=["zrow"])
            S.op("vector", lambda e: e.reciprocal(srow2[0:1, :], srow2[0:1, :]),
                 reads=["zrow"], writes=["zrow"], ses=True)
            S.op("vector", lambda e: e.scalar_tensor_tensor(
                srow[0:1, :], srow[0:1, :], -1.0, srow2[0:1, :],
                ALU.mult, ALU.mult),
                 reads=["zrow"], writes=["zrow"], ses=True)
            S.op("vector", lambda e: e.memset(rwst[:], 0.0), writes=["zrw"])

            def hilo(e):
                e.tensor_copy(rwst[0:1, 0:T], srow2[0:1, :])
                e.tensor_tensor(rwst[32:33, 0:T], srow2[0:1, :],
                                rwst[0:1, 0:T], ALU.subtract)
                e.tensor_copy(rwst[0:1, T:2 * T], srow[0:1, :])
                return e.tensor_tensor(rwst[32:33, T:2 * T], srow[0:1, :],
                                       rwst[0:1, T:2 * T], ALU.subtract)
            S.op("vector", hilo, reads=["zrow"], writes=["zrw"], ses=True)
            S.op("tensor", lambda e: e.matmul(
                psb0[:, 0:T], C["lnsel"][:], rwst[:, 0:T], start=True, stop=True),
                 reads=["zrw", "lnsel"], writes=["zps0"])
            S.op("tensor", lambda e: e.matmul(
                psb1[:, 0:T], C["lnsel"][:], rwst[:, T:2 * T], start=True, stop=True),
                 reads=["zrw", "lnsel"], writes=["zps1"])
            for dg in range(8):
                S.op("vector", lambda e, dg=dg: e.tensor_mul(
                    sp[:], xf_g[:, dg, :], psb0[:, 0:T]),
                     reads=["zxf", "zps0"], writes=["zsp"])
                S.op("vector", lambda e, dg=dg: e.tensor_add(
                    sp[:], sp[:], psb1[:, 0:T]),
                     reads=["zsp", "zps1"], writes=["zsp"])
                S.op("vector", lambda e, dg=dg: e.tensor_add(
                    o_g[:, dg, :], sp[:], x1_g[:, dg, :]),
                     reads=["zsp", "zx1"], writes=["zofm"])
            for q in range(4):
                for dgp in range(4):
                    def trx(e, q=q, dgp=dgp):
                        e.transpose(pst[:, 0:128],
                                    o_g[:, 2 * dgp, 128 * q:128 * (q + 1)],
                                    C["identf"][:])
                        return e.transpose(pst[:, 128:256],
                                           o_g[:, 2 * dgp + 1, 128 * q:128 * (q + 1)],
                                           C["identf"][:])
                    S.op("tensor", trx, reads=["zofm", "identf"], writes=["zpst"])

                    def evx(e, dgp=dgp):
                        return e.tensor_copy(otm[:, 256 * dgp:256 * (dgp + 1)],
                                             pst[:, 0:256])
                    S.op("vector", evx, reads=["zpst"], writes=["zotm"])
                S.op("sync", lambda e, c=c, q=q: e.dma_start(
                    out=out.ap()[512 * c + 128 * q:512 * c + 128 * (q + 1), :],
                    in_=otm[:]),
                    reads=["zotm"], writes=["zout"], dma=True)
    S.barrier()



def _prep_inputs(inputs):
    f32 = np.float32
    asc = np.ascontiguousarray
    rqa, rqb, rka, rkb = _rotary_tables()
    maskL, maskU = _band_masks()
    bsel = np.zeros((32, 32 * 128), dtype=BF16NP)
    for s in range(32):
        bsel[s, 128 * s:128 * (s + 1)] = 1
    lnsel = np.zeros((64, 128), dtype=BF16NP)
    lnsel[0, :] = 1
    lnsel[32, :] = 1
    common = {
        "lnsel": lnsel,
        "masklaf": np.where(maskL > 0, 0.0, -1e30).astype(np.float32),
        "maskuaf": np.where(maskU > 0, 0.0, -1e30).astype(np.float32),
        "bsel": bsel,
        "rqa": rqa, "rqb": rqb, "rka": rka, "rkb": rkb,
        "identb": np.eye(128, dtype=BF16NP),
        "identf": np.eye(128, dtype=f32),
        "jflipb": np.fliplr(np.eye(128)).astype(BF16NP),
        "maskl": maskL, "masku": maskU,
    }
    for d in (1, 2):
        common[f"inwT{d}"] = asc(inputs[f"L0_attn_in_w{d}"].T)
        common[f"xprojT{d}"] = asc(inputs[f"L0_attn_xproj_w{d}"].T)
        common[f"dtwT{d}"] = asc(inputs[f"L0_attn_dt_w{d}"].T)
        common[f"outwT{d}"] = asc(inputs[f"L0_attn_out_w{d}"].T)
        common[f"convw{d}"] = _pg(inputs[f"L0_attn_conv_w{d}"])
        common[f"convb{d}"] = _pg(inputs[f"L0_attn_conv_b{d}"])
        common[f"dtb{d}"] = _pg(inputs[f"L0_attn_dt_b{d}"])
        common[f"dd{d}"] = _pg(inputs[f"L0_attn_D{d}"])
        common[f"alog{d}"] = _pg(inputs[f"L0_attn_A_log{d}"])
    for l in (0, 1):
        common[f"wqkv{l}"] = asc(inputs["L0_ffn_wqkv"][l])
        common[f"wo{l}"] = asc(inputs["L0_ffn_wo"][l])
        common[f"w1p{l}"] = _w1_perm_cols(inputs["L0_ffn_w1"][l])
        common[f"b1p{l}"] = _b1_perm(inputs["L0_ffn_b1"][l])
        common[f"w2p{l}"] = _w2_perm_rows(inputs["L0_ffn_w2"][l])
        common[f"lng{l}"] = _pg(inputs["L0_ffn_ln_g"][l])
        common[f"lnb{l}"] = _pg(inputs["L0_ffn_ln_b"][l])
        common[f"b2{l}"] = _pg(inputs["L0_ffn_b2"][l])
        common[f"qsrow{l}"] = asc(np.tile(inputs["L0_ffn_q_scale"][l], HEADS)[None, :])
        common[f"ksrow{l}"] = asc(np.tile(inputs["L0_ffn_k_scale"][l], HEADS)[None, :])

    x = inputs["L0_attn_x"]
    in_maps = []
    for i in range(x.shape[0]):
        m = dict(common)
        m["x"] = asc(x[i])
        in_maps.append(m)
    return in_maps


def kernel(**inputs):
    in_maps = _prep_inputs(inputs)
    nc, _ = build_program(dbg=DEBUG_REQUEST)
    res = run_bass_kernel_spmd(nc, in_maps, core_ids=list(range(len(in_maps))))
    outs = [np.asarray(r["out"], dtype=np.float32) for r in res.results]
    if DEBUG_REQUEST:
        kernel.debug_results = res.results
    return np.stack(outs, axis=0)


# version 20
# speedup vs baseline: 1.2205x; 1.2205x over previous
import numpy as np
import ml_dtypes
import concourse.bass as bass
import concourse.mybir as mybir
from concourse.bass_utils import run_bass_kernel_spmd

F32 = mybir.dt.float32
BF16 = mybir.dt.bfloat16
AF = mybir.ActivationFunctionType
ALU = mybir.AluOpType
AX = mybir.AxisListType
BF16NP = ml_dtypes.bfloat16

L = 2048
D = 1024
DM = 512
DI = 1024
DS = 16
DTR = 32
TCH = 512
NCH = L // TCH
HEADS = 8
DH = 64
WIN = 512
NW = L // WIN
HID = 2730
NPAIR = 22
HIDP = NPAIR * 128
XFALL = ["xfscr"] + [f"xfr{m}" for m in range(8)]
EPS = 1e-5

DEBUG_REQUEST = []
STRICT_SAME_ENGINE = [True]
ENGINES = ("tensor", "vector", "scalar", "gpsimd", "sync")


class Sched:
    def __init__(self):
        self.units = []
        self.cnt = {e: 0 for e in ENGINES}
        self.last_w = {}
        self.readers = {}

    def op(self, engine, emit, reads=(), writes=(), dma=False, ses=False,
           sem=None):
        assert engine in ENGINES
        key = sem if sem is not None else engine
        inc = 16 if dma else 1
        waits = {}
        strict = ses or STRICT_SAME_ENGINE[0] or engine != "vector"

        def need(eng2, cnt2):
            if eng2 == engine and engine != "sync" and not strict:
                return
            if waits.get(eng2, 0) < cnt2:
                waits[eng2] = cnt2

        for t in reads:
            w = self.last_w.get(t)
            if w is not None:
                need(*w)
        for t in writes:
            w = self.last_w.get(t)
            if w is not None:
                need(*w)
            for eng2, cnt2 in self.readers.get(t, {}).items():
                need(eng2, cnt2)

        self.cnt[key] = self.cnt.get(key, 0) + inc
        me = (key, self.cnt[key])
        for t in reads:
            self.readers.setdefault(t, {})[key] = self.cnt[key]
        for t in writes:
            self.last_w[t] = me
            self.readers[t] = {}
        self.units.append((engine, emit, inc, waits, key))

    def barrier(self):
        snapshot = dict(self.cnt)
        for e in ENGINES:
            waits = {e2: c for e2, c in snapshot.items()
                     if (e2 != e or e == "sync") and c > 0}
            self.cnt[e] += 1
            self.units.append((e, None, 1, waits, e))
        self.last_w = {}
        self.readers = {}

    def finalize(self, block, sems):
        per_engine = {e: [] for e in ENGINES}
        for engine, emit, inc, waits, key in self.units:
            per_engine[engine].append((emit, inc, waits, key))
        final_cnt = dict(self.cnt)

        def body_for(engname):
            def body(eng):
                watermark = {e: 0 for e in final_cnt}
                for emit, inc, waits, key in per_engine[engname]:
                    for e2, c2 in sorted(waits.items()):
                        if watermark[e2] < c2:
                            eng.wait_ge(sems[e2], c2)
                            watermark[e2] = c2
                    inst = eng.nop() if emit is None else emit(eng)
                    inst.then_inc(sems[key], inc)
                if engname == "sync":
                    for e2, c2 in final_cnt.items():
                        eng.wait_ge(sems[e2], c2)
            return body

        block.sync(body_for("sync"))
        block.tensor(body_for("tensor"))
        block.vector(body_for("vector"))
        block.scalar(body_for("scalar"))
        block.gpsimd(body_for("gpsimd"))



def _rotary_tables():
    t = np.arange(L, dtype=np.float64)
    half = np.arange(0, DH, 2, dtype=np.float64)
    inv_freq = 1.0 / (10000.0 ** (half / DH))
    freqs = t[:, None] * inv_freq[None, :]
    freqs = np.concatenate([freqs, freqs], axis=1)
    base = (half + 0.4 * DH) / (1.4 * DH)
    power = (t - L // 2) / float(WIN // 2)
    scale = base[None, :] ** power[:, None]
    scale = np.concatenate([scale, scale], axis=1)
    cos = np.cos(freqs)
    sin = np.sin(freqs)
    sign = np.ones((DH,))
    sign[:DH // 2] = -1.0
    qa = cos * scale * 8.0
    qb = sign[None, :] * sin * scale * 8.0
    ka = cos / scale
    kb = sign[None, :] * sin / scale

    def rep(x):
        return np.ascontiguousarray(np.tile(x.astype(np.float32), (1, HEADS)))
    return rep(qa), rep(qb), rep(ka), rep(kb)


def _band_masks():
    i = np.arange(128)[:, None]
    j = np.arange(128)[None, :]
    maskL = (j <= i).astype(BF16NP)
    maskU = (j >= i).astype(BF16NP)
    return maskL, maskU


def _pg(x):
    x = np.asarray(x)
    if x.ndim == 1:
        x = x[:, None]
    k = x.shape[1]
    return np.ascontiguousarray(
        x.reshape(8, 128, k).transpose(1, 0, 2).reshape(128, 8 * k))


def _w1_perm_cols(w1):
    a = w1[:, :HID]
    g = w1[:, HID:]
    cols = []
    zpad = np.zeros((w1.shape[0], 128 - 42), w1.dtype)
    for j in range(21):
        cols.append(a[:, 128 * j:128 * (j + 1)])
        cols.append(g[:, 128 * j:128 * (j + 1)])
    cols.append(np.concatenate([a[:, 2688:], zpad], axis=1))
    cols.append(np.concatenate([g[:, 2688:], zpad], axis=1))
    return np.ascontiguousarray(np.concatenate(cols, axis=1))


def _b1_perm(b1):
    a = b1[:HID]
    g = b1[HID:]
    cols = []
    for j in range(21):
        cols.append(a[128 * j:128 * (j + 1)])
        cols.append(g[128 * j:128 * (j + 1)])
    pad = np.zeros(128 - 42, b1.dtype)
    cols.append(np.concatenate([a[2688:], pad]))
    cols.append(np.concatenate([g[2688:], pad]))
    return np.ascontiguousarray(np.stack(cols, axis=1))


def _w2_perm_rows(w2):
    rows = [w2[128 * j:128 * (j + 1)] for j in range(21)]
    rows.append(np.concatenate(
        [w2[2688:], np.zeros((128 - 42, w2.shape[1]), w2.dtype)], axis=0))
    return np.ascontiguousarray(np.concatenate(rows, axis=0))



class Build:
    def __init__(self, nc, S, stk, dbg):
        self.nc = nc
        self.S = S
        self.stk = stk
        self.dbg = list(dbg)
        self.dram = {}

    def sb(self, stk, name, shape, dt=F32):
        return stk.enter_context(self.nc.sbuf_tensor(name, shape, dt))

    def ps(self, stk, name, shape, dt=F32):
        return stk.enter_context(self.nc.psum_tensor(name, shape, dt))

    def din(self, name, shape, dt=F32):
        h = self.nc.dram_tensor(name, list(shape), dt, kind="ExternalInput")
        self.dram[name] = h
        return h

    def dscratch(self, name, shape, dt=F32):
        h = self.nc.dram_tensor(name, list(shape), dt, kind="Internal")
        self.dram[name] = h
        return h

    def dout(self, name, shape, dt=F32):
        h = self.nc.dram_tensor(name, list(shape), dt, kind="ExternalOutput")
        self.dram[name] = h
        return h

    def dump(self, name, ap, shape, dt=F32, tags=()):
        if name not in self.dbg:
            return
        h = self.dout("dbg_" + name, shape, dt)
        self.S.op("sync", lambda e, h=h, ap=ap: e.dma_start(out=h.ap(), in_=ap),
                  reads=list(tags), dma=True)


def build_program(dbg=()):
    from contextlib import ExitStack
    nc = bass.Bass("TRN2", target_bir_lowering=False)
    S = Sched()
    stk = ExitStack()
    B = Build(nc, S, stk, dbg)

    B.din("x", (L, D))
    for d in (1, 2):
        B.din(f"inwT{d}", (DM, 2 * DI))
        B.din(f"xprojT{d}", (DI, 2 * DTR))
        B.din(f"dtwT{d}", (DTR, DI))
        B.din(f"outwT{d}", (DI, DM))
        B.din(f"convw{d}", (128, 32))
        B.din(f"convb{d}", (128, 8))
        B.din(f"dtb{d}", (128, 8))
        B.din(f"dd{d}", (128, 8))
        B.din(f"alog{d}", (128, 128))
    for l in (0, 1):
        B.din(f"wqkv{l}", (D, 3 * 512))
        B.din(f"wo{l}", (512, D))
        B.din(f"w1p{l}", (D, NPAIR * 256))
        B.din(f"b1p{l}", (128, 2 * NPAIR))
        B.din(f"w2p{l}", (HIDP, D))
        B.din(f"lng{l}", (128, 8))
        B.din(f"lnb{l}", (128, 8))
        B.din(f"b2{l}", (128, 8))
        B.din(f"qsrow{l}", (1, 512))
        B.din(f"ksrow{l}", (1, 512))
    B.din("rqa", (L, 512))
    B.din("rqb", (L, 512))
    B.din("rka", (L, 512))
    B.din("rkb", (L, 512))
    B.din("bsel", (32, 32 * 128), BF16)
    B.din("lnsel", (64, 128), BF16)
    B.din("identb", (128, 128), BF16)
    B.din("identf", (128, 128), F32)
    B.din("jflipb", (128, 128), BF16)
    B.din("maskl", (128, 128), BF16)
    B.din("masku", (128, 128), BF16)
    B.din("masklTb", (128, 128), BF16)
    B.din("maskuTb", (128, 128), BF16)

    B.dscratch("ln1scr", (L, D), BF16)
    B.dscratch("x1scr", (D, L))
    B.dscratch("xfscr", (D, L))
    out = B.dout("out", (L, D))

    with stk:
        B.gstk = stk
        sems = {e: stk.enter_context(nc.semaphore("sem_" + e)) for e in ENGINES}
        for k in range(4):
            sems[f"dsl{k}"] = stk.enter_context(nc.semaphore(f"sem_dsl{k}"))
        for k in range(2):
            sems[f"dtb{k}"] = stk.enter_context(nc.semaphore(f"sem_dtb{k}"))
        for k in range(2):
            sems[f"dj{k}"] = stk.enter_context(nc.semaphore(f"sem_dj{k}"))
        for k in range(2):
            sems[f"dw{k}"] = stk.enter_context(nc.semaphore(f"sem_dw{k}"))
        block = stk.enter_context(nc.Block())
        C = {}
        C["identb"] = B.sb(stk, "c_identb", [128, 128], BF16)
        C["identf"] = B.sb(stk, "c_identf", [128, 128], F32)
        C["jflipb"] = B.sb(stk, "c_jflipb", [128, 128], BF16)
        C["maskl"] = B.sb(stk, "c_maskl", [128, 128], BF16)
        C["masku"] = B.sb(stk, "c_masku", [128, 128], BF16)
        C["masklTb"] = B.sb(stk, "c_masklTb", [128, 128], BF16)
        C["maskuTb"] = B.sb(stk, "c_maskuTb", [128, 128], BF16)
        C["cst"] = B.sb(stk, "c_cst", [128, 2], F32)
        C["lnsel"] = B.sb(stk, "c_lnsel", [64, 128], BF16)
        C["onesb"] = B.sb(stk, "c_onesb", [128, 1], BF16)
        C["onesf"] = B.sb(stk, "c_onesf", [128, 1], F32)
        C["onesr"] = B.sb(stk, "c_onesr", [1, 128], BF16)
        B.common = C
        B.mjw = (B.sb(stk, "mj_wst0", [128, 256]),
                 B.sb(stk, "mj_wsb0", [128, 256], BF16))
        for nm in ("identb", "identf", "jflipb", "maskl", "masku",
                   "masklTb", "maskuTb", "lnsel"):
            S.op("sync", lambda e, nm=nm: e.dma_start(
                out=C[nm][:], in_=B.dram[nm].ap()), writes=[nm], dma=True)
        S.op("vector", lambda e: e.memset(C["onesb"][:], 1.0), writes=["onesb"])
        S.op("vector", lambda e: e.memset(C["onesf"][:], 1.0), writes=["onesb"])
        S.op("vector", lambda e: e.memset(C["onesr"][:], 1.0), writes=["onesb"])
        S.op("vector", lambda e: e.memset(C["cst"][:, 0:1], EPS), writes=["cst"])
        S.op("vector", lambda e: e.memset(C["cst"][:, 1:2], 1.0), writes=["cst"])
        build_phase0(B)
        build_mamba(B)
        build_ffn(B)
        build_final(B, out)
        S.finalize(block, sems)
    return nc, B



def build_phase0(B):
    from contextlib import ExitStack
    nc, S = B.nc, B.S
    x = B.dram["x"]
    ln1scr = B.dram["ln1scr"]
    with ExitStack() as stk:
        xt = B.sb(stk, "p0_x", [128, D])
        x2t = B.sb(stk, "p0_x2", [128, D])
        lnt = B.sb(stk, "p0_ln", [128, D], BF16)
        st = B.sb(stk, "p0_st", [128, 8])
        cst = B.common["cst"][:, 0:1]

        jobs = []
        for l in (0, 1):
            B.dscratch(f"w1pb{l}", (D, NPAIR * 256), BF16)
            B.dscratch(f"w2pb{l}", (HIDP, D), BF16)
            B.dscratch(f"wqkvbb{l}", (D, 3 * 512), BF16)
            B.dscratch(f"wobb{l}", (512, D), BF16)
            def tile(srcnm, dstnm, rows, width):
                for r in range(rows):
                    for c0 in range(0, width, 256):
                        wdt = min(256, width - c0)
                        jobs.append((
                            B.dram[srcnm].ap()[128 * r:128 * (r + 1),
                                               c0:c0 + wdt],
                            B.dram[dstnm].ap()[128 * r:128 * (r + 1),
                                               c0:c0 + wdt], wdt))
            tile(f"w1p{l}", f"w1pb{l}", 8, NPAIR * 256)
            tile(f"w2p{l}", f"w2pb{l}", 22, 1024)
            tile(f"wqkv{l}", f"wqkvbb{l}", 8, 1536)
            tile(f"wo{l}", f"wobb{l}", 4, 1024)
        B.wjobs = jobs

        for i in range(L // 128):
            S.op("sync", lambda e, i=i: e.dma_start(
                out=xt[:], in_=x.ap()[128 * i:128 * (i + 1), :]),
                writes=["p0x"], dma=True)
            S.op("vector", lambda e: e.reduce_sum(st[:, 0:1], xt[:], axis=AX.X),
                 reads=["p0x"], writes=["p0st"], ses=True)
            S.op("scalar", lambda e: e.activation(x2t[:], xt[:], AF.Square),
                 reads=["p0x"], writes=["p0x2"])
            S.op("vector", lambda e: e.reduce_sum(st[:, 1:2], x2t[:], axis=AX.X),
                 reads=["p0x2"], writes=["p0st"], ses=True)
            S.op("vector", lambda e: e.tensor_scalar_mul(st[:, 2:3], st[:, 0:1], 1.0 / D),
                 reads=["p0st"], writes=["p0st"], ses=True)
            S.op("vector", lambda e: e.tensor_scalar_mul(st[:, 3:4], st[:, 1:2], 1.0 / D),
                 reads=["p0st"], writes=["p0st"], ses=True)
            S.op("vector", lambda e: e.tensor_tensor(
                st[:, 4:5], st[:, 2:3], st[:, 2:3], ALU.mult),
                 reads=["p0st"], writes=["p0st"], ses=True)
            S.op("vector", lambda e: e.tensor_sub(st[:, 3:4], st[:, 3:4], st[:, 4:5]),
                 reads=["p0st"], writes=["p0st"], ses=True)
            S.op("scalar", lambda e: e.activation(st[:, 5:6], st[:, 3:4], AF.Sqrt, bias=cst),
                 reads=["p0st", "cst"], writes=["p0st"])
            S.op("vector", lambda e: e.reciprocal(st[:, 6:7], st[:, 5:6]),
                 reads=["p0st"], writes=["p0st"], ses=True)
            S.op("vector", lambda e: e.scalar_tensor_tensor(
                st[:, 7:8], st[:, 2:3], -1.0, st[:, 6:7], ALU.mult, ALU.mult),
                 reads=["p0st"], writes=["p0st"], ses=True)
            S.op("vector", lambda e: e.tensor_scalar(
                lnt[:], xt[:], st[:, 6:7], st[:, 7:8], ALU.mult, ALU.add),
                 reads=["p0x", "p0st"], writes=["p0ln"], ses=True)
            S.op("sync", lambda e, i=i: e.dma_start(
                out=ln1scr.ap()[128 * i:128 * (i + 1), :], in_=lnt[:]),
                reads=["p0ln"], writes=["ln1scr"], dma=True)
    S.barrier()



def build_mamba(B):
    from contextlib import ExitStack
    nc, S = B.nc, B.S
    dr = B.dram
    with ExitStack() as stk:
        E = {}
        sb, ps = B.sb, B.ps
        E["identb"] = B.common["identb"]
        E["identf"] = B.common["identf"]
        E["jflipb"] = B.common["jflipb"]
        E["onesb"] = B.common["onesb"]
        E["onesr"] = B.common["onesr"]
        E["cst"] = B.common["cst"]
        E["sm1"] = sb(stk, "m_small1", [128, 64])
        E["sm2"] = sb(stk, "m_small2", [128, 64])
        E["A1"] = sb(stk, "m_A1", [128, 128])
        E["A2"] = sb(stk, "m_A2", [128, 128])
        E["xprojT"] = sb(stk, "m_xprojT", [128, 2 * 512], BF16)
        E["dtwT"] = sb(stk, "m_dtwT", [32, 2 * DI], BF16)
        E["inwTb"] = sb(stk, "m_inwT", [128, 4 * 1024], BF16)
        E["outwTb"] = sb(stk, "m_outwT", [128, 8 * 512], BF16)
        E["ln1f"] = sb(stk, "m_ln1f", [128, 4 * TCH], BF16)
        E["lntm"] = sb(stk, "m_lntm", [128, 512], BF16)
        E["xh"] = sb(stk, "m_xh", [128, 8 * (TCH + 3)])
        E["halo"] = sb(stk, "m_halo", [128, 2 * 24])
        E["xc"] = sb(stk, "m_xc", [128, 8 * TCH])
        E["sz"] = sb(stk, "m_sz", [128, 8 * TCH])
        E["du"] = sb(stk, "m_du", [128, 8 * TCH])
        E["ubuf"] = sb(stk, "m_ubuf", [128, 8 * TCH])
        E["ybuf"] = sb(stk, "m_y", [128, 8 * TCH])
        E["bsel"] = sb(stk, "m_bsel", [32, 32 * 128], BF16)
        E["bcf32"] = sb(stk, "m_bcf32", [32, TCH])
        E["bchi"] = sb(stk, "m_bchi", [32, TCH], BF16)
        E["bclo"] = sb(stk, "m_bclo", [32, TCH], BF16)
        E["dbcb"] = sb(stk, "m_dbcb", [32, TCH], BF16)
        E["sp"] = sb(stk, "m_sp", [128, 2 * TCH])
        E["H"] = sb(stk, "m_H", [128, 3 * 2 * TCH])
        E["dA"] = sb(stk, "m_dA", [128, 2 * 2 * TCH])
        E["dBu"] = sb(stk, "m_dBu", [128, 2 * 2 * TCH])
        E["bcB"] = sb(stk, "m_bcB", [128, 2 * TCH])
        E["bcC"] = sb(stk, "m_bcC", [128, 2 * TCH])
        E["carry"] = sb(stk, "m_carry", [128, 2 * 128])
        E["ygb"] = sb(stk, "m_ygb", [128, 8 * TCH], BF16)
        E["hid"] = sb(stk, "m_hid", [128, 8 * TCH])
        E["srow"] = E["bcC"][0:1, 0:512]
        E["srow2"] = E["bcC"][0:1, 512:1024]
        E["rw0"] = E["bcB"][0:1, 0:256].bitcast(BF16)
        E["rw1"] = E["bcB"][0:1, 256:512].bitcast(BF16)
        E["rw2"] = E["bcB"][0:1, 512:768].bitcast(BF16)
        E["rw3"] = E["bcB"][0:1, 768:1024].bitcast(BF16)

        E["hbb"] = E["ygb"]
        E["xcb"] = E["ygb"]
        E["xfmc"] = E["sz"]
        E["x1c"] = E["ybuf"]
        E["xfc"] = E["du"]
        for i in range(4):
            E[f"ps{i}"] = ps(stk, f"m_ps{i}", [128, 512])
        E["psbc0"] = ps(stk, "m_psbc0", [128, 512])
        E["psbc1"] = ps(stk, "m_psbc1", [128, 512])
        E["pst"] = ps(stk, "m_pst", [128, 512])
        E["pstb"] = ps(stk, "m_pstb", [128, 1024], BF16)

        def load(dst_ap, src_ap, tag):
            S.op("sync", lambda e, dst_ap=dst_ap, src_ap=src_ap: e.dma_start(
                out=dst_ap, in_=src_ap), writes=[tag], dma=True)

        load(E["identb"][:], dr["identb"].ap(), "identb")
        load(E["bsel"][:], dr["bsel"].ap(), "bsel")
        load(E["identf"][:], dr["identf"].ap(), "identf")
        load(E["jflipb"][:], dr["jflipb"].ap(), "jflipb")
        S.op("vector", lambda e: e.memset(E["onesb"][:], 1.0), writes=["onesb"])
        S.op("vector", lambda e: e.memset(E["onesr"][:], 1.0), writes=["onesb"])
        S.op("vector", lambda e: e.memset(E["cst"][:, 0:1], EPS), writes=["cst"])
        S.op("vector", lambda e: e.memset(E["cst"][:, 1:2], 1.0), writes=["cst"])
        E["inwTscr"] = {}
        E["outwTscr"] = {}
        with ExitStack() as pstk:
            wstage = E["xc"][:, 0:2048]
            wstb = E["ln1f"]
            for d in (1, 2):
                sm = E[f"sm{d}"]
                load(sm[:, 0:32], dr[f"convw{d}"].ap(), f"sm{d}")
                load(sm[:, 32:40], dr[f"convb{d}"].ap(), f"sm{d}")
                load(sm[:, 40:48], dr[f"dtb{d}"].ap(), f"sm{d}")
                load(sm[:, 48:56], dr[f"dd{d}"].ap(), f"sm{d}")
                A = E[f"A{d}"]
                load(A[:], dr[f"alog{d}"].ap(), f"A{d}")
                S.op("scalar", lambda e, A=A: e.activation(A[:], A[:], AF.Exp),
                     reads=[f"A{d}"], writes=[f"A{d}"])
                S.op("vector", lambda e, A=A: e.tensor_scalar_mul(A[:], A[:], -1.0),
                     reads=[f"A{d}"], writes=[f"A{d}"])
                src = dr[f"xprojT{d}"].ap().rearrange("(g p) e -> p g e", p=128)
                load(wstage[:, 0:512].rearrange("p (g e) -> p g e", g=8), src, "wstage")
                S.op("vector", lambda e, d=d: e.tensor_copy(
                    E["xprojT"][:, 512 * (d - 1):512 * d], wstage[:, 0:512]),
                     reads=["wstage"], writes=["xprojT"])
                load(wstage[0:32, 0:DI], dr[f"dtwT{d}"].ap(), "wstage")
                S.op("vector", lambda e, d=d: e.tensor_copy(
                    E["dtwT"][:, DI * (d - 1):DI * d], wstage[0:32, 0:DI]),
                     reads=["wstage"], writes=["dtwT"])

            for d in (1, 2):
                E["inwTscr"][d] = B.dscratch(f"inwTbscr{d}", (DM, 2 * DI), BF16)
                E["outwTscr"][d] = B.dscratch(f"outwTbscr{d}", (DI, DM), BF16)
                for piece in range(4):
                    sl = slice(128 * piece, 128 * (piece + 1))
                    load(wstage[:, 0:2048], dr[f"inwT{d}"].ap()[sl, :], "wstage")
                    S.op("vector", lambda e: e.tensor_copy(
                        wstb[:], wstage[:, 0:2048]),
                         reads=["wstage"], writes=["wstb"])
                    S.op("sync", lambda e, d=d, sl=sl: e.dma_start(
                        out=E["inwTscr"][d].ap()[sl, :], in_=wstb[:]),
                        reads=["wstb"], writes=["wstage", f"inwTscr{d}"], dma=True)
                for piece in range(2):
                    src = dr[f"outwT{d}"].ap()[512 * piece:512 * (piece + 1), :]
                    src = src.rearrange("(g p) m -> p g m", p=128)
                    load(wstage[:, 0:2048].rearrange("p (g m) -> p g m", g=4), src, "wstage")
                    S.op("vector", lambda e: e.tensor_copy(
                        wstb[:], wstage[:, 0:2048]),
                         reads=["wstage"], writes=["wstb"])
                    dst = E["outwTscr"][d].ap()[512 * piece:512 * (piece + 1), :]
                    dst = dst.rearrange("(g p) m -> p g m", p=128)
                    S.op("sync", lambda e, dst=dst: e.dma_start(
                        out=dst, in_=wstb[:].rearrange("p (g m) -> p g m", g=4)),
                        reads=["wstb"], writes=["wstage", f"outwTscr{d}"], dma=True)
            S.barrier()

        S.op("vector", lambda e: e.memset(E["carry"][:], 0.0), writes=["carry"])
        S.op("vector", lambda e: e.memset(E["halo"][:], 0.0), writes=["halo"])

        wsts = [B.mjw[0]]
        wsbs = [B.mjw[1]]
        jobs = B.wjobs
        tick_bound = [len(jobs)]
        B.tick_bound = tick_bound

        def job_in(ji):
            sap, dap, wdt = jobs[ji]
            wst = wsts[0]
            S.op("sync", lambda e, sap=sap, wst=wst, wdt=wdt: e.dma_start(
                out=wst[:, 0:wdt], in_=sap),
                writes=["p0wst0"], dma=True, sem="dj0")

        def job_castout(ji):
            sap, dap, wdt = jobs[ji]
            wst = wsts[0]
            wsb = wsbs[0]
            S.op("scalar", lambda e, wst=wst, wsb=wsb, wdt=wdt: e.copy(
                wsb[:, 0:wdt], wst[:, 0:wdt]),
                 reads=["p0wst0"], writes=["p0wsb0"])
            S.op("scalar", lambda e, dap=dap, wsb=wsb, wdt=wdt: e.dma_start(
                out=dap, in_=wsb[:, 0:wdt]),
                reads=["p0wsb0"], writes=["p0out0"], dma=True, sem="dj0")

        tick_state = [0, 0]

        def job_tick():
            tick_state[0] += 1
            for _ in range(2):
                ji = tick_state[1]
                if ji >= tick_bound[0]:
                    return
                job_castout(ji)
                if ji + 1 < len(jobs):
                    job_in(ji + 1)
                tick_state[1] = ji + 1

        def job_flush():
            while tick_state[1] < tick_bound[0]:
                ji = tick_state[1]
                job_castout(ji)
                if ji + 1 < len(jobs):
                    job_in(ji + 1)
                tick_state[1] = ji + 1
        B.job_tick = job_tick
        B.job_flush = job_flush
        job_in(0)

        for c in range(NCH):
            for d in (1, 2):
                _mamba_dir_chunk(B, E, c, d)
            _mix_tail_chunk(B, E, c)
        job_flush()
        B.job_tick = None
    S.barrier()


def _mamba_dir_chunk(B, E, c, d):
    nc, S = B.nc, B.S
    dr = B.dram
    T = TCH
    ln1scr = dr["ln1scr"]
    A = E[f"A{d}"]
    sm = E[f"sm{d}"]
    smtag = f"sm{d}"
    inwTscr = E["inwTscr"][d]
    outwTscr = E["outwTscr"][d]
    inwTb, outwTb = E["inwTb"], E["outwTb"]
    ln1f, lntm = E["ln1f"], E["lntm"]
    xh, halo, xc, xcb, sz = E["xh"], E["halo"], E["xc"], E["xcb"], E["sz"]
    du, ybuf, dbcb, sp = E["du"], E["ybuf"], E["dbcb"], E["sp"]
    prod = sp[:, TCH:2 * TCH]
    bcf32, bchi, bclo, bsel = E["bcf32"], E["bchi"], E["bclo"], E["bsel"]
    Hbuf, dAbuf, dBubuf = E["H"], E["dA"], E["dBu"]
    bcB, bcC, carry, ygb, hid = E["bcB"], E["bcC"], E["carry"], E["ygb"], E["hid"]
    onesb = E["onesb"]
    psrot = [E["ps0"], E["ps1"], E["ps2"], E["ps3"]]
    psbc0, psbc1, pstb = E["psbc0"], E["psbc1"], E["pstb"]

    xh_g = xh[:].rearrange("p (g t) -> p g t", g=8)
    xc_g = xc[:].rearrange("p (g t) -> p g t", g=8)
    sz_g = sz[:].rearrange("p (g t) -> p g t", g=8)
    du_g = du[:].rearrange("p (g t) -> p g t", g=8)
    y_g = ybuf[:].rearrange("p (g t) -> p g t", g=8)
    yg_g = ygb[:].rearrange("p (g t) -> p g t", g=8)
    hid_g = hid[:].rearrange("p (g t) -> p g t", g=8)
    halo_d = halo[:].rearrange("p (d g k) -> p d g k", d=2, g=8)
    ln1f_g = ln1f[:].rearrange("p (f t) -> p f t", f=4)


    for q in range(T // 128):
        if d == 1:
            trow = c * T + q * 128
            src = ln1scr.ap()[trow:trow + 128, 0:DM]
        else:
            trow = L - (c * T + q * 128) - 128
            src = ln1scr.ap()[trow:trow + 128, DM:D]
        S.op("sync", lambda e, src=src: e.dma_start(out=lntm[:], in_=src),
             reads=["ln1scr"], writes=["lntm"], dma=True)
        for p in range(4):
            def tr(e, p=p, d=d):
                ident = E["identb"] if d == 1 else E["jflipb"]
                return e.transpose(pstb[:, 128 * p:128 * (p + 1)],
                                   lntm[:, 128 * p:128 * (p + 1)], ident[:])
            S.op("tensor", tr, reads=["lntm", "identb", "jflipb"], writes=["pstb"])

        def evac(e, q=q):
            return e.tensor_copy(ln1f_g[:, :, 128 * q:128 * (q + 1)],
                                 pstb[:, 0:512].rearrange("p (f t) -> p f t", f=4))
        S.op("vector", evac, reads=["pstb"], writes=["ln1f"])

    for half in range(2):
        for cc in range(4):
            src_ap = inwTscr.ap()[128 * cc:128 * (cc + 1),
                                  1024 * half:1024 * (half + 1)]
            S.op("sync", lambda e, cc=cc, src_ap=src_ap: e.dma_start(
                out=inwTb[:, 1024 * cc:1024 * (cc + 1)], in_=src_ap),
                reads=[f"inwTscr{d}"], writes=["inwTb"], dma=True)
        for sub in range(8):
            eg = half * 8 + sub
            psu = psrot[sub % 4]
            pstag = f"mps{sub % 4}"

            def mm(e, sub=sub, psu=psu):
                for cc in range(4):
                    lhsT = inwTb[:, 1024 * cc + 128 * sub: 1024 * cc + 128 * (sub + 1)]
                    inst = e.matmul(psu[:, 0:T], lhsT, ln1f_g[:, cc, :],
                                    start=(cc == 0), stop=(cc == 3))
                return inst
            S.op("tensor", mm, reads=["inwTb", "ln1f"], writes=[pstag])
            if eg < 8:
                def evacxh(e, eg=eg, psu=psu):
                    return e.copy(xh_g[:, eg, 3:3 + T], psu[:, 0:T])
                S.op("scalar", evacxh, reads=[pstag], writes=["xh"])
            else:
                def evacz(e, eg=eg, psu=psu):
                    return e.activation(sz_g[:, eg - 8, :], psu[:, 0:T], AF.Silu)
                S.op("scalar", evacz, reads=[pstag], writes=["sz"])

    S.op("vector", lambda e: e.tensor_copy(xh_g[:, :, 0:3], halo_d[:, d - 1]),
         reads=["halo"], writes=["xh"])
    for dg in range(8):
        acc = sp[:, 0:T]

        def conv0(e, dg=dg, acc=acc):
            return e.tensor_scalar_mul(acc, xh_g[:, dg, 0:T], sm[:, 4 * dg:4 * dg + 1])
        S.op("vector", conv0, reads=["xh", smtag], writes=["spA"])
        for k in (1, 2, 3):
            def convk(e, dg=dg, k=k, acc=acc):
                return e.scalar_tensor_tensor(
                    acc, xh_g[:, dg, k:k + T], sm[:, 4 * dg + k:4 * dg + k + 1],
                    acc, ALU.mult, ALU.add)
            S.op("vector", convk, reads=["xh", smtag, "spA"], writes=["spA"])

        def silu(e, dg=dg, acc=acc):
            return e.activation(xc_g[:, dg, :], acc, AF.Silu,
                                bias=sm[:, 32 + dg:33 + dg])
        S.op("scalar", silu, reads=["spA", smtag], writes=["xc"])
    S.op("vector", lambda e: e.tensor_copy(xcb[:], xc[:]), reads=["xc"], writes=["ygb"])
    S.op("vector", lambda e, d=d: e.tensor_copy(
        halo_d[:, d - 1], xh_g[:, :, T:T + 3]), reads=["xh"], writes=["halo"])

    xcb_g = xcb[:].rearrange("p (g t) -> p g t", g=8)

    def xproj(e, d=d):
        xp = E["xprojT"][:, 512 * (d - 1):512 * d].rearrange("p (g e) -> p g e", g=8)
        for cc in range(8):
            inst = e.matmul(E["ps0"][0:64, 0:T], xp[:, cc, :], xcb_g[:, cc, :],
                            start=(cc == 0), stop=(cc == 7))
        return inst
    S.op("tensor", xproj, reads=["xprojT", "ygb"], writes=["mps0"])
    S.op("scalar", lambda e: e.copy(dbcb[:, 0:T], E["ps0"][0:32, 0:T]),
         reads=["mps0"], writes=["dbcb"])
    S.op("scalar", lambda e: e.copy(bcf32[:, 0:T], E["ps0"][32:64, 0:T]),
         reads=["mps0"], writes=["dbc"])
    S.op("vector", lambda e: e.tensor_copy(bchi[:, 0:T], bcf32[:, 0:T]),
         reads=["dbc"], writes=["bchi"], ses=True)
    S.op("vector", lambda e: e.tensor_tensor(bclo[:, 0:T], bcf32[:, 0:T],
                                             bchi[:, 0:T], ALU.subtract),
         reads=["dbc", "bchi"], writes=["bchi"], ses=True)

    for mg in range(8):
        def dtmm(e, mg=mg, d=d):
            return e.matmul(
                E["ps1"][:, 0:T],
                E["dtwT"][:, DI * (d - 1) + 128 * mg: DI * (d - 1) + 128 * (mg + 1)],
                dbcb[:, 0:T], start=True, stop=True)
        S.op("tensor", dtmm, reads=["dtwT", "dbcb"], writes=["mps1"])
        raw = sp[:, 0:T]
        t1 = sp[:, T:2 * T]

        def rawev(e, mg=mg):
            return e.add(raw, E["ps1"][:, 0:T], sm[:, 40 + mg:41 + mg])
        S.op("scalar", rawev, reads=["mps1", smtag], writes=["spA"])
        S.op("vector", lambda e: e.scalar_tensor_tensor(
            t1, raw, -1.0, raw, ALU.mult, ALU.min),
             reads=["spA"], writes=["spB"])
        S.op("scalar", lambda e: e.activation(t1, t1, AF.Exp),
             reads=["spB"], writes=["spB"])
        S.op("scalar", lambda e: e.activation(t1, t1, AF.Ln, bias=E["cst"][:, 1:2]),
             reads=["spB", "cst"], writes=["spB"])

        def sfin(e, mg=mg):
            return e.scalar_tensor_tensor(du_g[:, mg, :], raw, 0.0, t1,
                                          ALU.max, ALU.add)
        S.op("vector", sfin, reads=["spA", "spB"], writes=["du"])

    ubuf = E["ubuf"]
    u_g = ubuf[:].rearrange("p (g t) -> p g t", g=8)
    S.op("vector", lambda e: e.tensor_mul(ubuf[:], du[:], xc[:]),
         reads=["du", "xc"], writes=["ubuf"])

    prodA = sp[:, 0:2 * T]
    treA = ln1f[:].bitcast(F32)[:, 0:T]
    prodB = xh[:, 0:2 * T]
    treB = xh[:, 2 * T:3 * T]
    NG = 8
    for grp in range(NG):
        SLg = slice((grp % 2) * 2 * T, ((grp % 2) + 1) * 2 * T)
        bcB2 = bcB[:, 0:2 * T]
        bcC2 = bcC[:, 0:2 * T]
        tBc, tCc = "bcB", "bcC"
        for i in range(2):
            s = 2 * grp + i

            def bcmmB(e, s=s):
                sel = bsel[:, 128 * s:128 * (s + 1)]
                e.matmul(psbc0[:, 0:T], sel, bchi[:, 0:T], start=True, stop=False)
                return e.matmul(psbc0[:, 0:T], sel, bclo[:, 0:T], start=False, stop=True)
            S.op("tensor", bcmmB, reads=["bchi", "bsel"], writes=["psbc0"])
            S.op("scalar", lambda e, i=i, bcB2=bcB2: e.copy(
                bcB2[:, i * T:(i + 1) * T], psbc0[:, 0:T]),
                 reads=["psbc0"], writes=[tBc])

            def bcmmC(e, s=s):
                sel = bsel[:, 128 * (16 + s):128 * (17 + s)]
                e.matmul(psbc1[:, 0:T], sel, bchi[:, 0:T], start=True, stop=False)
                return e.matmul(psbc1[:, 0:T], sel, bclo[:, 0:T], start=False, stop=True)
            S.op("tensor", bcmmC, reads=["bchi", "bsel"], writes=["psbc1"])
            S.op("scalar", lambda e, i=i, bcC2=bcC2: e.copy(
                bcC2[:, i * T:(i + 1) * T], psbc1[:, 0:T]),
                 reads=["psbc1"], writes=[tCc])

        def stage_p1(dg, grp=grp):
            slot = dg % 2
            dA2 = dAbuf[:, slot * 2 * T:(slot + 1) * 2 * T]

            def p1(e, dg=dg, grp=grp, dA2=dA2):
                for i in range(2):
                    s = 2 * grp + i
                    inst = e.activation(dA2[:, i * T:(i + 1) * T], du_g[:, dg, :],
                                        AF.Exp, scale=A[:, 16 * dg + s:16 * dg + s + 1])
                return inst
            S.op("scalar", p1, reads=["du", f"A{d}"], writes=[f"dA{slot}"])

        def stage_p2(dg, grp=grp):
            slot = dg % 2
            dBu2 = dBubuf[:, slot * 2 * T:(slot + 1) * 2 * T]

            def p2(e, dg=dg, dBu2=dBu2):
                ub = u_g[:, dg, :].unsqueeze(1).broadcast_to((128, 2, T))
                return e.tensor_tensor(
                    dBu2.rearrange("p (i t) -> p i t", i=2), ub,
                    bcB2.rearrange("p (i t) -> p i t", i=2), ALU.mult)
            S.op("gpsimd", p2, reads=["ubuf", tBc], writes=[f"dB{slot}"])

        def stage_p3(dg, grp=grp):
            slot = dg % 2
            hslot = dg % 3
            dA2 = dAbuf[:, slot * 2 * T:(slot + 1) * 2 * T]
            dBu2 = dBubuf[:, slot * 2 * T:(slot + 1) * 2 * T]
            H2 = Hbuf[:, hslot * 2 * T:(hslot + 1) * 2 * T]

            def p3(e, dg=dg, grp=grp, dA2=dA2, dBu2=dBu2, H2=H2, d=d):
                for i in range(2):
                    s = 2 * grp + i
                    init = carry[:, 128 * (d - 1) + 16 * dg + s:
                                 128 * (d - 1) + 16 * dg + s + 1]
                    inst = e.tensor_tensor_scan(
                        H2[:, i * T:(i + 1) * T], dA2[:, i * T:(i + 1) * T],
                        dBu2[:, i * T:(i + 1) * T], init, ALU.mult, ALU.add)
                return inst
            S.op("vector", p3, reads=[f"dA{slot}", f"dB{slot}", "carry"],
                 writes=[f"H{hslot}", "carry"])

        def stage_p4a(dg, grp=grp):
            hslot = dg % 3
            pslot = dg % 2
            H2 = Hbuf[:, hslot * 2 * T:(hslot + 1) * 2 * T]
            prod2 = prodA if pslot == 0 else prodB
            ptags = ["spA", "spB"] if pslot == 0 else ["xh"]

            def p4a(e, H2=H2, prod2=prod2):
                return e.tensor_mul(prod2, H2, bcC2)
            S.op("gpsimd", p4a, reads=[f"H{hslot}", tCc], writes=ptags)

        def stage_p4t(dg, grp=grp):
            hslot = dg % 3
            pslot = dg % 2
            H2 = Hbuf[:, hslot * 2 * T:(hslot + 1) * 2 * T]
            prod2 = prodA if pslot == 0 else prodB
            tre = treA if pslot == 0 else treB
            ptags = ["spA", "spB"] if pslot == 0 else ["xh"]
            ttags = ["ln1f"] if pslot == 0 else ["xh"]

            def cup(e, dg=dg, grp=grp, d=d, H2=H2):
                dst = carry[:, 128 * (d - 1) + 16 * dg + 2 * grp:
                            128 * (d - 1) + 16 * dg + 2 * grp + 2]
                return e.copy(
                    dst, H2.rearrange("p (i t) -> p i t", i=2)[:, :, T - 1:T])
            S.op("scalar", cup, reads=[f"H{hslot}"], writes=["carry"])

            def p4t(e, dg=dg, grp=grp, prod2=prod2, tre=tre):
                e.tensor_add(tre, prod2[:, 0:T], prod2[:, T:2 * T])
                if grp == 0:
                    return e.tensor_copy(y_g[:, dg, :], tre)
                return e.tensor_add(y_g[:, dg, :], y_g[:, dg, :], tre)
            S.op("vector", p4t, reads=ptags + ["ybuf"] + ttags,
                 writes=["ybuf"] + ttags)

        stage_p1(0); stage_p2(0)
        stage_p1(1); stage_p2(1)
        stage_p3(0)
        for dg in range(8):
            if getattr(B, "job_tick", None):
                B.job_tick()
            if dg + 1 < 8:
                stage_p3(dg + 1)
            stage_p4a(dg)
            if dg + 2 < 8:
                stage_p1(dg + 2)
                stage_p2(dg + 2)
            stage_p4t(dg)

    for dg in range(8):
        def gate1(e, dg=dg):
            return e.scalar_tensor_tensor(
                y_g[:, dg, :], xc_g[:, dg, :], sm[:, 48 + dg:49 + dg],
                y_g[:, dg, :], ALU.mult, ALU.add)
        S.op("vector", gate1, reads=["xc", smtag, "ybuf"], writes=["ybuf"])

        def gate2(e, dg=dg):
            return e.tensor_mul(yg_g[:, dg, :], y_g[:, dg, :], sz_g[:, dg, :])
        S.op("vector", gate2, reads=["ybuf", "sz"], writes=["ygb"])

    for gg in range(8):
        S.op("sync", lambda e, gg=gg: e.dma_start(
            out=outwTb[:, 512 * gg:512 * (gg + 1)],
            in_=outwTscr.ap()[128 * gg:128 * (gg + 1), :]),
            reads=[f"outwTscr{d}"], writes=["outwTb"], dma=True)
    for mg in range(4):
        psu = psrot[mg % 2]
        pstag = f"mps{mg % 2}"

        def opmm(e, mg=mg, psu=psu):
            for cc in range(8):
                lhsT = outwTb[:, 512 * cc + 128 * mg: 512 * cc + 128 * (mg + 1)]
                inst = e.matmul(psu[:, 0:T], lhsT, yg_g[:, cc, :],
                                start=(cc == 0), stop=(cc == 7))
            return inst
        S.op("tensor", opmm, reads=["outwTb", "ygb"], writes=[pstag])
        row = (d - 1) * 4 + mg

        def opev(e, row=row, psu=psu):
            return e.copy(hid_g[:, row, :], psu[:, 0:T])
        S.op("scalar", opev, reads=[pstag], writes=["hid"])

    if c == 0:
        B.dump(f"dA_s15_d{d}", dAbuf[:], (128, 2 * TCH), F32, tags=["dA0", "dA1"])
        B.dump(f"bcB_s15_d{d}", bcB[:], (128, TCH), F32, tags=["bcB"])
        B.dump(f"bcC_s15_d{d}", bcC[:], (128, TCH), F32, tags=["bcC"])
        B.dump(f"H_s15_d{d}", Hbuf[:], (128, 2 * TCH), F32, tags=["H0", "H1"])
        B.dump(f"sz_c0_d{d}", sz[:], (128, 8 * TCH), F32, tags=["sz"])
        B.dump(f"u_c0_d{d}", E["ubuf"][:], (128, 8 * TCH), F32, tags=["ubuf"])
    B.dump(f"xc_c{c}_d{d}", xc[:], (128, 8 * TCH), F32, tags=["xc"])
    B.dump(f"du_c{c}_d{d}", du[:], (128, 8 * TCH), F32, tags=["du"])
    B.dump(f"y_c{c}_d{d}", ybuf[:], (128, 8 * TCH), F32, tags=["ybuf"])
    B.dump(f"hid_c{c}_d{d}", hid[:], (128, 8 * TCH), F32, tags=["hid"])


def _fm_ln(B, E, src_ap, src_tag, T):
    nc, S = B.nc, B.S
    hbb, onesb = E["hbb"], E["onesb"]
    srow, srow2, sp = E["srow"], E["srow2"], E["sp"]
    rw0, rw1, rw2, rw3 = E["rw0"], E["rw1"], E["rw2"], E["rw3"]
    T_ = TCH
    psbc0, psbc1 = E["psbc0"], E["psbc1"]
    hb_g = hbb[:].rearrange("p (g t) -> p g t", g=8)

    S.op("scalar", lambda e: e.activation(hbb[:], src_ap, AF.Copy),
         reads=[src_tag], writes=["ygb"])

    def mm1(e):
        for cc in range(8):
            inst = e.matmul(psbc0[0:1, 0:T], onesb[:, 0:1], hb_g[:, cc, :],
                            start=(cc == 0), stop=(cc == 7))
        return inst
    S.op("tensor", mm1, reads=["ygb", "onesb"], writes=["psbc0"])
    S.op("scalar", lambda e: e.copy(srow[0:1, 0:T], psbc0[0:1, 0:T]),
         reads=["psbc0"], writes=["bcC"])
    S.op("scalar", lambda e: e.activation(hbb[:], src_ap, AF.Square),
         reads=[src_tag], writes=["ygb"])

    def mm2(e):
        for cc in range(8):
            inst = e.matmul(psbc0[0:1, 0:T], onesb[:, 0:1], hb_g[:, cc, :],
                            start=(cc == 0), stop=(cc == 7))
        return inst
    S.op("tensor", mm2, reads=["ygb", "onesb"], writes=["psbc0"])
    S.op("scalar", lambda e: e.copy(srow2[0:1, 0:T], psbc0[0:1, 0:T]),
         reads=["psbc0"], writes=["bcC"])

    S.op("vector", lambda e: e.tensor_scalar_mul(srow[0:1, 0:T], srow[0:1, 0:T], 1.0 / D),
         reads=["bcC"], writes=["bcC"], ses=True)
    S.op("vector", lambda e: e.tensor_scalar_mul(srow2[0:1, 0:T], srow2[0:1, 0:T], 1.0 / D),
         reads=["bcC"], writes=["bcC"], ses=True)
    S.op("vector", lambda e: e.tensor_tensor(sp[0:1, 0:T], srow[0:1, 0:T],
                                             srow[0:1, 0:T], ALU.mult),
         reads=["bcC"], writes=["spA"], ses=True)
    S.op("vector", lambda e: e.tensor_sub(srow2[0:1, 0:T], srow2[0:1, 0:T], sp[0:1, 0:T]),
         reads=["bcC", "spA"], writes=["bcC"], ses=True)
    S.op("scalar", lambda e: e.activation(srow2[0:1, 0:T], srow2[0:1, 0:T],
                                          AF.Sqrt, bias=E["cst"][0:1, 0:1]),
         reads=["bcC", "cst"], writes=["bcC"])
    S.op("vector", lambda e: e.reciprocal(srow2[0:1, 0:T], srow2[0:1, 0:T]),
         reads=["bcC"], writes=["bcC"], ses=True)
    S.op("vector", lambda e: e.scalar_tensor_tensor(
        srow[0:1, 0:T], srow[0:1, 0:T], -1.0, srow2[0:1, 0:T], ALU.mult, ALU.mult),
         reads=["bcC"], writes=["bcC"], ses=True)

    def hilo(e):
        e.tensor_copy(rw0, srow2[0:1, 0:T])
        e.tensor_tensor(rw1, srow2[0:1, 0:T], rw0, ALU.subtract)
        e.tensor_copy(rw2, srow[0:1, 0:T])
        return e.tensor_tensor(rw3, srow[0:1, 0:T], rw2, ALU.subtract)
    S.op("vector", hilo, reads=["bcC"], writes=["rowhl", "bcB"], ses=True)

    def bmr(e):
        onesr = E["onesr"]
        e.matmul(psbc0[:, 0:T], onesr[:], rw0, start=True, stop=False)
        return e.matmul(psbc0[:, 0:T], onesr[:], rw1, start=False, stop=True)
    S.op("tensor", bmr, reads=["rowhl", "bcB", "onesb"], writes=["psbc0"])

    def bmn(e):
        onesr = E["onesr"]
        e.matmul(psbc1[:, 0:T], onesr[:], rw2, start=True, stop=False)
        return e.matmul(psbc1[:, 0:T], onesr[:], rw3, start=False, stop=True)
    S.op("tensor", bmn, reads=["rowhl", "bcB", "onesb"], writes=["psbc1"])


def _mix_tail_chunk(B, E, c):
    nc, S = B.nc, B.S
    T = TCH
    x = B.dram["x"]
    x1scr = B.dram["x1scr"]
    xfscr = B.dram["xfscr"]
    hid, sp = E["hid"], E["sp"]
    xtm = E["ln1f"][:].bitcast(F32)
    xfmc, x1c, xfc = E["xfmc"], E["x1c"], E["xfc"]
    psbc0, psbc1, pst = E["psbc0"], E["psbc1"], E["pst"]
    identf = E["identf"]
    hid_g = hid[:].rearrange("p (g t) -> p g t", g=8)
    x1_g = x1c[:].rearrange("p (g t) -> p g t", g=8)
    xf_g = xfc[:].rearrange("p (g t) -> p g t", g=8)
    xfm_g = xfmc[:].rearrange("p (g t) -> p g t", g=8)

    for q in range(T // 128):
        S.op("sync", lambda e, q=q: e.dma_start(
            out=xtm, in_=x.ap()[c * T + 128 * q: c * T + 128 * (q + 1), :]),
            writes=["ln1f"], dma=True)
        for dgp in range(4):
            def trx(e, dgp=dgp):
                e.transpose(pst[:, 0:128], xtm[:, 256 * dgp:256 * dgp + 128], identf[:])
                return e.transpose(pst[:, 128:256],
                                   xtm[:, 256 * dgp + 128:256 * dgp + 256], identf[:])
            S.op("tensor", trx, reads=["ln1f", "identf"], writes=["pstT"])

            def evx(e, q=q, dgp=dgp):
                return e.tensor_copy(
                    xfm_g[:, 2 * dgp:2 * dgp + 2, 128 * q:128 * (q + 1)],
                    pst[:, 0:256].rearrange("p (g t) -> p g t", g=2))
            S.op("vector", evx, reads=["pstT"], writes=["sz"])

    _fm_ln(B, E, hid[:], "hid", T)
    for dg in range(8):
        S.op("vector", lambda e, dg=dg: e.tensor_mul(
            sp[:, 0:T], hid_g[:, dg, :], psbc0[:, 0:T]),
             reads=["hid", "psbc0"], writes=["spA"])
        S.op("vector", lambda e: e.tensor_add(sp[:, 0:T], sp[:, 0:T], psbc1[:, 0:T]),
             reads=["spA", "psbc1"], writes=["spA"])
        S.op("vector", lambda e, dg=dg: e.tensor_add(
            x1_g[:, dg, :], sp[:, 0:T], xfm_g[:, dg, :]),
             reads=["spA", "sz"], writes=["ybuf"])
    S.op("sync", lambda e, c=c: e.dma_start(
        out=x1scr.ap().rearrange("(g p) t -> p g t", p=128)[:, :, c * T:(c + 1) * T],
        in_=x1_g), reads=["ybuf"], writes=["x1scr"], dma=True)

    _fm_ln(B, E, x1c[:], "ybuf", T)
    for dg in range(8):
        S.op("vector", lambda e, dg=dg: e.tensor_mul(
            sp[:, 0:T], x1_g[:, dg, :], psbc0[:, 0:T]),
             reads=["ybuf", "psbc0"], writes=["spA"])
        S.op("vector", lambda e, dg=dg: e.tensor_add(
            xf_g[:, dg, :], sp[:, 0:T], psbc1[:, 0:T]),
             reads=["spA", "psbc1"], writes=["du"])
    S.op("sync", lambda e, c=c: e.dma_start(
        out=xfscr.ap().rearrange("(g p) t -> p g t", p=128)[:, :, c * T:(c + 1) * T],
        in_=xf_g), reads=["du"], writes=["xfscr"], dma=True)

    B.dump(f"x1_c{c}", x1c[:], (128, 8 * TCH), F32, tags=["ybuf"])
    B.dump(f"xf_c{c}", xfc[:], (128, 8 * TCH), F32, tags=["du"])



def build_ffn(B):
    from contextlib import ExitStack
    S = B.S
    for l in (0, 1):
        if l == 0:
            B.tick_bound[0] = len(B.wjobs)
        with ExitStack() as stk:
            _ffn_attn(B, stk, l)
        S.barrier()
        with ExitStack() as stk:
            _ffn_mlp(B, stk, l)
            if l == 0:
                B.job_flush()
                B.job_tick = None
        S.barrier()


def _ffn_attn(B, stk, l):
    nc, S = B.nc, B.S
    dr = B.dram
    C = B.common
    def sb(stk_, name, shape, dt=F32):
        return B.sb(stk_, f"{name}_{l}", shape, dt)
    def ps(stk_, name, shape, dt=F32):
        return B.ps(stk_, f"{name}_{l}", shape, dt)
    E = {}
    E["hb"] = sb(stk, "f_hb", [128, 8 * L], BF16)
    E["wqkvb"] = sb(stk, "f_wqkvb", [128, 8 * 1536], BF16)
    E["wob"] = sb(stk, "f_wob", [128, 4 * 1024], BF16)

    E["qfm"] = sb(stk, "f_qfm", [128, 4 * L], BF16)
    E["kfm"] = sb(stk, "f_kfm", [128, 4 * L], BF16)
    E["vsb"] = sb(stk, "f_vsb", [128, 16 * 512], BF16)
    E["osb"] = sb(stk, "f_osb", [128, 4 * L], BF16)
    E["qsbc"] = sb(stk, "f_qsbc", [128, 512], F32)
    E["ksbc"] = sb(stk, "f_ksbc", [128, 512], F32)
    E["tbl"] = sb(stk, "f_tbl", [128, 2 * 4 * 512], F32)
    E["psq"] = sb(stk, "f_psq", [128, 2 * 512], F32)
    E["rst"] = sb(stk, "f_rst", [128, 64], F32)
    E["qtm"] = sb(stk, "f_qtm", [128, 2 * 512], F32)
    E["qtb"] = sb(stk, "f_qtb", [128, 2 * 512], BF16)
    E["attn"] = sb(stk, "f_attn", [128, 2 * 640], BF16)
    E["attnT"] = sb(stk, "f_attnT", [128, 2 * 8 * 512], BF16)
    E["nmx"] = sb(stk, "f_nmx", [128, 8], F32)
    E["slab2"] = sb(stk, "f_slab2", [128, 4 * 512], F32)
    E["slab"] = sb(stk, "f_slab", [128, L], F32)
    E["sp"] = sb(stk, "f_sp", [128, 512], F32)
    psQ = ps(stk, "f_psQ", [128, 512], F32)
    psK = ps(stk, "f_psK", [128, 512], F32)
    psV = ps(stk, "f_psV", [128, 512], F32)
    psS0 = ps(stk, "f_psS0", [128, 1024], F32)
    psS1 = ps(stk, "f_psS1", [128, 1024], F32)
    psTa = ps(stk, "f_psTa", [128, 1024], BF16)

    xfscr = dr["xfscr"]
    xf_r = xfscr.ap().rearrange("(g p) t -> p g t", p=128)
    hb = E["hb"]
    hb_g = hb[:].rearrange("p (g t) -> p g t", g=8)
    identb = C["identb"]

    def load(dst, srcap, tag, rtags=()):
        S.op("sync", lambda e, dst=dst, srcap=srcap: e.dma_start(
            out=dst, in_=srcap), reads=list(rtags), writes=[tag], dma=True)

    for cc in range(8):
        load(E["wqkvb"][:, 1536 * cc:1536 * (cc + 1)],
             dr[f"wqkvbb{l}"].ap()[128 * cc:128 * (cc + 1), :], "wqkvb", ["wqkvb"])
    for cc in range(4):
        load(E["wob"][:, 1024 * cc:1024 * (cc + 1)],
             dr[f"wobb{l}"].ap()[128 * cc:128 * (cc + 1), :], "wob", ["wob"])
    load(E["qsbc"][:], dr[f"qsrow{l}"].ap().to_broadcast((128, 512)), "qsbc")
    load(E["ksbc"][:], dr[f"ksrow{l}"].ap().to_broadcast((128, 512)), "ksbc")

    for dg in range(8):
        S.op("sync", lambda e, dg=dg: e.dma_start(
            out=E["slab"][:], in_=xf_r[:, dg, :]),
            reads=XFALL, writes=["fslab"], dma=True)
        if dg < 4:
            S.op("scalar", lambda e, dg=dg: e.copy(hb_g[:, dg, :], E["slab"][:]),
                 reads=["fslab"], writes=["hbf"])
        else:
            S.op("vector", lambda e, dg=dg: e.memset(hb_g[:, dg, 0:1], 0.0),
                 writes=["hbf"])
            S.op("scalar", lambda e, dg=dg: e.copy(
                hb_g[:, dg, 1:L], E["slab"][:, 0:L - 1]),
                 reads=["fslab"], writes=["hbf"])

    qfm, kfm, vsb = E["qfm"], E["kfm"], E["vsb"]

    def qkv_slots(sl):
        if sl == 0:
            return psQ[:], psK[:], psV[:], "psQ", "psK", "psV"
        return (psS0[:, 0:512], psS0[:, 512:1024], psS1[:, 0:512],
                "fpsS0", "fpsS0", "fpsS1")

    def qkv_mm(tt):
        sl = tt % 2
        t_tbl = f"tbl{sl}"
        tbl = E["tbl"][:, 2048 * sl:2048 * (sl + 1)]
        psQh, psKh, psVh, tQ, tK, tV = qkv_slots(sl)
        for ti, nm in enumerate(("rqa", "rqb", "rka", "rkb")):
            S.op("sync", lambda e, ti=ti, nm=nm, tt=tt, tbl=tbl: e.dma_start(
                out=tbl[:, 512 * ti:512 * (ti + 1)],
                in_=dr[nm].ap()[128 * tt:128 * (tt + 1), :]),
                reads=[t_tbl], writes=[t_tbl], dma=True, sem=f"dtb{sl}")

        def mmq(e, tt=tt, psQh=psQh, psKh=psKh, psVh=psVh):
            for cc in range(8):
                lhsT = hb_g[:, cc, 128 * tt:128 * (tt + 1)]
                e.matmul(psQh, lhsT,
                         E["wqkvb"][:, 1536 * cc:1536 * cc + 512],
                         start=(cc == 0), stop=(cc == 7))
                e.matmul(psKh, lhsT,
                         E["wqkvb"][:, 1536 * cc + 512:1536 * cc + 1024],
                         start=(cc == 0), stop=(cc == 7))
                inst = e.matmul(psVh, lhsT,
                                E["wqkvb"][:, 1536 * cc + 1024:1536 * cc + 1536],
                                start=(cc == 0), stop=(cc == 7))
            return inst
        S.op("tensor", mmq, reads=["hbf", "wqkvb"], writes=[tQ, tK, tV])

    def qkv_prep(tt):
        sl = tt % 2
        t_tbl, t_psq, t_qtb = f"tbl{sl}", f"psq{sl}", f"qtb{sl}"
        tbl = E["tbl"][:, 2048 * sl:2048 * (sl + 1)]
        psq = E["psq"][:, 512 * sl:512 * (sl + 1)]
        qtb = E["qtb"][:, 512 * sl:512 * (sl + 1)]
        psQh, psKh, psVh, tQ, tK, tV = qkv_slots(sl)
        S.op("scalar", lambda e, tt=tt, psVh=psVh: e.copy(
            vsb[:, 512 * tt:512 * (tt + 1)], psVh),
             reads=[tV], writes=["vsb"])
        for which, psu, pstag, qsb, tb0, tb1, dstfm, dtag in (
                ("q", psQh, tQ, E["qsbc"], 0, 1, qfm, "qfm"),
                ("k", psKh, tK, E["ksbc"], 2, 3, kfm, "kfm")):
            rst = E["rst"][:, 32 * sl:32 * (sl + 1)]
            t_rst = f"rst{sl}"
            qtm = E["qtm"]
            off = 0 if which == "q" else 8
            S.op("scalar", lambda e, psu=psu, psq=psq: e.activation(
                psq, psu, AF.Square),
                 reads=[pstag], writes=[t_psq])
            S.op("vector", lambda e, off=off, psq=psq, rst=rst: e.tensor_reduce(
                rst[:, off:off + 8],
                psq.rearrange("p (h d) -> p h d", h=8), AX.X, ALU.add),
                 reads=[t_psq], writes=[t_rst])
            S.op("scalar", lambda e, off=off, rst=rst: e.activation(
                rst[:, 16 + off:24 + off], rst[:, off:off + 8], AF.Sqrt),
                 reads=[t_rst], writes=[t_rst])
            S.op("vector", lambda e, off=off, rst=rst: e.tensor_scalar_max(
                rst[:, 16 + off:24 + off], rst[:, 16 + off:24 + off], 1e-12),
                 reads=[t_rst], writes=[t_rst], ses=True)
            S.op("vector", lambda e, off=off, rst=rst: e.reciprocal(
                rst[:, 16 + off:24 + off], rst[:, 16 + off:24 + off]),
                 reads=[t_rst], writes=[t_rst], ses=True)
            rbc = rst[:, 16 + off:24 + off].unsqueeze(2).broadcast_to((128, 8, 64))
            S.op("vector", lambda e, psu=psu, rbc=rbc: e.tensor_tensor(
                qtm[:, 0:512].rearrange("p (h d) -> p h d", h=8),
                psu.rearrange("p (h d) -> p h d", h=8), rbc, ALU.mult),
                 reads=[pstag, t_rst], writes=["qtm"])
            S.op("gpsimd", lambda e, qsb=qsb: e.tensor_mul(
                qtm[:, 512:1024], qtm[:, 0:512], qsb[:]),
                 reads=["qtm", "qsbc", "ksbc"], writes=["qtm"])
            S.op("gpsimd", lambda e, tb0=tb0, tbl=tbl: e.tensor_mul(
                qtm[:, 0:512], qtm[:, 512:1024],
                tbl[:, 512 * tb0:512 * (tb0 + 1)]),
                 reads=["qtm", t_tbl], writes=["qtm"])

            def swmul(e, tb1=tb1, tbl=tbl):
                qa = qtm[:, 512:1024].rearrange("p (h s d) -> p h s d", h=8, s=2)
                tblB = tbl[:, 512 * tb1:512 * (tb1 + 1)].rearrange(
                    "p (h s d) -> p h s d", h=8, s=2)
                spv = E["sp"][:, 0:512].rearrange("p (h s d) -> p h s d", h=8, s=2)
                e.tensor_tensor(spv[:, :, 0, :], qa[:, :, 1, :],
                                tblB[:, :, 0, :], ALU.mult)
                return e.tensor_tensor(spv[:, :, 1, :], qa[:, :, 0, :],
                                       tblB[:, :, 1, :], ALU.mult)
            S.op("gpsimd", swmul, reads=["qtm", t_tbl], writes=["fspA"])
            S.op("vector", lambda e, qtb=qtb: e.tensor_add(
                qtb, qtm[:, 0:512], E["sp"][:, 0:512]),
                 reads=["qtm", "fspA"], writes=[t_qtb])

            def trq(e, qtb=qtb):
                for hp in range(4):
                    inst = e.transpose(psTa[:, 128 * hp:128 * (hp + 1)],
                                       qtb[:, 128 * hp:128 * (hp + 1)],
                                       identb[:])
                return inst
            S.op("tensor", trq, reads=[t_qtb, "identb"], writes=["psTa"])

            def evq(e, tt=tt, dstfm=dstfm):
                return e.copy(
                    dstfm[:].rearrange("p (f t) -> p f t", f=4)[
                        :, :, 128 * tt:128 * (tt + 1)],
                    psTa[:, 0:512].rearrange("p (f t) -> p f t", f=4))
            S.op("scalar", evq, reads=["psTa"], writes=[dtag])

    qkv_mm(0)
    for tt in range(16):
        if getattr(B, "job_tick", None):
            B.job_tick()
        if tt + 1 < 16:
            qkv_mm(tt + 1)
        qkv_prep(tt)

    wob, osb = E["wob"], E["osb"]
    for w in range(NW):
        for h in range(HEADS):
            hp, hs = h // 2, (h % 2) * 64
            asl = h % 2
            attnT = E["attnT"][:, 4096 * asl:4096 * (asl + 1)]
            t_aT = f"fattnT{asl}"
            qh = qfm[:].rearrange("p (f t) -> p f t", f=4)[hs:hs + 64, hp, :]
            kh = kfm[:].rearrange("p (f t) -> p f t", f=4)[hs:hs + 64, hp, :]
            kb_lo = 4 if w == 0 else 0

            def qctx(qb):
                ssl = qb % 2
                return dict(
                    qb=qb, ssl=ssl,
                    psS=psS0 if ssl == 0 else psS1,
                    t_psS=f"fpsS{ssl}", t_psS2=f"fpsS{ssl}",
                    attn=E["attn"][:, 640 * ssl:640 * (ssl + 1)],
                    t_attn=f"fattn{ssl}",
                    nmx=E["nmx"][:, 4 * ssl:4 * (ssl + 1)],
                    t_nmx=f"nmx{ssl}",
                    kbs=[kb for kb in range(qb, qb + 5) if kb >= kb_lo],
                )

            def st_mms(cx, w=w, qh=qh, kh=kh):
                qb, kbs, psS = cx["qb"], cx["kbs"], cx["psS"]
                kb0 = kbs[0]

                nv = len(kbs)

                def mms(e, w=w, qb=qb, kbs=kbs, kb0=kb0, nv=nv,
                        qh=qh, kh=kh, psS=psS):
                    used_banks = set()
                    for kb in kbs:
                        koff = (w - 1) * 512 + 128 * kb
                        so = 128 * (kb - kb0)
                        bank = so // 512
                        st = bank not in used_banks
                        used_banks.add(bank)
                        e.matmul(
                            psS[:, so:so + 128],
                            qh[:, 512 * w + 128 * qb: 512 * w + 128 * (qb + 1)],
                            kh[:, koff:koff + 128], start=st, stop=True)
                    if kb0 == qb:
                        e.matmul(psS[:, 0:128], C["maskuTb"][:], identb[:],
                                 start=False, stop=True)
                    return e.matmul(psS[:, 128 * (nv - 1):128 * nv],
                                    C["masklTb"][:], identb[:],
                                    start=False, stop=True)
                S.op("tensor", mms, reads=["qfm", "kfm", "maskuTb",
                                           "masklTb", "identb"],
                     writes=[cx["t_psS"], cx["t_psS2"]])

            def st_maskmax(cx):
                kbs, psS, nmx = cx["kbs"], cx["psS"], cx["nmx"]
                nv = len(kbs)
                S.op("vector", lambda e, nv=nv, psS=psS, nmx=nmx: e.reduce_max(
                    nmx[:, 0:1], psS[:, 0:128 * nv], axis=AX.X, negate=True),
                     reads=[cx["t_psS"], cx["t_psS2"]], writes=[cx["t_nmx"]])

            def st_exp(cx):
                nv = len(cx["kbs"])
                psS, nmx, attn = cx["psS"], cx["nmx"], cx["attn"]
                S.op("scalar", lambda e, nv=nv, psS=psS, nmx=nmx, attn=attn: e.activation(
                    attn[:, 0:128 * nv], psS[:, 0:128 * nv], AF.Exp,
                    bias=nmx[:, 0:1], accum_out=nmx[:, 1:2]),
                     reads=[cx["t_psS"], cx["t_psS2"], cx["t_nmx"]],
                     writes=[cx["t_attn"], cx["t_nmx"]])

            def st_recip(cx):
                nmx = cx["nmx"]
                S.op("vector", lambda e, nmx=nmx: e.reciprocal(
                    nmx[:, 2:3], nmx[:, 1:2]),
                     reads=[cx["t_nmx"]], writes=[cx["t_nmx"]], ses=True)

            def st_scale(cx):
                nv = len(cx["kbs"])
                attn, nmx = cx["attn"], cx["nmx"]
                S.op("scalar", lambda e, nv=nv, attn=attn, nmx=nmx: e.activation(
                    attn[:, 0:128 * nv], attn[:, 0:128 * nv],
                    AF.Copy, scale=nmx[:, 2:3]),
                     reads=[cx["t_attn"], cx["t_nmx"]], writes=[cx["t_attn"]])

            def st_mta(cx):
                kbs, attn = cx["kbs"], cx["attn"]
                kb0 = kbs[0]

                def mta(e, kbs=kbs, kb0=kb0, attn=attn):
                    for i, kb in enumerate(kbs):
                        so = 128 * (kb - kb0)
                        inst = e.transpose(psTa[:, 128 * i:128 * (i + 1)],
                                           attn[:, so:so + 128], identb[:])
                    return inst
                S.op("tensor", mta, reads=[cx["t_attn"], "identb"], writes=["psTa"])

            def st_eta(cx, attnT=attnT, t_aT=t_aT):
                kbs, qb = cx["kbs"], cx["qb"]
                engn = "vector" if qb % 2 == 0 else "scalar"

                def eta(e, kbs=kbs, qb=qb, attnT=attnT, engn=engn):
                    cp = e.tensor_copy if engn == "vector" else e.copy
                    for i, kb in enumerate(kbs):
                        inst = cp(
                            attnT[:, 512 * kb + 128 * qb:
                                  512 * kb + 128 * (qb + 1)],
                            psTa[:, 128 * i:128 * (i + 1)])
                    return inst
                S.op(engn, eta, reads=["psTa"], writes=[t_aT])

            cxs = [qctx(qb) for qb in range(4)]
            st_mms(cxs[0]); st_maskmax(cxs[0])
            st_mms(cxs[1])
            for k in range(4):
                if getattr(B, "job_tick", None):
                    B.job_tick()
                st_exp(cxs[k])
                if k + 1 < 4:
                    st_maskmax(cxs[k + 1])
                st_recip(cxs[k])
                st_scale(cxs[k])
                if k + 2 < 4:
                    st_mms(cxs[k + 2])
                st_mta(cxs[k])
                st_eta(cxs[k])

            def mav(e, w=w, h=h, kb_lo=kb_lo, attnT=attnT):
                started = False
                for kb in range(kb_lo, 8):
                    qmin = max(0, kb - 4)
                    qmax = min(3, kb)
                    if qmin > qmax:
                        continue
                    kt = 4 * (w - 1) + kb
                    lhsT = vsb[:, 512 * kt + 64 * h: 512 * kt + 64 * (h + 1)]
                    rhs = attnT[:, 512 * kb + 128 * qmin:
                                512 * kb + 128 * (qmax + 1)]
                    inst = e.matmul(psV[0:64, 128 * qmin:128 * (qmax + 1)],
                                    lhsT, rhs, start=(not started), stop=True)
                    started = True
                return inst
            S.op("tensor", mav, reads=["vsb", t_aT], writes=["psV"])
            S.op("scalar", lambda e, w=w, hp=hp, hs=hs: e.copy(
                osb[hs:hs + 64, 2048 * hp + 512 * w: 2048 * hp + 512 * (w + 1)],
                psV[0:64, 0:512]),
                 reads=["psV"], writes=["osb"])

        def wo_in(mg, w=w):
            sl2 = E["slab2"][:, 512 * (mg % 4):512 * (mg % 4 + 1)]
            S.op("sync", lambda e, w=w, mg=mg, sl2=sl2: e.dma_start(
                out=sl2, in_=xf_r[:, mg, 512 * w:512 * (w + 1)]),
                reads=["xfscr", f"xfr{mg}"], writes=[f"fslab2{mg % 4}"],
                dma=True, sem=f"dsl{mg % 4}")
        for mg in range(3):
            wo_in(mg)
        for mg in range(8):
            if getattr(B, "job_tick", None):
                B.job_tick()
            msl = mg % 2
            psu = psQ[:] if msl == 0 else psK[:]
            pstag = "psQ" if msl == 0 else "psK"
            sl2 = E["slab2"][:, 512 * (mg % 4):512 * (mg % 4 + 1)]
            t_sl2 = f"fslab2{mg % 4}"

            def mmo(e, w=w, mg=mg, psu=psu):
                for cc in range(4):
                    lhsT = wob[:, 1024 * cc + 128 * mg: 1024 * cc + 128 * (mg + 1)]
                    rhs = osb[:, 2048 * cc + 512 * w: 2048 * cc + 512 * (w + 1)]
                    inst = e.matmul(psu, lhsT, rhs,
                                    start=(cc == 0), stop=(cc == 3))
                return inst
            S.op("tensor", mmo, reads=["wob", "osb"], writes=[pstag])
            S.op("vector", lambda e, psu=psu, sl2=sl2: e.tensor_add(
                sl2, sl2, psu),
                 reads=[t_sl2, pstag], writes=[t_sl2])
            S.op("scalar", lambda e, w=w, mg=mg, sl2=sl2: e.dma_start(
                out=xf_r[:, mg, 512 * w:512 * (w + 1)], in_=sl2),
                reads=[t_sl2], writes=[f"xfr{mg}"], dma=True,
                sem=f"dsl{mg % 4}")
            if mg + 3 < 8:
                wo_in(mg + 3)

    B.dump(f"attnout_l{l}", E["osb"][:], (128, 4 * L), BF16, tags=["osb"])
    B.dump(f"wqkvb_l{l}", E["wqkvb"][:], (128, 8 * 1536), BF16, tags=["wqkvb"])
    if l == 0 and "w1pb_l0" in B.dbg:
        B.dump("w1pb_l0", B.dram["w1pb0"].ap(), (D, NPAIR * 256), BF16, tags=[])
    B.dump(f"hb_l{l}", E["hb"][:], (128, 8 * L), BF16, tags=["hbf"])
    B.dump(f"qfm_l{l}", E["qfm"][:], (128, 4 * L), BF16, tags=["qfm"])
    B.dump(f"kfm_l{l}", E["kfm"][:], (128, 4 * L), BF16, tags=["kfm"])
    B.dump(f"vsb_l{l}", E["vsb"][:], (128, 16 * 512), BF16, tags=["vsb"])
    B.dump(f"xfattn_l{l}", B.dram["xfscr"].ap(), (D, L), F32, tags=XFALL)


def _ffn_mlp(B, stk, l):
    nc, S = B.nc, B.S
    dr = B.dram
    C = B.common
    def sb(stk_, name, shape, dt=F32):
        return B.sb(stk_, f"{name}_{l}", shape, dt)
    def ps(stk_, name, shape, dt=F32):
        return B.ps(stk_, f"{name}_{l}", shape, dt)
    E = {}
    E["slab"] = sb(stk, "g_slab", [128, 2 * L], F32)
    E["slab2"] = sb(stk, "g_slab2", [128, 4 * 512], F32)
    E["shb"] = sb(stk, "g_shb", [128, L], BF16)
    E["srow"] = sb(stk, "g_srow", [1, L], F32)
    E["srow2"] = sb(stk, "g_srow2", [1, L], F32)
    E["fmu2"] = sb(stk, "g_fmu2", [1, L], F32)
    E["rwst"] = sb(stk, "g_rwst", [64, 2 * L], BF16)
    E["fmu2"] = E["rwst"][0:1, 0:2 * L].bitcast(F32)
    E["smc"] = sb(stk, "g_smc", [128, 68], F32)
    E["ln2h"] = sb(stk, "g_ln2h", [128, 8 * L], BF16)
    E["sp"] = sb(stk, "g_sp", [128, 2 * 512], F32)
    E["w1a"] = sb(stk, "g_w1a", [128, 2048], BF16)
    E["w1b"] = sb(stk, "g_w1b", [128, 2048], BF16)
    E["gga"] = sb(stk, "g_gga", [128, 2 * 512], F32)
    E["ggb"] = sb(stk, "g_ggb", [128, 2 * 512], F32)
    E["gg"] = sb(stk, "g_gg", [128, 22 * 1024], BF16)
    E["w2b"] = sb(stk, "g_w2b", [128, 22 * 1024], BF16)
    psA = ps(stk, "g_psA", [128, 512], F32)
    psB = ps(stk, "g_psB", [128, 512], F32)
    psC = ps(stk, "g_psC", [128, 512], F32)
    psD = ps(stk, "g_psD", [128, 512], F32)
    psq4 = [psA, psB, psC, psD]
    pstags = ["fpsA", "fpsB", "fpsC", "fpsD"]

    xfscr = dr["xfscr"]
    xf_r = xfscr.ap().rearrange("(g p) t -> p g t", p=128)
    smc = E["smc"]
    onesb = C["onesb"]
    ln2h = E["ln2h"]
    ln2h_g = ln2h[:].rearrange("p (g t) -> p g t", g=8)
    gg = E["gg"]

    def load(dst, srcap, tag, rtags=()):
        S.op("sync", lambda e, dst=dst, srcap=srcap: e.dma_start(
            out=dst, in_=srcap), reads=list(rtags), writes=[tag], dma=True)

    load(smc[:, 0:44], dr[f"b1p{l}"].ap(), "smc")
    load(smc[:, 44:52], dr[f"b2{l}"].ap(), "smc")
    load(smc[:, 52:60], dr[f"lng{l}"].ap(), "smc")
    load(smc[:, 60:68], dr[f"lnb{l}"].ap(), "smc")

    for dg in range(8):
        if getattr(B, "job_tick", None):
            B.job_tick()
        sl = dg % 2
        slab = E["slab"][:, L * sl:L * (sl + 1)]
        shb = E["shb"][:]
        t_sl = f"fslab{sl}"
        S.op("sync" if sl == 0 else "scalar",
             lambda e, dg=dg, slab=slab: e.dma_start(
                 out=slab, in_=xf_r[:, dg, :]),
             reads=XFALL, writes=[t_sl], dma=True, sem=f"dw{sl}")
        if dg < 4:
            S.op("scalar", lambda e, slab=slab, shb=shb: e.activation(
                shb, slab, AF.Square),
                 reads=[t_sl], writes=["fshb"])
        else:
            S.op("vector", lambda e, shb=shb: e.memset(
                shb[:, 0:1], 0.0), writes=["fshb"])
            S.op("scalar", lambda e, slab=slab, shb=shb: e.activation(
                shb[:, 1:L], slab[:, 0:L - 1], AF.Square),
                 reads=[t_sl], writes=["fshb"])

        def mmsum(e, dg=dg, slab=slab, shb=shb):
            sh = 1 if dg >= 4 else 0
            for ch in range(4):
                if sh and ch == 0:
                    e.matmul(psq4[0][0:1, 1:512], C["onesf"][:, 0:1],
                             slab[:, 0:511], start=False, stop=(dg == 7))
                else:
                    e.matmul(psq4[ch][0:1, 0:512], C["onesf"][:, 0:1],
                             slab[:, 512 * ch - sh:512 * (ch + 1) - sh],
                             start=(dg == 0), stop=(dg == 7))
                inst = e.matmul(psq4[ch][32:33, 0:512], onesb[:, 0:1],
                                shb[:, 512 * ch:512 * (ch + 1)],
                                start=(dg == 0), stop=(dg == 7))
            return inst
        S.op("tensor", mmsum, reads=[t_sl, "fshb", "onesb"], writes=pstags)
    for ch in range(4):
        S.op("scalar", lambda e, ch=ch: e.copy(
            E["srow"][0:1, 512 * ch:512 * (ch + 1)], psq4[ch][0:1, 0:512]),
             reads=pstags, writes=["fsrow"])
        S.op("scalar", lambda e, ch=ch: e.copy(
            E["srow2"][0:1, 512 * ch:512 * (ch + 1)], psq4[ch][32:33, 0:512]),
             reads=pstags, writes=["fsrow"])

    B.dump(f"srow_l{l}", E["srow"][0:1, :], (1, L), F32, tags=["fsrow"])
    B.dump(f"srow2_l{l}", E["srow2"][0:1, :], (1, L), F32, tags=["fsrow"])
    S.op("vector", lambda e: e.tensor_scalar_mul(
        E["srow"][0:1, :], E["srow"][0:1, :], 1.0 / D),
         reads=["fsrow"], writes=["fsrow"], ses=True)
    S.op("vector", lambda e: e.tensor_scalar_mul(
        E["srow2"][0:1, :], E["srow2"][0:1, :], 1.0 / D),
         reads=["fsrow"], writes=["fsrow"], ses=True)
    S.op("vector", lambda e: e.tensor_tensor(
        E["fmu2"], E["srow"][0:1, :], E["srow"][0:1, :], ALU.mult),
         reads=["fsrow"], writes=["frwst"], ses=True)
    S.op("vector", lambda e: e.tensor_sub(
        E["srow2"][0:1, :], E["srow2"][0:1, :], E["fmu2"]),
         reads=["fsrow", "frwst"], writes=["fsrow"], ses=True)
    S.op("scalar", lambda e: e.activation(
        E["srow2"][0:1, :], E["srow2"][0:1, :], AF.Sqrt, bias=C["cst"][0:1, 0:1]),
         reads=["fsrow", "cst"], writes=["fsrow"])
    S.op("vector", lambda e: e.reciprocal(E["srow2"][0:1, :], E["srow2"][0:1, :]),
         reads=["fsrow"], writes=["fsrow"], ses=True)
    S.op("vector", lambda e: e.scalar_tensor_tensor(
        E["srow"][0:1, :], E["srow"][0:1, :], -1.0, E["srow2"][0:1, :],
        ALU.mult, ALU.mult),
         reads=["fsrow"], writes=["fsrow"], ses=True)
    S.op("vector", lambda e: e.memset(E["rwst"][:], 0.0), writes=["frwst"])

    def hilo(e):
        e.tensor_copy(E["rwst"][0:1, 0:L], E["srow2"][0:1, :])
        e.tensor_tensor(E["rwst"][32:33, 0:L], E["srow2"][0:1, :],
                        E["rwst"][0:1, 0:L], ALU.subtract)
        e.tensor_copy(E["rwst"][0:1, L:2 * L], E["srow"][0:1, :])
        return e.tensor_tensor(E["rwst"][32:33, L:2 * L], E["srow"][0:1, :],
                               E["rwst"][0:1, L:2 * L], ALU.subtract)
    S.op("vector", hilo, reads=["fsrow"], writes=["frwst"], ses=True)

    for dg in range(8):
        if getattr(B, "job_tick", None):
            B.job_tick()
        sl = dg % 2
        slab = E["slab"][:, L * sl:L * (sl + 1)]
        t_sl = f"fslab{sl}"
        S.op("sync" if sl == 0 else "scalar",
             lambda e, dg=dg, slab=slab: e.dma_start(
                 out=slab, in_=xf_r[:, dg, :]),
             reads=XFALL, writes=[t_sl], dma=True, sem=f"dw{sl}")
        for ch in range(4):
            csl = ch % 2
            psR = psA if csl == 0 else psC
            psN = psB if csl == 0 else psD
            tR = "fpsA" if csl == 0 else "fpsC"
            tN = "fpsB" if csl == 0 else "fpsD"
            spx = E["sp"][:, 0:512] if csl == 0 else E["gga"][:, 512:1024]
            tspx = "fspA" if csl == 0 else "fggax"
            spy = E["sp"][:, 512:1024] if csl == 0 else E["ggb"][:, 512:1024]
            tspy = "fspB" if csl == 0 else "fggbx"

            def bmr(e, ch=ch, psR=psR):
                return e.matmul(psR[:, 0:512], C["lnsel"][:],
                                E["rwst"][:, 512 * ch:512 * (ch + 1)],
                                start=True, stop=True)
            S.op("tensor", bmr, reads=["frwst", "lnsel"], writes=[tR])

            def bmn(e, ch=ch, psN=psN):
                return e.matmul(psN[:, 0:512], C["lnsel"][:],
                                E["rwst"][:, L + 512 * ch:L + 512 * (ch + 1)],
                                start=True, stop=True)
            S.op("tensor", bmn, reads=["frwst", "lnsel"], writes=[tN])
            if dg < 4:
                S.op("vector", lambda e, ch=ch, slab=slab, psR=psR, spx=spx: e.tensor_mul(
                    spx, slab[:, 512 * ch:512 * (ch + 1)], psR[:, 0:512]),
                     reads=[t_sl, tR], writes=[tspx])
            elif ch == 0:
                S.op("vector", lambda e, spx=spx: e.memset(spx[:, 0:1], 0.0),
                     writes=[tspx])
                S.op("vector", lambda e, slab=slab, psR=psR, spx=spx: e.tensor_mul(
                    spx[:, 1:512], slab[:, 0:511], psR[:, 1:512]),
                     reads=[t_sl, tR], writes=[tspx])
            else:
                S.op("vector", lambda e, ch=ch, slab=slab, psR=psR, spx=spx: e.tensor_mul(
                    spx, slab[:, 512 * ch - 1:512 * (ch + 1) - 1],
                    psR[:, 0:512]),
                     reads=[t_sl, tR], writes=[tspx])
            S.op("vector", lambda e, spx=spx, spy=spy, psN=psN: e.tensor_add(
                spy, spx, psN[:, 0:512]),
                 reads=[tspx, tN], writes=[tspy])
            S.op("vector", lambda e, dg=dg, ch=ch, spy=spy: e.tensor_scalar(
                ln2h_g[:, dg, 512 * ch:512 * (ch + 1)], spy,
                smc[:, 52 + dg:53 + dg], smc[:, 60 + dg:61 + dg],
                ALU.mult, ALU.add),
                 reads=[tspy, "smc"], writes=["fln2h"])

    for half in range(2):
        rows = slice(128 * 11 * half, 128 * 11 * (half + 1))
        cols = slice(11264 * half, 11264 * (half + 1))
        src_ap = dr[f"w2pb{l}"].ap()[rows, :].rearrange("(j p) m -> p j m", p=128)
        dst_ap = E["w2b"][:, cols].rearrange("p (j m) -> p j m", j=11)
        S.op("sync", lambda e, src_ap=src_ap, dst_ap=dst_ap: e.dma_start(
            out=dst_ap, in_=src_ap),
            reads=["fw2b"], writes=["fw2b"], dma=True)

    for th in range(2):
        for j in range(NPAIR):
            if getattr(B, "job_tick", None):
                B.job_tick()
            wslot = E["w1a"] if j % 2 == 0 else E["w1b"]
            wstag = f"fw1a{j % 2}"
            src_ap = dr[f"w1pb{l}"].ap().rearrange("(c p) m -> p c m", p=128)[
                :, :, 256 * j:256 * (j + 1)]
            w1rd = [wstag] + ([f"fslab{j % 2}"] if j < 2 else [])
            S.op("sync" if j % 2 == 0 else "scalar",
                 lambda e, src_ap=src_ap, wslot=wslot: e.dma_start(
                     out=wslot[:].rearrange("p (c m) -> p c m", c=8),
                in_=src_ap),
                reads=w1rd, writes=[wstag], dma=True, sem=f"dw{j % 2}")
            for tc2 in range(2):
                tc = 2 * th + tc2
                psA_ = psA if tc2 % 2 == 0 else psC
                psB_ = psB if tc2 % 2 == 0 else psD
                tA_ = "fpsA" if tc2 % 2 == 0 else "fpsC"
                tB_ = "fpsB" if tc2 % 2 == 0 else "fpsD"

                def mm1(e, tc=tc, wslot=wslot, psA_=psA_, psB_=psB_):
                    for cc in range(8):
                        a_t = wslot[:, 256 * cc + 0:256 * cc + 128]
                        g_t = wslot[:, 256 * cc + 128:256 * cc + 256]
                        e.matmul(psA_[:, 0:512], a_t,
                                 ln2h_g[:, cc, 512 * tc:512 * (tc + 1)],
                                 start=(cc == 0), stop=(cc == 7))
                        inst = e.matmul(psB_[:, 0:512], g_t,
                                        ln2h_g[:, cc, 512 * tc:512 * (tc + 1)],
                                        start=(cc == 0), stop=(cc == 7))
                    return inst
                S.op("tensor", mm1, reads=[wstag, "fln2h"], writes=[tA_, tB_])
                gsl = tc2 % 2
                gga = E["gga"][:, 512 * gsl:512 * (gsl + 1)]
                ggb = E["ggb"][:, 512 * gsl:512 * (gsl + 1)]
                t_ga, t_gb = f"fgga{gsl}", f"fggb{gsl}"
                S.op("scalar", lambda e, j=j, gga=gga, psA_=psA_: e.add(
                    gga, psA_[:, 0:512], smc[:, 2 * j:2 * j + 1]),
                     reads=[tA_, "smc"], writes=[t_ga])
                S.op("scalar", lambda e, j=j, ggb=ggb, psB_=psB_: e.activation(
                    ggb, psB_[:, 0:512], AF.Gelu,
                    bias=smc[:, 2 * j + 1:2 * j + 2]),
                     reads=[tB_, "smc"], writes=[t_gb])
                S.op("vector", lambda e, j=j, tc2=tc2, gga=gga, ggb=ggb: e.tensor_mul(
                    gg[:, 1024 * j + 512 * tc2: 1024 * j + 512 * (tc2 + 1)],
                    gga, ggb),
                     reads=[t_ga, t_gb], writes=["fgg"])

        for tc2 in range(2):
            tc = 2 * th + tc2

            def r_in(mg, tc=tc):
                sl2 = E["slab2"][:, 512 * (mg % 4):512 * (mg % 4 + 1)]
                S.op("sync", lambda e, tc=tc, mg=mg, sl2=sl2: e.dma_start(
                    out=sl2, in_=xf_r[:, mg, 512 * tc:512 * (tc + 1)]),
                    reads=["xfscr", f"xfr{mg}"], writes=[f"fslab2{mg % 4}"],
                    dma=True, sem=f"dsl{mg % 4}")
            for mg in range(3):
                r_in(mg)
            for mg in range(8):
                if getattr(B, "job_tick", None):
                    B.job_tick()
                psu = [psA, psB, psC, psD][mg % 4]
                pstag = ["fpsA", "fpsB", "fpsC", "fpsD"][mg % 4]
                sl2 = E["slab2"][:, 512 * (mg % 4):512 * (mg % 4 + 1)]
                t_sl2 = f"fslab2{mg % 4}"

                def mm2(e, tc2=tc2, mg=mg, psu=psu):
                    for j in range(NPAIR):
                        lhsT = E["w2b"][:, 1024 * j + 128 * mg:
                                        1024 * j + 128 * (mg + 1)]
                        rhs = gg[:, 1024 * j + 512 * tc2: 1024 * j + 512 * (tc2 + 1)]
                        inst = e.matmul(psu[:, 0:512], lhsT, rhs,
                                        start=(j == 0), stop=(j == NPAIR - 1))
                    return inst
                S.op("tensor", mm2, reads=["fw2b", "fgg"], writes=[pstag])
                S.op("vector", lambda e, mg=mg, psu=psu, sl2=sl2: e.scalar_tensor_tensor(
                    sl2, psu[:, 0:512], smc[:, 44 + mg:45 + mg],
                    sl2, ALU.add, ALU.add),
                     reads=[t_sl2, pstag, "smc"], writes=[t_sl2])
                S.op("scalar", lambda e, tc=tc, mg=mg, sl2=sl2: e.dma_start(
                    out=xf_r[:, mg, 512 * tc:512 * (tc + 1)], in_=sl2),
                    reads=[t_sl2], writes=[f"xfr{mg}"], dma=True,
                    sem=f"dsl{mg % 4}")
                if mg + 3 < 8:
                    r_in(mg + 3)
    B.dump(f"ln2h_l{l}", E["ln2h"][:], (128, 8 * L), BF16, tags=["fln2h"])
    B.dump(f"xfmlp_l{l}", B.dram["xfscr"].ap(), (D, L), F32, tags=XFALL)


def build_final(B, out):
    from contextlib import ExitStack
    nc, S = B.nc, B.S
    dr = B.dram
    C = B.common
    onesb = C["onesb"]
    with ExitStack() as stk:
        sb, ps = B.sb, B.ps
        xfc = sb(stk, "z_xfc", [128, 8 * 512], F32)
        x1c = sb(stk, "z_x1c", [128, 8 * 512], F32)
        hbb = sb(stk, "z_hbb", [128, 8 * 512], BF16)
        srow = sb(stk, "z_srow", [1, 512], F32)
        srow2 = sb(stk, "z_srow2", [1, 512], F32)
        mu2 = sb(stk, "z_mu2", [1, 512], F32)
        rwst = sb(stk, "z_rwst", [64, 2 * 512], BF16)
        sp = sb(stk, "z_sp", [128, 512], F32)
        ofm = sb(stk, "z_ofm", [128, 8 * 512], F32)
        otm = sb(stk, "z_otm", [128, 1024], F32)
        psb0 = ps(stk, "z_psb0", [128, 512], F32)
        psb1 = ps(stk, "z_psb1", [128, 512], F32)
        pst = ps(stk, "z_pst", [128, 512], F32)
        xf_r = dr["xfscr"].ap().rearrange("(g p) t -> p g t", p=128)
        x1_r = dr["x1scr"].ap().rearrange("(g p) t -> p g t", p=128)
        xf_g = xfc[:].rearrange("p (g t) -> p g t", g=8)
        x1_g = x1c[:].rearrange("p (g t) -> p g t", g=8)
        o_g = ofm[:].rearrange("p (g t) -> p g t", g=8)
        hb_g = hbb[:].rearrange("p (g t) -> p g t", g=8)
        T = 512
        for c in range(4):
            S.op("sync", lambda e, c=c: e.dma_start(
                out=xf_g, in_=xf_r[:, :, c * T:(c + 1) * T]),
                reads=XFALL, writes=["zxf"], dma=True)
            S.op("sync", lambda e, c=c: e.dma_start(
                out=x1_g, in_=x1_r[:, :, c * T:(c + 1) * T]),
                reads=["x1scr"], writes=["zx1"], dma=True)
            for pas in range(2):
                func = AF.Copy if pas == 0 else AF.Square
                dstrow = srow if pas == 0 else srow2
                S.op("scalar", lambda e, func=func: e.activation(
                    hbb[:], xfc[:], func), reads=["zxf"], writes=["zhb"])

                def mmsum(e):
                    for cc in range(8):
                        inst = e.matmul(psb0[0:1, 0:T], onesb[:, 0:1],
                                        hb_g[:, cc, :],
                                        start=(cc == 0), stop=(cc == 7))
                    return inst
                S.op("tensor", mmsum, reads=["zhb", "onesb"], writes=["zps0"])
                S.op("scalar", lambda e, dstrow=dstrow: e.copy(
                    dstrow[0:1, :], psb0[0:1, 0:T]),
                     reads=["zps0"], writes=["zrow"])
            S.op("vector", lambda e: e.tensor_scalar_mul(
                srow[0:1, :], srow[0:1, :], 1.0 / D),
                 reads=["zrow"], writes=["zrow"], ses=True)
            S.op("vector", lambda e: e.tensor_scalar_mul(
                srow2[0:1, :], srow2[0:1, :], 1.0 / D),
                 reads=["zrow"], writes=["zrow"], ses=True)
            S.op("vector", lambda e: e.tensor_tensor(
                mu2[0:1, :], srow[0:1, :], srow[0:1, :], ALU.mult),
                 reads=["zrow"], writes=["zmu2"], ses=True)
            S.op("vector", lambda e: e.tensor_sub(
                srow2[0:1, :], srow2[0:1, :], mu2[0:1, :]),
                 reads=["zrow", "zmu2"], writes=["zrow"], ses=True)
            S.op("scalar", lambda e: e.activation(
                srow2[0:1, :], srow2[0:1, :], AF.Sqrt, bias=C["cst"][0:1, 0:1]),
                 reads=["zrow", "cst"], writes=["zrow"])
            S.op("vector", lambda e: e.reciprocal(srow2[0:1, :], srow2[0:1, :]),
                 reads=["zrow"], writes=["zrow"], ses=True)
            S.op("vector", lambda e: e.scalar_tensor_tensor(
                srow[0:1, :], srow[0:1, :], -1.0, srow2[0:1, :],
                ALU.mult, ALU.mult),
                 reads=["zrow"], writes=["zrow"], ses=True)
            S.op("vector", lambda e: e.memset(rwst[:], 0.0), writes=["zrw"])

            def hilo(e):
                e.tensor_copy(rwst[0:1, 0:T], srow2[0:1, :])
                e.tensor_tensor(rwst[32:33, 0:T], srow2[0:1, :],
                                rwst[0:1, 0:T], ALU.subtract)
                e.tensor_copy(rwst[0:1, T:2 * T], srow[0:1, :])
                return e.tensor_tensor(rwst[32:33, T:2 * T], srow[0:1, :],
                                       rwst[0:1, T:2 * T], ALU.subtract)
            S.op("vector", hilo, reads=["zrow"], writes=["zrw"], ses=True)
            S.op("tensor", lambda e: e.matmul(
                psb0[:, 0:T], C["lnsel"][:], rwst[:, 0:T], start=True, stop=True),
                 reads=["zrw", "lnsel"], writes=["zps0"])
            S.op("tensor", lambda e: e.matmul(
                psb1[:, 0:T], C["lnsel"][:], rwst[:, T:2 * T], start=True, stop=True),
                 reads=["zrw", "lnsel"], writes=["zps1"])
            for dg in range(8):
                S.op("vector", lambda e, dg=dg: e.tensor_mul(
                    sp[:], xf_g[:, dg, :], psb0[:, 0:T]),
                     reads=["zxf", "zps0"], writes=["zsp"])
                S.op("vector", lambda e, dg=dg: e.tensor_add(
                    sp[:], sp[:], psb1[:, 0:T]),
                     reads=["zsp", "zps1"], writes=["zsp"])
                S.op("vector", lambda e, dg=dg: e.tensor_add(
                    o_g[:, dg, :], sp[:], x1_g[:, dg, :]),
                     reads=["zsp", "zx1"], writes=["zofm"])
            for q in range(4):
                for dgp in range(4):
                    def trx(e, q=q, dgp=dgp):
                        e.transpose(pst[:, 0:128],
                                    o_g[:, 2 * dgp, 128 * q:128 * (q + 1)],
                                    C["identf"][:])
                        return e.transpose(pst[:, 128:256],
                                           o_g[:, 2 * dgp + 1, 128 * q:128 * (q + 1)],
                                           C["identf"][:])
                    S.op("tensor", trx, reads=["zofm", "identf"], writes=["zpst"])

                    def evx(e, dgp=dgp):
                        return e.tensor_copy(otm[:, 256 * dgp:256 * (dgp + 1)],
                                             pst[:, 0:256])
                    S.op("vector", evx, reads=["zpst"], writes=["zotm"])
                S.op("sync", lambda e, c=c, q=q: e.dma_start(
                    out=out.ap()[512 * c + 128 * q:512 * c + 128 * (q + 1), :],
                    in_=otm[:]),
                    reads=["zotm"], writes=["zout"], dma=True)
    S.barrier()



def _prep_inputs(inputs):
    f32 = np.float32
    asc = np.ascontiguousarray
    rqa, rqb, rka, rkb = _rotary_tables()
    maskL, maskU = _band_masks()
    bsel = np.zeros((32, 32 * 128), dtype=BF16NP)
    for s in range(32):
        bsel[s, 128 * s:128 * (s + 1)] = 1
    lnsel = np.zeros((64, 128), dtype=BF16NP)
    lnsel[0, :] = 1
    lnsel[32, :] = 1
    common = {
        "lnsel": lnsel,
        "masklTb": np.ascontiguousarray(
            np.where(maskL > 0, 0.0, -1e30).T.astype(BF16NP)),
        "maskuTb": np.ascontiguousarray(
            np.where(maskU > 0, 0.0, -1e30).T.astype(BF16NP)),
        "bsel": bsel,
        "rqa": rqa, "rqb": rqb, "rka": rka, "rkb": rkb,
        "identb": np.eye(128, dtype=BF16NP),
        "identf": np.eye(128, dtype=f32),
        "jflipb": np.fliplr(np.eye(128)).astype(BF16NP),
        "maskl": maskL, "masku": maskU,
    }
    for d in (1, 2):
        common[f"inwT{d}"] = asc(inputs[f"L0_attn_in_w{d}"].T)
        common[f"xprojT{d}"] = asc(inputs[f"L0_attn_xproj_w{d}"].T)
        common[f"dtwT{d}"] = asc(inputs[f"L0_attn_dt_w{d}"].T)
        common[f"outwT{d}"] = asc(inputs[f"L0_attn_out_w{d}"].T)
        common[f"convw{d}"] = _pg(inputs[f"L0_attn_conv_w{d}"])
        common[f"convb{d}"] = _pg(inputs[f"L0_attn_conv_b{d}"])
        common[f"dtb{d}"] = _pg(inputs[f"L0_attn_dt_b{d}"])
        common[f"dd{d}"] = _pg(inputs[f"L0_attn_D{d}"])
        common[f"alog{d}"] = _pg(inputs[f"L0_attn_A_log{d}"])
    for l in (0, 1):
        common[f"wqkv{l}"] = asc(inputs["L0_ffn_wqkv"][l])
        common[f"wo{l}"] = asc(inputs["L0_ffn_wo"][l])
        common[f"w1p{l}"] = _w1_perm_cols(inputs["L0_ffn_w1"][l])
        common[f"b1p{l}"] = _b1_perm(inputs["L0_ffn_b1"][l])
        common[f"w2p{l}"] = _w2_perm_rows(inputs["L0_ffn_w2"][l])
        common[f"lng{l}"] = _pg(inputs["L0_ffn_ln_g"][l])
        common[f"lnb{l}"] = _pg(inputs["L0_ffn_ln_b"][l])
        common[f"b2{l}"] = _pg(inputs["L0_ffn_b2"][l])
        common[f"qsrow{l}"] = asc(np.tile(inputs["L0_ffn_q_scale"][l], HEADS)[None, :])
        common[f"ksrow{l}"] = asc(np.tile(inputs["L0_ffn_k_scale"][l], HEADS)[None, :])

    x = inputs["L0_attn_x"]
    in_maps = []
    for i in range(x.shape[0]):
        m = dict(common)
        m["x"] = asc(x[i])
        in_maps.append(m)
    return in_maps


def kernel(**inputs):
    in_maps = _prep_inputs(inputs)
    nc, _ = build_program(dbg=DEBUG_REQUEST)
    res = run_bass_kernel_spmd(nc, in_maps, core_ids=list(range(len(in_maps))))
    outs = [np.asarray(r["out"], dtype=np.float32) for r in res.results]
    if DEBUG_REQUEST:
        kernel.debug_results = res.results
    return np.stack(outs, axis=0)


# version 23
# speedup vs baseline: 1.2660x; 1.0374x over previous
import numpy as np
import ml_dtypes
import concourse.bass as bass
import concourse.mybir as mybir
from concourse.bass_utils import run_bass_kernel_spmd

F32 = mybir.dt.float32
BF16 = mybir.dt.bfloat16
AF = mybir.ActivationFunctionType
ALU = mybir.AluOpType
AX = mybir.AxisListType
BF16NP = ml_dtypes.bfloat16

L = 2048
D = 1024
DM = 512
DI = 1024
DS = 16
DTR = 32
TCH = 512
NCH = L // TCH
HEADS = 8
DH = 64
WIN = 512
NW = L // WIN
HID = 2730
NPAIR = 22
HIDP = NPAIR * 128
XFALL = ["xfscr"] + [f"xfr{m}" for m in range(8)]
EPS = 1e-5

DEBUG_REQUEST = []
STRICT_SAME_ENGINE = [True]
ENGINES = ("tensor", "vector", "scalar", "gpsimd", "sync")


class Sched:
    def __init__(self):
        self.units = []
        self.cnt = {e: 0 for e in ENGINES}
        self.last_w = {}
        self.readers = {}

    def op(self, engine, emit, reads=(), writes=(), dma=False, ses=False,
           sem=None):
        assert engine in ENGINES
        key = sem if sem is not None else engine
        inc = 16 if dma else 1
        waits = {}
        strict = ses or STRICT_SAME_ENGINE[0] or engine != "vector"

        def need(eng2, cnt2):
            if eng2 == engine and engine != "sync" and not strict:
                return
            if waits.get(eng2, 0) < cnt2:
                waits[eng2] = cnt2

        for t in reads:
            w = self.last_w.get(t)
            if w is not None:
                need(*w)
        for t in writes:
            w = self.last_w.get(t)
            if w is not None:
                need(*w)
            for eng2, cnt2 in self.readers.get(t, {}).items():
                need(eng2, cnt2)

        self.cnt[key] = self.cnt.get(key, 0) + inc
        me = (key, self.cnt[key])
        for t in reads:
            self.readers.setdefault(t, {})[key] = self.cnt[key]
        for t in writes:
            self.last_w[t] = me
            self.readers[t] = {}
        self.units.append((engine, emit, inc, waits, key))

    def barrier(self):
        snapshot = dict(self.cnt)
        for e in ENGINES:
            waits = {e2: c for e2, c in snapshot.items()
                     if (e2 != e or e == "sync") and c > 0}
            self.cnt[e] += 1
            self.units.append((e, None, 1, waits, e))
        self.last_w = {}
        self.readers = {}

    def finalize(self, block, sems):
        per_engine = {e: [] for e in ENGINES}
        for engine, emit, inc, waits, key in self.units:
            per_engine[engine].append((emit, inc, waits, key))
        final_cnt = dict(self.cnt)

        def body_for(engname):
            def body(eng):
                watermark = {e: 0 for e in final_cnt}
                for emit, inc, waits, key in per_engine[engname]:
                    for e2, c2 in sorted(waits.items()):
                        if watermark[e2] < c2:
                            eng.wait_ge(sems[e2], c2)
                            watermark[e2] = c2
                    inst = eng.nop() if emit is None else emit(eng)
                    inst.then_inc(sems[key], inc)
                if engname == "sync":
                    for e2, c2 in final_cnt.items():
                        eng.wait_ge(sems[e2], c2)
            return body

        block.sync(body_for("sync"))
        block.tensor(body_for("tensor"))
        block.vector(body_for("vector"))
        block.scalar(body_for("scalar"))
        block.gpsimd(body_for("gpsimd"))



def _rotary_tables():
    t = np.arange(L, dtype=np.float64)
    half = np.arange(0, DH, 2, dtype=np.float64)
    inv_freq = 1.0 / (10000.0 ** (half / DH))
    freqs = t[:, None] * inv_freq[None, :]
    freqs = np.concatenate([freqs, freqs], axis=1)
    base = (half + 0.4 * DH) / (1.4 * DH)
    power = (t - L // 2) / float(WIN // 2)
    scale = base[None, :] ** power[:, None]
    scale = np.concatenate([scale, scale], axis=1)
    cos = np.cos(freqs)
    sin = np.sin(freqs)
    sign = np.ones((DH,))
    sign[:DH // 2] = -1.0
    qa = cos * scale * 8.0
    qb = sign[None, :] * sin * scale * 8.0
    ka = cos / scale
    kb = sign[None, :] * sin / scale

    def rep(x):
        return np.ascontiguousarray(np.tile(x.astype(np.float32), (1, HEADS)))
    return rep(qa), rep(qb), rep(ka), rep(kb)


def _band_masks():
    i = np.arange(128)[:, None]
    j = np.arange(128)[None, :]
    maskL = (j <= i).astype(BF16NP)
    maskU = (j >= i).astype(BF16NP)
    return maskL, maskU


def _pg(x):
    x = np.asarray(x)
    if x.ndim == 1:
        x = x[:, None]
    k = x.shape[1]
    return np.ascontiguousarray(
        x.reshape(8, 128, k).transpose(1, 0, 2).reshape(128, 8 * k))


def _w1_perm_cols(w1):
    a = w1[:, :HID]
    g = w1[:, HID:]
    cols = []
    zpad = np.zeros((w1.shape[0], 128 - 42), w1.dtype)
    for j in range(21):
        cols.append(a[:, 128 * j:128 * (j + 1)])
        cols.append(g[:, 128 * j:128 * (j + 1)])
    cols.append(np.concatenate([a[:, 2688:], zpad], axis=1))
    cols.append(np.concatenate([g[:, 2688:], zpad], axis=1))
    return np.ascontiguousarray(np.concatenate(cols, axis=1))


def _b1_perm(b1):
    a = b1[:HID]
    g = b1[HID:]
    cols = []
    for j in range(21):
        cols.append(a[128 * j:128 * (j + 1)])
        cols.append(g[128 * j:128 * (j + 1)])
    pad = np.zeros(128 - 42, b1.dtype)
    cols.append(np.concatenate([a[2688:], pad]))
    cols.append(np.concatenate([g[2688:], pad]))
    return np.ascontiguousarray(np.stack(cols, axis=1))


def _w2_perm_rows(w2):
    rows = [w2[128 * j:128 * (j + 1)] for j in range(21)]
    rows.append(np.concatenate(
        [w2[2688:], np.zeros((128 - 42, w2.shape[1]), w2.dtype)], axis=0))
    return np.ascontiguousarray(np.concatenate(rows, axis=0))



class Build:
    def __init__(self, nc, S, stk, dbg):
        self.nc = nc
        self.S = S
        self.stk = stk
        self.dbg = list(dbg)
        self.dram = {}

    def sb(self, stk, name, shape, dt=F32):
        return stk.enter_context(self.nc.sbuf_tensor(name, shape, dt))

    def ps(self, stk, name, shape, dt=F32):
        return stk.enter_context(self.nc.psum_tensor(name, shape, dt))

    def din(self, name, shape, dt=F32):
        h = self.nc.dram_tensor(name, list(shape), dt, kind="ExternalInput")
        self.dram[name] = h
        return h

    def dscratch(self, name, shape, dt=F32):
        h = self.nc.dram_tensor(name, list(shape), dt, kind="Internal")
        self.dram[name] = h
        return h

    def dout(self, name, shape, dt=F32):
        h = self.nc.dram_tensor(name, list(shape), dt, kind="ExternalOutput")
        self.dram[name] = h
        return h

    def dump(self, name, ap, shape, dt=F32, tags=()):
        if name not in self.dbg:
            return
        h = self.dout("dbg_" + name, shape, dt)
        self.S.op("sync", lambda e, h=h, ap=ap: e.dma_start(out=h.ap(), in_=ap),
                  reads=list(tags), dma=True)


def build_program(dbg=()):
    from contextlib import ExitStack
    nc = bass.Bass("TRN2", target_bir_lowering=False)
    S = Sched()
    stk = ExitStack()
    B = Build(nc, S, stk, dbg)

    B.din("x", (L, D))
    for d in (1, 2):
        B.din(f"inwT{d}", (DM, 2 * DI))
        B.din(f"xprojT{d}", (DI, 2 * DTR))
        B.din(f"dtwT{d}", (DTR, DI))
        B.din(f"outwT{d}", (DI, DM))
        B.din(f"convw{d}", (128, 32))
        B.din(f"convb{d}", (128, 8))
        B.din(f"dtb{d}", (128, 8))
        B.din(f"dd{d}", (128, 8))
        B.din(f"alog{d}", (128, 128))
    for l in (0, 1):
        B.din(f"wqkv{l}", (D, 3 * 512))
        B.din(f"wo{l}", (512, D))
        B.din(f"w1p{l}", (D, NPAIR * 256))
        B.din(f"b1p{l}", (128, 2 * NPAIR))
        B.din(f"w2p{l}", (HIDP, D))
        B.din(f"lng{l}", (128, 8))
        B.din(f"lnb{l}", (128, 8))
        B.din(f"b2{l}", (128, 8))
        B.din(f"qsrow{l}", (1, 512))
        B.din(f"ksrow{l}", (1, 512))
    B.din("rqa", (L, 512))
    B.din("rqb", (L, 512))
    B.din("rka", (L, 512))
    B.din("rkb", (L, 512))
    B.din("bsel", (32, 32 * 128), BF16)
    B.din("lnsel", (64, 128), BF16)
    B.din("identb", (128, 128), BF16)
    B.din("identf", (128, 128), F32)
    B.din("jflipb", (128, 128), BF16)
    B.din("maskl", (128, 128), BF16)
    B.din("masku", (128, 128), BF16)
    B.din("masklTb", (128, 128), BF16)
    B.din("maskuTb", (128, 128), BF16)

    B.dscratch("ln1scr", (L, D), BF16)
    B.dscratch("x1scr", (D, L))
    B.dscratch("xfscr", (D, L))
    out = B.dout("out", (L, D))

    with stk:
        B.gstk = stk
        sems = {e: stk.enter_context(nc.semaphore("sem_" + e)) for e in ENGINES}
        for k in range(4):
            sems[f"dsl{k}"] = stk.enter_context(nc.semaphore(f"sem_dsl{k}"))
        for k in range(2):
            sems[f"dtb{k}"] = stk.enter_context(nc.semaphore(f"sem_dtb{k}"))
        for k in range(2):
            sems[f"dj{k}"] = stk.enter_context(nc.semaphore(f"sem_dj{k}"))
        for k in range(2):
            sems[f"dw{k}"] = stk.enter_context(nc.semaphore(f"sem_dw{k}"))
        block = stk.enter_context(nc.Block())
        C = {}
        C["identb"] = B.sb(stk, "c_identb", [128, 128], BF16)
        C["identf"] = B.sb(stk, "c_identf", [128, 128], F32)
        C["jflipb"] = B.sb(stk, "c_jflipb", [128, 128], BF16)
        C["maskl"] = B.sb(stk, "c_maskl", [128, 128], BF16)
        C["masku"] = B.sb(stk, "c_masku", [128, 128], BF16)
        C["masklTb"] = B.sb(stk, "c_masklTb", [128, 128], BF16)
        C["maskuTb"] = B.sb(stk, "c_maskuTb", [128, 128], BF16)
        C["cst"] = B.sb(stk, "c_cst", [128, 2], F32)
        C["lnsel"] = B.sb(stk, "c_lnsel", [64, 128], BF16)
        C["onesb"] = B.sb(stk, "c_onesb", [128, 1], BF16)
        C["onesf"] = B.sb(stk, "c_onesf", [128, 1], F32)
        C["onesr"] = B.sb(stk, "c_onesr", [1, 128], BF16)
        B.common = C
        B.mjw = (B.sb(stk, "mj_wst0", [128, 256]),
                 B.sb(stk, "mj_wsb0", [128, 256], BF16))
        for nm in ("identb", "identf", "jflipb", "maskl", "masku",
                   "masklTb", "maskuTb", "lnsel"):
            S.op("sync", lambda e, nm=nm: e.dma_start(
                out=C[nm][:], in_=B.dram[nm].ap()), writes=[nm], dma=True)
        S.op("vector", lambda e: e.memset(C["onesb"][:], 1.0), writes=["onesb"])
        S.op("vector", lambda e: e.memset(C["onesf"][:], 1.0), writes=["onesb"])
        S.op("vector", lambda e: e.memset(C["onesr"][:], 1.0), writes=["onesb"])
        S.op("vector", lambda e: e.memset(C["cst"][:, 0:1], EPS), writes=["cst"])
        S.op("vector", lambda e: e.memset(C["cst"][:, 1:2], 1.0), writes=["cst"])
        build_phase0(B)
        build_mamba(B)
        build_ffn(B)
        build_final(B, out)
        S.finalize(block, sems)
    return nc, B



def build_phase0(B):
    from contextlib import ExitStack
    nc, S = B.nc, B.S
    x = B.dram["x"]
    ln1scr = B.dram["ln1scr"]
    with ExitStack() as stk:
        xt = B.sb(stk, "p0_x", [128, D])
        x2t = B.sb(stk, "p0_x2", [128, D])
        lnt = B.sb(stk, "p0_ln", [128, D], BF16)
        st = B.sb(stk, "p0_st", [128, 8])
        cst = B.common["cst"][:, 0:1]

        jobs = []
        for l in (0, 1):
            B.dscratch(f"w1pb{l}", (D, NPAIR * 256), BF16)
            B.dscratch(f"w2pb{l}", (HIDP, D), BF16)
            B.dscratch(f"wqkvbb{l}", (D, 3 * 512), BF16)
            B.dscratch(f"wobb{l}", (512, D), BF16)
            def tile(srcnm, dstnm, rows, width):
                for r in range(rows):
                    for c0 in range(0, width, 256):
                        wdt = min(256, width - c0)
                        jobs.append((
                            B.dram[srcnm].ap()[128 * r:128 * (r + 1),
                                               c0:c0 + wdt],
                            B.dram[dstnm].ap()[128 * r:128 * (r + 1),
                                               c0:c0 + wdt], wdt))
            tile(f"w1p{l}", f"w1pb{l}", 8, NPAIR * 256)
            tile(f"w2p{l}", f"w2pb{l}", 22, 1024)
            tile(f"wqkv{l}", f"wqkvbb{l}", 8, 1536)
            tile(f"wo{l}", f"wobb{l}", 4, 1024)
        B.wjobs = jobs

        for i in range(L // 128):
            S.op("sync", lambda e, i=i: e.dma_start(
                out=xt[:], in_=x.ap()[128 * i:128 * (i + 1), :]),
                writes=["p0x"], dma=True)
            S.op("vector", lambda e: e.reduce_sum(st[:, 0:1], xt[:], axis=AX.X),
                 reads=["p0x"], writes=["p0st"], ses=True)
            S.op("scalar", lambda e: e.activation(x2t[:], xt[:], AF.Square),
                 reads=["p0x"], writes=["p0x2"])
            S.op("vector", lambda e: e.reduce_sum(st[:, 1:2], x2t[:], axis=AX.X),
                 reads=["p0x2"], writes=["p0st"], ses=True)
            S.op("vector", lambda e: e.tensor_scalar_mul(st[:, 2:3], st[:, 0:1], 1.0 / D),
                 reads=["p0st"], writes=["p0st"], ses=True)
            S.op("vector", lambda e: e.tensor_scalar_mul(st[:, 3:4], st[:, 1:2], 1.0 / D),
                 reads=["p0st"], writes=["p0st"], ses=True)
            S.op("vector", lambda e: e.tensor_tensor(
                st[:, 4:5], st[:, 2:3], st[:, 2:3], ALU.mult),
                 reads=["p0st"], writes=["p0st"], ses=True)
            S.op("vector", lambda e: e.tensor_sub(st[:, 3:4], st[:, 3:4], st[:, 4:5]),
                 reads=["p0st"], writes=["p0st"], ses=True)
            S.op("scalar", lambda e: e.activation(st[:, 5:6], st[:, 3:4], AF.Sqrt, bias=cst),
                 reads=["p0st", "cst"], writes=["p0st"])
            S.op("vector", lambda e: e.reciprocal(st[:, 6:7], st[:, 5:6]),
                 reads=["p0st"], writes=["p0st"], ses=True)
            S.op("vector", lambda e: e.scalar_tensor_tensor(
                st[:, 7:8], st[:, 2:3], -1.0, st[:, 6:7], ALU.mult, ALU.mult),
                 reads=["p0st"], writes=["p0st"], ses=True)
            S.op("vector", lambda e: e.tensor_scalar(
                lnt[:], xt[:], st[:, 6:7], st[:, 7:8], ALU.mult, ALU.add),
                 reads=["p0x", "p0st"], writes=["p0ln"], ses=True)
            S.op("sync", lambda e, i=i: e.dma_start(
                out=ln1scr.ap()[128 * i:128 * (i + 1), :], in_=lnt[:]),
                reads=["p0ln"], writes=["ln1scr"], dma=True)
    S.barrier()



def build_mamba(B):
    from contextlib import ExitStack
    nc, S = B.nc, B.S
    dr = B.dram
    with ExitStack() as stk:
        E = {}
        sb, ps = B.sb, B.ps
        E["identb"] = B.common["identb"]
        E["identf"] = B.common["identf"]
        E["jflipb"] = B.common["jflipb"]
        E["onesb"] = B.common["onesb"]
        E["onesr"] = B.common["onesr"]
        E["cst"] = B.common["cst"]
        E["sm1"] = sb(stk, "m_small1", [128, 64])
        E["sm2"] = sb(stk, "m_small2", [128, 64])
        E["A1"] = sb(stk, "m_A1", [128, 128])
        E["A2"] = sb(stk, "m_A2", [128, 128])
        E["xprojT"] = sb(stk, "m_xprojT", [128, 2 * 512], BF16)
        E["dtwT"] = sb(stk, "m_dtwT", [32, 2 * DI], BF16)
        E["inwTb"] = sb(stk, "m_inwT", [128, 4 * 1024], BF16)
        E["outwTb"] = sb(stk, "m_outwT", [128, 8 * 512], BF16)
        E["ln1f"] = sb(stk, "m_ln1f", [128, 4 * TCH], BF16)
        E["lntm"] = sb(stk, "m_lntm", [128, 512], BF16)
        E["xh"] = sb(stk, "m_xh", [128, 8 * (TCH + 3)])
        E["halo"] = sb(stk, "m_halo", [128, 2 * 24])
        E["xc"] = sb(stk, "m_xc", [128, 8 * TCH])
        E["sz"] = sb(stk, "m_sz", [128, 8 * TCH])
        E["du"] = sb(stk, "m_du", [128, 8 * TCH])
        E["ubuf"] = sb(stk, "m_ubuf", [128, 8 * TCH])
        E["ybuf"] = sb(stk, "m_y", [128, 8 * TCH])
        E["bsel"] = sb(stk, "m_bsel", [32, 32 * 128], BF16)
        E["bcf32"] = sb(stk, "m_bcf32", [32, TCH])
        E["bchi"] = sb(stk, "m_bchi", [32, TCH], BF16)
        E["bclo"] = sb(stk, "m_bclo", [32, TCH], BF16)
        E["dbcb"] = sb(stk, "m_dbcb", [32, TCH], BF16)
        E["sp"] = sb(stk, "m_sp", [128, 2 * TCH])
        E["H"] = sb(stk, "m_H", [128, 3 * 2 * TCH])
        E["dA"] = sb(stk, "m_dA", [128, 2 * 2 * TCH])
        E["dBu"] = sb(stk, "m_dBu", [128, 2 * 2 * TCH])
        E["bcB"] = sb(stk, "m_bcB", [128, 2 * TCH])
        E["bcC"] = sb(stk, "m_bcC", [128, 2 * TCH])
        E["carry"] = sb(stk, "m_carry", [128, 2 * 128])
        E["ygb"] = sb(stk, "m_ygb", [128, 8 * TCH], BF16)
        E["hid"] = sb(stk, "m_hid", [128, 8 * TCH])
        E["srow"] = E["bcC"][0:1, 0:512]
        E["srow2"] = E["bcC"][0:1, 512:1024]
        E["rw0"] = E["bcB"][0:1, 0:256].bitcast(BF16)
        E["rw1"] = E["bcB"][0:1, 256:512].bitcast(BF16)
        E["rw2"] = E["bcB"][0:1, 512:768].bitcast(BF16)
        E["rw3"] = E["bcB"][0:1, 768:1024].bitcast(BF16)

        E["hbb"] = E["ygb"]
        E["xcb"] = E["ygb"]
        E["xfmc"] = E["sz"]
        E["x1c"] = E["ybuf"]
        E["xfc"] = E["du"]
        for i in range(4):
            E[f"ps{i}"] = ps(stk, f"m_ps{i}", [128, 512])
        E["psbc0"] = ps(stk, "m_psbc0", [128, 512])
        E["psbc1"] = ps(stk, "m_psbc1", [128, 512])
        E["pst"] = ps(stk, "m_pst", [128, 512])
        E["pstb"] = ps(stk, "m_pstb", [128, 1024], BF16)

        def load(dst_ap, src_ap, tag):
            S.op("sync", lambda e, dst_ap=dst_ap, src_ap=src_ap: e.dma_start(
                out=dst_ap, in_=src_ap), writes=[tag], dma=True)

        load(E["identb"][:], dr["identb"].ap(), "identb")
        load(E["bsel"][:], dr["bsel"].ap(), "bsel")
        load(E["identf"][:], dr["identf"].ap(), "identf")
        load(E["jflipb"][:], dr["jflipb"].ap(), "jflipb")
        S.op("vector", lambda e: e.memset(E["onesb"][:], 1.0), writes=["onesb"])
        S.op("vector", lambda e: e.memset(E["onesr"][:], 1.0), writes=["onesb"])
        S.op("vector", lambda e: e.memset(E["cst"][:, 0:1], EPS), writes=["cst"])
        S.op("vector", lambda e: e.memset(E["cst"][:, 1:2], 1.0), writes=["cst"])
        E["inwTscr"] = {}
        E["outwTscr"] = {}
        with ExitStack() as pstk:
            wstage = E["xc"][:, 0:2048]
            wstb = E["ln1f"]
            for d in (1, 2):
                sm = E[f"sm{d}"]
                load(sm[:, 0:32], dr[f"convw{d}"].ap(), f"sm{d}")
                load(sm[:, 32:40], dr[f"convb{d}"].ap(), f"sm{d}")
                load(sm[:, 40:48], dr[f"dtb{d}"].ap(), f"sm{d}")
                load(sm[:, 48:56], dr[f"dd{d}"].ap(), f"sm{d}")
                A = E[f"A{d}"]
                load(A[:], dr[f"alog{d}"].ap(), f"A{d}")
                S.op("scalar", lambda e, A=A: e.activation(A[:], A[:], AF.Exp),
                     reads=[f"A{d}"], writes=[f"A{d}"])
                S.op("vector", lambda e, A=A: e.tensor_scalar_mul(A[:], A[:], -1.0),
                     reads=[f"A{d}"], writes=[f"A{d}"])
                src = dr[f"xprojT{d}"].ap().rearrange("(g p) e -> p g e", p=128)
                load(wstage[:, 0:512].rearrange("p (g e) -> p g e", g=8), src, "wstage")
                S.op("vector", lambda e, d=d: e.tensor_copy(
                    E["xprojT"][:, 512 * (d - 1):512 * d], wstage[:, 0:512]),
                     reads=["wstage"], writes=["xprojT"])
                load(wstage[0:32, 0:DI], dr[f"dtwT{d}"].ap(), "wstage")
                S.op("vector", lambda e, d=d: e.tensor_copy(
                    E["dtwT"][:, DI * (d - 1):DI * d], wstage[0:32, 0:DI]),
                     reads=["wstage"], writes=["dtwT"])

            for d in (1, 2):
                E["inwTscr"][d] = B.dscratch(f"inwTbscr{d}", (DM, 2 * DI), BF16)
                E["outwTscr"][d] = B.dscratch(f"outwTbscr{d}", (DI, DM), BF16)
                for piece in range(4):
                    sl = slice(128 * piece, 128 * (piece + 1))
                    load(wstage[:, 0:2048], dr[f"inwT{d}"].ap()[sl, :], "wstage")
                    S.op("vector", lambda e: e.tensor_copy(
                        wstb[:], wstage[:, 0:2048]),
                         reads=["wstage"], writes=["wstb"])
                    S.op("sync", lambda e, d=d, sl=sl: e.dma_start(
                        out=E["inwTscr"][d].ap()[sl, :], in_=wstb[:]),
                        reads=["wstb"], writes=["wstage", f"inwTscr{d}"], dma=True)
                for piece in range(2):
                    src = dr[f"outwT{d}"].ap()[512 * piece:512 * (piece + 1), :]
                    src = src.rearrange("(g p) m -> p g m", p=128)
                    load(wstage[:, 0:2048].rearrange("p (g m) -> p g m", g=4), src, "wstage")
                    S.op("vector", lambda e: e.tensor_copy(
                        wstb[:], wstage[:, 0:2048]),
                         reads=["wstage"], writes=["wstb"])
                    dst = E["outwTscr"][d].ap()[512 * piece:512 * (piece + 1), :]
                    dst = dst.rearrange("(g p) m -> p g m", p=128)
                    S.op("sync", lambda e, dst=dst: e.dma_start(
                        out=dst, in_=wstb[:].rearrange("p (g m) -> p g m", g=4)),
                        reads=["wstb"], writes=["wstage", f"outwTscr{d}"], dma=True)
            S.barrier()

        S.op("vector", lambda e: e.memset(E["carry"][:], 0.0), writes=["carry"])
        S.op("vector", lambda e: e.memset(E["halo"][:], 0.0), writes=["halo"])

        wsts = [B.mjw[0]]
        wsbs = [B.mjw[1]]
        jobs = B.wjobs
        tick_bound = [len(jobs)]
        B.tick_bound = tick_bound

        def job_in(ji):
            sap, dap, wdt = jobs[ji]
            wst = wsts[0]
            S.op("sync", lambda e, sap=sap, wst=wst, wdt=wdt: e.dma_start(
                out=wst[:, 0:wdt], in_=sap),
                writes=["p0wst0"], dma=True, sem="dj0")

        def job_castout(ji):
            sap, dap, wdt = jobs[ji]
            wst = wsts[0]
            wsb = wsbs[0]
            S.op("scalar", lambda e, wst=wst, wsb=wsb, wdt=wdt: e.copy(
                wsb[:, 0:wdt], wst[:, 0:wdt]),
                 reads=["p0wst0"], writes=["p0wsb0"])
            S.op("scalar", lambda e, dap=dap, wsb=wsb, wdt=wdt: e.dma_start(
                out=dap, in_=wsb[:, 0:wdt]),
                reads=["p0wsb0"], writes=["p0out0"], dma=True, sem="dj0")

        tick_state = [0, 0]

        def job_tick():
            tick_state[0] += 1
            for _ in range(2):
                ji = tick_state[1]
                if ji >= tick_bound[0]:
                    return
                job_castout(ji)
                if ji + 1 < len(jobs):
                    job_in(ji + 1)
                tick_state[1] = ji + 1

        def job_flush():
            while tick_state[1] < tick_bound[0]:
                ji = tick_state[1]
                job_castout(ji)
                if ji + 1 < len(jobs):
                    job_in(ji + 1)
                tick_state[1] = ji + 1
        B.job_tick = job_tick
        B.job_flush = job_flush
        job_in(0)

        for c in range(NCH):
            for d in (1, 2):
                _mamba_dir_chunk(B, E, c, d)
            _mix_tail_chunk(B, E, c)
        job_flush()
        B.job_tick = None
    S.barrier()


def _mamba_dir_chunk(B, E, c, d):
    nc, S = B.nc, B.S
    dr = B.dram
    T = TCH
    ln1scr = dr["ln1scr"]
    A = E[f"A{d}"]
    sm = E[f"sm{d}"]
    smtag = f"sm{d}"
    inwTscr = E["inwTscr"][d]
    outwTscr = E["outwTscr"][d]
    inwTb, outwTb = E["inwTb"], E["outwTb"]
    ln1f, lntm = E["ln1f"], E["lntm"]
    xh, halo, xc, xcb, sz = E["xh"], E["halo"], E["xc"], E["xcb"], E["sz"]
    du, ybuf, dbcb, sp = E["du"], E["ybuf"], E["dbcb"], E["sp"]
    prod = sp[:, TCH:2 * TCH]
    bcf32, bchi, bclo, bsel = E["bcf32"], E["bchi"], E["bclo"], E["bsel"]
    Hbuf, dAbuf, dBubuf = E["H"], E["dA"], E["dBu"]
    bcB, bcC, carry, ygb, hid = E["bcB"], E["bcC"], E["carry"], E["ygb"], E["hid"]
    onesb = E["onesb"]
    psrot = [E["ps0"], E["ps1"], E["ps2"], E["ps3"]]
    psbc0, psbc1, pstb = E["psbc0"], E["psbc1"], E["pstb"]

    xh_g = xh[:].rearrange("p (g t) -> p g t", g=8)
    xc_g = xc[:].rearrange("p (g t) -> p g t", g=8)
    sz_g = sz[:].rearrange("p (g t) -> p g t", g=8)
    du_g = du[:].rearrange("p (g t) -> p g t", g=8)
    y_g = ybuf[:].rearrange("p (g t) -> p g t", g=8)
    yg_g = ygb[:].rearrange("p (g t) -> p g t", g=8)
    hid_g = hid[:].rearrange("p (g t) -> p g t", g=8)
    halo_d = halo[:].rearrange("p (d g k) -> p d g k", d=2, g=8)
    ln1f_g = ln1f[:].rearrange("p (f t) -> p f t", f=4)


    for q in range(T // 128):
        if d == 1:
            trow = c * T + q * 128
            src = ln1scr.ap()[trow:trow + 128, 0:DM]
        else:
            trow = L - (c * T + q * 128) - 128
            src = ln1scr.ap()[trow:trow + 128, DM:D]
        S.op("sync", lambda e, src=src: e.dma_start(out=lntm[:], in_=src),
             reads=["ln1scr"], writes=["lntm"], dma=True)
        for p in range(4):
            def tr(e, p=p, d=d):
                ident = E["identb"] if d == 1 else E["jflipb"]
                return e.transpose(pstb[:, 128 * p:128 * (p + 1)],
                                   lntm[:, 128 * p:128 * (p + 1)], ident[:])
            S.op("tensor", tr, reads=["lntm", "identb", "jflipb"], writes=["pstb"])

        def evac(e, q=q):
            return e.tensor_copy(ln1f_g[:, :, 128 * q:128 * (q + 1)],
                                 pstb[:, 0:512].rearrange("p (f t) -> p f t", f=4))
        S.op("vector", evac, reads=["pstb"], writes=["ln1f"])

    for half in range(2):
        for cc in range(4):
            src_ap = inwTscr.ap()[128 * cc:128 * (cc + 1),
                                  1024 * half:1024 * (half + 1)]
            S.op("sync", lambda e, cc=cc, src_ap=src_ap: e.dma_start(
                out=inwTb[:, 1024 * cc:1024 * (cc + 1)], in_=src_ap),
                reads=[f"inwTscr{d}"], writes=["inwTb"], dma=True)
        for sub in range(8):
            eg = half * 8 + sub
            psu = psrot[sub % 4]
            pstag = f"mps{sub % 4}"

            def mm(e, sub=sub, psu=psu):
                for cc in range(4):
                    lhsT = inwTb[:, 1024 * cc + 128 * sub: 1024 * cc + 128 * (sub + 1)]
                    inst = e.matmul(psu[:, 0:T], lhsT, ln1f_g[:, cc, :],
                                    start=(cc == 0), stop=(cc == 3))
                return inst
            S.op("tensor", mm, reads=["inwTb", "ln1f"], writes=[pstag])
            if eg < 8:
                def evacxh(e, eg=eg, psu=psu):
                    return e.copy(xh_g[:, eg, 3:3 + T], psu[:, 0:T])
                S.op("scalar", evacxh, reads=[pstag], writes=["xh"])
            else:
                def evacz(e, eg=eg, psu=psu):
                    return e.activation(sz_g[:, eg - 8, :], psu[:, 0:T], AF.Silu)
                S.op("scalar", evacz, reads=[pstag], writes=["sz"])

    S.op("vector", lambda e: e.tensor_copy(xh_g[:, :, 0:3], halo_d[:, d - 1]),
         reads=["halo"], writes=["xh"])
    for dg in range(8):
        acc = sp[:, 0:T]

        def conv0(e, dg=dg, acc=acc):
            return e.tensor_scalar_mul(acc, xh_g[:, dg, 0:T], sm[:, 4 * dg:4 * dg + 1])
        S.op("vector", conv0, reads=["xh", smtag], writes=["spA"])
        for k in (1, 2, 3):
            def convk(e, dg=dg, k=k, acc=acc):
                return e.scalar_tensor_tensor(
                    acc, xh_g[:, dg, k:k + T], sm[:, 4 * dg + k:4 * dg + k + 1],
                    acc, ALU.mult, ALU.add)
            S.op("vector", convk, reads=["xh", smtag, "spA"], writes=["spA"])

        def silu(e, dg=dg, acc=acc):
            return e.activation(xc_g[:, dg, :], acc, AF.Silu,
                                bias=sm[:, 32 + dg:33 + dg])
        S.op("scalar", silu, reads=["spA", smtag], writes=["xc"])
    S.op("vector", lambda e: e.tensor_copy(xcb[:], xc[:]), reads=["xc"], writes=["ygb"])
    S.op("vector", lambda e, d=d: e.tensor_copy(
        halo_d[:, d - 1], xh_g[:, :, T:T + 3]), reads=["xh"], writes=["halo"])

    xcb_g = xcb[:].rearrange("p (g t) -> p g t", g=8)

    def xproj(e, d=d):
        xp = E["xprojT"][:, 512 * (d - 1):512 * d].rearrange("p (g e) -> p g e", g=8)
        for cc in range(8):
            inst = e.matmul(E["ps0"][0:64, 0:T], xp[:, cc, :], xcb_g[:, cc, :],
                            start=(cc == 0), stop=(cc == 7))
        return inst
    S.op("tensor", xproj, reads=["xprojT", "ygb"], writes=["mps0"])
    S.op("scalar", lambda e: e.copy(dbcb[:, 0:T], E["ps0"][0:32, 0:T]),
         reads=["mps0"], writes=["dbcb"])
    S.op("scalar", lambda e: e.copy(bcf32[:, 0:T], E["ps0"][32:64, 0:T]),
         reads=["mps0"], writes=["dbc"])
    S.op("vector", lambda e: e.tensor_copy(bchi[:, 0:T], bcf32[:, 0:T]),
         reads=["dbc"], writes=["bchi"], ses=True)
    S.op("vector", lambda e: e.tensor_tensor(bclo[:, 0:T], bcf32[:, 0:T],
                                             bchi[:, 0:T], ALU.subtract),
         reads=["dbc", "bchi"], writes=["bchi"], ses=True)

    for mg in range(8):
        def dtmm(e, mg=mg, d=d):
            return e.matmul(
                E["ps1"][:, 0:T],
                E["dtwT"][:, DI * (d - 1) + 128 * mg: DI * (d - 1) + 128 * (mg + 1)],
                dbcb[:, 0:T], start=True, stop=True)
        S.op("tensor", dtmm, reads=["dtwT", "dbcb"], writes=["mps1"])
        raw = sp[:, 0:T]
        t1 = sp[:, T:2 * T]

        def rawev(e, mg=mg):
            return e.add(raw, E["ps1"][:, 0:T], sm[:, 40 + mg:41 + mg])
        S.op("scalar", rawev, reads=["mps1", smtag], writes=["spA"])
        S.op("vector", lambda e: e.scalar_tensor_tensor(
            t1, raw, -1.0, raw, ALU.mult, ALU.min),
             reads=["spA"], writes=["spB"])
        S.op("scalar", lambda e: e.activation(t1, t1, AF.Exp),
             reads=["spB"], writes=["spB"])
        S.op("scalar", lambda e: e.activation(t1, t1, AF.Ln, bias=E["cst"][:, 1:2]),
             reads=["spB", "cst"], writes=["spB"])

        def sfin(e, mg=mg):
            return e.scalar_tensor_tensor(du_g[:, mg, :], raw, 0.0, t1,
                                          ALU.max, ALU.add)
        S.op("vector", sfin, reads=["spA", "spB"], writes=["du"])

    ubuf = E["ubuf"]
    u_g = ubuf[:].rearrange("p (g t) -> p g t", g=8)
    S.op("vector", lambda e: e.tensor_mul(ubuf[:], du[:], xc[:]),
         reads=["du", "xc"], writes=["ubuf"])

    prodA = sp[:, 0:2 * T]
    treA = ln1f[:].bitcast(F32)[:, 0:T]
    prodB = xh[:, 0:2 * T]
    treB = xh[:, 2 * T:3 * T]
    NG = 8
    for grp in range(NG):
        SLg = slice((grp % 2) * 2 * T, ((grp % 2) + 1) * 2 * T)
        bcB2 = bcB[:, 0:2 * T]
        bcC2 = bcC[:, 0:2 * T]
        tBc, tCc = "bcB", "bcC"
        for i in range(2):
            s = 2 * grp + i

            def bcmmB(e, s=s):
                sel = bsel[:, 128 * s:128 * (s + 1)]
                e.matmul(psbc0[:, 0:T], sel, bchi[:, 0:T], start=True, stop=False)
                return e.matmul(psbc0[:, 0:T], sel, bclo[:, 0:T], start=False, stop=True)
            S.op("tensor", bcmmB, reads=["bchi", "bsel"], writes=["psbc0"])
            S.op("scalar", lambda e, i=i, bcB2=bcB2: e.copy(
                bcB2[:, i * T:(i + 1) * T], psbc0[:, 0:T]),
                 reads=["psbc0"], writes=[tBc])

            def bcmmC(e, s=s):
                sel = bsel[:, 128 * (16 + s):128 * (17 + s)]
                e.matmul(psbc1[:, 0:T], sel, bchi[:, 0:T], start=True, stop=False)
                return e.matmul(psbc1[:, 0:T], sel, bclo[:, 0:T], start=False, stop=True)
            S.op("tensor", bcmmC, reads=["bchi", "bsel"], writes=["psbc1"])
            S.op("scalar", lambda e, i=i, bcC2=bcC2: e.copy(
                bcC2[:, i * T:(i + 1) * T], psbc1[:, 0:T]),
                 reads=["psbc1"], writes=[tCc])

        def stage_p1(dg, grp=grp):
            slot = dg % 2
            dA2 = dAbuf[:, slot * 2 * T:(slot + 1) * 2 * T]

            def p1(e, dg=dg, grp=grp, dA2=dA2):
                for i in range(2):
                    s = 2 * grp + i
                    inst = e.activation(dA2[:, i * T:(i + 1) * T], du_g[:, dg, :],
                                        AF.Exp, scale=A[:, 16 * dg + s:16 * dg + s + 1])
                return inst
            S.op("scalar", p1, reads=["du", f"A{d}"], writes=[f"dA{slot}"])

        def stage_p2(dg, grp=grp):
            slot = dg % 2
            dBu2 = dBubuf[:, slot * 2 * T:(slot + 1) * 2 * T]

            def p2(e, dg=dg, dBu2=dBu2):
                ub = u_g[:, dg, :].unsqueeze(1).broadcast_to((128, 2, T))
                return e.tensor_tensor(
                    dBu2.rearrange("p (i t) -> p i t", i=2), ub,
                    bcB2.rearrange("p (i t) -> p i t", i=2), ALU.mult)
            S.op("gpsimd", p2, reads=["ubuf", tBc], writes=[f"dB{slot}"])

        def stage_p3(dg, grp=grp):
            slot = dg % 2
            hslot = dg % 3
            dA2 = dAbuf[:, slot * 2 * T:(slot + 1) * 2 * T]
            dBu2 = dBubuf[:, slot * 2 * T:(slot + 1) * 2 * T]
            H2 = Hbuf[:, hslot * 2 * T:(hslot + 1) * 2 * T]

            def p3(e, dg=dg, grp=grp, dA2=dA2, dBu2=dBu2, H2=H2, d=d):
                for i in range(2):
                    s = 2 * grp + i
                    init = carry[:, 128 * (d - 1) + 16 * dg + s:
                                 128 * (d - 1) + 16 * dg + s + 1]
                    inst = e.tensor_tensor_scan(
                        H2[:, i * T:(i + 1) * T], dA2[:, i * T:(i + 1) * T],
                        dBu2[:, i * T:(i + 1) * T], init, ALU.mult, ALU.add)
                return inst
            S.op("vector", p3, reads=[f"dA{slot}", f"dB{slot}", "carry"],
                 writes=[f"H{hslot}", "carry"])

        def stage_p4a(dg, grp=grp):
            hslot = dg % 3
            pslot = dg % 2
            H2 = Hbuf[:, hslot * 2 * T:(hslot + 1) * 2 * T]
            prod2 = prodA if pslot == 0 else prodB
            ptags = ["spA", "spB"] if pslot == 0 else ["xh"]

            def p4a(e, H2=H2, prod2=prod2):
                return e.tensor_mul(prod2, H2, bcC2)
            S.op("gpsimd", p4a, reads=[f"H{hslot}", tCc], writes=ptags)

        def stage_p4t(dg, grp=grp):
            hslot = dg % 3
            pslot = dg % 2
            H2 = Hbuf[:, hslot * 2 * T:(hslot + 1) * 2 * T]
            prod2 = prodA if pslot == 0 else prodB
            tre = treA if pslot == 0 else treB
            ptags = ["spA", "spB"] if pslot == 0 else ["xh"]
            ttags = ["ln1f"] if pslot == 0 else ["xh"]

            def cup(e, dg=dg, grp=grp, d=d, H2=H2):
                dst = carry[:, 128 * (d - 1) + 16 * dg + 2 * grp:
                            128 * (d - 1) + 16 * dg + 2 * grp + 2]
                return e.copy(
                    dst, H2.rearrange("p (i t) -> p i t", i=2)[:, :, T - 1:T])
            S.op("scalar", cup, reads=[f"H{hslot}"], writes=["carry"])

            def p4t(e, dg=dg, grp=grp, prod2=prod2, tre=tre):
                e.tensor_add(tre, prod2[:, 0:T], prod2[:, T:2 * T])
                if grp == 0:
                    return e.tensor_copy(y_g[:, dg, :], tre)
                return e.tensor_add(y_g[:, dg, :], y_g[:, dg, :], tre)
            S.op("vector", p4t, reads=ptags + ["ybuf"] + ttags,
                 writes=["ybuf"] + ttags)

        stage_p1(0); stage_p2(0)
        stage_p1(1); stage_p2(1)
        stage_p3(0)
        for dg in range(8):
            if getattr(B, "job_tick", None):
                B.job_tick()
            if dg + 1 < 8:
                stage_p3(dg + 1)
            stage_p4a(dg)
            if dg + 2 < 8:
                stage_p1(dg + 2)
                stage_p2(dg + 2)
            stage_p4t(dg)

    for dg in range(8):
        def gate1(e, dg=dg):
            return e.scalar_tensor_tensor(
                y_g[:, dg, :], xc_g[:, dg, :], sm[:, 48 + dg:49 + dg],
                y_g[:, dg, :], ALU.mult, ALU.add)
        S.op("vector", gate1, reads=["xc", smtag, "ybuf"], writes=["ybuf"])

        def gate2(e, dg=dg):
            return e.tensor_mul(yg_g[:, dg, :], y_g[:, dg, :], sz_g[:, dg, :])
        S.op("vector", gate2, reads=["ybuf", "sz"], writes=["ygb"])

    for gg in range(8):
        S.op("sync", lambda e, gg=gg: e.dma_start(
            out=outwTb[:, 512 * gg:512 * (gg + 1)],
            in_=outwTscr.ap()[128 * gg:128 * (gg + 1), :]),
            reads=[f"outwTscr{d}"], writes=["outwTb"], dma=True)
    for mg in range(4):
        psu = psrot[mg % 2]
        pstag = f"mps{mg % 2}"

        def opmm(e, mg=mg, psu=psu):
            for cc in range(8):
                lhsT = outwTb[:, 512 * cc + 128 * mg: 512 * cc + 128 * (mg + 1)]
                inst = e.matmul(psu[:, 0:T], lhsT, yg_g[:, cc, :],
                                start=(cc == 0), stop=(cc == 7))
            return inst
        S.op("tensor", opmm, reads=["outwTb", "ygb"], writes=[pstag])
        row = (d - 1) * 4 + mg

        def opev(e, row=row, psu=psu):
            return e.copy(hid_g[:, row, :], psu[:, 0:T])
        S.op("scalar", opev, reads=[pstag], writes=["hid"])

    if c == 0:
        B.dump(f"dA_s15_d{d}", dAbuf[:], (128, 2 * TCH), F32, tags=["dA0", "dA1"])
        B.dump(f"bcB_s15_d{d}", bcB[:], (128, TCH), F32, tags=["bcB"])
        B.dump(f"bcC_s15_d{d}", bcC[:], (128, TCH), F32, tags=["bcC"])
        B.dump(f"H_s15_d{d}", Hbuf[:], (128, 2 * TCH), F32, tags=["H0", "H1"])
        B.dump(f"sz_c0_d{d}", sz[:], (128, 8 * TCH), F32, tags=["sz"])
        B.dump(f"u_c0_d{d}", E["ubuf"][:], (128, 8 * TCH), F32, tags=["ubuf"])
    B.dump(f"xc_c{c}_d{d}", xc[:], (128, 8 * TCH), F32, tags=["xc"])
    B.dump(f"du_c{c}_d{d}", du[:], (128, 8 * TCH), F32, tags=["du"])
    B.dump(f"y_c{c}_d{d}", ybuf[:], (128, 8 * TCH), F32, tags=["ybuf"])
    B.dump(f"hid_c{c}_d{d}", hid[:], (128, 8 * TCH), F32, tags=["hid"])


def _fm_ln(B, E, src_ap, src_tag, T):
    nc, S = B.nc, B.S
    hbb, onesb = E["hbb"], E["onesb"]
    srow, srow2, sp = E["srow"], E["srow2"], E["sp"]
    rw0, rw1, rw2, rw3 = E["rw0"], E["rw1"], E["rw2"], E["rw3"]
    T_ = TCH
    psbc0, psbc1 = E["psbc0"], E["psbc1"]
    hb_g = hbb[:].rearrange("p (g t) -> p g t", g=8)

    S.op("scalar", lambda e: e.activation(hbb[:], src_ap, AF.Copy),
         reads=[src_tag], writes=["ygb"])

    def mm1(e):
        for cc in range(8):
            inst = e.matmul(psbc0[0:1, 0:T], onesb[:, 0:1], hb_g[:, cc, :],
                            start=(cc == 0), stop=(cc == 7))
        return inst
    S.op("tensor", mm1, reads=["ygb", "onesb"], writes=["psbc0"])
    S.op("scalar", lambda e: e.copy(srow[0:1, 0:T], psbc0[0:1, 0:T]),
         reads=["psbc0"], writes=["bcC"])
    S.op("scalar", lambda e: e.activation(hbb[:], src_ap, AF.Square),
         reads=[src_tag], writes=["ygb"])

    def mm2(e):
        for cc in range(8):
            inst = e.matmul(psbc0[0:1, 0:T], onesb[:, 0:1], hb_g[:, cc, :],
                            start=(cc == 0), stop=(cc == 7))
        return inst
    S.op("tensor", mm2, reads=["ygb", "onesb"], writes=["psbc0"])
    S.op("scalar", lambda e: e.copy(srow2[0:1, 0:T], psbc0[0:1, 0:T]),
         reads=["psbc0"], writes=["bcC"])

    S.op("vector", lambda e: e.tensor_scalar_mul(srow[0:1, 0:T], srow[0:1, 0:T], 1.0 / D),
         reads=["bcC"], writes=["bcC"], ses=True)
    S.op("vector", lambda e: e.tensor_scalar_mul(srow2[0:1, 0:T], srow2[0:1, 0:T], 1.0 / D),
         reads=["bcC"], writes=["bcC"], ses=True)
    S.op("vector", lambda e: e.tensor_tensor(sp[0:1, 0:T], srow[0:1, 0:T],
                                             srow[0:1, 0:T], ALU.mult),
         reads=["bcC"], writes=["spA"], ses=True)
    S.op("vector", lambda e: e.tensor_sub(srow2[0:1, 0:T], srow2[0:1, 0:T], sp[0:1, 0:T]),
         reads=["bcC", "spA"], writes=["bcC"], ses=True)
    S.op("scalar", lambda e: e.activation(srow2[0:1, 0:T], srow2[0:1, 0:T],
                                          AF.Sqrt, bias=E["cst"][0:1, 0:1]),
         reads=["bcC", "cst"], writes=["bcC"])
    S.op("vector", lambda e: e.reciprocal(srow2[0:1, 0:T], srow2[0:1, 0:T]),
         reads=["bcC"], writes=["bcC"], ses=True)
    S.op("vector", lambda e: e.scalar_tensor_tensor(
        srow[0:1, 0:T], srow[0:1, 0:T], -1.0, srow2[0:1, 0:T], ALU.mult, ALU.mult),
         reads=["bcC"], writes=["bcC"], ses=True)

    def hilo(e):
        e.tensor_copy(rw0, srow2[0:1, 0:T])
        e.tensor_tensor(rw1, srow2[0:1, 0:T], rw0, ALU.subtract)
        e.tensor_copy(rw2, srow[0:1, 0:T])
        return e.tensor_tensor(rw3, srow[0:1, 0:T], rw2, ALU.subtract)
    S.op("vector", hilo, reads=["bcC"], writes=["rowhl", "bcB"], ses=True)

    def bmr(e):
        onesr = E["onesr"]
        e.matmul(psbc0[:, 0:T], onesr[:], rw0, start=True, stop=False)
        return e.matmul(psbc0[:, 0:T], onesr[:], rw1, start=False, stop=True)
    S.op("tensor", bmr, reads=["rowhl", "bcB", "onesb"], writes=["psbc0"])

    def bmn(e):
        onesr = E["onesr"]
        e.matmul(psbc1[:, 0:T], onesr[:], rw2, start=True, stop=False)
        return e.matmul(psbc1[:, 0:T], onesr[:], rw3, start=False, stop=True)
    S.op("tensor", bmn, reads=["rowhl", "bcB", "onesb"], writes=["psbc1"])


def _mix_tail_chunk(B, E, c):
    nc, S = B.nc, B.S
    T = TCH
    x = B.dram["x"]
    x1scr = B.dram["x1scr"]
    xfscr = B.dram["xfscr"]
    hid, sp = E["hid"], E["sp"]
    xtm = E["ln1f"][:].bitcast(F32)
    xfmc, x1c, xfc = E["xfmc"], E["x1c"], E["xfc"]
    psbc0, psbc1, pst = E["psbc0"], E["psbc1"], E["pst"]
    identf = E["identf"]
    hid_g = hid[:].rearrange("p (g t) -> p g t", g=8)
    x1_g = x1c[:].rearrange("p (g t) -> p g t", g=8)
    xf_g = xfc[:].rearrange("p (g t) -> p g t", g=8)
    xfm_g = xfmc[:].rearrange("p (g t) -> p g t", g=8)

    for q in range(T // 128):
        S.op("sync", lambda e, q=q: e.dma_start(
            out=xtm, in_=x.ap()[c * T + 128 * q: c * T + 128 * (q + 1), :]),
            writes=["ln1f"], dma=True)
        for dgp in range(4):
            def trx(e, dgp=dgp):
                e.transpose(pst[:, 0:128], xtm[:, 256 * dgp:256 * dgp + 128], identf[:])
                return e.transpose(pst[:, 128:256],
                                   xtm[:, 256 * dgp + 128:256 * dgp + 256], identf[:])
            S.op("tensor", trx, reads=["ln1f", "identf"], writes=["pstT"])

            def evx(e, q=q, dgp=dgp):
                return e.tensor_copy(
                    xfm_g[:, 2 * dgp:2 * dgp + 2, 128 * q:128 * (q + 1)],
                    pst[:, 0:256].rearrange("p (g t) -> p g t", g=2))
            S.op("vector", evx, reads=["pstT"], writes=["sz"])

    _fm_ln(B, E, hid[:], "hid", T)
    for dg in range(8):
        S.op("vector", lambda e, dg=dg: e.tensor_mul(
            sp[:, 0:T], hid_g[:, dg, :], psbc0[:, 0:T]),
             reads=["hid", "psbc0"], writes=["spA"])
        S.op("vector", lambda e: e.tensor_add(sp[:, 0:T], sp[:, 0:T], psbc1[:, 0:T]),
             reads=["spA", "psbc1"], writes=["spA"])
        S.op("vector", lambda e, dg=dg: e.tensor_add(
            x1_g[:, dg, :], sp[:, 0:T], xfm_g[:, dg, :]),
             reads=["spA", "sz"], writes=["ybuf"])
    S.op("sync", lambda e, c=c: e.dma_start(
        out=x1scr.ap().rearrange("(g p) t -> p g t", p=128)[:, :, c * T:(c + 1) * T],
        in_=x1_g), reads=["ybuf"], writes=["x1scr"], dma=True)

    _fm_ln(B, E, x1c[:], "ybuf", T)
    for dg in range(8):
        S.op("vector", lambda e, dg=dg: e.tensor_mul(
            sp[:, 0:T], x1_g[:, dg, :], psbc0[:, 0:T]),
             reads=["ybuf", "psbc0"], writes=["spA"])
        S.op("vector", lambda e, dg=dg: e.tensor_add(
            xf_g[:, dg, :], sp[:, 0:T], psbc1[:, 0:T]),
             reads=["spA", "psbc1"], writes=["du"])
    S.op("sync", lambda e, c=c: e.dma_start(
        out=xfscr.ap().rearrange("(g p) t -> p g t", p=128)[:, :, c * T:(c + 1) * T],
        in_=xf_g), reads=["du"], writes=["xfscr"], dma=True)

    B.dump(f"x1_c{c}", x1c[:], (128, 8 * TCH), F32, tags=["ybuf"])
    B.dump(f"xf_c{c}", xfc[:], (128, 8 * TCH), F32, tags=["du"])



def build_ffn(B):
    from contextlib import ExitStack
    S = B.S
    for l in (0, 1):
        if l == 0:
            B.tick_bound[0] = len(B.wjobs)
        with ExitStack() as stk:
            _ffn_attn(B, stk, l)
        S.barrier()
        with ExitStack() as stk:
            _ffn_mlp(B, stk, l)
            if l == 0:
                B.job_flush()
                B.job_tick = None
        S.barrier()


def _ffn_attn(B, stk, l):
    nc, S = B.nc, B.S
    dr = B.dram
    C = B.common
    def sb(stk_, name, shape, dt=F32):
        return B.sb(stk_, f"{name}_{l}", shape, dt)
    def ps(stk_, name, shape, dt=F32):
        return B.ps(stk_, f"{name}_{l}", shape, dt)
    E = {}
    E["hb"] = sb(stk, "f_hb", [128, 8 * L], BF16)
    E["wqkvb"] = sb(stk, "f_wqkvb", [128, 8 * 1536], BF16)
    E["wob"] = sb(stk, "f_wob", [128, 4 * 1024], BF16)

    E["qfm"] = sb(stk, "f_qfm", [128, 4 * L], BF16)
    E["kfm"] = sb(stk, "f_kfm", [128, 4 * L], BF16)
    E["vsb"] = sb(stk, "f_vsb", [128, 16 * 512], BF16)
    E["osb"] = sb(stk, "f_osb", [128, 4 * L], BF16)
    E["qsbc"] = sb(stk, "f_qsbc", [128, 512], F32)
    E["ksbc"] = sb(stk, "f_ksbc", [128, 512], F32)
    E["tbl"] = sb(stk, "f_tbl", [128, 2 * 4 * 512], F32)
    E["psq"] = sb(stk, "f_psq", [128, 2 * 512], F32)
    E["rst"] = sb(stk, "f_rst", [128, 64], F32)
    E["qtm"] = sb(stk, "f_qtm", [128, 2 * 512], F32)
    E["qtb"] = sb(stk, "f_qtb", [128, 2 * 512], BF16)
    E["attn"] = sb(stk, "f_attn", [128, 2 * 640], BF16)
    E["attnT"] = sb(stk, "f_attnT", [128, 2 * 8 * 512], BF16)
    E["nmx"] = sb(stk, "f_nmx", [128, 8], F32)
    E["slab2"] = sb(stk, "f_slab2", [128, 4 * 512], F32)
    E["slab"] = sb(stk, "f_slab", [128, L], F32)
    E["sp"] = sb(stk, "f_sp", [128, 512], F32)
    psQ = ps(stk, "f_psQ", [128, 512], F32)
    psK = ps(stk, "f_psK", [128, 512], F32)
    psV = ps(stk, "f_psV", [128, 512], F32)
    psS0 = ps(stk, "f_psS0", [128, 1024], F32)
    psS1 = ps(stk, "f_psS1", [128, 1024], F32)
    psTa = ps(stk, "f_psTa", [128, 1024], BF16)

    xfscr = dr["xfscr"]
    xf_r = xfscr.ap().rearrange("(g p) t -> p g t", p=128)
    hb = E["hb"]
    hb_g = hb[:].rearrange("p (g t) -> p g t", g=8)
    identb = C["identb"]

    def load(dst, srcap, tag, rtags=()):
        S.op("sync", lambda e, dst=dst, srcap=srcap: e.dma_start(
            out=dst, in_=srcap), reads=list(rtags), writes=[tag], dma=True)

    for cc in range(8):
        load(E["wqkvb"][:, 1536 * cc:1536 * (cc + 1)],
             dr[f"wqkvbb{l}"].ap()[128 * cc:128 * (cc + 1), :], "wqkvb", ["wqkvb"])
    for cc in range(4):
        load(E["wob"][:, 1024 * cc:1024 * (cc + 1)],
             dr[f"wobb{l}"].ap()[128 * cc:128 * (cc + 1), :], "wob", ["wob"])
    load(E["qsbc"][:], dr[f"qsrow{l}"].ap().to_broadcast((128, 512)), "qsbc")
    load(E["ksbc"][:], dr[f"ksrow{l}"].ap().to_broadcast((128, 512)), "ksbc")

    for dg in range(8):
        S.op("sync", lambda e, dg=dg: e.dma_start(
            out=E["slab"][:], in_=xf_r[:, dg, :]),
            reads=XFALL, writes=["fslab"], dma=True)
        if dg < 4:
            S.op("scalar", lambda e, dg=dg: e.copy(hb_g[:, dg, :], E["slab"][:]),
                 reads=["fslab"], writes=["hbf"])
        else:
            S.op("vector", lambda e, dg=dg: e.memset(hb_g[:, dg, 0:1], 0.0),
                 writes=["hbf"])
            S.op("scalar", lambda e, dg=dg: e.copy(
                hb_g[:, dg, 1:L], E["slab"][:, 0:L - 1]),
                 reads=["fslab"], writes=["hbf"])

    qfm, kfm, vsb = E["qfm"], E["kfm"], E["vsb"]

    def qkv_slots(sl):
        if sl == 0:
            return psQ[:], psK[:], psV[:], "psQ", "psK", "psV"
        return (psS0[:, 0:512], psS0[:, 512:1024], psS1[:, 0:512],
                "fpsS0", "fpsS0", "fpsS1")

    def qkv_mm(tt):
        sl = tt % 2
        t_tbl = f"tbl{sl}"
        tbl = E["tbl"][:, 2048 * sl:2048 * (sl + 1)]
        psQh, psKh, psVh, tQ, tK, tV = qkv_slots(sl)
        for ti, nm in enumerate(("rqa", "rqb", "rka", "rkb")):
            S.op("sync", lambda e, ti=ti, nm=nm, tt=tt, tbl=tbl: e.dma_start(
                out=tbl[:, 512 * ti:512 * (ti + 1)],
                in_=dr[nm].ap()[128 * tt:128 * (tt + 1), :]),
                reads=[t_tbl], writes=[t_tbl], dma=True, sem=f"dtb{sl}")

        def mmq(e, tt=tt, psQh=psQh, psKh=psKh, psVh=psVh):
            for cc in range(8):
                lhsT = hb_g[:, cc, 128 * tt:128 * (tt + 1)]
                e.matmul(psQh, lhsT,
                         E["wqkvb"][:, 1536 * cc:1536 * cc + 512],
                         start=(cc == 0), stop=(cc == 7))
                e.matmul(psKh, lhsT,
                         E["wqkvb"][:, 1536 * cc + 512:1536 * cc + 1024],
                         start=(cc == 0), stop=(cc == 7))
                inst = e.matmul(psVh, lhsT,
                                E["wqkvb"][:, 1536 * cc + 1024:1536 * cc + 1536],
                                start=(cc == 0), stop=(cc == 7))
            return inst
        S.op("tensor", mmq, reads=["hbf", "wqkvb"], writes=[tQ, tK, tV])

    def qkv_prep(tt):
        sl = tt % 2
        t_tbl, t_psq, t_qtb = f"tbl{sl}", f"psq{sl}", f"qtb{sl}"
        tbl = E["tbl"][:, 2048 * sl:2048 * (sl + 1)]
        psq = E["psq"][:, 512 * sl:512 * (sl + 1)]
        qtb = E["qtb"][:, 512 * sl:512 * (sl + 1)]
        psQh, psKh, psVh, tQ, tK, tV = qkv_slots(sl)
        S.op("scalar", lambda e, tt=tt, psVh=psVh: e.copy(
            vsb[:, 512 * tt:512 * (tt + 1)], psVh),
             reads=[tV], writes=["vsb"])
        for which, psu, pstag, qsb, tb0, tb1, dstfm, dtag in (
                ("q", psQh, tQ, E["qsbc"], 0, 1, qfm, "qfm"),
                ("k", psKh, tK, E["ksbc"], 2, 3, kfm, "kfm")):
            rst = E["rst"][:, 32 * sl:32 * (sl + 1)]
            t_rst = f"rst{sl}"
            qtm = E["qtm"]
            off = 0 if which == "q" else 8
            S.op("scalar", lambda e, psu=psu, psq=psq: e.activation(
                psq, psu, AF.Square),
                 reads=[pstag], writes=[t_psq])
            S.op("vector", lambda e, off=off, psq=psq, rst=rst: e.tensor_reduce(
                rst[:, off:off + 8],
                psq.rearrange("p (h d) -> p h d", h=8), AX.X, ALU.add),
                 reads=[t_psq], writes=[t_rst])
            S.op("scalar", lambda e, off=off, rst=rst: e.activation(
                rst[:, 16 + off:24 + off], rst[:, off:off + 8], AF.Sqrt),
                 reads=[t_rst], writes=[t_rst])
            S.op("vector", lambda e, off=off, rst=rst: e.tensor_scalar_max(
                rst[:, 16 + off:24 + off], rst[:, 16 + off:24 + off], 1e-12),
                 reads=[t_rst], writes=[t_rst], ses=True)
            S.op("vector", lambda e, off=off, rst=rst: e.reciprocal(
                rst[:, 16 + off:24 + off], rst[:, 16 + off:24 + off]),
                 reads=[t_rst], writes=[t_rst], ses=True)
            rbc = rst[:, 16 + off:24 + off].unsqueeze(2).broadcast_to((128, 8, 64))
            S.op("vector", lambda e, psu=psu, rbc=rbc: e.tensor_tensor(
                qtm[:, 0:512].rearrange("p (h d) -> p h d", h=8),
                psu.rearrange("p (h d) -> p h d", h=8), rbc, ALU.mult),
                 reads=[pstag, t_rst], writes=["qtm"])
            S.op("gpsimd", lambda e, qsb=qsb: e.tensor_mul(
                qtm[:, 512:1024], qtm[:, 0:512], qsb[:]),
                 reads=["qtm", "qsbc", "ksbc"], writes=["qtm"])
            S.op("gpsimd", lambda e, tb0=tb0, tbl=tbl: e.tensor_mul(
                qtm[:, 0:512], qtm[:, 512:1024],
                tbl[:, 512 * tb0:512 * (tb0 + 1)]),
                 reads=["qtm", t_tbl], writes=["qtm"])

            def swmul(e, tb1=tb1, tbl=tbl):
                qa = qtm[:, 512:1024].rearrange("p (h s d) -> p h s d", h=8, s=2)
                tblB = tbl[:, 512 * tb1:512 * (tb1 + 1)].rearrange(
                    "p (h s d) -> p h s d", h=8, s=2)
                spv = E["sp"][:, 0:512].rearrange("p (h s d) -> p h s d", h=8, s=2)
                e.tensor_tensor(spv[:, :, 0, :], qa[:, :, 1, :],
                                tblB[:, :, 0, :], ALU.mult)
                return e.tensor_tensor(spv[:, :, 1, :], qa[:, :, 0, :],
                                       tblB[:, :, 1, :], ALU.mult)
            S.op("gpsimd", swmul, reads=["qtm", t_tbl], writes=["fspA"])
            S.op("vector", lambda e, qtb=qtb: e.tensor_add(
                qtb, qtm[:, 0:512], E["sp"][:, 0:512]),
                 reads=["qtm", "fspA"], writes=[t_qtb])

            def trq(e, qtb=qtb):
                for hp in range(4):
                    inst = e.transpose(psTa[:, 128 * hp:128 * (hp + 1)],
                                       qtb[:, 128 * hp:128 * (hp + 1)],
                                       identb[:])
                return inst
            S.op("tensor", trq, reads=[t_qtb, "identb"], writes=["psTa"])

            def evq(e, tt=tt, dstfm=dstfm):
                return e.copy(
                    dstfm[:].rearrange("p (f t) -> p f t", f=4)[
                        :, :, 128 * tt:128 * (tt + 1)],
                    psTa[:, 0:512].rearrange("p (f t) -> p f t", f=4))
            S.op("scalar", evq, reads=["psTa"], writes=[dtag])

    qkv_mm(0)
    for tt in range(16):
        if getattr(B, "job_tick", None):
            B.job_tick()
        if tt + 1 < 16:
            qkv_mm(tt + 1)
        qkv_prep(tt)

    wob, osb = E["wob"], E["osb"]
    for w in range(NW):
        for h in range(HEADS):
            hp, hs = h // 2, (h % 2) * 64
            asl = h % 2
            attnT = E["attnT"][:, 4096 * asl:4096 * (asl + 1)]
            t_aT = f"fattnT{asl}"
            qh = qfm[:].rearrange("p (f t) -> p f t", f=4)[hs:hs + 64, hp, :]
            kh = kfm[:].rearrange("p (f t) -> p f t", f=4)[hs:hs + 64, hp, :]
            kb_lo = 4 if w == 0 else 0

            def qctx(qb):
                ssl = qb % 2
                return dict(
                    qb=qb, ssl=ssl,
                    psS=psS0 if ssl == 0 else psS1,
                    t_psS=f"fpsS{ssl}", t_psS2=f"fpsS{ssl}",
                    attn=E["attn"][:, 640 * ssl:640 * (ssl + 1)],
                    t_attn=f"fattn{ssl}",
                    nmx=E["nmx"][:, 4 * ssl:4 * (ssl + 1)],
                    t_nmx=f"nmx{ssl}",
                    kbs=[kb for kb in range(qb, qb + 5) if kb >= kb_lo],
                )

            def st_mms(cx, w=w, qh=qh, kh=kh):
                qb, kbs, psS = cx["qb"], cx["kbs"], cx["psS"]
                kb0 = kbs[0]

                nv = len(kbs)

                def mms(e, w=w, qb=qb, kbs=kbs, kb0=kb0, nv=nv,
                        qh=qh, kh=kh, psS=psS):
                    used_banks = set()
                    for kb in kbs:
                        koff = (w - 1) * 512 + 128 * kb
                        so = 128 * (kb - kb0)
                        bank = so // 512
                        st = bank not in used_banks
                        used_banks.add(bank)
                        e.matmul(
                            psS[:, so:so + 128],
                            qh[:, 512 * w + 128 * qb: 512 * w + 128 * (qb + 1)],
                            kh[:, koff:koff + 128], start=st, stop=True)
                    if kb0 == qb:
                        e.matmul(psS[:, 0:128], C["maskuTb"][:], identb[:],
                                 start=False, stop=True)
                    return e.matmul(psS[:, 128 * (nv - 1):128 * nv],
                                    C["masklTb"][:], identb[:],
                                    start=False, stop=True)
                S.op("tensor", mms, reads=["qfm", "kfm", "maskuTb",
                                           "masklTb", "identb"],
                     writes=[cx["t_psS"], cx["t_psS2"]])

            def st_maskmax(cx):
                kbs, psS, nmx = cx["kbs"], cx["psS"], cx["nmx"]
                nv = len(kbs)
                S.op("vector", lambda e, nv=nv, psS=psS, nmx=nmx: e.reduce_max(
                    nmx[:, 0:1], psS[:, 0:128 * nv], axis=AX.X, negate=True),
                     reads=[cx["t_psS"], cx["t_psS2"]], writes=[cx["t_nmx"]])

            def st_exp(cx):
                nv = len(cx["kbs"])
                psS, nmx, attn = cx["psS"], cx["nmx"], cx["attn"]
                S.op("scalar", lambda e, nv=nv, psS=psS, nmx=nmx, attn=attn: e.activation(
                    attn[:, 0:128 * nv], psS[:, 0:128 * nv], AF.Exp,
                    bias=nmx[:, 0:1], accum_out=nmx[:, 1:2]),
                     reads=[cx["t_psS"], cx["t_psS2"], cx["t_nmx"]],
                     writes=[cx["t_attn"], cx["t_nmx"]])

            def st_recip(cx):
                nmx = cx["nmx"]
                S.op("vector", lambda e, nmx=nmx: e.reciprocal(
                    nmx[:, 2:3], nmx[:, 1:2]),
                     reads=[cx["t_nmx"]], writes=[cx["t_nmx"]], ses=True)

            def st_scale(cx):
                nv = len(cx["kbs"])
                attn, nmx = cx["attn"], cx["nmx"]
                if cx["qb"] % 2 == 0:
                    S.op("scalar", lambda e, nv=nv, attn=attn, nmx=nmx:
                         e.activation(attn[:, 0:128 * nv], attn[:, 0:128 * nv],
                                      AF.Copy, scale=nmx[:, 2:3]),
                         reads=[cx["t_attn"], cx["t_nmx"]],
                         writes=[cx["t_attn"]])
                else:
                    S.op("vector", lambda e, nv=nv, attn=attn, nmx=nmx:
                         e.tensor_scalar(attn[:, 0:128 * nv],
                                         attn[:, 0:128 * nv],
                                         nmx[:, 2:3], None, ALU.mult),
                         reads=[cx["t_attn"], cx["t_nmx"]],
                         writes=[cx["t_attn"]])

            def st_mta(cx):
                kbs, attn = cx["kbs"], cx["attn"]
                kb0 = kbs[0]

                def mta(e, kbs=kbs, kb0=kb0, attn=attn):
                    for i, kb in enumerate(kbs):
                        so = 128 * (kb - kb0)
                        inst = e.transpose(psTa[:, 128 * i:128 * (i + 1)],
                                           attn[:, so:so + 128], identb[:])
                    return inst
                S.op("tensor", mta, reads=[cx["t_attn"], "identb"], writes=["psTa"])

            def st_eta(cx, attnT=attnT, t_aT=t_aT):
                kbs, qb = cx["kbs"], cx["qb"]
                engn = "vector" if qb % 2 == 0 else "scalar"

                def eta(e, kbs=kbs, qb=qb, attnT=attnT, engn=engn):
                    cp = e.tensor_copy if engn == "vector" else e.copy
                    for i, kb in enumerate(kbs):
                        inst = cp(
                            attnT[:, 512 * kb + 128 * qb:
                                  512 * kb + 128 * (qb + 1)],
                            psTa[:, 128 * i:128 * (i + 1)])
                    return inst
                S.op(engn, eta, reads=["psTa"], writes=[t_aT])

            cxs = [qctx(qb) for qb in range(4)]
            st_mms(cxs[0]); st_maskmax(cxs[0])
            st_mms(cxs[1])
            for k in range(4):
                if getattr(B, "job_tick", None):
                    B.job_tick()
                st_exp(cxs[k])
                if k + 1 < 4:
                    st_maskmax(cxs[k + 1])
                st_recip(cxs[k])
                st_scale(cxs[k])
                if k + 2 < 4:
                    st_mms(cxs[k + 2])
                st_mta(cxs[k])
                st_eta(cxs[k])

            def mav(e, w=w, h=h, kb_lo=kb_lo, attnT=attnT):
                started = False
                for kb in range(kb_lo, 8):
                    qmin = max(0, kb - 4)
                    qmax = min(3, kb)
                    if qmin > qmax:
                        continue
                    kt = 4 * (w - 1) + kb
                    lhsT = vsb[:, 512 * kt + 64 * h: 512 * kt + 64 * (h + 1)]
                    rhs = attnT[:, 512 * kb + 128 * qmin:
                                512 * kb + 128 * (qmax + 1)]
                    inst = e.matmul(psV[0:64, 128 * qmin:128 * (qmax + 1)],
                                    lhsT, rhs, start=(not started), stop=True)
                    started = True
                return inst
            S.op("tensor", mav, reads=["vsb", t_aT], writes=["psV"])
            S.op("scalar", lambda e, w=w, hp=hp, hs=hs: e.copy(
                osb[hs:hs + 64, 2048 * hp + 512 * w: 2048 * hp + 512 * (w + 1)],
                psV[0:64, 0:512]),
                 reads=["psV"], writes=["osb"])

        def wo_in(mg, w=w):
            sl2 = E["slab2"][:, 512 * (mg % 4):512 * (mg % 4 + 1)]
            S.op("sync", lambda e, w=w, mg=mg, sl2=sl2: e.dma_start(
                out=sl2, in_=xf_r[:, mg, 512 * w:512 * (w + 1)]),
                reads=["xfscr", f"xfr{mg}"], writes=[f"fslab2{mg % 4}"],
                dma=True, sem=f"dsl{mg % 4}")
        for mg in range(3):
            wo_in(mg)
        for mg in range(8):
            if getattr(B, "job_tick", None):
                B.job_tick()
            msl = mg % 2
            psu = psQ[:] if msl == 0 else psK[:]
            pstag = "psQ" if msl == 0 else "psK"
            sl2 = E["slab2"][:, 512 * (mg % 4):512 * (mg % 4 + 1)]
            t_sl2 = f"fslab2{mg % 4}"

            def mmo(e, w=w, mg=mg, psu=psu):
                for cc in range(4):
                    lhsT = wob[:, 1024 * cc + 128 * mg: 1024 * cc + 128 * (mg + 1)]
                    rhs = osb[:, 2048 * cc + 512 * w: 2048 * cc + 512 * (w + 1)]
                    inst = e.matmul(psu, lhsT, rhs,
                                    start=(cc == 0), stop=(cc == 3))
                return inst
            S.op("tensor", mmo, reads=["wob", "osb"], writes=[pstag])
            S.op("vector", lambda e, psu=psu, sl2=sl2: e.tensor_add(
                sl2, sl2, psu),
                 reads=[t_sl2, pstag], writes=[t_sl2])
            S.op("scalar", lambda e, w=w, mg=mg, sl2=sl2: e.dma_start(
                out=xf_r[:, mg, 512 * w:512 * (w + 1)], in_=sl2),
                reads=[t_sl2], writes=[f"xfr{mg}"], dma=True,
                sem=f"dsl{mg % 4}")
            if mg + 3 < 8:
                wo_in(mg + 3)

    B.dump(f"attnout_l{l}", E["osb"][:], (128, 4 * L), BF16, tags=["osb"])
    B.dump(f"wqkvb_l{l}", E["wqkvb"][:], (128, 8 * 1536), BF16, tags=["wqkvb"])
    if l == 0 and "w1pb_l0" in B.dbg:
        B.dump("w1pb_l0", B.dram["w1pb0"].ap(), (D, NPAIR * 256), BF16, tags=[])
    B.dump(f"hb_l{l}", E["hb"][:], (128, 8 * L), BF16, tags=["hbf"])
    B.dump(f"qfm_l{l}", E["qfm"][:], (128, 4 * L), BF16, tags=["qfm"])
    B.dump(f"kfm_l{l}", E["kfm"][:], (128, 4 * L), BF16, tags=["kfm"])
    B.dump(f"vsb_l{l}", E["vsb"][:], (128, 16 * 512), BF16, tags=["vsb"])
    B.dump(f"xfattn_l{l}", B.dram["xfscr"].ap(), (D, L), F32, tags=XFALL)


def _ffn_mlp(B, stk, l):
    nc, S = B.nc, B.S
    dr = B.dram
    C = B.common
    def sb(stk_, name, shape, dt=F32):
        return B.sb(stk_, f"{name}_{l}", shape, dt)
    def ps(stk_, name, shape, dt=F32):
        return B.ps(stk_, f"{name}_{l}", shape, dt)
    E = {}
    E["slab"] = sb(stk, "g_slab", [128, 2 * L], F32)
    E["slab2"] = sb(stk, "g_slab2", [128, 4 * 512], F32)
    E["shb"] = sb(stk, "g_shb", [128, L], BF16)
    E["srow"] = sb(stk, "g_srow", [1, L], F32)
    E["srow2"] = sb(stk, "g_srow2", [1, L], F32)
    E["fmu2"] = sb(stk, "g_fmu2", [1, L], F32)
    E["rwst"] = sb(stk, "g_rwst", [64, 2 * L], BF16)
    E["fmu2"] = E["rwst"][0:1, 0:2 * L].bitcast(F32)
    E["smc"] = sb(stk, "g_smc", [128, 68], F32)
    E["ln2h"] = sb(stk, "g_ln2h", [128, 8 * L], BF16)
    E["sp"] = sb(stk, "g_sp", [128, 2 * 512], F32)
    E["w1a"] = sb(stk, "g_w1a", [128, 2048], BF16)
    E["w1b"] = sb(stk, "g_w1b", [128, 2048], BF16)
    E["gga"] = sb(stk, "g_gga", [128, 2 * 512], F32)
    E["ggb"] = sb(stk, "g_ggb", [128, 2 * 512], F32)
    E["gg"] = sb(stk, "g_gg", [128, 22 * 1024], BF16)
    E["w2b"] = sb(stk, "g_w2b", [128, 22 * 1024], BF16)
    psA = ps(stk, "g_psA", [128, 512], F32)
    psB = ps(stk, "g_psB", [128, 512], F32)
    psC = ps(stk, "g_psC", [128, 512], F32)
    psD = ps(stk, "g_psD", [128, 512], F32)
    psq4 = [psA, psB, psC, psD]
    pstags = ["fpsA", "fpsB", "fpsC", "fpsD"]

    xfscr = dr["xfscr"]
    xf_r = xfscr.ap().rearrange("(g p) t -> p g t", p=128)
    smc = E["smc"]
    onesb = C["onesb"]
    ln2h = E["ln2h"]
    ln2h_g = ln2h[:].rearrange("p (g t) -> p g t", g=8)
    gg = E["gg"]

    def load(dst, srcap, tag, rtags=()):
        S.op("sync", lambda e, dst=dst, srcap=srcap: e.dma_start(
            out=dst, in_=srcap), reads=list(rtags), writes=[tag], dma=True)

    load(smc[:, 0:44], dr[f"b1p{l}"].ap(), "smc")
    load(smc[:, 44:52], dr[f"b2{l}"].ap(), "smc")
    load(smc[:, 52:60], dr[f"lng{l}"].ap(), "smc")
    load(smc[:, 60:68], dr[f"lnb{l}"].ap(), "smc")

    for dg in range(8):
        if getattr(B, "job_tick", None):
            B.job_tick()
        sl = dg % 2
        slab = E["slab"][:, L * sl:L * (sl + 1)]
        shb = E["shb"][:]
        t_sl = f"fslab{sl}"
        S.op("sync", lambda e, dg=dg, slab=slab: e.dma_start(
                 out=slab, in_=xf_r[:, dg, :]),
             reads=XFALL, writes=[t_sl], dma=True, sem=f"dw{sl}")
        if dg < 4:
            S.op("scalar", lambda e, slab=slab, shb=shb: e.activation(
                shb, slab, AF.Square),
                 reads=[t_sl], writes=["fshb"])
        else:
            S.op("vector", lambda e, shb=shb: e.memset(
                shb[:, 0:1], 0.0), writes=["fshb"])
            S.op("scalar", lambda e, slab=slab, shb=shb: e.activation(
                shb[:, 1:L], slab[:, 0:L - 1], AF.Square),
                 reads=[t_sl], writes=["fshb"])

        def mmsum(e, dg=dg, slab=slab, shb=shb):
            sh = 1 if dg >= 4 else 0
            for ch in range(4):
                if sh and ch == 0:
                    e.matmul(psq4[0][0:1, 1:512], C["onesf"][:, 0:1],
                             slab[:, 0:511], start=False, stop=(dg == 7))
                else:
                    e.matmul(psq4[ch][0:1, 0:512], C["onesf"][:, 0:1],
                             slab[:, 512 * ch - sh:512 * (ch + 1) - sh],
                             start=(dg == 0), stop=(dg == 7))
                inst = e.matmul(psq4[ch][32:33, 0:512], onesb[:, 0:1],
                                shb[:, 512 * ch:512 * (ch + 1)],
                                start=(dg == 0), stop=(dg == 7))
            return inst
        S.op("tensor", mmsum, reads=[t_sl, "fshb", "onesb"], writes=pstags)
    for ch in range(4):
        S.op("scalar", lambda e, ch=ch: e.copy(
            E["srow"][0:1, 512 * ch:512 * (ch + 1)], psq4[ch][0:1, 0:512]),
             reads=pstags, writes=["fsrow"])
        S.op("scalar", lambda e, ch=ch: e.copy(
            E["srow2"][0:1, 512 * ch:512 * (ch + 1)], psq4[ch][32:33, 0:512]),
             reads=pstags, writes=["fsrow"])

    B.dump(f"srow_l{l}", E["srow"][0:1, :], (1, L), F32, tags=["fsrow"])
    B.dump(f"srow2_l{l}", E["srow2"][0:1, :], (1, L), F32, tags=["fsrow"])
    S.op("vector", lambda e: e.tensor_scalar_mul(
        E["srow"][0:1, :], E["srow"][0:1, :], 1.0 / D),
         reads=["fsrow"], writes=["fsrow"], ses=True)
    S.op("vector", lambda e: e.tensor_scalar_mul(
        E["srow2"][0:1, :], E["srow2"][0:1, :], 1.0 / D),
         reads=["fsrow"], writes=["fsrow"], ses=True)
    S.op("vector", lambda e: e.tensor_tensor(
        E["fmu2"], E["srow"][0:1, :], E["srow"][0:1, :], ALU.mult),
         reads=["fsrow"], writes=["frwst"], ses=True)
    S.op("vector", lambda e: e.tensor_sub(
        E["srow2"][0:1, :], E["srow2"][0:1, :], E["fmu2"]),
         reads=["fsrow", "frwst"], writes=["fsrow"], ses=True)
    S.op("scalar", lambda e: e.activation(
        E["srow2"][0:1, :], E["srow2"][0:1, :], AF.Sqrt, bias=C["cst"][0:1, 0:1]),
         reads=["fsrow", "cst"], writes=["fsrow"])
    S.op("vector", lambda e: e.reciprocal(E["srow2"][0:1, :], E["srow2"][0:1, :]),
         reads=["fsrow"], writes=["fsrow"], ses=True)
    S.op("vector", lambda e: e.scalar_tensor_tensor(
        E["srow"][0:1, :], E["srow"][0:1, :], -1.0, E["srow2"][0:1, :],
        ALU.mult, ALU.mult),
         reads=["fsrow"], writes=["fsrow"], ses=True)
    S.op("vector", lambda e: e.memset(E["rwst"][:], 0.0), writes=["frwst"])

    def hilo(e):
        e.tensor_copy(E["rwst"][0:1, 0:L], E["srow2"][0:1, :])
        e.tensor_tensor(E["rwst"][32:33, 0:L], E["srow2"][0:1, :],
                        E["rwst"][0:1, 0:L], ALU.subtract)
        e.tensor_copy(E["rwst"][0:1, L:2 * L], E["srow"][0:1, :])
        return e.tensor_tensor(E["rwst"][32:33, L:2 * L], E["srow"][0:1, :],
                               E["rwst"][0:1, L:2 * L], ALU.subtract)
    S.op("vector", hilo, reads=["fsrow"], writes=["frwst"], ses=True)

    for dg in range(8):
        if getattr(B, "job_tick", None):
            B.job_tick()
        sl = dg % 2
        slab = E["slab"][:, L * sl:L * (sl + 1)]
        t_sl = f"fslab{sl}"
        S.op("sync", lambda e, dg=dg, slab=slab: e.dma_start(
                 out=slab, in_=xf_r[:, dg, :]),
             reads=XFALL, writes=[t_sl], dma=True, sem=f"dw{sl}")
        for ch in range(4):
            csl = ch % 2
            psR = psA if csl == 0 else psC
            psN = psB if csl == 0 else psD
            tR = "fpsA" if csl == 0 else "fpsC"
            tN = "fpsB" if csl == 0 else "fpsD"
            spx = E["sp"][:, 0:512] if csl == 0 else E["gga"][:, 512:1024]
            tspx = "fspA" if csl == 0 else "fggax"
            spy = E["sp"][:, 512:1024] if csl == 0 else E["ggb"][:, 512:1024]
            tspy = "fspB" if csl == 0 else "fggbx"

            def bmr(e, ch=ch, psR=psR):
                return e.matmul(psR[:, 0:512], C["lnsel"][:],
                                E["rwst"][:, 512 * ch:512 * (ch + 1)],
                                start=True, stop=True)
            S.op("tensor", bmr, reads=["frwst", "lnsel"], writes=[tR])

            def bmn(e, ch=ch, psN=psN):
                return e.matmul(psN[:, 0:512], C["lnsel"][:],
                                E["rwst"][:, L + 512 * ch:L + 512 * (ch + 1)],
                                start=True, stop=True)
            S.op("tensor", bmn, reads=["frwst", "lnsel"], writes=[tN])
            if dg < 4:
                S.op("vector", lambda e, ch=ch, slab=slab, psR=psR, spx=spx: e.tensor_mul(
                    spx, slab[:, 512 * ch:512 * (ch + 1)], psR[:, 0:512]),
                     reads=[t_sl, tR], writes=[tspx])
            elif ch == 0:
                S.op("vector", lambda e, spx=spx: e.memset(spx[:, 0:1], 0.0),
                     writes=[tspx])
                S.op("vector", lambda e, slab=slab, psR=psR, spx=spx: e.tensor_mul(
                    spx[:, 1:512], slab[:, 0:511], psR[:, 1:512]),
                     reads=[t_sl, tR], writes=[tspx])
            else:
                S.op("vector", lambda e, ch=ch, slab=slab, psR=psR, spx=spx: e.tensor_mul(
                    spx, slab[:, 512 * ch - 1:512 * (ch + 1) - 1],
                    psR[:, 0:512]),
                     reads=[t_sl, tR], writes=[tspx])
            S.op("vector", lambda e, spx=spx, spy=spy, psN=psN: e.tensor_add(
                spy, spx, psN[:, 0:512]),
                 reads=[tspx, tN], writes=[tspy])
            S.op("vector", lambda e, dg=dg, ch=ch, spy=spy: e.tensor_scalar(
                ln2h_g[:, dg, 512 * ch:512 * (ch + 1)], spy,
                smc[:, 52 + dg:53 + dg], smc[:, 60 + dg:61 + dg],
                ALU.mult, ALU.add),
                 reads=[tspy, "smc"], writes=["fln2h"])

    for half in range(2):
        rows = slice(128 * 11 * half, 128 * 11 * (half + 1))
        cols = slice(11264 * half, 11264 * (half + 1))
        src_ap = dr[f"w2pb{l}"].ap()[rows, :].rearrange("(j p) m -> p j m", p=128)
        dst_ap = E["w2b"][:, cols].rearrange("p (j m) -> p j m", j=11)
        S.op("sync", lambda e, src_ap=src_ap, dst_ap=dst_ap: e.dma_start(
            out=dst_ap, in_=src_ap),
            reads=["fw2b"], writes=["fw2b"], dma=True)

    for th in range(2):
        for j in range(NPAIR):
            if getattr(B, "job_tick", None):
                B.job_tick()
            wslot = E["w1a"] if j % 2 == 0 else E["w1b"]
            wstag = f"fw1a{j % 2}"
            src_ap = dr[f"w1pb{l}"].ap().rearrange("(c p) m -> p c m", p=128)[
                :, :, 256 * j:256 * (j + 1)]
            w1rd = [wstag] + ([f"fslab{j % 2}"] if j < 2 else [])
            S.op("sync",
                 lambda e, src_ap=src_ap, wslot=wslot: e.dma_start(
                     out=wslot[:].rearrange("p (c m) -> p c m", c=8),
                in_=src_ap),
                reads=w1rd, writes=[wstag], dma=True, sem=f"dw{j % 2}")
            for tc2 in range(2):
                tc = 2 * th + tc2
                psA_ = psA if tc2 % 2 == 0 else psC
                psB_ = psB if tc2 % 2 == 0 else psD
                tA_ = "fpsA" if tc2 % 2 == 0 else "fpsC"
                tB_ = "fpsB" if tc2 % 2 == 0 else "fpsD"

                def mm1(e, tc=tc, wslot=wslot, psA_=psA_, psB_=psB_):
                    for cc in range(8):
                        a_t = wslot[:, 256 * cc + 0:256 * cc + 128]
                        g_t = wslot[:, 256 * cc + 128:256 * cc + 256]
                        e.matmul(psA_[:, 0:512], a_t,
                                 ln2h_g[:, cc, 512 * tc:512 * (tc + 1)],
                                 start=(cc == 0), stop=(cc == 7))
                        inst = e.matmul(psB_[:, 0:512], g_t,
                                        ln2h_g[:, cc, 512 * tc:512 * (tc + 1)],
                                        start=(cc == 0), stop=(cc == 7))
                    return inst
                S.op("tensor", mm1, reads=[wstag, "fln2h"], writes=[tA_, tB_])
                gsl = tc2 % 2
                gga = E["gga"][:, 512 * gsl:512 * (gsl + 1)]
                ggb = E["ggb"][:, 512 * gsl:512 * (gsl + 1)]
                t_ga, t_gb = f"fgga{gsl}", f"fggb{gsl}"
                S.op("scalar", lambda e, j=j, gga=gga, psA_=psA_: e.add(
                    gga, psA_[:, 0:512], smc[:, 2 * j:2 * j + 1]),
                     reads=[tA_, "smc"], writes=[t_ga])
                S.op("scalar", lambda e, j=j, ggb=ggb, psB_=psB_: e.activation(
                    ggb, psB_[:, 0:512], AF.Gelu,
                    bias=smc[:, 2 * j + 1:2 * j + 2]),
                     reads=[tB_, "smc"], writes=[t_gb])
                S.op("vector", lambda e, j=j, tc2=tc2, gga=gga, ggb=ggb: e.tensor_mul(
                    gg[:, 1024 * j + 512 * tc2: 1024 * j + 512 * (tc2 + 1)],
                    gga, ggb),
                     reads=[t_ga, t_gb], writes=["fgg"])

        for tc2 in range(2):
            tc = 2 * th + tc2

            def r_in(mg, tc=tc):
                sl2 = E["slab2"][:, 512 * (mg % 4):512 * (mg % 4 + 1)]
                S.op("sync", lambda e, tc=tc, mg=mg, sl2=sl2: e.dma_start(
                    out=sl2, in_=xf_r[:, mg, 512 * tc:512 * (tc + 1)]),
                    reads=["xfscr", f"xfr{mg}"], writes=[f"fslab2{mg % 4}"],
                    dma=True, sem=f"dsl{mg % 4}")
            for mg in range(3):
                r_in(mg)
            for mg in range(8):
                if getattr(B, "job_tick", None):
                    B.job_tick()
                psu = [psA, psB, psC, psD][mg % 4]
                pstag = ["fpsA", "fpsB", "fpsC", "fpsD"][mg % 4]
                sl2 = E["slab2"][:, 512 * (mg % 4):512 * (mg % 4 + 1)]
                t_sl2 = f"fslab2{mg % 4}"

                def mm2(e, tc2=tc2, mg=mg, psu=psu):
                    for j in range(NPAIR):
                        lhsT = E["w2b"][:, 1024 * j + 128 * mg:
                                        1024 * j + 128 * (mg + 1)]
                        rhs = gg[:, 1024 * j + 512 * tc2: 1024 * j + 512 * (tc2 + 1)]
                        inst = e.matmul(psu[:, 0:512], lhsT, rhs,
                                        start=(j == 0), stop=(j == NPAIR - 1))
                    return inst
                S.op("tensor", mm2, reads=["fw2b", "fgg"], writes=[pstag])
                S.op("vector", lambda e, mg=mg, psu=psu, sl2=sl2: e.scalar_tensor_tensor(
                    sl2, psu[:, 0:512], smc[:, 44 + mg:45 + mg],
                    sl2, ALU.add, ALU.add),
                     reads=[t_sl2, pstag, "smc"], writes=[t_sl2])
                S.op("scalar", lambda e, tc=tc, mg=mg, sl2=sl2: e.dma_start(
                    out=xf_r[:, mg, 512 * tc:512 * (tc + 1)], in_=sl2),
                    reads=[t_sl2], writes=[f"xfr{mg}"], dma=True,
                    sem=f"dsl{mg % 4}")
                if mg + 3 < 8:
                    r_in(mg + 3)
    B.dump(f"ln2h_l{l}", E["ln2h"][:], (128, 8 * L), BF16, tags=["fln2h"])
    B.dump(f"xfmlp_l{l}", B.dram["xfscr"].ap(), (D, L), F32, tags=XFALL)


def build_final(B, out):
    from contextlib import ExitStack
    nc, S = B.nc, B.S
    dr = B.dram
    C = B.common
    onesb = C["onesb"]
    with ExitStack() as stk:
        sb, ps = B.sb, B.ps
        xfc = sb(stk, "z_xfc", [128, 8 * 512], F32)
        x1c = sb(stk, "z_x1c", [128, 8 * 512], F32)
        hbb = sb(stk, "z_hbb", [128, 8 * 512], BF16)
        srow = sb(stk, "z_srow", [1, 512], F32)
        srow2 = sb(stk, "z_srow2", [1, 512], F32)
        mu2 = sb(stk, "z_mu2", [1, 512], F32)
        rwst = sb(stk, "z_rwst", [64, 2 * 512], BF16)
        sp = sb(stk, "z_sp", [128, 512], F32)
        ofm = sb(stk, "z_ofm", [128, 8 * 512], F32)
        otm = sb(stk, "z_otm", [128, 1024], F32)
        psb0 = ps(stk, "z_psb0", [128, 512], F32)
        psb1 = ps(stk, "z_psb1", [128, 512], F32)
        pst = ps(stk, "z_pst", [128, 512], F32)
        xf_r = dr["xfscr"].ap().rearrange("(g p) t -> p g t", p=128)
        x1_r = dr["x1scr"].ap().rearrange("(g p) t -> p g t", p=128)
        xf_g = xfc[:].rearrange("p (g t) -> p g t", g=8)
        x1_g = x1c[:].rearrange("p (g t) -> p g t", g=8)
        o_g = ofm[:].rearrange("p (g t) -> p g t", g=8)
        hb_g = hbb[:].rearrange("p (g t) -> p g t", g=8)
        T = 512
        for c in range(4):
            S.op("sync", lambda e, c=c: e.dma_start(
                out=xf_g, in_=xf_r[:, :, c * T:(c + 1) * T]),
                reads=XFALL, writes=["zxf"], dma=True)
            S.op("sync", lambda e, c=c: e.dma_start(
                out=x1_g, in_=x1_r[:, :, c * T:(c + 1) * T]),
                reads=["x1scr"], writes=["zx1"], dma=True)
            for pas in range(2):
                func = AF.Copy if pas == 0 else AF.Square
                dstrow = srow if pas == 0 else srow2
                S.op("scalar", lambda e, func=func: e.activation(
                    hbb[:], xfc[:], func), reads=["zxf"], writes=["zhb"])

                def mmsum(e):
                    for cc in range(8):
                        inst = e.matmul(psb0[0:1, 0:T], onesb[:, 0:1],
                                        hb_g[:, cc, :],
                                        start=(cc == 0), stop=(cc == 7))
                    return inst
                S.op("tensor", mmsum, reads=["zhb", "onesb"], writes=["zps0"])
                S.op("scalar", lambda e, dstrow=dstrow: e.copy(
                    dstrow[0:1, :], psb0[0:1, 0:T]),
                     reads=["zps0"], writes=["zrow"])
            S.op("vector", lambda e: e.tensor_scalar_mul(
                srow[0:1, :], srow[0:1, :], 1.0 / D),
                 reads=["zrow"], writes=["zrow"], ses=True)
            S.op("vector", lambda e: e.tensor_scalar_mul(
                srow2[0:1, :], srow2[0:1, :], 1.0 / D),
                 reads=["zrow"], writes=["zrow"], ses=True)
            S.op("vector", lambda e: e.tensor_tensor(
                mu2[0:1, :], srow[0:1, :], srow[0:1, :], ALU.mult),
                 reads=["zrow"], writes=["zmu2"], ses=True)
            S.op("vector", lambda e: e.tensor_sub(
                srow2[0:1, :], srow2[0:1, :], mu2[0:1, :]),
                 reads=["zrow", "zmu2"], writes=["zrow"], ses=True)
            S.op("scalar", lambda e: e.activation(
                srow2[0:1, :], srow2[0:1, :], AF.Sqrt, bias=C["cst"][0:1, 0:1]),
                 reads=["zrow", "cst"], writes=["zrow"])
            S.op("vector", lambda e: e.reciprocal(srow2[0:1, :], srow2[0:1, :]),
                 reads=["zrow"], writes=["zrow"], ses=True)
            S.op("vector", lambda e: e.scalar_tensor_tensor(
                srow[0:1, :], srow[0:1, :], -1.0, srow2[0:1, :],
                ALU.mult, ALU.mult),
                 reads=["zrow"], writes=["zrow"], ses=True)
            S.op("vector", lambda e: e.memset(rwst[:], 0.0), writes=["zrw"])

            def hilo(e):
                e.tensor_copy(rwst[0:1, 0:T], srow2[0:1, :])
                e.tensor_tensor(rwst[32:33, 0:T], srow2[0:1, :],
                                rwst[0:1, 0:T], ALU.subtract)
                e.tensor_copy(rwst[0:1, T:2 * T], srow[0:1, :])
                return e.tensor_tensor(rwst[32:33, T:2 * T], srow[0:1, :],
                                       rwst[0:1, T:2 * T], ALU.subtract)
            S.op("vector", hilo, reads=["zrow"], writes=["zrw"], ses=True)
            S.op("tensor", lambda e: e.matmul(
                psb0[:, 0:T], C["lnsel"][:], rwst[:, 0:T], start=True, stop=True),
                 reads=["zrw", "lnsel"], writes=["zps0"])
            S.op("tensor", lambda e: e.matmul(
                psb1[:, 0:T], C["lnsel"][:], rwst[:, T:2 * T], start=True, stop=True),
                 reads=["zrw", "lnsel"], writes=["zps1"])
            for dg in range(8):
                S.op("vector", lambda e, dg=dg: e.tensor_mul(
                    sp[:], xf_g[:, dg, :], psb0[:, 0:T]),
                     reads=["zxf", "zps0"], writes=["zsp"])
                S.op("vector", lambda e, dg=dg: e.tensor_add(
                    sp[:], sp[:], psb1[:, 0:T]),
                     reads=["zsp", "zps1"], writes=["zsp"])
                S.op("vector", lambda e, dg=dg: e.tensor_add(
                    o_g[:, dg, :], sp[:], x1_g[:, dg, :]),
                     reads=["zsp", "zx1"], writes=["zofm"])
            for q in range(4):
                for dgp in range(4):
                    def trx(e, q=q, dgp=dgp):
                        e.transpose(pst[:, 0:128],
                                    o_g[:, 2 * dgp, 128 * q:128 * (q + 1)],
                                    C["identf"][:])
                        return e.transpose(pst[:, 128:256],
                                           o_g[:, 2 * dgp + 1, 128 * q:128 * (q + 1)],
                                           C["identf"][:])
                    S.op("tensor", trx, reads=["zofm", "identf"], writes=["zpst"])

                    def evx(e, dgp=dgp):
                        return e.tensor_copy(otm[:, 256 * dgp:256 * (dgp + 1)],
                                             pst[:, 0:256])
                    S.op("vector", evx, reads=["zpst"], writes=["zotm"])
                S.op("sync", lambda e, c=c, q=q: e.dma_start(
                    out=out.ap()[512 * c + 128 * q:512 * c + 128 * (q + 1), :],
                    in_=otm[:]),
                    reads=["zotm"], writes=["zout"], dma=True)
    S.barrier()



def _prep_inputs(inputs):
    f32 = np.float32
    asc = np.ascontiguousarray
    rqa, rqb, rka, rkb = _rotary_tables()
    maskL, maskU = _band_masks()
    bsel = np.zeros((32, 32 * 128), dtype=BF16NP)
    for s in range(32):
        bsel[s, 128 * s:128 * (s + 1)] = 1
    lnsel = np.zeros((64, 128), dtype=BF16NP)
    lnsel[0, :] = 1
    lnsel[32, :] = 1
    common = {
        "lnsel": lnsel,
        "masklTb": np.ascontiguousarray(
            np.where(maskL > 0, 0.0, -1e30).T.astype(BF16NP)),
        "maskuTb": np.ascontiguousarray(
            np.where(maskU > 0, 0.0, -1e30).T.astype(BF16NP)),
        "bsel": bsel,
        "rqa": rqa, "rqb": rqb, "rka": rka, "rkb": rkb,
        "identb": np.eye(128, dtype=BF16NP),
        "identf": np.eye(128, dtype=f32),
        "jflipb": np.fliplr(np.eye(128)).astype(BF16NP),
        "maskl": maskL, "masku": maskU,
    }
    for d in (1, 2):
        common[f"inwT{d}"] = asc(inputs[f"L0_attn_in_w{d}"].T)
        common[f"xprojT{d}"] = asc(inputs[f"L0_attn_xproj_w{d}"].T)
        common[f"dtwT{d}"] = asc(inputs[f"L0_attn_dt_w{d}"].T)
        common[f"outwT{d}"] = asc(inputs[f"L0_attn_out_w{d}"].T)
        common[f"convw{d}"] = _pg(inputs[f"L0_attn_conv_w{d}"])
        common[f"convb{d}"] = _pg(inputs[f"L0_attn_conv_b{d}"])
        common[f"dtb{d}"] = _pg(inputs[f"L0_attn_dt_b{d}"])
        common[f"dd{d}"] = _pg(inputs[f"L0_attn_D{d}"])
        common[f"alog{d}"] = _pg(inputs[f"L0_attn_A_log{d}"])
    for l in (0, 1):
        common[f"wqkv{l}"] = asc(inputs["L0_ffn_wqkv"][l])
        common[f"wo{l}"] = asc(inputs["L0_ffn_wo"][l])
        common[f"w1p{l}"] = _w1_perm_cols(inputs["L0_ffn_w1"][l])
        common[f"b1p{l}"] = _b1_perm(inputs["L0_ffn_b1"][l])
        common[f"w2p{l}"] = _w2_perm_rows(inputs["L0_ffn_w2"][l])
        common[f"lng{l}"] = _pg(inputs["L0_ffn_ln_g"][l])
        common[f"lnb{l}"] = _pg(inputs["L0_ffn_ln_b"][l])
        common[f"b2{l}"] = _pg(inputs["L0_ffn_b2"][l])
        common[f"qsrow{l}"] = asc(np.tile(inputs["L0_ffn_q_scale"][l], HEADS)[None, :])
        common[f"ksrow{l}"] = asc(np.tile(inputs["L0_ffn_k_scale"][l], HEADS)[None, :])

    x = inputs["L0_attn_x"]
    in_maps = []
    for i in range(x.shape[0]):
        m = dict(common)
        m["x"] = asc(x[i])
        in_maps.append(m)
    return in_maps


def kernel(**inputs):
    in_maps = _prep_inputs(inputs)
    nc, _ = build_program(dbg=DEBUG_REQUEST)
    res = run_bass_kernel_spmd(nc, in_maps, core_ids=list(range(len(in_maps))))
    outs = [np.asarray(r["out"], dtype=np.float32) for r in res.results]
    if DEBUG_REQUEST:
        kernel.debug_results = res.results
    return np.stack(outs, axis=0)


# version 25
# speedup vs baseline: 1.2775x; 1.0090x over previous
import numpy as np
import ml_dtypes
import concourse.bass as bass
import concourse.mybir as mybir
from concourse.bass_utils import run_bass_kernel_spmd

F32 = mybir.dt.float32
BF16 = mybir.dt.bfloat16
AF = mybir.ActivationFunctionType
ALU = mybir.AluOpType
AX = mybir.AxisListType
BF16NP = ml_dtypes.bfloat16

L = 2048
D = 1024
DM = 512
DI = 1024
DS = 16
DTR = 32
TCH = 512
NCH = L // TCH
HEADS = 8
DH = 64
WIN = 512
NW = L // WIN
HID = 2730
NPAIR = 22
HIDP = NPAIR * 128
XFALL = ["xfscr"] + [f"xfr{m}" for m in range(8)]
EPS = 1e-5

DEBUG_REQUEST = []
STRICT_SAME_ENGINE = [True]
ENGINES = ("tensor", "vector", "scalar", "gpsimd", "sync")


class Sched:
    def __init__(self):
        self.units = []
        self.cnt = {e: 0 for e in ENGINES}
        self.last_w = {}
        self.readers = {}

    def op(self, engine, emit, reads=(), writes=(), dma=False, ses=False,
           sem=None):
        assert engine in ENGINES
        key = sem if sem is not None else engine
        inc = 16 if dma else 1
        waits = {}
        strict = ses or STRICT_SAME_ENGINE[0] or engine != "vector"

        def need(eng2, cnt2):
            if eng2 == engine and engine != "sync" and not strict:
                return
            if waits.get(eng2, 0) < cnt2:
                waits[eng2] = cnt2

        for t in reads:
            w = self.last_w.get(t)
            if w is not None:
                need(*w)
        for t in writes:
            w = self.last_w.get(t)
            if w is not None:
                need(*w)
            for eng2, cnt2 in self.readers.get(t, {}).items():
                need(eng2, cnt2)

        self.cnt[key] = self.cnt.get(key, 0) + inc
        me = (key, self.cnt[key])
        for t in reads:
            self.readers.setdefault(t, {})[key] = self.cnt[key]
        for t in writes:
            self.last_w[t] = me
            self.readers[t] = {}
        self.units.append((engine, emit, inc, waits, key))

    def barrier(self):
        snapshot = dict(self.cnt)
        for e in ENGINES:
            waits = {e2: c for e2, c in snapshot.items()
                     if (e2 != e or e == "sync") and c > 0}
            self.cnt[e] += 1
            self.units.append((e, None, 1, waits, e))
        self.last_w = {}
        self.readers = {}

    def finalize(self, block, sems):
        per_engine = {e: [] for e in ENGINES}
        for engine, emit, inc, waits, key in self.units:
            per_engine[engine].append((emit, inc, waits, key))
        final_cnt = dict(self.cnt)

        def body_for(engname):
            def body(eng):
                watermark = {e: 0 for e in final_cnt}
                for emit, inc, waits, key in per_engine[engname]:
                    for e2, c2 in sorted(waits.items()):
                        if watermark[e2] < c2:
                            eng.wait_ge(sems[e2], c2)
                            watermark[e2] = c2
                    inst = eng.nop() if emit is None else emit(eng)
                    inst.then_inc(sems[key], inc)
                if engname == "sync":
                    for e2, c2 in final_cnt.items():
                        eng.wait_ge(sems[e2], c2)
            return body

        block.sync(body_for("sync"))
        block.tensor(body_for("tensor"))
        block.vector(body_for("vector"))
        block.scalar(body_for("scalar"))
        block.gpsimd(body_for("gpsimd"))



def _rotary_tables():
    t = np.arange(L, dtype=np.float64)
    half = np.arange(0, DH, 2, dtype=np.float64)
    inv_freq = 1.0 / (10000.0 ** (half / DH))
    freqs = t[:, None] * inv_freq[None, :]
    freqs = np.concatenate([freqs, freqs], axis=1)
    base = (half + 0.4 * DH) / (1.4 * DH)
    power = (t - L // 2) / float(WIN // 2)
    scale = base[None, :] ** power[:, None]
    scale = np.concatenate([scale, scale], axis=1)
    cos = np.cos(freqs)
    sin = np.sin(freqs)
    sign = np.ones((DH,))
    sign[:DH // 2] = -1.0
    qa = cos * scale * 8.0
    qb = sign[None, :] * sin * scale * 8.0
    ka = cos / scale
    kb = sign[None, :] * sin / scale

    def rep(x):
        return np.ascontiguousarray(np.tile(x.astype(np.float32), (1, HEADS)))
    return rep(qa), rep(qb), rep(ka), rep(kb)


def _band_masks():
    i = np.arange(128)[:, None]
    j = np.arange(128)[None, :]
    maskL = (j <= i).astype(BF16NP)
    maskU = (j >= i).astype(BF16NP)
    return maskL, maskU


def _pg(x):
    x = np.asarray(x)
    if x.ndim == 1:
        x = x[:, None]
    k = x.shape[1]
    return np.ascontiguousarray(
        x.reshape(8, 128, k).transpose(1, 0, 2).reshape(128, 8 * k))


def _w1_perm_cols(w1):
    a = w1[:, :HID]
    g = w1[:, HID:]
    cols = []
    zpad = np.zeros((w1.shape[0], 128 - 42), w1.dtype)
    for j in range(21):
        cols.append(a[:, 128 * j:128 * (j + 1)])
        cols.append(g[:, 128 * j:128 * (j + 1)])
    cols.append(np.concatenate([a[:, 2688:], zpad], axis=1))
    cols.append(np.concatenate([g[:, 2688:], zpad], axis=1))
    return np.ascontiguousarray(np.concatenate(cols, axis=1))


def _b1_perm(b1):
    a = b1[:HID]
    g = b1[HID:]
    cols = []
    for j in range(21):
        cols.append(a[128 * j:128 * (j + 1)])
        cols.append(g[128 * j:128 * (j + 1)])
    pad = np.zeros(128 - 42, b1.dtype)
    cols.append(np.concatenate([a[2688:], pad]))
    cols.append(np.concatenate([g[2688:], pad]))
    return np.ascontiguousarray(np.stack(cols, axis=1))


def _w2_perm_rows(w2):
    rows = [w2[128 * j:128 * (j + 1)] for j in range(21)]
    rows.append(np.concatenate(
        [w2[2688:], np.zeros((128 - 42, w2.shape[1]), w2.dtype)], axis=0))
    return np.ascontiguousarray(np.concatenate(rows, axis=0))



class Build:
    def __init__(self, nc, S, stk, dbg):
        self.nc = nc
        self.S = S
        self.stk = stk
        self.dbg = list(dbg)
        self.dram = {}

    def sb(self, stk, name, shape, dt=F32):
        return stk.enter_context(self.nc.sbuf_tensor(name, shape, dt))

    def ps(self, stk, name, shape, dt=F32):
        return stk.enter_context(self.nc.psum_tensor(name, shape, dt))

    def din(self, name, shape, dt=F32):
        h = self.nc.dram_tensor(name, list(shape), dt, kind="ExternalInput")
        self.dram[name] = h
        return h

    def dscratch(self, name, shape, dt=F32):
        h = self.nc.dram_tensor(name, list(shape), dt, kind="Internal")
        self.dram[name] = h
        return h

    def dout(self, name, shape, dt=F32):
        h = self.nc.dram_tensor(name, list(shape), dt, kind="ExternalOutput")
        self.dram[name] = h
        return h

    def dump(self, name, ap, shape, dt=F32, tags=()):
        if name not in self.dbg:
            return
        h = self.dout("dbg_" + name, shape, dt)
        self.S.op("sync", lambda e, h=h, ap=ap: e.dma_start(out=h.ap(), in_=ap),
                  reads=list(tags), dma=True)


def build_program(dbg=()):
    from contextlib import ExitStack
    nc = bass.Bass("TRN2", target_bir_lowering=False)
    S = Sched()
    stk = ExitStack()
    B = Build(nc, S, stk, dbg)

    B.din("x", (L, D))
    for d in (1, 2):
        B.din(f"inwT{d}", (DM, 2 * DI))
        B.din(f"xprojT{d}", (DI, 2 * DTR))
        B.din(f"dtwT{d}", (DTR, DI))
        B.din(f"outwT{d}", (DI, DM))
        B.din(f"convw{d}", (128, 32))
        B.din(f"convb{d}", (128, 8))
        B.din(f"dtb{d}", (128, 8))
        B.din(f"dd{d}", (128, 8))
        B.din(f"alog{d}", (128, 128))
    for l in (0, 1):
        B.din(f"wqkv{l}", (D, 3 * 512))
        B.din(f"wo{l}", (512, D))
        B.din(f"w1p{l}", (D, NPAIR * 256))
        B.din(f"b1p{l}", (128, 2 * NPAIR))
        B.din(f"w2p{l}", (HIDP, D))
        B.din(f"lng{l}", (128, 8))
        B.din(f"lnb{l}", (128, 8))
        B.din(f"b2{l}", (128, 8))
        B.din(f"qsrow{l}", (1, 512))
        B.din(f"ksrow{l}", (1, 512))
    B.din("rqa", (L, 512))
    B.din("rqb", (L, 512))
    B.din("rka", (L, 512))
    B.din("rkb", (L, 512))
    B.din("bsel", (32, 32 * 128), BF16)
    B.din("lnsel", (64, 128), BF16)
    B.din("identb", (128, 128), BF16)
    B.din("identf", (128, 128), F32)
    B.din("jflipb", (128, 128), BF16)
    B.din("maskl", (128, 128), BF16)
    B.din("masku", (128, 128), BF16)
    B.din("masklTb", (128, 128), BF16)
    B.din("maskuTb", (128, 128), BF16)

    B.dscratch("ln1scr", (L, D), BF16)
    B.dscratch("x1scr", (D, L))
    B.dscratch("xfscr", (D, L))
    out = B.dout("out", (L, D))

    with stk:
        B.gstk = stk
        sems = {e: stk.enter_context(nc.semaphore("sem_" + e)) for e in ENGINES}
        for k in range(4):
            sems[f"dsl{k}"] = stk.enter_context(nc.semaphore(f"sem_dsl{k}"))
        for k in range(2):
            sems[f"dtb{k}"] = stk.enter_context(nc.semaphore(f"sem_dtb{k}"))
        for k in range(2):
            sems[f"dj{k}"] = stk.enter_context(nc.semaphore(f"sem_dj{k}"))
        for k in range(2):
            sems[f"dw{k}"] = stk.enter_context(nc.semaphore(f"sem_dw{k}"))
        block = stk.enter_context(nc.Block())
        C = {}
        C["identb"] = B.sb(stk, "c_identb", [128, 128], BF16)
        C["identf"] = B.sb(stk, "c_identf", [128, 128], F32)
        C["jflipb"] = B.sb(stk, "c_jflipb", [128, 128], BF16)
        C["maskl"] = B.sb(stk, "c_maskl", [128, 128], BF16)
        C["masku"] = B.sb(stk, "c_masku", [128, 128], BF16)
        C["masklTb"] = B.sb(stk, "c_masklTb", [128, 128], BF16)
        C["maskuTb"] = B.sb(stk, "c_maskuTb", [128, 128], BF16)
        C["cst"] = B.sb(stk, "c_cst", [128, 2], F32)
        C["lnsel"] = B.sb(stk, "c_lnsel", [64, 128], BF16)
        C["onesb"] = B.sb(stk, "c_onesb", [128, 1], BF16)
        C["onesf"] = B.sb(stk, "c_onesf", [128, 1], F32)
        C["onesr"] = B.sb(stk, "c_onesr", [1, 128], BF16)
        B.common = C
        B.mjw = (B.sb(stk, "mj_wst0", [128, 256]),
                 B.sb(stk, "mj_wsb0", [128, 256], BF16))
        for nm in ("identb", "identf", "jflipb", "maskl", "masku",
                   "masklTb", "maskuTb", "lnsel"):
            S.op("sync", lambda e, nm=nm: e.dma_start(
                out=C[nm][:], in_=B.dram[nm].ap()), writes=[nm], dma=True)
        S.op("vector", lambda e: e.memset(C["onesb"][:], 1.0), writes=["onesb"])
        S.op("vector", lambda e: e.memset(C["onesf"][:], 1.0), writes=["onesb"])
        S.op("vector", lambda e: e.memset(C["onesr"][:], 1.0), writes=["onesb"])
        S.op("vector", lambda e: e.memset(C["cst"][:, 0:1], EPS), writes=["cst"])
        S.op("vector", lambda e: e.memset(C["cst"][:, 1:2], 1.0), writes=["cst"])
        build_phase0(B)
        build_mamba(B)
        build_ffn(B)
        build_final(B, out)
        S.finalize(block, sems)
    return nc, B



def build_phase0(B):
    from contextlib import ExitStack
    nc, S = B.nc, B.S
    x = B.dram["x"]
    ln1scr = B.dram["ln1scr"]
    with ExitStack() as stk:
        xt = B.sb(stk, "p0_x", [128, 2 * D])
        x2t = B.sb(stk, "p0_x2", [128, 2 * D])
        lnt = B.sb(stk, "p0_ln", [128, 2 * D], BF16)
        st = B.sb(stk, "p0_st", [128, 16])
        cst = B.common["cst"][:, 0:1]

        jobs = []
        for l in (0, 1):
            B.dscratch(f"w1pb{l}", (D, NPAIR * 256), BF16)
            B.dscratch(f"w2pb{l}", (HIDP, D), BF16)
            B.dscratch(f"wqkvbb{l}", (D, 3 * 512), BF16)
            B.dscratch(f"wobb{l}", (512, D), BF16)
            def tile(srcnm, dstnm, rows, width):
                for r in range(rows):
                    for c0 in range(0, width, 256):
                        wdt = min(256, width - c0)
                        jobs.append((
                            B.dram[srcnm].ap()[128 * r:128 * (r + 1),
                                               c0:c0 + wdt],
                            B.dram[dstnm].ap()[128 * r:128 * (r + 1),
                                               c0:c0 + wdt], wdt))
            tile(f"w1p{l}", f"w1pb{l}", 8, NPAIR * 256)
            tile(f"w2p{l}", f"w2pb{l}", 22, 1024)
            tile(f"wqkv{l}", f"wqkvbb{l}", 8, 1536)
            tile(f"wo{l}", f"wobb{l}", 4, 1024)
        B.wjobs = jobs

        for i in range(L // 128):
            sl = i % 2
            xs = xt[:, D * sl:D * (sl + 1)]
            x2s = x2t[:, D * sl:D * (sl + 1)]
            lns = lnt[:, D * sl:D * (sl + 1)]
            sts = st[:, 8 * sl:8 * (sl + 1)]
            tx, tx2, tst, tln = (f"p0x{sl}", f"p0x2{sl}", f"p0st{sl}",
                                 f"p0ln{sl}")
            S.op("sync", lambda e, i=i, xs=xs: e.dma_start(
                out=xs, in_=x.ap()[128 * i:128 * (i + 1), :]),
                writes=[tx], dma=True, sem=f"dj{sl}")
            S.op("vector", lambda e, xs=xs, sts=sts: e.reduce_sum(
                sts[:, 0:1], xs, axis=AX.X),
                 reads=[tx], writes=[tst], ses=True)
            S.op("scalar", lambda e, xs=xs, x2s=x2s: e.activation(
                x2s, xs, AF.Square),
                 reads=[tx], writes=[tx2])
            S.op("vector", lambda e, x2s=x2s, sts=sts: e.reduce_sum(
                sts[:, 1:2], x2s, axis=AX.X),
                 reads=[tx2], writes=[tst], ses=True)
            S.op("vector", lambda e, sts=sts: e.tensor_scalar_mul(
                sts[:, 2:3], sts[:, 0:1], 1.0 / D),
                 reads=[tst], writes=[tst], ses=True)
            S.op("vector", lambda e, sts=sts: e.tensor_scalar_mul(
                sts[:, 3:4], sts[:, 1:2], 1.0 / D),
                 reads=[tst], writes=[tst], ses=True)
            S.op("vector", lambda e, sts=sts: e.tensor_tensor(
                sts[:, 4:5], sts[:, 2:3], sts[:, 2:3], ALU.mult),
                 reads=[tst], writes=[tst], ses=True)
            S.op("vector", lambda e, sts=sts: e.tensor_sub(
                sts[:, 3:4], sts[:, 3:4], sts[:, 4:5]),
                 reads=[tst], writes=[tst], ses=True)
            S.op("scalar", lambda e, sts=sts: e.activation(
                sts[:, 5:6], sts[:, 3:4], AF.Sqrt, bias=cst),
                 reads=[tst, "cst"], writes=[tst])
            S.op("vector", lambda e, sts=sts: e.reciprocal(
                sts[:, 6:7], sts[:, 5:6]),
                 reads=[tst], writes=[tst], ses=True)
            S.op("vector", lambda e, sts=sts: e.scalar_tensor_tensor(
                sts[:, 7:8], sts[:, 2:3], -1.0, sts[:, 6:7],
                ALU.mult, ALU.mult),
                 reads=[tst], writes=[tst], ses=True)
            S.op("vector", lambda e, xs=xs, lns=lns, sts=sts: e.tensor_scalar(
                lns, xs, sts[:, 6:7], sts[:, 7:8], ALU.mult, ALU.add),
                 reads=[tx, tst], writes=[tln], ses=True)
            S.op("sync", lambda e, i=i, lns=lns: e.dma_start(
                out=ln1scr.ap()[128 * i:128 * (i + 1), :], in_=lns),
                reads=[tln], writes=["ln1scr"], dma=True, sem=f"dj{sl}")
    S.barrier()



def build_mamba(B):
    from contextlib import ExitStack
    nc, S = B.nc, B.S
    dr = B.dram
    with ExitStack() as stk:
        E = {}
        sb, ps = B.sb, B.ps
        E["identb"] = B.common["identb"]
        E["identf"] = B.common["identf"]
        E["jflipb"] = B.common["jflipb"]
        E["onesb"] = B.common["onesb"]
        E["onesr"] = B.common["onesr"]
        E["cst"] = B.common["cst"]
        E["sm1"] = sb(stk, "m_small1", [128, 64])
        E["sm2"] = sb(stk, "m_small2", [128, 64])
        E["A1"] = sb(stk, "m_A1", [128, 128])
        E["A2"] = sb(stk, "m_A2", [128, 128])
        E["xprojT"] = sb(stk, "m_xprojT", [128, 2 * 512], BF16)
        E["dtwT"] = sb(stk, "m_dtwT", [32, 2 * DI], BF16)
        E["inwTb"] = sb(stk, "m_inwT", [128, 4 * 1024], BF16)
        E["outwTb"] = sb(stk, "m_outwT", [128, 8 * 512], BF16)
        E["ln1f"] = sb(stk, "m_ln1f", [128, 4 * TCH], BF16)
        E["lntm"] = sb(stk, "m_lntm", [128, 512], BF16)
        E["xh"] = sb(stk, "m_xh", [128, 8 * (TCH + 3)])
        E["halo"] = sb(stk, "m_halo", [128, 2 * 24])
        E["xc"] = sb(stk, "m_xc", [128, 8 * TCH])
        E["sz"] = sb(stk, "m_sz", [128, 8 * TCH])
        E["du"] = sb(stk, "m_du", [128, 8 * TCH])
        E["ubuf"] = sb(stk, "m_ubuf", [128, 8 * TCH])
        E["ybuf"] = sb(stk, "m_y", [128, 8 * TCH])
        E["bsel"] = sb(stk, "m_bsel", [32, 32 * 128], BF16)
        E["bcf32"] = sb(stk, "m_bcf32", [32, TCH])
        E["bchi"] = sb(stk, "m_bchi", [32, TCH], BF16)
        E["bclo"] = sb(stk, "m_bclo", [32, TCH], BF16)
        E["dbcb"] = sb(stk, "m_dbcb", [32, TCH], BF16)
        E["sp"] = sb(stk, "m_sp", [128, 2 * TCH])
        E["H"] = sb(stk, "m_H", [128, 3 * 2 * TCH])
        E["dA"] = sb(stk, "m_dA", [128, 2 * 2 * TCH])
        E["dBu"] = sb(stk, "m_dBu", [128, 2 * 2 * TCH])
        E["bcB"] = sb(stk, "m_bcB", [128, 2 * TCH])
        E["bcC"] = sb(stk, "m_bcC", [128, 2 * TCH])
        E["carry"] = sb(stk, "m_carry", [128, 2 * 128])
        E["ygb"] = sb(stk, "m_ygb", [128, 8 * TCH], BF16)
        E["hid"] = sb(stk, "m_hid", [128, 8 * TCH])
        E["srow"] = E["bcC"][0:1, 0:512]
        E["srow2"] = E["bcC"][0:1, 512:1024]
        E["rw0"] = E["bcB"][0:1, 0:256].bitcast(BF16)
        E["rw1"] = E["bcB"][0:1, 256:512].bitcast(BF16)
        E["rw2"] = E["bcB"][0:1, 512:768].bitcast(BF16)
        E["rw3"] = E["bcB"][0:1, 768:1024].bitcast(BF16)

        E["hbb"] = E["ygb"]
        E["xcb"] = E["ygb"]
        E["xfmc"] = E["sz"]
        E["x1c"] = E["ybuf"]
        E["xfc"] = E["du"]
        for i in range(4):
            E[f"ps{i}"] = ps(stk, f"m_ps{i}", [128, 512])
        E["psbc0"] = ps(stk, "m_psbc0", [128, 512])
        E["psbc1"] = ps(stk, "m_psbc1", [128, 512])
        E["pst"] = ps(stk, "m_pst", [128, 512])
        E["pstb"] = ps(stk, "m_pstb", [128, 1024], BF16)

        def load(dst_ap, src_ap, tag):
            S.op("sync", lambda e, dst_ap=dst_ap, src_ap=src_ap: e.dma_start(
                out=dst_ap, in_=src_ap), writes=[tag], dma=True)

        load(E["identb"][:], dr["identb"].ap(), "identb")
        load(E["bsel"][:], dr["bsel"].ap(), "bsel")
        load(E["identf"][:], dr["identf"].ap(), "identf")
        load(E["jflipb"][:], dr["jflipb"].ap(), "jflipb")
        S.op("vector", lambda e: e.memset(E["onesb"][:], 1.0), writes=["onesb"])
        S.op("vector", lambda e: e.memset(E["onesr"][:], 1.0), writes=["onesb"])
        S.op("vector", lambda e: e.memset(E["cst"][:, 0:1], EPS), writes=["cst"])
        S.op("vector", lambda e: e.memset(E["cst"][:, 1:2], 1.0), writes=["cst"])
        E["inwTscr"] = {}
        E["outwTscr"] = {}
        with ExitStack() as pstk:
            wstage = E["xc"][:, 0:2048]
            wstb = E["ln1f"]
            for d in (1, 2):
                sm = E[f"sm{d}"]
                load(sm[:, 0:32], dr[f"convw{d}"].ap(), f"sm{d}")
                load(sm[:, 32:40], dr[f"convb{d}"].ap(), f"sm{d}")
                load(sm[:, 40:48], dr[f"dtb{d}"].ap(), f"sm{d}")
                load(sm[:, 48:56], dr[f"dd{d}"].ap(), f"sm{d}")
                A = E[f"A{d}"]
                load(A[:], dr[f"alog{d}"].ap(), f"A{d}")
                S.op("scalar", lambda e, A=A: e.activation(A[:], A[:], AF.Exp),
                     reads=[f"A{d}"], writes=[f"A{d}"])
                S.op("vector", lambda e, A=A: e.tensor_scalar_mul(A[:], A[:], -1.0),
                     reads=[f"A{d}"], writes=[f"A{d}"])
                src = dr[f"xprojT{d}"].ap().rearrange("(g p) e -> p g e", p=128)
                load(wstage[:, 0:512].rearrange("p (g e) -> p g e", g=8), src, "wstage")
                S.op("vector", lambda e, d=d: e.tensor_copy(
                    E["xprojT"][:, 512 * (d - 1):512 * d], wstage[:, 0:512]),
                     reads=["wstage"], writes=["xprojT"])
                load(wstage[0:32, 0:DI], dr[f"dtwT{d}"].ap(), "wstage")
                S.op("vector", lambda e, d=d: e.tensor_copy(
                    E["dtwT"][:, DI * (d - 1):DI * d], wstage[0:32, 0:DI]),
                     reads=["wstage"], writes=["dtwT"])

            for d in (1, 2):
                E["inwTscr"][d] = B.dscratch(f"inwTbscr{d}", (DM, 2 * DI), BF16)
                E["outwTscr"][d] = B.dscratch(f"outwTbscr{d}", (DI, DM), BF16)
                for piece in range(4):
                    sl = slice(128 * piece, 128 * (piece + 1))
                    load(wstage[:, 0:2048], dr[f"inwT{d}"].ap()[sl, :], "wstage")
                    S.op("vector", lambda e: e.tensor_copy(
                        wstb[:], wstage[:, 0:2048]),
                         reads=["wstage"], writes=["wstb"])
                    S.op("sync", lambda e, d=d, sl=sl: e.dma_start(
                        out=E["inwTscr"][d].ap()[sl, :], in_=wstb[:]),
                        reads=["wstb"], writes=["wstage", f"inwTscr{d}"], dma=True)
                for piece in range(2):
                    src = dr[f"outwT{d}"].ap()[512 * piece:512 * (piece + 1), :]
                    src = src.rearrange("(g p) m -> p g m", p=128)
                    load(wstage[:, 0:2048].rearrange("p (g m) -> p g m", g=4), src, "wstage")
                    S.op("vector", lambda e: e.tensor_copy(
                        wstb[:], wstage[:, 0:2048]),
                         reads=["wstage"], writes=["wstb"])
                    dst = E["outwTscr"][d].ap()[512 * piece:512 * (piece + 1), :]
                    dst = dst.rearrange("(g p) m -> p g m", p=128)
                    S.op("sync", lambda e, dst=dst: e.dma_start(
                        out=dst, in_=wstb[:].rearrange("p (g m) -> p g m", g=4)),
                        reads=["wstb"], writes=["wstage", f"outwTscr{d}"], dma=True)
            S.barrier()

        S.op("vector", lambda e: e.memset(E["carry"][:], 0.0), writes=["carry"])
        S.op("vector", lambda e: e.memset(E["halo"][:], 0.0), writes=["halo"])

        wsts = [B.mjw[0]]
        wsbs = [B.mjw[1]]
        jobs = B.wjobs
        tick_bound = [len(jobs)]
        B.tick_bound = tick_bound

        def job_in(ji):
            sap, dap, wdt = jobs[ji]
            wst = wsts[0]
            S.op("sync", lambda e, sap=sap, wst=wst, wdt=wdt: e.dma_start(
                out=wst[:, 0:wdt], in_=sap),
                writes=["p0wst0"], dma=True, sem="dj0")

        def job_castout(ji):
            sap, dap, wdt = jobs[ji]
            wst = wsts[0]
            wsb = wsbs[0]
            S.op("scalar", lambda e, wst=wst, wsb=wsb, wdt=wdt: e.copy(
                wsb[:, 0:wdt], wst[:, 0:wdt]),
                 reads=["p0wst0"], writes=["p0wsb0"])
            S.op("scalar", lambda e, dap=dap, wsb=wsb, wdt=wdt: e.dma_start(
                out=dap, in_=wsb[:, 0:wdt]),
                reads=["p0wsb0"], writes=["p0out0"], dma=True, sem="dj0")

        tick_state = [0, 0]

        def job_tick():
            tick_state[0] += 1
            for _ in range(2):
                ji = tick_state[1]
                if ji >= tick_bound[0]:
                    return
                job_castout(ji)
                if ji + 1 < len(jobs):
                    job_in(ji + 1)
                tick_state[1] = ji + 1

        def job_flush():
            while tick_state[1] < tick_bound[0]:
                ji = tick_state[1]
                job_castout(ji)
                if ji + 1 < len(jobs):
                    job_in(ji + 1)
                tick_state[1] = ji + 1
        B.job_tick = job_tick
        B.job_flush = job_flush
        job_in(0)

        for c in range(NCH):
            for d in (1, 2):
                _mamba_dir_chunk(B, E, c, d)
            _mix_tail_chunk(B, E, c)
        job_flush()
        B.job_tick = None
    S.barrier()


def _mamba_dir_chunk(B, E, c, d):
    nc, S = B.nc, B.S
    dr = B.dram
    T = TCH
    ln1scr = dr["ln1scr"]
    A = E[f"A{d}"]
    sm = E[f"sm{d}"]
    smtag = f"sm{d}"
    inwTscr = E["inwTscr"][d]
    outwTscr = E["outwTscr"][d]
    inwTb, outwTb = E["inwTb"], E["outwTb"]
    ln1f, lntm = E["ln1f"], E["lntm"]
    xh, halo, xc, xcb, sz = E["xh"], E["halo"], E["xc"], E["xcb"], E["sz"]
    du, ybuf, dbcb, sp = E["du"], E["ybuf"], E["dbcb"], E["sp"]
    prod = sp[:, TCH:2 * TCH]
    bcf32, bchi, bclo, bsel = E["bcf32"], E["bchi"], E["bclo"], E["bsel"]
    Hbuf, dAbuf, dBubuf = E["H"], E["dA"], E["dBu"]
    bcB, bcC, carry, ygb, hid = E["bcB"], E["bcC"], E["carry"], E["ygb"], E["hid"]
    onesb = E["onesb"]
    psrot = [E["ps0"], E["ps1"], E["ps2"], E["ps3"]]
    psbc0, psbc1, pstb = E["psbc0"], E["psbc1"], E["pstb"]

    xh_g = xh[:].rearrange("p (g t) -> p g t", g=8)
    xc_g = xc[:].rearrange("p (g t) -> p g t", g=8)
    sz_g = sz[:].rearrange("p (g t) -> p g t", g=8)
    du_g = du[:].rearrange("p (g t) -> p g t", g=8)
    y_g = ybuf[:].rearrange("p (g t) -> p g t", g=8)
    yg_g = ygb[:].rearrange("p (g t) -> p g t", g=8)
    hid_g = hid[:].rearrange("p (g t) -> p g t", g=8)
    halo_d = halo[:].rearrange("p (d g k) -> p d g k", d=2, g=8)
    ln1f_g = ln1f[:].rearrange("p (f t) -> p f t", f=4)


    for q in range(T // 128):
        if d == 1:
            trow = c * T + q * 128
            src = ln1scr.ap()[trow:trow + 128, 0:DM]
        else:
            trow = L - (c * T + q * 128) - 128
            src = ln1scr.ap()[trow:trow + 128, DM:D]
        S.op("sync", lambda e, src=src: e.dma_start(out=lntm[:], in_=src),
             reads=["ln1scr"], writes=["lntm"], dma=True)
        for p in range(4):
            def tr(e, p=p, d=d):
                ident = E["identb"] if d == 1 else E["jflipb"]
                return e.transpose(pstb[:, 128 * p:128 * (p + 1)],
                                   lntm[:, 128 * p:128 * (p + 1)], ident[:])
            S.op("tensor", tr, reads=["lntm", "identb", "jflipb"], writes=["pstb"])

        def evac(e, q=q):
            return e.tensor_copy(ln1f_g[:, :, 128 * q:128 * (q + 1)],
                                 pstb[:, 0:512].rearrange("p (f t) -> p f t", f=4))
        S.op("vector", evac, reads=["pstb"], writes=["ln1f"])

    for half in range(2):
        for cc in range(4):
            src_ap = inwTscr.ap()[128 * cc:128 * (cc + 1),
                                  1024 * half:1024 * (half + 1)]
            S.op("sync", lambda e, cc=cc, src_ap=src_ap: e.dma_start(
                out=inwTb[:, 1024 * cc:1024 * (cc + 1)], in_=src_ap),
                reads=[f"inwTscr{d}"], writes=["inwTb"], dma=True)
        for sub in range(8):
            eg = half * 8 + sub
            psu = psrot[sub % 4]
            pstag = f"mps{sub % 4}"

            def mm(e, sub=sub, psu=psu):
                for cc in range(4):
                    lhsT = inwTb[:, 1024 * cc + 128 * sub: 1024 * cc + 128 * (sub + 1)]
                    inst = e.matmul(psu[:, 0:T], lhsT, ln1f_g[:, cc, :],
                                    start=(cc == 0), stop=(cc == 3))
                return inst
            S.op("tensor", mm, reads=["inwTb", "ln1f"], writes=[pstag])
            if eg < 8:
                def evacxh(e, eg=eg, psu=psu):
                    return e.copy(xh_g[:, eg, 3:3 + T], psu[:, 0:T])
                S.op("scalar", evacxh, reads=[pstag], writes=["xh"])
            else:
                def evacz(e, eg=eg, psu=psu):
                    return e.activation(sz_g[:, eg - 8, :], psu[:, 0:T], AF.Silu)
                S.op("scalar", evacz, reads=[pstag], writes=["sz"])

    S.op("vector", lambda e: e.tensor_copy(xh_g[:, :, 0:3], halo_d[:, d - 1]),
         reads=["halo"], writes=["xh"])
    for dg in range(8):
        asl = dg % 2
        acc = sp[:, T * asl:T * (asl + 1)]
        t_acc = "spA" if asl == 0 else "spB"

        def conv0(e, dg=dg, acc=acc):
            return e.tensor_scalar_mul(acc, xh_g[:, dg, 0:T], sm[:, 4 * dg:4 * dg + 1])
        S.op("vector", conv0, reads=["xh", smtag], writes=[t_acc])
        for k in (1, 2, 3):
            def convk(e, dg=dg, k=k, acc=acc):
                return e.scalar_tensor_tensor(
                    acc, xh_g[:, dg, k:k + T], sm[:, 4 * dg + k:4 * dg + k + 1],
                    acc, ALU.mult, ALU.add)
            S.op("vector", convk, reads=["xh", smtag, t_acc], writes=[t_acc])

        def silu(e, dg=dg, acc=acc):
            return e.activation(xc_g[:, dg, :], acc, AF.Silu,
                                bias=sm[:, 32 + dg:33 + dg])
        S.op("scalar", silu, reads=[t_acc, smtag], writes=["xc"])
    S.op("vector", lambda e: e.tensor_copy(xcb[:], xc[:]), reads=["xc"], writes=["ygb"])
    S.op("vector", lambda e, d=d: e.tensor_copy(
        halo_d[:, d - 1], xh_g[:, :, T:T + 3]), reads=["xh"], writes=["halo"])

    xcb_g = xcb[:].rearrange("p (g t) -> p g t", g=8)

    def xproj(e, d=d):
        xp = E["xprojT"][:, 512 * (d - 1):512 * d].rearrange("p (g e) -> p g e", g=8)
        for cc in range(8):
            inst = e.matmul(E["ps0"][0:64, 0:T], xp[:, cc, :], xcb_g[:, cc, :],
                            start=(cc == 0), stop=(cc == 7))
        return inst
    S.op("tensor", xproj, reads=["xprojT", "ygb"], writes=["mps0"])
    S.op("scalar", lambda e: e.copy(dbcb[:, 0:T], E["ps0"][0:32, 0:T]),
         reads=["mps0"], writes=["dbcb"])
    S.op("scalar", lambda e: e.copy(bcf32[:, 0:T], E["ps0"][32:64, 0:T]),
         reads=["mps0"], writes=["dbc"])
    S.op("vector", lambda e: e.tensor_copy(bchi[:, 0:T], bcf32[:, 0:T]),
         reads=["dbc"], writes=["bchi"], ses=True)
    S.op("vector", lambda e: e.tensor_tensor(bclo[:, 0:T], bcf32[:, 0:T],
                                             bchi[:, 0:T], ALU.subtract),
         reads=["dbc", "bchi"], writes=["bchi"], ses=True)

    for mg in range(8):
        def dtmm(e, mg=mg, d=d):
            return e.matmul(
                E["ps1"][:, 0:T],
                E["dtwT"][:, DI * (d - 1) + 128 * mg: DI * (d - 1) + 128 * (mg + 1)],
                dbcb[:, 0:T], start=True, stop=True)
        S.op("tensor", dtmm, reads=["dtwT", "dbcb"], writes=["mps1"])
        raw = sp[:, 0:T]
        t1 = sp[:, T:2 * T]

        def rawev(e, mg=mg):
            return e.add(raw, E["ps1"][:, 0:T], sm[:, 40 + mg:41 + mg])
        S.op("scalar", rawev, reads=["mps1", smtag], writes=["spA"])
        S.op("vector", lambda e: e.scalar_tensor_tensor(
            t1, raw, -1.0, raw, ALU.mult, ALU.min),
             reads=["spA"], writes=["spB"])
        S.op("scalar", lambda e: e.activation(t1, t1, AF.Exp),
             reads=["spB"], writes=["spB"])
        S.op("scalar", lambda e: e.activation(t1, t1, AF.Ln, bias=E["cst"][:, 1:2]),
             reads=["spB", "cst"], writes=["spB"])

        def sfin(e, mg=mg):
            return e.scalar_tensor_tensor(du_g[:, mg, :], raw, 0.0, t1,
                                          ALU.max, ALU.add)
        S.op("vector", sfin, reads=["spA", "spB"], writes=["du"])

    ubuf = E["ubuf"]
    u_g = ubuf[:].rearrange("p (g t) -> p g t", g=8)
    S.op("vector", lambda e: e.tensor_mul(ubuf[:], du[:], xc[:]),
         reads=["du", "xc"], writes=["ubuf"])

    prodA = sp[:, 0:2 * T]
    treA = ln1f[:].bitcast(F32)[:, 0:T]
    prodB = xh[:, 0:2 * T]
    treB = xh[:, 2 * T:3 * T]
    NG = 8
    for grp in range(NG):
        SLg = slice((grp % 2) * 2 * T, ((grp % 2) + 1) * 2 * T)
        bcB2 = bcB[:, 0:2 * T]
        bcC2 = bcC[:, 0:2 * T]
        tBc, tCc = "bcB", "bcC"
        for i in range(2):
            s = 2 * grp + i

            def bcmmB(e, s=s):
                sel = bsel[:, 128 * s:128 * (s + 1)]
                e.matmul(psbc0[:, 0:T], sel, bchi[:, 0:T], start=True, stop=False)
                return e.matmul(psbc0[:, 0:T], sel, bclo[:, 0:T], start=False, stop=True)
            S.op("tensor", bcmmB, reads=["bchi", "bsel"], writes=["psbc0"])
            S.op("scalar", lambda e, i=i, bcB2=bcB2: e.copy(
                bcB2[:, i * T:(i + 1) * T], psbc0[:, 0:T]),
                 reads=["psbc0"], writes=[tBc])

            def bcmmC(e, s=s):
                sel = bsel[:, 128 * (16 + s):128 * (17 + s)]
                e.matmul(psbc1[:, 0:T], sel, bchi[:, 0:T], start=True, stop=False)
                return e.matmul(psbc1[:, 0:T], sel, bclo[:, 0:T], start=False, stop=True)
            S.op("tensor", bcmmC, reads=["bchi", "bsel"], writes=["psbc1"])
            S.op("scalar", lambda e, i=i, bcC2=bcC2: e.copy(
                bcC2[:, i * T:(i + 1) * T], psbc1[:, 0:T]),
                 reads=["psbc1"], writes=[tCc])

        def stage_p1(dg, grp=grp):
            slot = dg % 2
            dA2 = dAbuf[:, slot * 2 * T:(slot + 1) * 2 * T]

            def p1(e, dg=dg, grp=grp, dA2=dA2):
                for i in range(2):
                    s = 2 * grp + i
                    inst = e.activation(dA2[:, i * T:(i + 1) * T], du_g[:, dg, :],
                                        AF.Exp, scale=A[:, 16 * dg + s:16 * dg + s + 1])
                return inst
            S.op("scalar", p1, reads=["du", f"A{d}"], writes=[f"dA{slot}"])

        def stage_p2(dg, grp=grp):
            slot = dg % 2
            dBu2 = dBubuf[:, slot * 2 * T:(slot + 1) * 2 * T]

            def p2(e, dg=dg, dBu2=dBu2):
                ub = u_g[:, dg, :].unsqueeze(1).broadcast_to((128, 2, T))
                return e.tensor_tensor(
                    dBu2.rearrange("p (i t) -> p i t", i=2), ub,
                    bcB2.rearrange("p (i t) -> p i t", i=2), ALU.mult)
            S.op("gpsimd", p2, reads=["ubuf", tBc], writes=[f"dB{slot}"])

        def stage_p3(dg, grp=grp):
            slot = dg % 2
            hslot = dg % 3
            dA2 = dAbuf[:, slot * 2 * T:(slot + 1) * 2 * T]
            dBu2 = dBubuf[:, slot * 2 * T:(slot + 1) * 2 * T]
            H2 = Hbuf[:, hslot * 2 * T:(hslot + 1) * 2 * T]

            def p3(e, dg=dg, grp=grp, dA2=dA2, dBu2=dBu2, H2=H2, d=d):
                for i in range(2):
                    s = 2 * grp + i
                    init = carry[:, 128 * (d - 1) + 16 * dg + s:
                                 128 * (d - 1) + 16 * dg + s + 1]
                    inst = e.tensor_tensor_scan(
                        H2[:, i * T:(i + 1) * T], dA2[:, i * T:(i + 1) * T],
                        dBu2[:, i * T:(i + 1) * T], init, ALU.mult, ALU.add)
                return inst
            S.op("vector", p3, reads=[f"dA{slot}", f"dB{slot}", "carry"],
                 writes=[f"H{hslot}", "carry"])

        def stage_p4a(dg, grp=grp):
            hslot = dg % 3
            pslot = dg % 2
            H2 = Hbuf[:, hslot * 2 * T:(hslot + 1) * 2 * T]
            prod2 = prodA if pslot == 0 else prodB
            ptags = ["spA", "spB"] if pslot == 0 else ["xh"]

            def p4a(e, H2=H2, prod2=prod2):
                return e.tensor_mul(prod2, H2, bcC2)
            S.op("gpsimd", p4a, reads=[f"H{hslot}", tCc], writes=ptags)

        def stage_p4t(dg, grp=grp):
            hslot = dg % 3
            pslot = dg % 2
            H2 = Hbuf[:, hslot * 2 * T:(hslot + 1) * 2 * T]
            prod2 = prodA if pslot == 0 else prodB
            tre = treA if pslot == 0 else treB
            ptags = ["spA", "spB"] if pslot == 0 else ["xh"]
            ttags = ["ln1f"] if pslot == 0 else ["xh"]

            def cup(e, dg=dg, grp=grp, d=d, H2=H2):
                dst = carry[:, 128 * (d - 1) + 16 * dg + 2 * grp:
                            128 * (d - 1) + 16 * dg + 2 * grp + 2]
                return e.copy(
                    dst, H2.rearrange("p (i t) -> p i t", i=2)[:, :, T - 1:T])
            S.op("scalar", cup, reads=[f"H{hslot}"], writes=["carry"])

            def p4t(e, dg=dg, grp=grp, prod2=prod2, tre=tre):
                e.tensor_add(tre, prod2[:, 0:T], prod2[:, T:2 * T])
                if grp == 0:
                    return e.tensor_copy(y_g[:, dg, :], tre)
                return e.tensor_add(y_g[:, dg, :], y_g[:, dg, :], tre)
            S.op("vector", p4t, reads=ptags + ["ybuf"] + ttags,
                 writes=["ybuf"] + ttags)

        stage_p1(0); stage_p2(0)
        stage_p1(1); stage_p2(1)
        stage_p3(0)
        for dg in range(8):
            if getattr(B, "job_tick", None):
                B.job_tick()
            if dg + 1 < 8:
                stage_p3(dg + 1)
            stage_p4a(dg)
            if dg + 2 < 8:
                stage_p1(dg + 2)
                stage_p2(dg + 2)
            stage_p4t(dg)

    for dg in range(8):
        def gate1(e, dg=dg):
            return e.scalar_tensor_tensor(
                y_g[:, dg, :], xc_g[:, dg, :], sm[:, 48 + dg:49 + dg],
                y_g[:, dg, :], ALU.mult, ALU.add)
        S.op("vector", gate1, reads=["xc", smtag, "ybuf"], writes=["ybuf"])

        def gate2(e, dg=dg):
            return e.tensor_mul(yg_g[:, dg, :], y_g[:, dg, :], sz_g[:, dg, :])
        S.op("vector", gate2, reads=["ybuf", "sz"], writes=["ygb"])

    for gg in range(8):
        S.op("sync", lambda e, gg=gg: e.dma_start(
            out=outwTb[:, 512 * gg:512 * (gg + 1)],
            in_=outwTscr.ap()[128 * gg:128 * (gg + 1), :]),
            reads=[f"outwTscr{d}"], writes=["outwTb"], dma=True)
    for mg in range(4):
        psu = psrot[mg % 2]
        pstag = f"mps{mg % 2}"

        def opmm(e, mg=mg, psu=psu):
            for cc in range(8):
                lhsT = outwTb[:, 512 * cc + 128 * mg: 512 * cc + 128 * (mg + 1)]
                inst = e.matmul(psu[:, 0:T], lhsT, yg_g[:, cc, :],
                                start=(cc == 0), stop=(cc == 7))
            return inst
        S.op("tensor", opmm, reads=["outwTb", "ygb"], writes=[pstag])
        row = (d - 1) * 4 + mg

        def opev(e, row=row, psu=psu):
            return e.copy(hid_g[:, row, :], psu[:, 0:T])
        S.op("scalar", opev, reads=[pstag], writes=["hid"])

    if c == 0:
        B.dump(f"dA_s15_d{d}", dAbuf[:], (128, 2 * TCH), F32, tags=["dA0", "dA1"])
        B.dump(f"bcB_s15_d{d}", bcB[:], (128, TCH), F32, tags=["bcB"])
        B.dump(f"bcC_s15_d{d}", bcC[:], (128, TCH), F32, tags=["bcC"])
        B.dump(f"H_s15_d{d}", Hbuf[:], (128, 2 * TCH), F32, tags=["H0", "H1"])
        B.dump(f"sz_c0_d{d}", sz[:], (128, 8 * TCH), F32, tags=["sz"])
        B.dump(f"u_c0_d{d}", E["ubuf"][:], (128, 8 * TCH), F32, tags=["ubuf"])
    B.dump(f"xc_c{c}_d{d}", xc[:], (128, 8 * TCH), F32, tags=["xc"])
    B.dump(f"du_c{c}_d{d}", du[:], (128, 8 * TCH), F32, tags=["du"])
    B.dump(f"y_c{c}_d{d}", ybuf[:], (128, 8 * TCH), F32, tags=["ybuf"])
    B.dump(f"hid_c{c}_d{d}", hid[:], (128, 8 * TCH), F32, tags=["hid"])


def _fm_ln(B, E, src_ap, src_tag, T):
    nc, S = B.nc, B.S
    hbb, onesb = E["hbb"], E["onesb"]
    srow, srow2, sp = E["srow"], E["srow2"], E["sp"]
    rw0, rw1, rw2, rw3 = E["rw0"], E["rw1"], E["rw2"], E["rw3"]
    T_ = TCH
    psbc0, psbc1 = E["psbc0"], E["psbc1"]
    hb_g = hbb[:].rearrange("p (g t) -> p g t", g=8)

    S.op("scalar", lambda e: e.activation(hbb[:], src_ap, AF.Copy),
         reads=[src_tag], writes=["ygb"])

    def mm1(e):
        for cc in range(8):
            inst = e.matmul(psbc0[0:1, 0:T], onesb[:, 0:1], hb_g[:, cc, :],
                            start=(cc == 0), stop=(cc == 7))
        return inst
    S.op("tensor", mm1, reads=["ygb", "onesb"], writes=["psbc0"])
    S.op("scalar", lambda e: e.copy(srow[0:1, 0:T], psbc0[0:1, 0:T]),
         reads=["psbc0"], writes=["bcC"])
    S.op("scalar", lambda e: e.activation(hbb[:], src_ap, AF.Square),
         reads=[src_tag], writes=["ygb"])

    def mm2(e):
        for cc in range(8):
            inst = e.matmul(psbc0[0:1, 0:T], onesb[:, 0:1], hb_g[:, cc, :],
                            start=(cc == 0), stop=(cc == 7))
        return inst
    S.op("tensor", mm2, reads=["ygb", "onesb"], writes=["psbc0"])
    S.op("scalar", lambda e: e.copy(srow2[0:1, 0:T], psbc0[0:1, 0:T]),
         reads=["psbc0"], writes=["bcC"])

    S.op("vector", lambda e: e.tensor_scalar_mul(srow[0:1, 0:T], srow[0:1, 0:T], 1.0 / D),
         reads=["bcC"], writes=["bcC"], ses=True)
    S.op("vector", lambda e: e.tensor_scalar_mul(srow2[0:1, 0:T], srow2[0:1, 0:T], 1.0 / D),
         reads=["bcC"], writes=["bcC"], ses=True)
    S.op("vector", lambda e: e.tensor_tensor(sp[0:1, 0:T], srow[0:1, 0:T],
                                             srow[0:1, 0:T], ALU.mult),
         reads=["bcC"], writes=["spA"], ses=True)
    S.op("vector", lambda e: e.tensor_sub(srow2[0:1, 0:T], srow2[0:1, 0:T], sp[0:1, 0:T]),
         reads=["bcC", "spA"], writes=["bcC"], ses=True)
    S.op("scalar", lambda e: e.activation(srow2[0:1, 0:T], srow2[0:1, 0:T],
                                          AF.Sqrt, bias=E["cst"][0:1, 0:1]),
         reads=["bcC", "cst"], writes=["bcC"])
    S.op("vector", lambda e: e.reciprocal(srow2[0:1, 0:T], srow2[0:1, 0:T]),
         reads=["bcC"], writes=["bcC"], ses=True)
    S.op("vector", lambda e: e.scalar_tensor_tensor(
        srow[0:1, 0:T], srow[0:1, 0:T], -1.0, srow2[0:1, 0:T], ALU.mult, ALU.mult),
         reads=["bcC"], writes=["bcC"], ses=True)

    def hilo(e):
        e.tensor_copy(rw0, srow2[0:1, 0:T])
        e.tensor_tensor(rw1, srow2[0:1, 0:T], rw0, ALU.subtract)
        e.tensor_copy(rw2, srow[0:1, 0:T])
        return e.tensor_tensor(rw3, srow[0:1, 0:T], rw2, ALU.subtract)
    S.op("vector", hilo, reads=["bcC"], writes=["rowhl", "bcB"], ses=True)

    def bmr(e):
        onesr = E["onesr"]
        e.matmul(psbc0[:, 0:T], onesr[:], rw0, start=True, stop=False)
        return e.matmul(psbc0[:, 0:T], onesr[:], rw1, start=False, stop=True)
    S.op("tensor", bmr, reads=["rowhl", "bcB", "onesb"], writes=["psbc0"])

    def bmn(e):
        onesr = E["onesr"]
        e.matmul(psbc1[:, 0:T], onesr[:], rw2, start=True, stop=False)
        return e.matmul(psbc1[:, 0:T], onesr[:], rw3, start=False, stop=True)
    S.op("tensor", bmn, reads=["rowhl", "bcB", "onesb"], writes=["psbc1"])


def _mix_tail_chunk(B, E, c):
    nc, S = B.nc, B.S
    T = TCH
    x = B.dram["x"]
    x1scr = B.dram["x1scr"]
    xfscr = B.dram["xfscr"]
    hid, sp = E["hid"], E["sp"]
    xtm = E["ln1f"][:].bitcast(F32)
    xfmc, x1c, xfc = E["xfmc"], E["x1c"], E["xfc"]
    psbc0, psbc1, pst = E["psbc0"], E["psbc1"], E["pst"]
    identf = E["identf"]
    hid_g = hid[:].rearrange("p (g t) -> p g t", g=8)
    x1_g = x1c[:].rearrange("p (g t) -> p g t", g=8)
    xf_g = xfc[:].rearrange("p (g t) -> p g t", g=8)
    xfm_g = xfmc[:].rearrange("p (g t) -> p g t", g=8)

    for q in range(T // 128):
        S.op("sync", lambda e, q=q: e.dma_start(
            out=xtm, in_=x.ap()[c * T + 128 * q: c * T + 128 * (q + 1), :]),
            writes=["ln1f"], dma=True)
        for dgp in range(4):
            def trx(e, dgp=dgp):
                e.transpose(pst[:, 0:128], xtm[:, 256 * dgp:256 * dgp + 128], identf[:])
                return e.transpose(pst[:, 128:256],
                                   xtm[:, 256 * dgp + 128:256 * dgp + 256], identf[:])
            S.op("tensor", trx, reads=["ln1f", "identf"], writes=["pstT"])

            def evx(e, q=q, dgp=dgp):
                return e.tensor_copy(
                    xfm_g[:, 2 * dgp:2 * dgp + 2, 128 * q:128 * (q + 1)],
                    pst[:, 0:256].rearrange("p (g t) -> p g t", g=2))
            S.op("vector", evx, reads=["pstT"], writes=["sz"])

    _fm_ln(B, E, hid[:], "hid", T)
    for dg in range(8):
        S.op("vector", lambda e, dg=dg: e.tensor_mul(
            sp[:, 0:T], hid_g[:, dg, :], psbc0[:, 0:T]),
             reads=["hid", "psbc0"], writes=["spA"])
        S.op("vector", lambda e: e.tensor_add(sp[:, 0:T], sp[:, 0:T], psbc1[:, 0:T]),
             reads=["spA", "psbc1"], writes=["spA"])
        S.op("vector", lambda e, dg=dg: e.tensor_add(
            x1_g[:, dg, :], sp[:, 0:T], xfm_g[:, dg, :]),
             reads=["spA", "sz"], writes=["ybuf"])
    S.op("sync", lambda e, c=c: e.dma_start(
        out=x1scr.ap().rearrange("(g p) t -> p g t", p=128)[:, :, c * T:(c + 1) * T],
        in_=x1_g), reads=["ybuf"], writes=["x1scr"], dma=True)

    _fm_ln(B, E, x1c[:], "ybuf", T)
    for dg in range(8):
        S.op("vector", lambda e, dg=dg: e.tensor_mul(
            sp[:, 0:T], x1_g[:, dg, :], psbc0[:, 0:T]),
             reads=["ybuf", "psbc0"], writes=["spA"])
        S.op("vector", lambda e, dg=dg: e.tensor_add(
            xf_g[:, dg, :], sp[:, 0:T], psbc1[:, 0:T]),
             reads=["spA", "psbc1"], writes=["du"])
    S.op("sync", lambda e, c=c: e.dma_start(
        out=xfscr.ap().rearrange("(g p) t -> p g t", p=128)[:, :, c * T:(c + 1) * T],
        in_=xf_g), reads=["du"], writes=["xfscr"], dma=True)

    B.dump(f"x1_c{c}", x1c[:], (128, 8 * TCH), F32, tags=["ybuf"])
    B.dump(f"xf_c{c}", xfc[:], (128, 8 * TCH), F32, tags=["du"])



def build_ffn(B):
    from contextlib import ExitStack
    S = B.S
    for l in (0, 1):
        if l == 0:
            B.tick_bound[0] = len(B.wjobs)
        with ExitStack() as stk:
            _ffn_attn(B, stk, l)
        S.barrier()
        with ExitStack() as stk:
            _ffn_mlp(B, stk, l)
            if l == 0:
                B.job_flush()
                B.job_tick = None
        S.barrier()


def _ffn_attn(B, stk, l):
    nc, S = B.nc, B.S
    dr = B.dram
    C = B.common
    def sb(stk_, name, shape, dt=F32):
        return B.sb(stk_, f"{name}_{l}", shape, dt)
    def ps(stk_, name, shape, dt=F32):
        return B.ps(stk_, f"{name}_{l}", shape, dt)
    E = {}
    E["hb"] = sb(stk, "f_hb", [128, 8 * L], BF16)
    E["wqkvb"] = sb(stk, "f_wqkvb", [128, 8 * 1536], BF16)
    E["wob"] = sb(stk, "f_wob", [128, 4 * 1024], BF16)

    E["qfm"] = sb(stk, "f_qfm", [128, 4 * L], BF16)
    E["kfm"] = sb(stk, "f_kfm", [128, 4 * L], BF16)
    E["vsb"] = sb(stk, "f_vsb", [128, 16 * 512], BF16)
    E["osb"] = sb(stk, "f_osb", [128, 4 * L], BF16)
    E["qsbc"] = sb(stk, "f_qsbc", [128, 512], F32)
    E["ksbc"] = sb(stk, "f_ksbc", [128, 512], F32)
    E["tbl"] = sb(stk, "f_tbl", [128, 2 * 4 * 512], F32)
    E["psq"] = sb(stk, "f_psq", [128, 2 * 512], F32)
    E["rst"] = sb(stk, "f_rst", [128, 64], F32)
    E["qtm"] = sb(stk, "f_qtm", [128, 2 * 512], F32)
    E["qtb"] = sb(stk, "f_qtb", [128, 2 * 512], BF16)
    E["attn"] = sb(stk, "f_attn", [128, 2 * 640], BF16)
    E["attnT"] = sb(stk, "f_attnT", [128, 2 * 8 * 512], BF16)
    E["nmx"] = sb(stk, "f_nmx", [128, 8], F32)
    E["slab2"] = sb(stk, "f_slab2", [128, 4 * 512], F32)
    E["slab"] = sb(stk, "f_slab", [128, L], F32)
    E["sp"] = sb(stk, "f_sp", [128, 512], F32)
    psQ = ps(stk, "f_psQ", [128, 512], F32)
    psK = ps(stk, "f_psK", [128, 512], F32)
    psV = ps(stk, "f_psV", [128, 512], F32)
    psS0 = ps(stk, "f_psS0", [128, 1024], F32)
    psS1 = ps(stk, "f_psS1", [128, 1024], F32)
    psTa = ps(stk, "f_psTa", [128, 1024], BF16)

    xfscr = dr["xfscr"]
    xf_r = xfscr.ap().rearrange("(g p) t -> p g t", p=128)
    hb = E["hb"]
    hb_g = hb[:].rearrange("p (g t) -> p g t", g=8)
    identb = C["identb"]

    def load(dst, srcap, tag, rtags=()):
        S.op("sync", lambda e, dst=dst, srcap=srcap: e.dma_start(
            out=dst, in_=srcap), reads=list(rtags), writes=[tag], dma=True)

    for cc in range(8):
        load(E["wqkvb"][:, 1536 * cc:1536 * (cc + 1)],
             dr[f"wqkvbb{l}"].ap()[128 * cc:128 * (cc + 1), :], "wqkvb", ["wqkvb"])
    for cc in range(4):
        load(E["wob"][:, 1024 * cc:1024 * (cc + 1)],
             dr[f"wobb{l}"].ap()[128 * cc:128 * (cc + 1), :], "wob", ["wob"])
    load(E["qsbc"][:], dr[f"qsrow{l}"].ap().to_broadcast((128, 512)), "qsbc")
    load(E["ksbc"][:], dr[f"ksrow{l}"].ap().to_broadcast((128, 512)), "ksbc")

    for dg in range(8):
        S.op("sync", lambda e, dg=dg: e.dma_start(
            out=E["slab"][:], in_=xf_r[:, dg, :]),
            reads=XFALL, writes=["fslab"], dma=True)
        if dg < 4:
            S.op("scalar", lambda e, dg=dg: e.copy(hb_g[:, dg, :], E["slab"][:]),
                 reads=["fslab"], writes=["hbf"])
        else:
            S.op("vector", lambda e, dg=dg: e.memset(hb_g[:, dg, 0:1], 0.0),
                 writes=["hbf"])
            S.op("scalar", lambda e, dg=dg: e.copy(
                hb_g[:, dg, 1:L], E["slab"][:, 0:L - 1]),
                 reads=["fslab"], writes=["hbf"])

    qfm, kfm, vsb = E["qfm"], E["kfm"], E["vsb"]

    def qkv_slots(sl):
        if sl == 0:
            return psQ[:], psK[:], psV[:], "psQ", "psK", "psV"
        return (psS0[:, 0:512], psS0[:, 512:1024], psS1[:, 0:512],
                "fpsS0", "fpsS0", "fpsS1")

    def qkv_mm(tt):
        sl = tt % 2
        t_tbl = f"tbl{sl}"
        tbl = E["tbl"][:, 2048 * sl:2048 * (sl + 1)]
        psQh, psKh, psVh, tQ, tK, tV = qkv_slots(sl)
        for ti, nm in enumerate(("rqa", "rqb", "rka", "rkb")):
            S.op("sync", lambda e, ti=ti, nm=nm, tt=tt, tbl=tbl: e.dma_start(
                out=tbl[:, 512 * ti:512 * (ti + 1)],
                in_=dr[nm].ap()[128 * tt:128 * (tt + 1), :]),
                reads=[t_tbl], writes=[t_tbl], dma=True, sem=f"dtb{sl}")

        def mmq(e, tt=tt, psQh=psQh, psKh=psKh, psVh=psVh):
            for cc in range(8):
                lhsT = hb_g[:, cc, 128 * tt:128 * (tt + 1)]
                e.matmul(psQh, lhsT,
                         E["wqkvb"][:, 1536 * cc:1536 * cc + 512],
                         start=(cc == 0), stop=(cc == 7))
                e.matmul(psKh, lhsT,
                         E["wqkvb"][:, 1536 * cc + 512:1536 * cc + 1024],
                         start=(cc == 0), stop=(cc == 7))
                inst = e.matmul(psVh, lhsT,
                                E["wqkvb"][:, 1536 * cc + 1024:1536 * cc + 1536],
                                start=(cc == 0), stop=(cc == 7))
            return inst
        S.op("tensor", mmq, reads=["hbf", "wqkvb"], writes=[tQ, tK, tV])

    def qkv_prep(tt):
        sl = tt % 2
        t_tbl, t_psq, t_qtb = f"tbl{sl}", f"psq{sl}", f"qtb{sl}"
        tbl = E["tbl"][:, 2048 * sl:2048 * (sl + 1)]
        psq = E["psq"][:, 512 * sl:512 * (sl + 1)]
        qtb = E["qtb"][:, 512 * sl:512 * (sl + 1)]
        psQh, psKh, psVh, tQ, tK, tV = qkv_slots(sl)
        S.op("scalar", lambda e, tt=tt, psVh=psVh: e.copy(
            vsb[:, 512 * tt:512 * (tt + 1)], psVh),
             reads=[tV], writes=["vsb"])
        for which, psu, pstag, qsb, tb0, tb1, dstfm, dtag in (
                ("q", psQh, tQ, E["qsbc"], 0, 1, qfm, "qfm"),
                ("k", psKh, tK, E["ksbc"], 2, 3, kfm, "kfm")):
            rst = E["rst"][:, 32 * sl:32 * (sl + 1)]
            t_rst = f"rst{sl}"
            qtm = E["qtm"]
            off = 0 if which == "q" else 8
            S.op("scalar", lambda e, psu=psu, psq=psq: e.activation(
                psq, psu, AF.Square),
                 reads=[pstag], writes=[t_psq])
            S.op("vector", lambda e, off=off, psq=psq, rst=rst: e.tensor_reduce(
                rst[:, off:off + 8],
                psq.rearrange("p (h d) -> p h d", h=8), AX.X, ALU.add),
                 reads=[t_psq], writes=[t_rst])
            S.op("scalar", lambda e, off=off, rst=rst: e.activation(
                rst[:, 16 + off:24 + off], rst[:, off:off + 8], AF.Sqrt),
                 reads=[t_rst], writes=[t_rst])
            S.op("vector", lambda e, off=off, rst=rst: e.tensor_scalar_max(
                rst[:, 16 + off:24 + off], rst[:, 16 + off:24 + off], 1e-12),
                 reads=[t_rst], writes=[t_rst], ses=True)
            S.op("vector", lambda e, off=off, rst=rst: e.reciprocal(
                rst[:, 16 + off:24 + off], rst[:, 16 + off:24 + off]),
                 reads=[t_rst], writes=[t_rst], ses=True)
            rbc = rst[:, 16 + off:24 + off].unsqueeze(2).broadcast_to((128, 8, 64))
            S.op("vector", lambda e, psu=psu, rbc=rbc: e.tensor_tensor(
                qtm[:, 0:512].rearrange("p (h d) -> p h d", h=8),
                psu.rearrange("p (h d) -> p h d", h=8), rbc, ALU.mult),
                 reads=[pstag, t_rst], writes=["qtm"])
            S.op("gpsimd", lambda e, qsb=qsb: e.tensor_mul(
                qtm[:, 512:1024], qtm[:, 0:512], qsb[:]),
                 reads=["qtm", "qsbc", "ksbc"], writes=["qtm"])
            S.op("gpsimd", lambda e, tb0=tb0, tbl=tbl: e.tensor_mul(
                qtm[:, 0:512], qtm[:, 512:1024],
                tbl[:, 512 * tb0:512 * (tb0 + 1)]),
                 reads=["qtm", t_tbl], writes=["qtm"])

            def swmul(e, tb1=tb1, tbl=tbl):
                qa = qtm[:, 512:1024].rearrange("p (h s d) -> p h s d", h=8, s=2)
                tblB = tbl[:, 512 * tb1:512 * (tb1 + 1)].rearrange(
                    "p (h s d) -> p h s d", h=8, s=2)
                spv = E["sp"][:, 0:512].rearrange("p (h s d) -> p h s d", h=8, s=2)
                e.tensor_tensor(spv[:, :, 0, :], qa[:, :, 1, :],
                                tblB[:, :, 0, :], ALU.mult)
                return e.tensor_tensor(spv[:, :, 1, :], qa[:, :, 0, :],
                                       tblB[:, :, 1, :], ALU.mult)
            S.op("gpsimd", swmul, reads=["qtm", t_tbl], writes=["fspA"])
            S.op("vector", lambda e, qtb=qtb: e.tensor_add(
                qtb, qtm[:, 0:512], E["sp"][:, 0:512]),
                 reads=["qtm", "fspA"], writes=[t_qtb])

            def trq(e, qtb=qtb):
                for hp in range(4):
                    inst = e.transpose(psTa[:, 128 * hp:128 * (hp + 1)],
                                       qtb[:, 128 * hp:128 * (hp + 1)],
                                       identb[:])
                return inst
            S.op("tensor", trq, reads=[t_qtb, "identb"], writes=["psTa"])

            def evq(e, tt=tt, dstfm=dstfm):
                return e.copy(
                    dstfm[:].rearrange("p (f t) -> p f t", f=4)[
                        :, :, 128 * tt:128 * (tt + 1)],
                    psTa[:, 0:512].rearrange("p (f t) -> p f t", f=4))
            S.op("scalar", evq, reads=["psTa"], writes=[dtag])

    qkv_mm(0)
    for tt in range(16):
        if getattr(B, "job_tick", None):
            B.job_tick()
        if tt + 1 < 16:
            qkv_mm(tt + 1)
        qkv_prep(tt)

    wob, osb = E["wob"], E["osb"]
    for w in range(NW):
        for h in range(HEADS):
            hp, hs = h // 2, (h % 2) * 64
            asl = h % 2
            attnT = E["attnT"][:, 4096 * asl:4096 * (asl + 1)]
            t_aT = f"fattnT{asl}"
            qh = qfm[:].rearrange("p (f t) -> p f t", f=4)[hs:hs + 64, hp, :]
            kh = kfm[:].rearrange("p (f t) -> p f t", f=4)[hs:hs + 64, hp, :]
            kb_lo = 4 if w == 0 else 0

            def qctx(qb):
                ssl = qb % 2
                return dict(
                    qb=qb, ssl=ssl,
                    psS=psS0 if ssl == 0 else psS1,
                    t_psS=f"fpsS{ssl}", t_psS2=f"fpsS{ssl}",
                    attn=E["attn"][:, 640 * ssl:640 * (ssl + 1)],
                    t_attn=f"fattn{ssl}",
                    nmx=E["nmx"][:, 4 * ssl:4 * (ssl + 1)],
                    t_nmx=f"nmx{ssl}",
                    kbs=[kb for kb in range(qb, qb + 5) if kb >= kb_lo],
                )

            def st_mms(cx, w=w, qh=qh, kh=kh):
                qb, kbs, psS = cx["qb"], cx["kbs"], cx["psS"]
                kb0 = kbs[0]

                nv = len(kbs)

                def mms(e, w=w, qb=qb, kbs=kbs, kb0=kb0, nv=nv,
                        qh=qh, kh=kh, psS=psS):
                    used_banks = set()
                    for kb in kbs:
                        koff = (w - 1) * 512 + 128 * kb
                        so = 128 * (kb - kb0)
                        bank = so // 512
                        st = bank not in used_banks
                        used_banks.add(bank)
                        e.matmul(
                            psS[:, so:so + 128],
                            qh[:, 512 * w + 128 * qb: 512 * w + 128 * (qb + 1)],
                            kh[:, koff:koff + 128], start=st, stop=True)
                    if kb0 == qb:
                        e.matmul(psS[:, 0:128], C["maskuTb"][:], identb[:],
                                 start=False, stop=True)
                    return e.matmul(psS[:, 128 * (nv - 1):128 * nv],
                                    C["masklTb"][:], identb[:],
                                    start=False, stop=True)
                S.op("tensor", mms, reads=["qfm", "kfm", "maskuTb",
                                           "masklTb", "identb"],
                     writes=[cx["t_psS"], cx["t_psS2"]])

            def st_maskmax(cx):
                kbs, psS, nmx = cx["kbs"], cx["psS"], cx["nmx"]
                nv = len(kbs)
                S.op("vector", lambda e, nv=nv, psS=psS, nmx=nmx: e.reduce_max(
                    nmx[:, 0:1], psS[:, 0:128 * nv], axis=AX.X, negate=True),
                     reads=[cx["t_psS"], cx["t_psS2"]], writes=[cx["t_nmx"]])

            def st_exp(cx):
                nv = len(cx["kbs"])
                psS, nmx, attn = cx["psS"], cx["nmx"], cx["attn"]
                S.op("scalar", lambda e, nv=nv, psS=psS, nmx=nmx, attn=attn: e.activation(
                    attn[:, 0:128 * nv], psS[:, 0:128 * nv], AF.Exp,
                    bias=nmx[:, 0:1], accum_out=nmx[:, 1:2]),
                     reads=[cx["t_psS"], cx["t_psS2"], cx["t_nmx"]],
                     writes=[cx["t_attn"], cx["t_nmx"]])

            def st_recip(cx):
                nmx = cx["nmx"]
                S.op("vector", lambda e, nmx=nmx: e.reciprocal(
                    nmx[:, 2:3], nmx[:, 1:2]),
                     reads=[cx["t_nmx"]], writes=[cx["t_nmx"]], ses=True)

            def st_scale(cx):
                nv = len(cx["kbs"])
                attn, nmx = cx["attn"], cx["nmx"]
                if cx["qb"] % 2 == 0:
                    S.op("scalar", lambda e, nv=nv, attn=attn, nmx=nmx:
                         e.activation(attn[:, 0:128 * nv], attn[:, 0:128 * nv],
                                      AF.Copy, scale=nmx[:, 2:3]),
                         reads=[cx["t_attn"], cx["t_nmx"]],
                         writes=[cx["t_attn"]])
                else:
                    S.op("vector", lambda e, nv=nv, attn=attn, nmx=nmx:
                         e.tensor_scalar(attn[:, 0:128 * nv],
                                         attn[:, 0:128 * nv],
                                         nmx[:, 2:3], None, ALU.mult),
                         reads=[cx["t_attn"], cx["t_nmx"]],
                         writes=[cx["t_attn"]])

            def st_mta(cx):
                kbs, attn = cx["kbs"], cx["attn"]
                kb0 = kbs[0]

                def mta(e, kbs=kbs, kb0=kb0, attn=attn):
                    for i, kb in enumerate(kbs):
                        so = 128 * (kb - kb0)
                        inst = e.transpose(psTa[:, 128 * i:128 * (i + 1)],
                                           attn[:, so:so + 128], identb[:])
                    return inst
                S.op("tensor", mta, reads=[cx["t_attn"], "identb"], writes=["psTa"])

            def st_eta(cx, attnT=attnT, t_aT=t_aT):
                kbs, qb = cx["kbs"], cx["qb"]
                engn = "vector" if qb % 2 == 0 else "scalar"

                def eta(e, kbs=kbs, qb=qb, attnT=attnT, engn=engn):
                    cp = e.tensor_copy if engn == "vector" else e.copy
                    for i, kb in enumerate(kbs):
                        inst = cp(
                            attnT[:, 512 * kb + 128 * qb:
                                  512 * kb + 128 * (qb + 1)],
                            psTa[:, 128 * i:128 * (i + 1)])
                    return inst
                S.op(engn, eta, reads=["psTa"], writes=[t_aT])

            cxs = [qctx(qb) for qb in range(4)]
            st_mms(cxs[0]); st_maskmax(cxs[0])
            st_mms(cxs[1])
            for k in range(4):
                if getattr(B, "job_tick", None):
                    B.job_tick()
                st_exp(cxs[k])
                if k + 1 < 4:
                    st_maskmax(cxs[k + 1])
                st_recip(cxs[k])
                st_scale(cxs[k])
                if k + 2 < 4:
                    st_mms(cxs[k + 2])
                st_mta(cxs[k])
                st_eta(cxs[k])

            def mav(e, w=w, h=h, kb_lo=kb_lo, attnT=attnT):
                started = False
                for kb in range(kb_lo, 8):
                    qmin = max(0, kb - 4)
                    qmax = min(3, kb)
                    if qmin > qmax:
                        continue
                    kt = 4 * (w - 1) + kb
                    lhsT = vsb[:, 512 * kt + 64 * h: 512 * kt + 64 * (h + 1)]
                    rhs = attnT[:, 512 * kb + 128 * qmin:
                                512 * kb + 128 * (qmax + 1)]
                    inst = e.matmul(psV[0:64, 128 * qmin:128 * (qmax + 1)],
                                    lhsT, rhs, start=(not started), stop=True)
                    started = True
                return inst
            S.op("tensor", mav, reads=["vsb", t_aT], writes=["psV"])
            S.op("scalar", lambda e, w=w, hp=hp, hs=hs: e.copy(
                osb[hs:hs + 64, 2048 * hp + 512 * w: 2048 * hp + 512 * (w + 1)],
                psV[0:64, 0:512]),
                 reads=["psV"], writes=["osb"])

        def wo_in(mg, w=w):
            sl2 = E["slab2"][:, 512 * (mg % 4):512 * (mg % 4 + 1)]
            S.op("sync", lambda e, w=w, mg=mg, sl2=sl2: e.dma_start(
                out=sl2, in_=xf_r[:, mg, 512 * w:512 * (w + 1)]),
                reads=["xfscr", f"xfr{mg}"], writes=[f"fslab2{mg % 4}"],
                dma=True, sem=f"dsl{mg % 4}")
        for mg in range(3):
            wo_in(mg)
        for mg in range(8):
            if getattr(B, "job_tick", None):
                B.job_tick()
            msl = mg % 2
            psu = psQ[:] if msl == 0 else psK[:]
            pstag = "psQ" if msl == 0 else "psK"
            sl2 = E["slab2"][:, 512 * (mg % 4):512 * (mg % 4 + 1)]
            t_sl2 = f"fslab2{mg % 4}"

            def mmo(e, w=w, mg=mg, psu=psu):
                for cc in range(4):
                    lhsT = wob[:, 1024 * cc + 128 * mg: 1024 * cc + 128 * (mg + 1)]
                    rhs = osb[:, 2048 * cc + 512 * w: 2048 * cc + 512 * (w + 1)]
                    inst = e.matmul(psu, lhsT, rhs,
                                    start=(cc == 0), stop=(cc == 3))
                return inst
            S.op("tensor", mmo, reads=["wob", "osb"], writes=[pstag])
            S.op("vector", lambda e, psu=psu, sl2=sl2: e.tensor_add(
                sl2, sl2, psu),
                 reads=[t_sl2, pstag], writes=[t_sl2])
            S.op("scalar", lambda e, w=w, mg=mg, sl2=sl2: e.dma_start(
                out=xf_r[:, mg, 512 * w:512 * (w + 1)], in_=sl2),
                reads=[t_sl2], writes=[f"xfr{mg}"], dma=True,
                sem=f"dsl{mg % 4}")
            if mg + 3 < 8:
                wo_in(mg + 3)

    B.dump(f"attnout_l{l}", E["osb"][:], (128, 4 * L), BF16, tags=["osb"])
    B.dump(f"wqkvb_l{l}", E["wqkvb"][:], (128, 8 * 1536), BF16, tags=["wqkvb"])
    if l == 0 and "w1pb_l0" in B.dbg:
        B.dump("w1pb_l0", B.dram["w1pb0"].ap(), (D, NPAIR * 256), BF16, tags=[])
    B.dump(f"hb_l{l}", E["hb"][:], (128, 8 * L), BF16, tags=["hbf"])
    B.dump(f"qfm_l{l}", E["qfm"][:], (128, 4 * L), BF16, tags=["qfm"])
    B.dump(f"kfm_l{l}", E["kfm"][:], (128, 4 * L), BF16, tags=["kfm"])
    B.dump(f"vsb_l{l}", E["vsb"][:], (128, 16 * 512), BF16, tags=["vsb"])
    B.dump(f"xfattn_l{l}", B.dram["xfscr"].ap(), (D, L), F32, tags=XFALL)


def _ffn_mlp(B, stk, l):
    nc, S = B.nc, B.S
    dr = B.dram
    C = B.common
    def sb(stk_, name, shape, dt=F32):
        return B.sb(stk_, f"{name}_{l}", shape, dt)
    def ps(stk_, name, shape, dt=F32):
        return B.ps(stk_, f"{name}_{l}", shape, dt)
    E = {}
    E["slab"] = sb(stk, "g_slab", [128, 2 * L], F32)
    E["slab2"] = sb(stk, "g_slab2", [128, 4 * 512], F32)
    E["shb"] = sb(stk, "g_shb", [128, L], BF16)
    E["srow"] = sb(stk, "g_srow", [1, L], F32)
    E["srow2"] = sb(stk, "g_srow2", [1, L], F32)
    E["fmu2"] = sb(stk, "g_fmu2", [1, L], F32)
    E["rwst"] = sb(stk, "g_rwst", [64, 2 * L], BF16)
    E["fmu2"] = E["rwst"][0:1, 0:2 * L].bitcast(F32)
    E["smc"] = sb(stk, "g_smc", [128, 68], F32)
    E["ln2h"] = sb(stk, "g_ln2h", [128, 8 * L], BF16)
    E["sp"] = sb(stk, "g_sp", [128, 2 * 512], F32)
    E["w1a"] = sb(stk, "g_w1a", [128, 2048], BF16)
    E["w1b"] = sb(stk, "g_w1b", [128, 2048], BF16)
    E["gga"] = sb(stk, "g_gga", [128, 2 * 512], F32)
    E["ggb"] = sb(stk, "g_ggb", [128, 2 * 512], F32)
    E["gg"] = sb(stk, "g_gg", [128, 22 * 1024], BF16)
    E["w2b"] = sb(stk, "g_w2b", [128, 22 * 1024], BF16)
    psA = ps(stk, "g_psA", [128, 512], F32)
    psB = ps(stk, "g_psB", [128, 512], F32)
    psC = ps(stk, "g_psC", [128, 512], F32)
    psD = ps(stk, "g_psD", [128, 512], F32)
    psq4 = [psA, psB, psC, psD]
    pstags = ["fpsA", "fpsB", "fpsC", "fpsD"]

    xfscr = dr["xfscr"]
    xf_r = xfscr.ap().rearrange("(g p) t -> p g t", p=128)
    smc = E["smc"]
    onesb = C["onesb"]
    ln2h = E["ln2h"]
    ln2h_g = ln2h[:].rearrange("p (g t) -> p g t", g=8)
    gg = E["gg"]

    def load(dst, srcap, tag, rtags=()):
        S.op("sync", lambda e, dst=dst, srcap=srcap: e.dma_start(
            out=dst, in_=srcap), reads=list(rtags), writes=[tag], dma=True)

    load(smc[:, 0:44], dr[f"b1p{l}"].ap(), "smc")
    load(smc[:, 44:52], dr[f"b2{l}"].ap(), "smc")
    load(smc[:, 52:60], dr[f"lng{l}"].ap(), "smc")
    load(smc[:, 60:68], dr[f"lnb{l}"].ap(), "smc")

    for dg in range(8):
        if getattr(B, "job_tick", None):
            B.job_tick()
        sl = dg % 2
        slab = E["slab"][:, L * sl:L * (sl + 1)]
        shb = E["shb"][:]
        t_sl = f"fslab{sl}"
        S.op("sync", lambda e, dg=dg, slab=slab: e.dma_start(
                 out=slab, in_=xf_r[:, dg, :]),
             reads=XFALL, writes=[t_sl], dma=True, sem=f"dw{sl}")
        if dg < 4:
            S.op("scalar", lambda e, slab=slab, shb=shb: e.activation(
                shb, slab, AF.Square),
                 reads=[t_sl], writes=["fshb"])
        else:
            S.op("vector", lambda e, shb=shb: e.memset(
                shb[:, 0:1], 0.0), writes=["fshb"])
            S.op("scalar", lambda e, slab=slab, shb=shb: e.activation(
                shb[:, 1:L], slab[:, 0:L - 1], AF.Square),
                 reads=[t_sl], writes=["fshb"])

        def mmsum(e, dg=dg, slab=slab, shb=shb):
            sh = 1 if dg >= 4 else 0
            for ch in range(4):
                if sh and ch == 0:
                    e.matmul(psq4[0][0:1, 1:512], C["onesf"][:, 0:1],
                             slab[:, 0:511], start=False, stop=(dg == 7))
                else:
                    e.matmul(psq4[ch][0:1, 0:512], C["onesf"][:, 0:1],
                             slab[:, 512 * ch - sh:512 * (ch + 1) - sh],
                             start=(dg == 0), stop=(dg == 7))
                inst = e.matmul(psq4[ch][32:33, 0:512], onesb[:, 0:1],
                                shb[:, 512 * ch:512 * (ch + 1)],
                                start=(dg == 0), stop=(dg == 7))
            return inst
        S.op("tensor", mmsum, reads=[t_sl, "fshb", "onesb"], writes=pstags)
    for ch in range(4):
        S.op("scalar", lambda e, ch=ch: e.copy(
            E["srow"][0:1, 512 * ch:512 * (ch + 1)], psq4[ch][0:1, 0:512]),
             reads=pstags, writes=["fsrow"])
        S.op("scalar", lambda e, ch=ch: e.copy(
            E["srow2"][0:1, 512 * ch:512 * (ch + 1)], psq4[ch][32:33, 0:512]),
             reads=pstags, writes=["fsrow"])

    B.dump(f"srow_l{l}", E["srow"][0:1, :], (1, L), F32, tags=["fsrow"])
    B.dump(f"srow2_l{l}", E["srow2"][0:1, :], (1, L), F32, tags=["fsrow"])
    S.op("vector", lambda e: e.tensor_scalar_mul(
        E["srow"][0:1, :], E["srow"][0:1, :], 1.0 / D),
         reads=["fsrow"], writes=["fsrow"], ses=True)
    S.op("vector", lambda e: e.tensor_scalar_mul(
        E["srow2"][0:1, :], E["srow2"][0:1, :], 1.0 / D),
         reads=["fsrow"], writes=["fsrow"], ses=True)
    S.op("vector", lambda e: e.tensor_tensor(
        E["fmu2"], E["srow"][0:1, :], E["srow"][0:1, :], ALU.mult),
         reads=["fsrow"], writes=["frwst"], ses=True)
    S.op("vector", lambda e: e.tensor_sub(
        E["srow2"][0:1, :], E["srow2"][0:1, :], E["fmu2"]),
         reads=["fsrow", "frwst"], writes=["fsrow"], ses=True)
    S.op("scalar", lambda e: e.activation(
        E["srow2"][0:1, :], E["srow2"][0:1, :], AF.Sqrt, bias=C["cst"][0:1, 0:1]),
         reads=["fsrow", "cst"], writes=["fsrow"])
    S.op("vector", lambda e: e.reciprocal(E["srow2"][0:1, :], E["srow2"][0:1, :]),
         reads=["fsrow"], writes=["fsrow"], ses=True)
    S.op("vector", lambda e: e.scalar_tensor_tensor(
        E["srow"][0:1, :], E["srow"][0:1, :], -1.0, E["srow2"][0:1, :],
        ALU.mult, ALU.mult),
         reads=["fsrow"], writes=["fsrow"], ses=True)
    S.op("vector", lambda e: e.memset(E["rwst"][:], 0.0), writes=["frwst"])

    def hilo(e):
        e.tensor_copy(E["rwst"][0:1, 0:L], E["srow2"][0:1, :])
        e.tensor_tensor(E["rwst"][32:33, 0:L], E["srow2"][0:1, :],
                        E["rwst"][0:1, 0:L], ALU.subtract)
        e.tensor_copy(E["rwst"][0:1, L:2 * L], E["srow"][0:1, :])
        return e.tensor_tensor(E["rwst"][32:33, L:2 * L], E["srow"][0:1, :],
                               E["rwst"][0:1, L:2 * L], ALU.subtract)
    S.op("vector", hilo, reads=["fsrow"], writes=["frwst"], ses=True)

    for dg in range(8):
        if getattr(B, "job_tick", None):
            B.job_tick()
        sl = dg % 2
        slab = E["slab"][:, L * sl:L * (sl + 1)]
        t_sl = f"fslab{sl}"
        S.op("sync", lambda e, dg=dg, slab=slab: e.dma_start(
                 out=slab, in_=xf_r[:, dg, :]),
             reads=XFALL, writes=[t_sl], dma=True, sem=f"dw{sl}")
        for ch in range(4):
            csl = ch % 2
            psR = psA if csl == 0 else psC
            psN = psB if csl == 0 else psD
            tR = "fpsA" if csl == 0 else "fpsC"
            tN = "fpsB" if csl == 0 else "fpsD"
            spx = E["sp"][:, 0:512] if csl == 0 else E["gga"][:, 512:1024]
            tspx = "fspA" if csl == 0 else "fggax"
            spy = E["sp"][:, 512:1024] if csl == 0 else E["ggb"][:, 512:1024]
            tspy = "fspB" if csl == 0 else "fggbx"

            def bmr(e, ch=ch, psR=psR):
                return e.matmul(psR[:, 0:512], C["lnsel"][:],
                                E["rwst"][:, 512 * ch:512 * (ch + 1)],
                                start=True, stop=True)
            S.op("tensor", bmr, reads=["frwst", "lnsel"], writes=[tR])

            def bmn(e, ch=ch, psN=psN):
                return e.matmul(psN[:, 0:512], C["lnsel"][:],
                                E["rwst"][:, L + 512 * ch:L + 512 * (ch + 1)],
                                start=True, stop=True)
            S.op("tensor", bmn, reads=["frwst", "lnsel"], writes=[tN])
            if dg < 4:
                S.op("vector", lambda e, ch=ch, slab=slab, psR=psR, spx=spx: e.tensor_mul(
                    spx, slab[:, 512 * ch:512 * (ch + 1)], psR[:, 0:512]),
                     reads=[t_sl, tR], writes=[tspx])
            elif ch == 0:
                S.op("vector", lambda e, spx=spx: e.memset(spx[:, 0:1], 0.0),
                     writes=[tspx])
                S.op("vector", lambda e, slab=slab, psR=psR, spx=spx: e.tensor_mul(
                    spx[:, 1:512], slab[:, 0:511], psR[:, 1:512]),
                     reads=[t_sl, tR], writes=[tspx])
            else:
                S.op("vector", lambda e, ch=ch, slab=slab, psR=psR, spx=spx: e.tensor_mul(
                    spx, slab[:, 512 * ch - 1:512 * (ch + 1) - 1],
                    psR[:, 0:512]),
                     reads=[t_sl, tR], writes=[tspx])
            S.op("vector", lambda e, spx=spx, spy=spy, psN=psN: e.tensor_add(
                spy, spx, psN[:, 0:512]),
                 reads=[tspx, tN], writes=[tspy])
            S.op("vector", lambda e, dg=dg, ch=ch, spy=spy: e.tensor_scalar(
                ln2h_g[:, dg, 512 * ch:512 * (ch + 1)], spy,
                smc[:, 52 + dg:53 + dg], smc[:, 60 + dg:61 + dg],
                ALU.mult, ALU.add),
                 reads=[tspy, "smc"], writes=["fln2h"])

    for half in range(2):
        rows = slice(128 * 11 * half, 128 * 11 * (half + 1))
        cols = slice(11264 * half, 11264 * (half + 1))
        src_ap = dr[f"w2pb{l}"].ap()[rows, :].rearrange("(j p) m -> p j m", p=128)
        dst_ap = E["w2b"][:, cols].rearrange("p (j m) -> p j m", j=11)
        S.op("sync", lambda e, src_ap=src_ap, dst_ap=dst_ap: e.dma_start(
            out=dst_ap, in_=src_ap),
            reads=["fw2b"], writes=["fw2b"], dma=True)

    for th in range(2):
        for j in range(NPAIR):
            if getattr(B, "job_tick", None):
                B.job_tick()
            wslot = E["w1a"] if j % 2 == 0 else E["w1b"]
            wstag = f"fw1a{j % 2}"
            src_ap = dr[f"w1pb{l}"].ap().rearrange("(c p) m -> p c m", p=128)[
                :, :, 256 * j:256 * (j + 1)]
            w1rd = [wstag] + ([f"fslab{j % 2}"] if j < 2 else [])
            S.op("sync",
                 lambda e, src_ap=src_ap, wslot=wslot: e.dma_start(
                     out=wslot[:].rearrange("p (c m) -> p c m", c=8),
                in_=src_ap),
                reads=w1rd, writes=[wstag], dma=True, sem=f"dw{j % 2}")
            for tc2 in range(2):
                tc = 2 * th + tc2
                psA_ = psA if tc2 % 2 == 0 else psC
                psB_ = psB if tc2 % 2 == 0 else psD
                tA_ = "fpsA" if tc2 % 2 == 0 else "fpsC"
                tB_ = "fpsB" if tc2 % 2 == 0 else "fpsD"

                def mm1(e, tc=tc, wslot=wslot, psA_=psA_, psB_=psB_):
                    for cc in range(8):
                        a_t = wslot[:, 256 * cc + 0:256 * cc + 128]
                        g_t = wslot[:, 256 * cc + 128:256 * cc + 256]
                        e.matmul(psA_[:, 0:512], a_t,
                                 ln2h_g[:, cc, 512 * tc:512 * (tc + 1)],
                                 start=(cc == 0), stop=(cc == 7))
                        inst = e.matmul(psB_[:, 0:512], g_t,
                                        ln2h_g[:, cc, 512 * tc:512 * (tc + 1)],
                                        start=(cc == 0), stop=(cc == 7))
                    return inst
                S.op("tensor", mm1, reads=[wstag, "fln2h"], writes=[tA_, tB_])
                gsl = tc2 % 2
                gga = E["gga"][:, 512 * gsl:512 * (gsl + 1)]
                ggb = E["ggb"][:, 512 * gsl:512 * (gsl + 1)]
                t_ga, t_gb = f"fgga{gsl}", f"fggb{gsl}"
                S.op("scalar", lambda e, j=j, gga=gga, psA_=psA_: e.add(
                    gga, psA_[:, 0:512], smc[:, 2 * j:2 * j + 1]),
                     reads=[tA_, "smc"], writes=[t_ga])
                S.op("scalar", lambda e, j=j, ggb=ggb, psB_=psB_: e.activation(
                    ggb, psB_[:, 0:512], AF.Gelu,
                    bias=smc[:, 2 * j + 1:2 * j + 2]),
                     reads=[tB_, "smc"], writes=[t_gb])
                S.op("vector", lambda e, j=j, tc2=tc2, gga=gga, ggb=ggb: e.tensor_mul(
                    gg[:, 1024 * j + 512 * tc2: 1024 * j + 512 * (tc2 + 1)],
                    gga, ggb),
                     reads=[t_ga, t_gb], writes=["fgg"])

        for tc2 in range(2):
            tc = 2 * th + tc2

            def r_in(mg, tc=tc):
                sl2 = E["slab2"][:, 512 * (mg % 4):512 * (mg % 4 + 1)]
                S.op("sync", lambda e, tc=tc, mg=mg, sl2=sl2: e.dma_start(
                    out=sl2, in_=xf_r[:, mg, 512 * tc:512 * (tc + 1)]),
                    reads=["xfscr", f"xfr{mg}"], writes=[f"fslab2{mg % 4}"],
                    dma=True, sem=f"dsl{mg % 4}")
            for mg in range(3):
                r_in(mg)
            for mg in range(8):
                if getattr(B, "job_tick", None):
                    B.job_tick()
                psu = [psA, psB, psC, psD][mg % 4]
                pstag = ["fpsA", "fpsB", "fpsC", "fpsD"][mg % 4]
                sl2 = E["slab2"][:, 512 * (mg % 4):512 * (mg % 4 + 1)]
                t_sl2 = f"fslab2{mg % 4}"

                def mm2(e, tc2=tc2, mg=mg, psu=psu):
                    for j in range(NPAIR):
                        lhsT = E["w2b"][:, 1024 * j + 128 * mg:
                                        1024 * j + 128 * (mg + 1)]
                        rhs = gg[:, 1024 * j + 512 * tc2: 1024 * j + 512 * (tc2 + 1)]
                        inst = e.matmul(psu[:, 0:512], lhsT, rhs,
                                        start=(j == 0), stop=(j == NPAIR - 1))
                    return inst
                S.op("tensor", mm2, reads=["fw2b", "fgg"], writes=[pstag])
                S.op("vector", lambda e, mg=mg, psu=psu, sl2=sl2: e.scalar_tensor_tensor(
                    sl2, psu[:, 0:512], smc[:, 44 + mg:45 + mg],
                    sl2, ALU.add, ALU.add),
                     reads=[t_sl2, pstag, "smc"], writes=[t_sl2])
                S.op("scalar", lambda e, tc=tc, mg=mg, sl2=sl2: e.dma_start(
                    out=xf_r[:, mg, 512 * tc:512 * (tc + 1)], in_=sl2),
                    reads=[t_sl2], writes=[f"xfr{mg}"], dma=True,
                    sem=f"dsl{mg % 4}")
                if mg + 3 < 8:
                    r_in(mg + 3)
    B.dump(f"ln2h_l{l}", E["ln2h"][:], (128, 8 * L), BF16, tags=["fln2h"])
    B.dump(f"xfmlp_l{l}", B.dram["xfscr"].ap(), (D, L), F32, tags=XFALL)


def build_final(B, out):
    from contextlib import ExitStack
    nc, S = B.nc, B.S
    dr = B.dram
    C = B.common
    onesb = C["onesb"]
    with ExitStack() as stk:
        sb, ps = B.sb, B.ps
        xfc = sb(stk, "z_xfc", [128, 8 * 512], F32)
        x1c = sb(stk, "z_x1c", [128, 8 * 512], F32)
        hbb = sb(stk, "z_hbb", [128, 8 * 512], BF16)
        srow = sb(stk, "z_srow", [1, 512], F32)
        srow2 = sb(stk, "z_srow2", [1, 512], F32)
        mu2 = sb(stk, "z_mu2", [1, 512], F32)
        rwst = sb(stk, "z_rwst", [64, 2 * 512], BF16)
        sp = sb(stk, "z_sp", [128, 512], F32)
        ofm = sb(stk, "z_ofm", [128, 8 * 512], F32)
        otm = sb(stk, "z_otm", [128, 1024], F32)
        psb0 = ps(stk, "z_psb0", [128, 512], F32)
        psb1 = ps(stk, "z_psb1", [128, 512], F32)
        pst = ps(stk, "z_pst", [128, 512], F32)
        xf_r = dr["xfscr"].ap().rearrange("(g p) t -> p g t", p=128)
        x1_r = dr["x1scr"].ap().rearrange("(g p) t -> p g t", p=128)
        xf_g = xfc[:].rearrange("p (g t) -> p g t", g=8)
        x1_g = x1c[:].rearrange("p (g t) -> p g t", g=8)
        o_g = ofm[:].rearrange("p (g t) -> p g t", g=8)
        hb_g = hbb[:].rearrange("p (g t) -> p g t", g=8)
        T = 512
        for c in range(4):
            S.op("sync", lambda e, c=c: e.dma_start(
                out=xf_g, in_=xf_r[:, :, c * T:(c + 1) * T]),
                reads=XFALL, writes=["zxf"], dma=True)
            S.op("sync", lambda e, c=c: e.dma_start(
                out=x1_g, in_=x1_r[:, :, c * T:(c + 1) * T]),
                reads=["x1scr"], writes=["zx1"], dma=True)
            for pas in range(2):
                func = AF.Copy if pas == 0 else AF.Square
                dstrow = srow if pas == 0 else srow2
                S.op("scalar", lambda e, func=func: e.activation(
                    hbb[:], xfc[:], func), reads=["zxf"], writes=["zhb"])

                def mmsum(e):
                    for cc in range(8):
                        inst = e.matmul(psb0[0:1, 0:T], onesb[:, 0:1],
                                        hb_g[:, cc, :],
                                        start=(cc == 0), stop=(cc == 7))
                    return inst
                S.op("tensor", mmsum, reads=["zhb", "onesb"], writes=["zps0"])
                S.op("scalar", lambda e, dstrow=dstrow: e.copy(
                    dstrow[0:1, :], psb0[0:1, 0:T]),
                     reads=["zps0"], writes=["zrow"])
            S.op("vector", lambda e: e.tensor_scalar_mul(
                srow[0:1, :], srow[0:1, :], 1.0 / D),
                 reads=["zrow"], writes=["zrow"], ses=True)
            S.op("vector", lambda e: e.tensor_scalar_mul(
                srow2[0:1, :], srow2[0:1, :], 1.0 / D),
                 reads=["zrow"], writes=["zrow"], ses=True)
            S.op("vector", lambda e: e.tensor_tensor(
                mu2[0:1, :], srow[0:1, :], srow[0:1, :], ALU.mult),
                 reads=["zrow"], writes=["zmu2"], ses=True)
            S.op("vector", lambda e: e.tensor_sub(
                srow2[0:1, :], srow2[0:1, :], mu2[0:1, :]),
                 reads=["zrow", "zmu2"], writes=["zrow"], ses=True)
            S.op("scalar", lambda e: e.activation(
                srow2[0:1, :], srow2[0:1, :], AF.Sqrt, bias=C["cst"][0:1, 0:1]),
                 reads=["zrow", "cst"], writes=["zrow"])
            S.op("vector", lambda e: e.reciprocal(srow2[0:1, :], srow2[0:1, :]),
                 reads=["zrow"], writes=["zrow"], ses=True)
            S.op("vector", lambda e: e.scalar_tensor_tensor(
                srow[0:1, :], srow[0:1, :], -1.0, srow2[0:1, :],
                ALU.mult, ALU.mult),
                 reads=["zrow"], writes=["zrow"], ses=True)
            S.op("vector", lambda e: e.memset(rwst[:], 0.0), writes=["zrw"])

            def hilo(e):
                e.tensor_copy(rwst[0:1, 0:T], srow2[0:1, :])
                e.tensor_tensor(rwst[32:33, 0:T], srow2[0:1, :],
                                rwst[0:1, 0:T], ALU.subtract)
                e.tensor_copy(rwst[0:1, T:2 * T], srow[0:1, :])
                return e.tensor_tensor(rwst[32:33, T:2 * T], srow[0:1, :],
                                       rwst[0:1, T:2 * T], ALU.subtract)
            S.op("vector", hilo, reads=["zrow"], writes=["zrw"], ses=True)
            S.op("tensor", lambda e: e.matmul(
                psb0[:, 0:T], C["lnsel"][:], rwst[:, 0:T], start=True, stop=True),
                 reads=["zrw", "lnsel"], writes=["zps0"])
            S.op("tensor", lambda e: e.matmul(
                psb1[:, 0:T], C["lnsel"][:], rwst[:, T:2 * T], start=True, stop=True),
                 reads=["zrw", "lnsel"], writes=["zps1"])
            for dg in range(8):
                S.op("vector", lambda e, dg=dg: e.tensor_mul(
                    sp[:], xf_g[:, dg, :], psb0[:, 0:T]),
                     reads=["zxf", "zps0"], writes=["zsp"])
                S.op("vector", lambda e, dg=dg: e.tensor_add(
                    sp[:], sp[:], psb1[:, 0:T]),
                     reads=["zsp", "zps1"], writes=["zsp"])
                S.op("vector", lambda e, dg=dg: e.tensor_add(
                    o_g[:, dg, :], sp[:], x1_g[:, dg, :]),
                     reads=["zsp", "zx1"], writes=["zofm"])
            for q in range(4):
                for dgp in range(4):
                    def trx(e, q=q, dgp=dgp):
                        e.transpose(pst[:, 0:128],
                                    o_g[:, 2 * dgp, 128 * q:128 * (q + 1)],
                                    C["identf"][:])
                        return e.transpose(pst[:, 128:256],
                                           o_g[:, 2 * dgp + 1, 128 * q:128 * (q + 1)],
                                           C["identf"][:])
                    S.op("tensor", trx, reads=["zofm", "identf"], writes=["zpst"])

                    def evx(e, dgp=dgp):
                        return e.tensor_copy(otm[:, 256 * dgp:256 * (dgp + 1)],
                                             pst[:, 0:256])
                    S.op("vector", evx, reads=["zpst"], writes=["zotm"])
                S.op("sync", lambda e, c=c, q=q: e.dma_start(
                    out=out.ap()[512 * c + 128 * q:512 * c + 128 * (q + 1), :],
                    in_=otm[:]),
                    reads=["zotm"], writes=["zout"], dma=True)
    S.barrier()



def _prep_inputs(inputs):
    f32 = np.float32
    asc = np.ascontiguousarray
    rqa, rqb, rka, rkb = _rotary_tables()
    maskL, maskU = _band_masks()
    bsel = np.zeros((32, 32 * 128), dtype=BF16NP)
    for s in range(32):
        bsel[s, 128 * s:128 * (s + 1)] = 1
    lnsel = np.zeros((64, 128), dtype=BF16NP)
    lnsel[0, :] = 1
    lnsel[32, :] = 1
    common = {
        "lnsel": lnsel,
        "masklTb": np.ascontiguousarray(
            np.where(maskL > 0, 0.0, -1e30).T.astype(BF16NP)),
        "maskuTb": np.ascontiguousarray(
            np.where(maskU > 0, 0.0, -1e30).T.astype(BF16NP)),
        "bsel": bsel,
        "rqa": rqa, "rqb": rqb, "rka": rka, "rkb": rkb,
        "identb": np.eye(128, dtype=BF16NP),
        "identf": np.eye(128, dtype=f32),
        "jflipb": np.fliplr(np.eye(128)).astype(BF16NP),
        "maskl": maskL, "masku": maskU,
    }
    for d in (1, 2):
        common[f"inwT{d}"] = asc(inputs[f"L0_attn_in_w{d}"].T)
        common[f"xprojT{d}"] = asc(inputs[f"L0_attn_xproj_w{d}"].T)
        common[f"dtwT{d}"] = asc(inputs[f"L0_attn_dt_w{d}"].T)
        common[f"outwT{d}"] = asc(inputs[f"L0_attn_out_w{d}"].T)
        common[f"convw{d}"] = _pg(inputs[f"L0_attn_conv_w{d}"])
        common[f"convb{d}"] = _pg(inputs[f"L0_attn_conv_b{d}"])
        common[f"dtb{d}"] = _pg(inputs[f"L0_attn_dt_b{d}"])
        common[f"dd{d}"] = _pg(inputs[f"L0_attn_D{d}"])
        common[f"alog{d}"] = _pg(inputs[f"L0_attn_A_log{d}"])
    for l in (0, 1):
        common[f"wqkv{l}"] = asc(inputs["L0_ffn_wqkv"][l])
        common[f"wo{l}"] = asc(inputs["L0_ffn_wo"][l])
        common[f"w1p{l}"] = _w1_perm_cols(inputs["L0_ffn_w1"][l])
        common[f"b1p{l}"] = _b1_perm(inputs["L0_ffn_b1"][l])
        common[f"w2p{l}"] = _w2_perm_rows(inputs["L0_ffn_w2"][l])
        common[f"lng{l}"] = _pg(inputs["L0_ffn_ln_g"][l])
        common[f"lnb{l}"] = _pg(inputs["L0_ffn_ln_b"][l])
        common[f"b2{l}"] = _pg(inputs["L0_ffn_b2"][l])
        common[f"qsrow{l}"] = asc(np.tile(inputs["L0_ffn_q_scale"][l], HEADS)[None, :])
        common[f"ksrow{l}"] = asc(np.tile(inputs["L0_ffn_k_scale"][l], HEADS)[None, :])

    x = inputs["L0_attn_x"]
    in_maps = []
    for i in range(x.shape[0]):
        m = dict(common)
        m["x"] = asc(x[i])
        in_maps.append(m)
    return in_maps


def kernel(**inputs):
    in_maps = _prep_inputs(inputs)
    nc, _ = build_program(dbg=DEBUG_REQUEST)
    res = run_bass_kernel_spmd(nc, in_maps, core_ids=list(range(len(in_maps))))
    outs = [np.asarray(r["out"], dtype=np.float32) for r in res.results]
    if DEBUG_REQUEST:
        kernel.debug_results = res.results
    return np.stack(outs, axis=0)
